# version 02
# baseline (speedup 1.0000x reference)
import numpy as np
import ml_dtypes
import concourse.bass as bass
import concourse.mybir as mybir
from concourse.bass_utils import run_bass_kernel_spmd

F32 = mybir.dt.float32
BF16 = mybir.dt.bfloat16
AF = mybir.ActivationFunctionType
ALU = mybir.AluOpType
AX = mybir.AxisListType
BF = ml_dtypes.bfloat16

T = 2048
DM = 1024
NCH = 8
NE = 160
INTER = 512

PHASE_GROUPS = [["moe", "mamba", "moe", "lff", "attn", "lff"]]


class Ctx:

    def __init__(self, nc):
        self.nc = nc
        self.ops = []
        self.cnt = {}
        self.handles = {}
        self.meta = {}
        self.bank_r = [{"vE": 0, "aE": 0} for _ in range(8)]
        self.bank_w = [0 for _ in range(8)]

    def sem(self, name):
        if name not in self.handles:
            self.handles[name] = self.nc.alloc_semaphore(name)
            self.cnt[name] = 0
        return self.handles[name]

    def emit(self, eng, build, waits=(), inc=None, inc_by=1):
        waits = [(n, v) for (n, v) in waits if v > 0]
        for n, _ in waits:
            self.sem(n)
        tgt = None
        if inc is not None:
            self.sem(inc)
            self.cnt[inc] += inc_by
            tgt = self.cnt[inc]

        def fn(e, build=build, waits=tuple(waits), inc=inc, inc_by=inc_by):
            for n, v in waits:
                e.wait_ge(self.handles[n], v)
            ins = build(e)
            if inc is not None:
                ins.then_inc(self.handles[inc], inc_by)
        self.ops.append((eng, fn))
        return tgt

    def bank_acquire(self, b, extra=()):
        r = self.bank_r[b]
        return [("vE", r["vE"]), ("aE", r["aE"])] + list(extra)

    def bank_filled(self, b):
        self.bank_w[b] = self.cnt["tE"]

    def bank_read_waits(self, b, extra=()):
        return [("tE", self.bank_w[b])] + list(extra)

    def bank_read_done(self, b, eng):
        self.bank_r[b][eng] = self.cnt[eng]

    def run(self):
        nc = self.nc
        with nc.Block() as block:
            for eng_name in ("sync", "scalar", "tensor", "vector", "gpsimd"):
                ops = [f for (e, f) in self.ops if e == eng_name]

                def mk(ops):
                    def runner(engine):
                        for f in ops:
                            f(engine)
                    return runner
                getattr(block, eng_name)(mk(ops))


class NS:
    pass



def declare_tensors(nc, phases):
    t = NS()
    d = lambda name, shape, dtype=F32, kind="ExternalInput": nc.dram_tensor(
        name, list(shape), dtype, kind=kind).ap()
    t.xin = d("xin", [DM, T])
    t.xout = d("xout", [DM, T], kind="ExternalOutput")
    t.c_eye_bf = d("c_eye_bf", [128, 128], BF16)
    t.c_eye_f32 = d("c_eye_f32", [128, 128])
    t.c_ones_bf = d("c_ones_bf", [128, 128], BF16)
    t.c_ones_f32 = d("c_ones_f32", [1, 128])
    t.c_onescol_bf = d("c_onescol_bf", [128, 1], BF16)
    if "moe" in phases:
        t.wgt = d("wgt", [NE, DM, INTER])
        t.wdt = d("wdt", [NE, INTER, DM])
        t.wgst = d("wgst", [DM, INTER])
        t.wdst = d("wdst", [INTER, DM])
        t.rwt = d("rwt", [DM, NE])
        t.eb_cm = d("eb_cm", [128, 2])
    if "mamba" in phases:
        t.w_in = d("w_in", [DM, 4256])
        t.w_out = d("w_out", [2 * DM, DM])
        t.dt_bias = d("dt_bias", [1, 32])
        t.a_log = d("a_log", [1, 32])
        t.dskip_rep = d("dskip_rep", [128, 32])
        t.nw_cm = d("nw_cm", [128, 16])
        t.c_tri = d("c_tri", [128, 128])
        t.c_minf = d("c_minf", [128, 128])
        t.c_invfreq = d("c_invfreq", [1, 512])
    if "attn" in phases:
        t.w_attn = d("w_attn", [DM, 3 * DM])
        t.w_proj = d("w_proj", [DM, DM])
        t.ba_cm = d("ba_cm", [128, 24])
        t.bp_cm = d("bp_cm", [128, 8])
        t.c_maskband = d("c_maskband", [128, 4224], BF16)
    if "lff" in phases:
        t.l1wt = d("l1wt", [DM, DM])
        t.lora_down = d("lora_down", [4, 100])
        t.lora_up = d("lora_up", [50, 4])
        t.down_aux = d("down_aux", [100, DM])
        t.upxt = d("upxt", [50, DM])
        t.bl_cm = d("bl_cm", [128, 8])
    return t


def alloc_sbuf(nc, phases):
    s = NS()
    a = lambda name, shape, dtype: nc.alloc_sbuf_tensor(name, list(shape), dtype)
    s.x_fm = a("x_fm", [128, NCH * T], F32)
    s.x_ln = a("x_ln", [128, NCH * T], BF16)
    s.eye_bf = a("eye_bf", [128, 128], BF16)
    s.eye_f32 = a("eye_f32", [128, 128], F32)
    s.ones_bf = a("ones_bf", [128, 128], BF16)
    s.ones_f32 = a("ones_f32", [1, 128], F32)
    s.onescol_bf = a("onescol_bf", [128, 1], BF16)
    s.ln_rows = a("ln_rows", [1, 3 * 512], F32)
    s.ln_sq = a("ln_sq", [128, 2 * 512], BF16)
    s.ln_t = a("ln_t", [128, 2 * 512], F32)
    s.AR = 48384
    s.arena = a("arena", [128, s.AR], BF16)
    s.ar_bf = s.arena.ap()
    s.ar_f32 = s.arena.ap().bitcast(F32)
    if "moe" in phases:
        s.wg_sb = [s.ar_bf[:, 0:8192], s.ar_bf[:, 8192:16384]]
        s.wd_sb = s.ar_bf[:, 16384:24576]
        s.h1w = s.ar_bf[:, 24576:40960]
        s.comb_em = s.ar_bf[:, 40960:45056]
        s.crow = [s.ar_bf[32:33, 43008:45056], s.ar_bf[64:65, 43008:45056]]
        s.silu_sc = s.ar_f32[:, 22528:23552]
        s.rw_sb = s.ar_bf[:, 47104:48384]
        s.eb_sb = a("eb_sb", [128, 2], F32)
    if "mamba" in phases:
        s.xs_fmh = s.ar_bf[:, 0:16384]
        s.y_q = s.ar_bf[:, 16384:24576]
        s.bcdt = s.ar_bf[:, 24576:26624]
        s.wslab = [s.ar_bf[:, 26624:27648], s.ar_bf[:, 27648:28672]]
        s.cs_c = s.ar_bf[:, 28672:29696]
        s.bhat = s.ar_bf[:, 29696:31744]
        s.ctil = s.ar_bf[:, 31744:33792]
        s.cbar = s.ar_bf[:, 33792:35840]
        s.g_bf = s.ar_bf[0:64, 35840:37888]
        MF = 18944
        s.dt_tm = s.ar_f32[:, MF:MF + 512]
        s.cumdt = s.ar_f32[:, MF + 512:MF + 1024]
        s.L_tm = s.ar_f32[:, MF + 1024:MF + 1536]
        s.L_emc = s.ar_f32[0:32, MF + 1536:MF + 1664]
        s.th_sc = s.ar_f32[:, MF + 1664:MF + 2176]
        s.mrows = s.ar_f32[0:1, MF + 2176:MF + 2688]
        s.fmscr = s.ar_bf[:, 2 * (MF + 2688):2 * (MF + 2688) + 512]
        s.attnT_m = s.ar_bf[:, 2 * (MF + 2944):2 * (MF + 2944) + 128]
        s.g_f32 = s.ar_f32[0:64, MF + 3008:MF + 5056]
        s.dtb_bc = a("dtb_bc", [128, 32], F32)
        s.nea_bc = a("nea_bc", [128, 32], F32)
        s.tri_sb = a("tri_sb", [128, 128], F32)
        s.minf_sb = a("minf_sb", [128, 128], F32)
        s.invf_row = a("invf_row", [1, 512], F32)
        s.dtb_row = a("dtb_row", [1, 32], F32)
        s.alog_row = a("alog_row", [1, 32], F32)
        s.dskip_sb = a("dskip_sb", [128, 32], F32)
        s.nw_sb = a("nw_sb", [128, 16], F32)
    if "lff" in phases:
        s.weff = s.ar_bf[:, 0:8192]
        s.l1w_bf = s.ar_bf[:, 8192:16384]
        s.upxt_sb = s.ar_f32[0:50, 8192:9216]
        s.p1_sb = s.ar_f32[0:4, 9216:10240]
        s.p2_sb = s.ar_f32[0:100, 10240:11264]
        s.dnaux_sb = s.ar_f32[0:100, 11264:12288]
        s.ld_sb = s.ar_f32[0:4, 12288:12388]
        s.lu_sb = s.ar_f32[0:50, 12416:12420]
        s.bl_sb = a("bl_sb", [128, 8], F32)
        s.hbl_sb = a("hbl_sb", [128, 8], F32)
    if "attn" in phases:
        s.qkv_g = s.ar_bf[:, 0:12288]
        s.attnout = s.ar_bf[0:64, 12288:20480]
        s.maskband = s.ar_bf[:, 20480:24704]
        s.v_tm = s.ar_bf[:, 24704:25728]
        s.attn_sb = [s.ar_bf[:, 25728:27776], s.ar_bf[:, 27776:29824]]
        s.attnT_sb = [s.ar_bf[:, 29824:31872], s.ar_bf[:, 31872:33920]]
        s.wproj_sb = s.ar_bf[0:64, 33920:38016]
        s.wattn_sb = [s.ar_bf[:, 38016:39040], s.ar_bf[:, 39040:40064]]
        s.ba_sb = s.ar_f32[:, 24000:24024]
        s.bp_sb = s.ar_f32[:, 24024:24032]
        s.qba_sb = s.ar_f32[:, 24032:24040]
        s.atc = s.ar_f32[:, 24040:24056]
    return s


def alloc_psum(nc):
    p = NS()
    p.b = [nc.alloc_psum_tensor("pb%d" % i, [128, 512], F32) for i in range(8)]
    return p



def emit_load_x(ctx, t, s):
    src = t.xin.rearrange("(c p) t -> p c t", p=128)
    dst = s.x_fm.ap().rearrange("p (c t) -> p c t", c=NCH)
    ctx.emit("sync", lambda e: e.dma_start(out=dst, in_=src), inc="dS", inc_by=16)
    ctx.meta["x_dS"] = ctx.cnt["dS"]


def emit_load_consts(ctx, t, s, phases):
    pairs = [(s.eye_bf, t.c_eye_bf), (s.eye_f32, t.c_eye_f32),
             (s.ones_bf, t.c_ones_bf), (s.ones_f32, t.c_ones_f32),
             (s.onescol_bf, t.c_onescol_bf)]
    if "moe" in phases:
        pairs += [(s.eb_sb, t.eb_cm)]
    for dst, src in pairs:
        ctx.emit("sync", lambda e, d=dst, sr=src: e.dma_start(out=d[:], in_=sr),
                 inc="dS", inc_by=16)
    ctx.meta["consts_dS"] = ctx.cnt["dS"]
    ctx.meta["consts_dW"] = ctx.cnt.get("dW", 0)


def emit_ln(ctx, t, s, p):
    dep_v = ctx.meta.get("xfm_vE", 0)
    first_dma = [("dS", max(ctx.meta["x_dS"], ctx.meta["consts_dS"]))]
    R = s.ln_rows
    for nt in range(4):
        t0 = nt * 512
        for c in range(NCH):
            sq = s.ln_sq[:, (c % 2) * 512:(c % 2) * 512 + 512]
            w = [("tE", ctx.meta.get("ln_sq_tE_%d" % (c % 2), 0))]
            if c == 0 and nt == 0:
                w += first_dma + [("vE", dep_v)]
            ctx.emit("scalar", lambda e, c=c, sq=sq, t0=t0: e.activation(
                out=sq, in_=s.x_fm[:, c * T + t0:c * T + t0 + 512], func=AF.Square),
                waits=w, inc="aE")
            aq = ctx.cnt["aE"]
            wv = first_dma if (c == 0 and nt == 0) else ()
            ctx.emit("vector", lambda e, c=c, t0=t0: e.tensor_copy(
                out=s.x_ln[:, c * T + t0:c * T + t0 + 512],
                in_=s.x_fm[:, c * T + t0:c * T + t0 + 512]), waits=wv, inc="vE")
            vq = ctx.cnt["vE"]
            wsum = ctx.bank_acquire(4, [("vE", vq)]) if c == 0 else [("vE", vq)]
            ctx.emit("tensor", lambda e, c=c, t0=t0: e.matmul(
                p.b[4][0:1, :], s.onescol_bf[:],
                s.x_ln[:, c * T + t0:c * T + t0 + 512],
                start=(c == 0), stop=(c == NCH - 1)), waits=wsum,
                inc="tE" if c == NCH - 1 else None)
            wsq = ctx.bank_acquire(5, [("aE", aq)]) if c == 0 else [("aE", aq)]
            tq = ctx.emit("tensor", lambda e, c=c, sq=sq: e.matmul(
                p.b[5][0:1, :], s.onescol_bf[:], sq,
                start=(c == 0), stop=(c == NCH - 1)), waits=wsq, inc="tE")
            if tq is None:
                tq = ctx.cnt["tE"]
            ctx.meta["ln_sq_tE_%d" % (c % 2)] = tq
        ctx.bank_filled(4)
        ctx.bank_filled(5)
        ctx.emit("vector", lambda e: e.tensor_scalar(
            out=R[:, 0:512], in0=p.b[4][0:1, :], scalar1=1.0 / DM, scalar2=None,
            op0=ALU.mult), waits=ctx.bank_read_waits(4), inc="vE")
        ctx.emit("vector", lambda e: e.tensor_copy(out=R[:, 512:1024], in_=p.b[5][0:1, :]),
                 waits=ctx.bank_read_waits(5), inc="vE")
        ctx.bank_read_done(4, "vE")
        ctx.bank_read_done(5, "vE")
        ctx.emit("vector", lambda e: e.tensor_tensor(
            out=R[:, 1024:1536], in0=R[:, 0:512], in1=R[:, 0:512], op=ALU.mult),
            inc="vE")
        ctx.emit("vector", lambda e: e.scalar_tensor_tensor(
            out=R[:, 512:1024], in0=R[:, 512:1024], scalar=1.0 / DM,
            in1=R[:, 1024:1536], op0=ALU.mult, op1=ALU.subtract), inc="vE")
        vq = ctx.emit("vector", lambda e: e.tensor_scalar(
            out=R[:, 512:1024], in0=R[:, 512:1024], scalar1=1e-5, scalar2=None,
            op0=ALU.add), inc="vE")
        aq1 = ctx.emit("scalar", lambda e: e.activation(
            out=R[:, 512:1024], in_=R[:, 512:1024], func=AF.Ln),
            waits=[("vE", vq)], inc="aE")
        vq = ctx.emit("vector", lambda e: e.tensor_scalar(
            out=R[:, 512:1024], in0=R[:, 512:1024], scalar1=-0.5, scalar2=None,
            op0=ALU.mult), waits=[("aE", aq1)], inc="vE")
        aq = ctx.emit("scalar", lambda e: e.activation(
            out=R[:, 512:1024], in_=R[:, 512:1024], func=AF.Exp),
            waits=[("vE", vq)], inc="aE")
        ctx.emit("tensor", lambda e: e.matmul(
            p.b[6][:], s.ones_f32[:], R[:, 0:512], start=True, stop=True),
            waits=ctx.bank_acquire(6, [("aE", aq)]), inc="tE")
        ctx.bank_filled(6)
        ctx.emit("tensor", lambda e: e.matmul(
            p.b[7][:], s.ones_f32[:], R[:, 512:1024], start=True, stop=True),
            waits=ctx.bank_acquire(7), inc="tE")
        ctx.bank_filled(7)
        for c in range(NCH):
            sc = s.ln_t[:, (c % 2) * 512:(c % 2) * 512 + 512]
            w = ctx.bank_read_waits(6) + ctx.bank_read_waits(7) if c == 0 else ()
            ctx.emit("vector", lambda e, c=c, sc=sc, t0=t0: e.tensor_sub(
                out=sc, in0=s.x_fm[:, c * T + t0:c * T + t0 + 512], in1=p.b[6][:]),
                waits=w, inc="vE")
            ctx.emit("vector", lambda e, c=c, sc=sc, t0=t0: e.tensor_mul(
                out=s.x_ln[:, c * T + t0:c * T + t0 + 512], in0=sc, in1=p.b[7][:]),
                inc="vE")
        ctx.bank_read_done(6, "vE")
        ctx.bank_read_done(7, "vE")
    ctx.meta["ln_vE"] = ctx.cnt["vE"]



def emit_router(ctx, t, s, p):
    scr = s.ar_f32[:, 12288:20480]
    probs_em = lambda ch: scr[:, ch * T:(ch + 1) * T]
    ptm = lambda tc: scr[:, 2 * T + tc * 160: 2 * T + tc * 160 + 160]
    work = scr[:, 2 * T + 16 * 160: 2 * T + 16 * 160 + 160]
    msk = scr[:, 2 * T + 17 * 160: 2 * T + 18 * 160]
    cols = scr[:, 2 * T + 18 * 160: 2 * T + 18 * 160 + 8]
    ctm = lambda tc: s.wg_sb[0][:, tc * 160: tc * 160 + 160]
    ln_v = ctx.meta["ln_vE"]
    ctx.emit("gpsimd", lambda e: e.dma_start(
        out=s.rw_sb.rearrange("p (kc n) -> p kc n", kc=8),
        in_=t.rwt.rearrange("(kc p) n -> p kc n", p=128)),
        waits=[("tE", ctx.cnt["tE"]), ("vE", ctx.cnt["vE"]),
               ("aE", ctx.cnt.get("aE", 0))], inc="dW", inc_by=16)
    ctx.meta["rw_dW"] = ctx.cnt["dW"]
    for nt in range(4):
        t0 = nt * 512
        for part in range(2):
            M = 128 if part == 0 else 32
            b = 4 + part
            for kc in range(8):
                w = ctx.bank_acquire(b, [("vE", ln_v), ("dW", ctx.meta["rw_dW"])]
                                     ) if kc == 0 else ()
                ctx.emit("tensor", lambda e, kc=kc, part=part, M=M, t0=t0, b=b:
                         e.matmul(p.b[b][0:M, :],
                                  s.rw_sb[:, kc * NE + part * 128:
                                          kc * NE + part * 128 + M],
                                  s.x_ln[:, kc * T + t0: kc * T + t0 + 512],
                                  start=(kc == 0), stop=(kc == 7)),
                         waits=w, inc="tE" if kc == 7 else None)
            ctx.bank_filled(b)
            ctx.emit("scalar", lambda e, part=part, M=M, t0=t0, b=b:
                     e.activation(out=probs_em(part)[0:M, t0:t0 + 512],
                                  in_=p.b[b][0:M, :], func=AF.Sigmoid,
                                  bias=s.eb_sb[0:M, part:part + 1]),
                     waits=ctx.bank_read_waits(b), inc="aE")
            ctx.bank_read_done(b, "aE")
    aq_sig = ctx.cnt["aE"]
    for tc in range(16):
        for part in range(2):
            M = 128 if part == 0 else 32
            b = 6 + (tc * 2 + part) % 2
            ctx.emit("tensor", lambda e, tc=tc, part=part, M=M, b=b:
                     e.transpose(p.b[b][0:128, 0:M],
                                 probs_em(part)[0:M, tc * 128:(tc + 1) * 128],
                                 s.eye_f32[0:M, 0:M]),
                     waits=ctx.bank_acquire(b, [("aE", aq_sig)]), inc="tE")
            ctx.bank_filled(b)
            ctx.emit("vector", lambda e, tc=tc, part=part, M=M, b=b:
                     e.tensor_copy(out=ptm(tc)[:, part * 128:part * 128 + M],
                                   in_=p.b[b][0:128, 0:M]),
                     waits=ctx.bank_read_waits(b), inc="vE")
            ctx.bank_read_done(b, "vE")
    scol = scr[:, 2 * T + 18 * 160: 2 * T + 18 * 160 + 16]
    mxcol = scr[:, 2 * T + 18 * 160 + 16: 2 * T + 18 * 160 + 32]
    sccol = scr[:, 2 * T + 18 * 160 + 32: 2 * T + 18 * 160 + 48]
    def vemit(build):
        vq = ctx.cnt["vE"]
        return ctx.emit("vector", build, waits=[("vE", vq)], inc="vE")
    for tc in range(16):
        vemit(lambda e, tc=tc: e.tensor_copy(out=work, in_=ptm(tc)))
        for it in range(8):
            vemit(lambda e, tc=tc: e.reduce_max(
                out=mxcol[:, tc:tc + 1], in_=work, axis=AX.X))
            if it < 7:
                vemit(lambda e, tc=tc: e.tensor_scalar(
                    out=msk, in0=work, scalar1=mxcol[:, tc:tc + 1], scalar2=None,
                    op0=ALU.is_ge))
                vemit(lambda e: e.scalar_tensor_tensor(
                    out=work, in0=msk, scalar=-1e30, in1=work,
                    op0=ALU.mult, op1=ALU.add))
        vemit(lambda e, tc=tc: e.tensor_scalar(
            out=msk, in0=ptm(tc), scalar1=mxcol[:, tc:tc + 1], scalar2=None,
            op0=ALU.is_ge))
        vemit(lambda e, tc=tc: e.tensor_mul(out=msk, in0=msk, in1=ptm(tc)))
        vemit(lambda e, tc=tc: e.tensor_copy(out=ctm(tc), in_=msk))
        vemit(lambda e, tc=tc: e.reduce_sum(
            out=scol[:, tc:tc + 1], in_=msk, axis=AX.X))
    vemit(lambda e: e.tensor_scalar(
        out=scol, in0=scol, scalar1=1e-9, scalar2=None, op0=ALU.add))
    vemit(lambda e: e.reciprocal(out=sccol, in_=scol))
    vemit(lambda e: e.tensor_scalar(
        out=sccol, in0=sccol, scalar1=2.5, scalar2=None, op0=ALU.mult))
    for tc in range(16):
        vemit(lambda e, tc=tc: e.tensor_scalar(
            out=ctm(tc), in0=ctm(tc), scalar1=sccol[:, tc:tc + 1], scalar2=None,
            op0=ALU.mult))
    vq_c = ctx.cnt["vE"]
    for tc in range(16):
        for part in range(2):
            M = 128 if part == 0 else 32
            b = 6 + (tc * 2 + part) % 2
            pbf = p.b[b].ap().bitcast(BF16)
            ctx.emit("tensor", lambda e, tc=tc, part=part, M=M, pbf=pbf:
                     e.transpose(pbf[0:M, 0:128],
                                 ctm(tc)[:, part * 128:part * 128 + M],
                                 s.eye_bf[:]),
                     waits=ctx.bank_acquire(b, [("vE", vq_c)]), inc="tE")
            ctx.bank_filled(b)
            ctx.emit("vector", lambda e, tc=tc, part=part, M=M, pbf=pbf:
                     e.tensor_copy(
                         out=s.comb_em[0:M, part * T + tc * 128:
                                       part * T + (tc + 1) * 128],
                         in_=pbf[0:M, 0:128]),
                     waits=ctx.bank_read_waits(b), inc="vE")
            ctx.bank_read_done(b, "vE")
    ctx.meta["comb_vE"] = ctx.cnt["vE"]
    ctx.meta["router_tE"] = ctx.cnt["tE"]


def emit_moe(ctx, t, s, p):
    experts = [-1] + list(range(NE))
    groups = [experts[i:i + 2] for i in range(0, len(experts), 2)]
    comb_v = ctx.meta["comb_vE"]
    rt_t = ctx.meta["router_tE"]
    ln_v = ctx.meta["ln_vE"]
    wg_free = [rt_t, 0]
    crow_free = 0
    down_done = 0
    ntile = 0

    for g, grp in enumerate(groups):
        sl = g % 2
        dma_w0 = [("tE", wg_free[sl])]
        for j, e_id in enumerate(grp):
            wg_src = (t.wgst if e_id < 0 else t.wgt[e_id]).rearrange(
                "(kc p) i -> p kc i", p=128)
            wd_src = (t.wdst if e_id < 0 else t.wdt[e_id]).rearrange(
                "(kc p) o -> p kc o", p=128)
            wg_dst = s.wg_sb[sl][:, j * 4096:(j + 1) * 4096].rearrange(
                "p (kc i) -> p kc i", kc=8)
            wd_dst = s.wd_sb[:, j * 4096:(j + 1) * 4096].rearrange(
                "p (kc o) -> p kc o", kc=4)
            ctx.emit("gpsimd", lambda e, a=wg_dst, b=wg_src: e.dma_start(out=a, in_=b),
                     waits=dma_w0 if j == 0 else (), inc="dW", inc_by=16)
            ctx.emit("gpsimd", lambda e, a=wd_dst, b=wd_src: e.dma_start(out=a, in_=b),
                     waits=[("tE", down_done)] if j == 0 else (), inc="dW", inc_by=16)
            crow_dst = s.crow[j]
            w_cr = [("vE", comb_v)] + ([("tE", crow_free)] if j == 0 else [])
            if e_id < 0:
                ctx.emit("vector", lambda e, a=crow_dst: e.memset(a, 1.0),
                         waits=w_cr, inc="vE")
                ctx.meta["crow_memset_vE"] = ctx.cnt["vE"]
            else:
                ch, r = divmod(e_id, 128)
                csrc = s.comb_em[r:r + 1, ch * T:(ch + 1) * T]
                ctx.emit("sync", lambda e, a=crow_dst, b=csrc: e.dma_start(out=a, in_=b),
                         waits=w_cr, inc="dS", inc_by=16)
        dW_at = ctx.cnt["dW"]
        dS_at = ctx.cnt["dS"]
        for j, e_id in enumerate(grp):
            for nt in range(4):
                t0 = nt * 512
                cb = 2 + nt % 2
                ctx.emit("tensor", lambda e, j=j, t0=t0, cb=cb: e.matmul(
                    p.b[cb][:], s.ones_bf[32 * (j + 1):32 * (j + 1) + 1, :],
                    s.crow[j][:, t0:t0 + 512],
                    start=True, stop=True),
                    waits=ctx.bank_acquire(cb, [("dS", dS_at),
                          ("vE", ctx.meta.get("crow_memset_vE", 0))]), inc="tE")
                ctx.bank_filled(cb)
                for mi in range(4):
                    gb = (nt * 4 + mi) % 2
                    w = ctx.bank_acquire(gb)
                    if nt == 0 and mi == 0 and j == 0:
                        w += [("dW", dW_at), ("vE", ln_v)]
                    for kc in range(8):
                        ctx.emit("tensor", lambda e, sl=sl, j=j, mi=mi, kc=kc, t0=t0,
                                 gb=gb: e.matmul(
                                     p.b[gb][:],
                                     s.wg_sb[sl][:, j * 4096 + kc * 512 + mi * 128:
                                                 j * 4096 + kc * 512 + mi * 128 + 128],
                                     s.x_ln[:, kc * T + t0:kc * T + t0 + 512],
                                     start=(kc == 0), stop=(kc == 7)),
                                 waits=w if kc == 0 else (),
                                 inc="tE" if kc == 7 else None)
                    ctx.bank_filled(gb)
                    sj = (nt * 4 + mi) % 2
                    sc = s.silu_sc[:, sj * 512:(sj + 1) * 512]
                    ctx.emit("scalar", lambda e, gb=gb, sc=sc: e.activation(
                        out=sc, in_=p.b[gb][:], func=AF.Silu),
                        waits=ctx.bank_read_waits(
                            gb, [("vE", ctx.meta.get("silu_vE_%d" % sj, 0))]),
                        inc="aE")
                    ctx.bank_read_done(gb, "aE")
                    aq = ctx.cnt["aE"]
                    dst = s.h1w[:, (j * 4 + mi) * T + t0:(j * 4 + mi) * T + t0 + 512]
                    w2 = ctx.bank_read_waits(cb, [("aE", aq)])
                    if nt == 0 and mi == 0 and j == 0:
                        w2 += [("tE", down_done)]
                    vq = ctx.emit("vector", lambda e, sc=sc, dst=dst, cb=cb:
                                  e.tensor_tensor(out=dst, in0=sc, in1=p.b[cb][:],
                                                  op=ALU.mult),
                                  waits=w2, inc="vE")
                    ctx.meta["silu_vE_%d" % sj] = vq
                    ctx.bank_read_done(cb, "vE")
        h1w_v = ctx.cnt["vE"]
        wg_free[sl] = ctx.cnt["tE"]
        crow_free = ctx.cnt["tE"]
        for mo in range(8):
            for nt in range(4):
                t0 = nt * 512
                db = 4 + ntile % 2
                nmm = len(grp) * 4
                i = 0
                for j in range(len(grp)):
                    for kc in range(4):
                        first = (i == 0)
                        i += 1
                        w = ()
                        if first:
                            w = ctx.bank_acquire(db)
                            if mo == 0 and nt == 0:
                                w += [("vE", h1w_v)]
                        ctx.emit("tensor", lambda e, j=j, kc=kc, mo=mo, t0=t0,
                                 db=db, st=first, sp=(i == nmm): e.matmul(
                                     p.b[db][:],
                                     s.wd_sb[:, j * 4096 + kc * 1024 + mo * 128:
                                             j * 4096 + kc * 1024 + mo * 128 + 128],
                                     s.h1w[:, (j * 4 + kc) * T + t0:
                                           (j * 4 + kc) * T + t0 + 512],
                                     start=st, stop=sp),
                                 waits=w, inc="tE" if i == nmm else None)
                ctx.bank_filled(db)
                dst = s.x_fm[:, mo * T + t0:mo * T + t0 + 512]
                ctx.emit("vector", lambda e, db=db, dst=dst:
                         e.scalar_tensor_tensor(out=dst, in0=p.b[db][:], scalar=0.5,
                                                in1=dst, op0=ALU.mult, op1=ALU.add),
                         waits=ctx.bank_read_waits(db), inc="vE")
                ctx.bank_read_done(db, "vE")
                ntile += 1
        down_done = ctx.cnt["tE"]
    ctx.meta["xfm_vE"] = ctx.cnt["vE"]


def emit_dump(ctx, nc, name, ap, cols, dtype, rows=128):
    out = nc.dram_tensor(name, [rows, cols], dtype, kind="ExternalOutput").ap()
    ctx.emit("sync", lambda e: e.dma_start(out=out, in_=ap),
             waits=[("vE", ctx.cnt.get("vE", 0)), ("aE", ctx.cnt.get("aE", 0)),
                    ("tE", ctx.cnt.get("tE", 0)), ("dW", ctx.cnt.get("dW", 0))],
             inc="dS", inc_by=16)


def emit_store_x(ctx, t, s):
    src = s.x_fm.ap().rearrange("p (c t) -> p c t", c=NCH)
    dst = t.xout.rearrange("(c p) t -> p c t", p=128)
    ctx.emit("sync", lambda e: e.dma_start(out=dst, in_=src),
             waits=[("vE", ctx.cnt.get("vE", 0)), ("aE", ctx.cnt.get("aE", 0)),
                    ("tE", ctx.cnt.get("tE", 0))], inc="dS", inc_by=16)
    ctx.emit("sync", lambda e: e.nop(), waits=[("dS", ctx.cnt["dS"])])


def emit_mamba(ctx, t, s, p):
    ln_v = ctx.meta["ln_vE"]
    ent_t, ent_v, ent_a = ctx.cnt["tE"], ctx.cnt["vE"], ctx.cnt.get("aE", 0)
    bar = [("tE", ent_t), ("vE", ent_v), ("aE", ent_a)]
    PI = 3.14159265358979
    TWO_PI = 2 * PI
    INV_2PI = 1.0 / TWO_PI
    MAGIC = 1.5 * 2.0 ** 23

    def ve(build, extra=()):
        vq = ctx.cnt["vE"]
        return ctx.emit("vector", build, waits=[("vE", vq)] + list(extra), inc="vE")

    for dst, srcap in ((s.tri_sb, t.c_tri), (s.minf_sb, t.c_minf),
                       (s.invf_row, t.c_invfreq), (s.dtb_row, t.dt_bias),
                       (s.alog_row, t.a_log), (s.dskip_sb, t.dskip_rep),
                       (s.nw_sb, t.nw_cm)):
        ctx.emit("sync", lambda e, a=dst, b=srcap: e.dma_start(out=a[:], in_=b),
                 waits=bar, inc="dS", inc_by=16)
    dS0 = ctx.cnt["dS"]
    ctx.emit("tensor", lambda e: e.matmul(
        p.b[6][:, 0:32], s.ones_f32[:], s.dtb_row[:], start=True, stop=True),
        waits=ctx.bank_acquire(6, [("dS", dS0)] + bar), inc="tE")
    ctx.bank_filled(6)
    ctx.emit("vector", lambda e: e.tensor_copy(out=s.dtb_bc[:], in_=p.b[6][:, 0:32]),
             waits=ctx.bank_read_waits(6), inc="vE")
    ctx.bank_read_done(6, "vE")
    aq = ctx.emit("scalar", lambda e: e.activation(
        out=s.mrows[:, 0:32], in_=s.alog_row[:], func=AF.Exp),
        waits=[("dS", dS0)] + bar, inc="aE")
    ve(lambda e: e.tensor_scalar(out=s.mrows[:, 0:32], in0=s.mrows[:, 0:32],
                                 scalar1=-1.0, scalar2=None, op0=ALU.mult),
       extra=[("aE", aq)])
    vq = ctx.cnt["vE"]
    ctx.emit("tensor", lambda e: e.matmul(
        p.b[6][:, 0:32], s.ones_f32[:], s.mrows[:, 0:32], start=True, stop=True),
        waits=ctx.bank_acquire(6, [("vE", vq)]), inc="tE")
    ctx.bank_filled(6)
    ctx.emit("vector", lambda e: e.tensor_copy(out=s.nea_bc[:], in_=p.b[6][:, 0:32]),
             waits=ctx.bank_read_waits(6), inc="vE")
    ctx.bank_read_done(6, "vE")
    ve(lambda e: e.memset(s.g_f32, 0.0))
    ve(lambda e: e.memset(s.g_bf, 0.0))

    scan_done_tE = 0
    for h2 in range(2):
        th0 = h2 * 1024
        for mi, m in enumerate(range(16, 34)):
            sl = mi % 2
            mw = 128 if m < 33 else 32
            ctx.emit("gpsimd", lambda e, m=m, sl=sl, mw=mw: e.dma_start(
                out=s.wslab[sl][:, 0:8 * mw].rearrange("p (kc m) -> p kc m", kc=8),
                in_=t.w_in[:, m * 128: m * 128 + mw].rearrange(
                    "(kc p) m -> p kc m", p=128)),
                waits=[("tE", ctx.meta.get("mm_wslab_free_%d" % sl, 0))] +
                (bar if h2 == 0 and mi < 2 else []), inc="dW", inc_by=16)
            dWs = ctx.cnt["dW"]
            for nth in range(2):
                t0 = th0 + nth * 512
                b = (mi * 2 + nth) % 2
                for kc in range(8):
                    w = ()
                    if kc == 0:
                        w = ctx.bank_acquire(b, [("dW", dWs), ("vE", ln_v),
                                                 ("tE", scan_done_tE)])
                    ctx.emit("tensor", lambda e, sl=sl, kc=kc, t0=t0, b=b, mw=mw:
                             e.matmul(p.b[b][0:mw, 0:512],
                                      s.wslab[sl][:, kc * mw:(kc + 1) * mw],
                                      s.x_ln[:, kc * T + t0: kc * T + t0 + 512],
                                      start=(kc == 0), stop=(kc == 7)),
                             waits=w, inc="tE" if kc == 7 else None)
                tq = ctx.cnt["tE"]
                if nth == 1:
                    ctx.meta["mm_wslab_free_%d" % sl] = tq
                ctx.bank_filled(b)
                lt0 = nth * 512
                if m < 32:
                    dst = s.xs_fmh[:, (m - 16) * 1024 + lt0:(m - 16) * 1024 + lt0 + 512]
                    ctx.emit("scalar", lambda e, b=b, dst=dst: e.activation(
                        out=dst, in_=p.b[b][:], func=AF.Identity),
                        waits=ctx.bank_read_waits(b), inc="aE")
                    ctx.bank_read_done(b, "aE")
                elif m == 32:
                    dst = s.bcdt[:, lt0:lt0 + 512]
                    ctx.emit("scalar", lambda e, b=b, dst=dst: e.activation(
                        out=dst, in_=p.b[b][:], func=AF.Identity),
                        waits=ctx.bank_read_waits(b), inc="aE")
                    ctx.bank_read_done(b, "aE")
                else:
                    dst = s.bcdt[0:32, 1024 + lt0:1024 + lt0 + 512]
                    ctx.emit("scalar", lambda e, b=b, dst=dst: e.activation(
                        out=dst, in_=p.b[b][0:32, :], func=AF.Identity),
                        waits=ctx.bank_read_waits(b), inc="aE")
                    ctx.bank_read_done(b, "aE")
        inproj_a = ctx.cnt["aE"]
        for c in range(8):
            gc = h2 * 8 + c
            co = gc * 32
            pbf = p.b[7].ap().bitcast(BF16)
            ctx.emit("tensor", lambda e, c=c, pbf=pbf: e.transpose(
                pbf[0:128, 0:32],
                s.bcdt[0:32, 1024 + c * 128:1024 + (c + 1) * 128],
                s.eye_bf[0:32, 0:32]),
                waits=ctx.bank_acquire(7, [("aE", inproj_a)]), inc="tE")
            ctx.bank_filled(7)
            DT = s.dt_tm[:, co:co + 32]
            SC = s.th_sc[:, 0:32]
            ve(lambda e, DT=DT, pbf=pbf: e.tensor_copy(out=DT, in_=pbf[0:128, 0:32]),
               extra=ctx.bank_read_waits(7))
            ctx.bank_read_done(7, "vE")
            ve(lambda e, DT=DT: e.tensor_add(out=DT, in0=DT, in1=s.dtb_bc[:]))
            vq = ctx.cnt["vE"]
            aq = ctx.emit("scalar", lambda e, DT=DT, SC=SC: e.activation(
                out=SC, in_=DT, func=AF.Abs), waits=[("vE", vq)], inc="aE")
            ve(lambda e, SC=SC: e.tensor_scalar(
                out=SC, in0=SC, scalar1=-1.0, scalar2=None, op0=ALU.mult),
               extra=[("aE", aq)])
            aq = ctx.emit("scalar", lambda e, SC=SC: e.activation(
                out=SC, in_=SC, func=AF.Exp), waits=[("vE", ctx.cnt["vE"])], inc="aE")
            ve(lambda e, DT=DT: e.tensor_relu(out=s.th_sc[:, 32:64], in_=DT),
               extra=[("aE", aq)])
            ve(lambda e, SC=SC: e.tensor_scalar(
                out=SC, in0=SC, scalar1=1.0, scalar2=None, op0=ALU.add))
            aq = ctx.emit("scalar", lambda e, SC=SC: e.activation(
                out=SC, in_=SC, func=AF.Ln), waits=[("vE", ctx.cnt["vE"])], inc="aE")
            ve(lambda e, DT=DT, SC=SC: e.tensor_add(
                out=DT, in0=s.th_sc[:, 32:64], in1=SC), extra=[("aE", aq)])
            CU = s.cumdt[:, co:co + 32]
            vq = ctx.cnt["vE"]
            ctx.emit("tensor", lambda e, DT=DT: e.matmul(
                p.b[7][:, 0:32], s.tri_sb[:], DT, start=True, stop=(gc == 0)),
                waits=ctx.bank_acquire(7, [("vE", vq), ("dS", dS0)]), inc="tE")
            if gc > 0:
                ctx.emit("tensor", lambda e: e.matmul(
                    p.b[7][:, 0:32], s.ones_f32[:], s.mrows[:, 64:96],
                    start=False, stop=True),
                    waits=[("dS", ctx.meta.get("mm_carry_dS", 0))], inc="tE")
                ctx.meta["mm_carry_used"] = ctx.cnt["tE"]
            ctx.bank_filled(7)
            ve(lambda e, CU=CU: e.tensor_copy(out=CU, in_=p.b[7][:, 0:32]),
               extra=ctx.bank_read_waits(7))
            ctx.bank_read_done(7, "vE")
            LL = s.L_tm[:, co:co + 32]
            ve(lambda e, LL=LL, CU=CU: e.tensor_mul(out=LL, in0=CU, in1=s.nea_bc[:]))
            vq = ctx.cnt["vE"]
            ctx.emit("sync", lambda e, CU=CU: e.dma_start(
                out=s.mrows[:, 64:96], in_=CU[127:128, :]),
                waits=[("vE", vq), ("tE", ctx.meta.get("mm_carry_used", 0))],
                inc="dS", inc_by=16)
            ctx.meta["mm_carry_dS"] = ctx.cnt["dS"]
        prep_v = ctx.cnt["vE"]
        for c in range(8):
            gc = h2 * 8 + c
            co = gc * 32
            qlocal = c % 4
            ctx.emit("tensor", lambda e: e.matmul(
                p.b[6][:], s.ones_f32[:], s.invf_row[:], start=True, stop=True),
                waits=ctx.bank_acquire(6, [("vE", prep_v)]), inc="tE")
            ctx.bank_filled(6)
            cum_exp = s.cumdt[:, co:co + 32].rearrange("p (h one) -> p h one", one=1).broadcast_to(
                [128, 32, 16])
            TH = s.th_sc[:, 0:512]
            TH3 = TH.rearrange("p (h j) -> p h j", h=32)
            P63 = p.b[6].ap().rearrange("p (h j) -> p h j", h=32)
            ve(lambda e, cum_exp=cum_exp, TH3=TH3, P63=P63: e.tensor_mul(
                out=TH3, in0=cum_exp, in1=P63),
               extra=ctx.bank_read_waits(6))
            ctx.bank_read_done(6, "vE")
            CSF = s.cs_c.bitcast(F32)
            ve(lambda e, TH=TH, CSF=CSF: e.tensor_scalar(
                out=CSF, in0=TH, scalar1=INV_2PI, scalar2=MAGIC,
                op0=ALU.mult, op1=ALU.add),
               extra=[("aE", ctx.meta.get("mm_cs_aE", 0))])
            ve(lambda e, CSF=CSF: e.tensor_scalar(
                out=CSF, in0=CSF, scalar1=MAGIC, scalar2=None,
                op0=ALU.subtract))
            ve(lambda e, TH=TH, CSF=CSF: e.scalar_tensor_tensor(
                out=TH, in0=CSF, scalar=-TWO_PI, in1=TH,
                op0=ALU.mult, op1=ALU.add))
            vq = ctx.cnt["vE"]
            aq = ctx.emit("scalar", lambda e, TH=TH: e.activation(
                out=s.cs_c[:, 512:1024], in_=TH, func=AF.Sin),
                waits=[("vE", vq)], inc="aE")
            aq2 = ctx.emit("scalar", lambda e, TH=TH: e.activation(
                out=TH, in_=TH, func=AF.Abs), waits=[("aE", aq),
                ("vE", ctx.cnt["vE"])], inc="aE")
            ve(lambda e, TH=TH: e.tensor_scalar(
                out=TH, in0=TH, scalar1=-1.0, scalar2=PI / 2,
                op0=ALU.mult, op1=ALU.add), extra=[("aE", aq2)])
            vq = ctx.cnt["vE"]
            aq_cs = ctx.emit("scalar", lambda e, TH=TH: e.activation(
                out=s.cs_c[:, 0:512], in_=TH, func=AF.Sin),
                waits=[("vE", vq)], inc="aE")
            ctx.meta["mm_cs_aE"] = aq_cs
            pbf7 = p.b[7].ap().bitcast(BF16)
            ctx.emit("tensor", lambda e, c=c, pbf7=pbf7: e.transpose(
                pbf7[0:128, 0:128], s.bcdt[:, c * 128:(c + 1) * 128],
                s.eye_bf[:]),
                waits=ctx.bank_acquire(7, [("aE", inproj_a)]), inc="tE")
            ctx.bank_filled(7)
            cosap = s.cs_c[:, 0:512].rearrange("p (h j) -> p h j", h=32)
            sinap = s.cs_c[:, 512:1024].rearrange("p (h j) -> p h j", h=32)
            Bev = pbf7[0:128, 0:64:2].rearrange("p (one j) -> p one j", one=1).broadcast_to(
                [128, 32, 32])
            Bod = pbf7[0:128, 1:64:2].rearrange("p (one j) -> p one j", one=1).broadcast_to(
                [128, 32, 32])
            Bnr = pbf7[0:128, 32:64].rearrange("p (one n) -> p one n", one=1).broadcast_to(
                [128, 32, 32])
            Cev = pbf7[0:128, 64:128:2].rearrange("p (one j) -> p one j", one=1).broadcast_to(
                [128, 32, 32])
            Cod = pbf7[0:128, 65:128:2].rearrange("p (one j) -> p one j", one=1).broadcast_to(
                [128, 32, 32])
            Cnr = pbf7[0:128, 96:128].rearrange("p (one n) -> p one n", one=1).broadcast_to(
                [128, 32, 32])
            for dst_t, ev, od in ((s.bhat, Bev, Bod), (s.ctil, Cev, Cod)):
                de = dst_t.rearrange("p (h n) -> p h n", h=32)[:, :, 0:31:2]
                do = dst_t.rearrange("p (h n) -> p h n", h=32)[:, :, 1:32:2]
                dn = dst_t.rearrange("p (h n) -> p h n", h=32)[:, :, 32:64]
                evj = ev[:, :, 0:16]
                odj = od[:, :, 0:16]
                scr2 = s.fmscr.rearrange("p (h j) -> p h j", h=32)
                cos3 = cosap
                sin3 = sinap
                ve(lambda e, de=de, evj=evj, cos3=cos3: e.tensor_tensor(
                    out=de, in0=cos3, in1=evj, op=ALU.mult),
                   extra=[("aE", aq_cs)] + ctx.bank_read_waits(7) +
                   [("tE", ctx.meta.get("mm_fmscr_used", 0))])
                ve(lambda e, scr2=scr2, odj=odj, sin3=sin3: e.tensor_tensor(
                    out=scr2, in0=sin3, in1=odj, op=ALU.mult))
                ve(lambda e, de=de, scr2=scr2: e.tensor_tensor(
                    out=de, in0=de, in1=scr2, op=ALU.add))
                ve(lambda e, do=do, odj=odj, cos3=cos3: e.tensor_tensor(
                    out=do, in0=cos3, in1=odj, op=ALU.mult))
                ve(lambda e, scr2=scr2, evj=evj, sin3=sin3: e.tensor_tensor(
                    out=scr2, in0=sin3, in1=evj, op=ALU.mult))
                ve(lambda e, do=do, scr2=scr2: e.tensor_tensor(
                    out=do, in0=do, in1=scr2, op=ALU.subtract))
                ve(lambda e, dn=dn, nr=(Bnr if dst_t is s.bhat else Cnr):
                   e.tensor_copy(out=dn, in_=nr))
            ctx.bank_read_done(7, "vE")
            dt_exp = s.dt_tm[:, co:co + 32].rearrange("p (h one) -> p h one", one=1).broadcast_to(
                [128, 32, 64])
            bhat3 = s.bhat.rearrange("p (h n) -> p h n", h=32)
            ve(lambda e, dt_exp=dt_exp, bhat3=bhat3: e.tensor_tensor(
                out=bhat3, in0=bhat3, in1=dt_exp, op=ALU.mult))
            if gc == 0:
                vq_pe = ve(lambda e: e.memset(s.mrows[:, 96:128], 0.0))
                ctx.meta["mm_pe_v"] = vq_pe
            else:
                pco = (gc - 1) * 32
                ctx.emit("sync", lambda e, pco=pco: e.dma_start(
                    out=s.mrows[:, 96:128], in_=s.L_tm[127:128, pco:pco + 32]),
                    waits=[("vE", prep_v), ("tE", ctx.meta.get("mm_rowbc_tE", 0))],
                    inc="dS", inc_by=16)
            ctx.emit("sync", lambda e, co=co: e.dma_start(
                out=s.mrows[:, 128:160], in_=s.L_tm[127:128, co:co + 32]),
                waits=[("vE", prep_v), ("tE", ctx.meta.get("mm_rowbc_tE", 0))],
                inc="dS", inc_by=16)
            dSr = ctx.cnt["dS"]
            ctx.emit("tensor", lambda e: e.matmul(
                p.b[6][:, 0:64], s.ones_f32[:],
                s.mrows[:, 96:160], start=True, stop=True),
                waits=ctx.bank_acquire(6, [("dS", dSr),
                      ("vE", ctx.meta.get("mm_pe_v", 0))]), inc="tE")
            ctx.bank_filled(6)
            ctx.meta["mm_rowbc_tE"] = ctx.cnt["tE"]
            COLF = s.th_sc[:, 0:32]
            FALL = s.th_sc[:, 32:64]
            ve(lambda e, COLF=COLF, co=co: e.tensor_sub(
                out=COLF, in0=s.L_tm[:, co:co + 32], in1=p.b[6][:, 0:32]),
               extra=ctx.bank_read_waits(6) + [("aE", aq_cs)])
            ve(lambda e, FALL=FALL, co=co: e.tensor_sub(
                out=FALL, in0=p.b[6][:, 32:64], in1=s.L_tm[:, co:co + 32]))
            ctx.bank_read_done(6, "vE")
            vq = ctx.cnt["vE"]
            aq = ctx.emit("scalar", lambda e, COLF=COLF: e.activation(
                out=COLF, in_=COLF, func=AF.Exp), waits=[("vE", vq)], inc="aE")
            aq = ctx.emit("scalar", lambda e, FALL=FALL: e.activation(
                out=FALL, in_=FALL, func=AF.Exp), inc="aE")
            colf_exp = COLF.rearrange("p (h one) -> p h one", one=1).broadcast_to([128, 32, 64])
            cbar3 = s.cbar.rearrange("p (h n) -> p h n", h=32)
            ctil3 = s.ctil.rearrange("p (h n) -> p h n", h=32)
            ve(lambda e, colf_exp=colf_exp, cbar3=cbar3, ctil3=ctil3:
               e.tensor_tensor(out=cbar3, in0=ctil3, in1=colf_exp, op=ALU.mult),
               extra=[("aE", aq)])
            ve(lambda e: e.tensor_sub(out=s.mrows[:, 160:192],
                                      in0=s.mrows[:, 128:160],
                                      in1=s.mrows[:, 96:128]),
               extra=[("dS", dSr)])
            vq = ctx.cnt["vE"]
            aq = ctx.emit("scalar", lambda e: e.activation(
                out=s.mrows[:, 160:192], in_=s.mrows[:, 160:192], func=AF.Exp),
                waits=[("vE", vq)], inc="aE")
            ctx.emit("tensor", lambda e: e.matmul(
                p.b[6][:, 64:96], s.ones_f32[:], s.mrows[:, 160:192],
                start=True, stop=True),
                waits=ctx.bank_acquire(6, [("aE", aq)]), inc="tE")
            ctx.bank_filled(6)
            GSC = s.th_sc[:, 64:96]
            ve(lambda e, GSC=GSC: e.tensor_copy(out=GSC[0:64, :],
                                                in_=p.b[6][0:64, 64:96]),
               extra=ctx.bank_read_waits(6))
            ctx.bank_read_done(6, "vE")
            ctx.emit("tensor", lambda e, co=co: e.transpose(
                p.b[6].ap()[0:32, 0:128], s.L_tm[:, co:co + 32], s.eye_f32[:]),
                waits=ctx.bank_acquire(6), inc="tE")
            ctx.bank_filled(6)
            ve(lambda e: e.tensor_copy(out=s.L_emc[:], in_=p.b[6].ap()[0:32, 0:128]),
               extra=ctx.bank_read_waits(6))
            ctx.bank_read_done(6, "vE")
            chunk_v = ctx.cnt["vE"]
            chunk_a = ctx.cnt["aE"]
            for h in range(32):
                base = (h % 2) * 64
                ctx.emit("sync", lambda e, h=h: e.dma_start(
                    out=s.mrows[:, 192:320], in_=s.L_emc[h:h + 1, :]),
                    waits=[("vE", chunk_v), ("tE", ctx.meta.get("mm_d_tE", 0))],
                    inc="dS", inc_by=16)
                dSh = ctx.cnt["dS"]
                ctx.emit("tensor", lambda e: e.matmul(
                    p.b[6][:, 128:256], s.ones_f32[:], s.mrows[:, 192:320],
                    start=True, stop=True),
                    waits=ctx.bank_acquire(6, [("dS", dSh)]), inc="tE")
                ctx.bank_filled(6)
                ctx.meta["mm_d_tE"] = ctx.cnt["tE"]
                DSB = s.th_sc[:, 128:256]
                ve(lambda e, DSB=DSB, co=co, h=h: e.tensor_scalar(
                    out=DSB, in0=p.b[6][:, 128:256],
                    scalar1=s.L_tm[:, co + h:co + h + 1], scalar2=None,
                    op0=ALU.subtract), extra=ctx.bank_read_waits(6))
                ctx.bank_read_done(6, "vE")
                ve(lambda e, DSB=DSB: e.tensor_add(out=DSB, in0=DSB,
                                                   in1=s.minf_sb[:]))
                vq = ctx.cnt["vE"]
                aqD = ctx.emit("scalar", lambda e, DSB=DSB: e.activation(
                    out=DSB, in_=DSB, func=AF.Exp), waits=[("vE", vq)], inc="aE")
                p2bf = p.b[2].ap().bitcast(BF16)
                xs_src = s.xs_fmh[base:base + 64,
                                  (h // 2) * 1024 + c * 128:
                                  (h // 2) * 1024 + (c + 1) * 128]
                ctx.emit("tensor", lambda e, xs_src=xs_src, p2bf=p2bf, base=base:
                         e.transpose(p2bf[0:128, 0:64], xs_src,
                                     s.eye_bf[base:base + 64, base:base + 64]),
                         waits=ctx.bank_acquire(2, [("vE", chunk_v)]), inc="tE")
                ctx.emit("tensor", lambda e, h=h, p2bf=p2bf: e.transpose(
                    p2bf[0:64, 64:192], s.bhat[:, h * 64:(h + 1) * 64],
                    s.eye_bf[:]), inc="tE")
                ctx.emit("tensor", lambda e, h=h, p2bf=p2bf: e.transpose(
                    p2bf[0:64, 192:320], s.ctil[:, h * 64:(h + 1) * 64],
                    s.eye_bf[:]), inc="tE")
                ctx.emit("tensor", lambda e, h=h, p2bf=p2bf: e.transpose(
                    p2bf[0:64, 320:448], s.cbar[:, h * 64:(h + 1) * 64],
                    s.eye_bf[:]), inc="tE")
                ctx.bank_filled(2)
                ve(lambda e, p2bf=p2bf: e.tensor_copy(
                    out=s.fmscr[:, 0:448], in_=p2bf[0:128, 0:448]),
                   extra=ctx.bank_read_waits(2) +
                   [("tE", ctx.meta.get("mm_fmscr_used", 0))])
                ctx.bank_read_done(2, "vE")
                fm_v = ctx.cnt["vE"]
                ve(lambda e, h=h, FALL=FALL: e.tensor_scalar(
                    out=s.bhat[:, h * 64:(h + 1) * 64],
                    in0=s.bhat[:, h * 64:(h + 1) * 64],
                    scalar1=FALL[:, h:h + 1], scalar2=None, op0=ALU.mult))
                bbar_v = ctx.cnt["vE"]
                sb_ = 4 + h % 2
                ctx.emit("tensor", lambda e, sb_=sb_: e.matmul(
                    p.b[sb_][:, 0:128], s.fmscr[0:64, 64:192],
                    s.fmscr[0:64, 192:320], start=True, stop=True),
                    waits=ctx.bank_acquire(sb_, [("vE", fm_v)]), inc="tE")
                ctx.bank_filled(sb_)
                ve(lambda e, sb_=sb_, DSB=DSB: e.tensor_tensor(
                    out=s.attnT_m[:, 0:128], in0=p.b[sb_][:, 0:128], in1=DSB,
                    op=ALU.mult),
                   extra=ctx.bank_read_waits(sb_) +
                   [("aE", aqD), ("tE", ctx.meta.get("mm_attnT_used", 0))])
                ctx.bank_read_done(sb_, "vE")
                ve(lambda e, h=h: e.scalar_tensor_tensor(
                    out=s.attnT_m[:, 0:128], in0=s.eye_bf[:],
                    scalar=s.dskip_sb[:, h:h + 1], in1=s.attnT_m[:, 0:128],
                    op0=ALU.mult, op1=ALU.add))
                at_v = ctx.cnt["vE"]
                ctx.emit("tensor", lambda e, h=h: e.matmul(
                    p.b[3][0:64, 0:128], s.g_bf[:, h * 64:(h + 1) * 64],
                    s.fmscr[0:64, 320:448], start=True, stop=False),
                    waits=ctx.bank_acquire(3, [("vE", at_v)]), inc="tE")
                ctx.emit("tensor", lambda e: e.matmul(
                    p.b[3][0:64, 0:128], s.fmscr[:, 0:64],
                    s.attnT_m[:, 0:128], start=False, stop=True), inc="tE")
                ctx.bank_filled(3)
                ctx.meta["mm_attnT_used"] = ctx.cnt["tE"]
                ydst = s.y_q[base:base + 64,
                             (h // 2) * 512 + qlocal * 128:
                             (h // 2) * 512 + (qlocal + 1) * 128]
                ctx.emit("vector", lambda e, ydst=ydst: e.tensor_copy(
                    out=ydst, in_=p.b[3][0:64, 0:128]),
                    waits=ctx.bank_read_waits(3, [("vE", ctx.meta.get(
                        "mm_yq_free", 0))]), inc="vE")
                ctx.bank_read_done(3, "vE")
                ctx.emit("tensor", lambda e, h=h: e.matmul(
                    p.b[7][0:64, 0:64], s.bhat[:, h * 64:(h + 1) * 64],
                    s.fmscr[:, 0:64], start=True, stop=True),
                    waits=ctx.bank_acquire(7, [("vE", bbar_v)]), inc="tE")
                ctx.bank_filled(7)
                ctx.meta["mm_fmscr_used"] = ctx.cnt["tE"]
                gs = s.g_f32[:, h * 64:(h + 1) * 64]
                ve(lambda e, gs=gs, GSC=GSC, h=h: e.tensor_scalar(
                    out=gs, in0=gs, scalar1=GSC[0:64, h:h + 1], scalar2=None,
                    op0=ALU.mult))
                ve(lambda e, gs=gs: e.tensor_add(out=gs, in0=gs,
                                                 in1=p.b[7][0:64, 0:64]),
                   extra=ctx.bank_read_waits(7))
                ctx.bank_read_done(7, "vE")
                ve(lambda e, gs=gs, h=h: e.tensor_copy(
                    out=s.g_bf[:, h * 64:(h + 1) * 64], in_=gs))
            if qlocal == 3:
                emit_mamba_zphase(ctx, t, s, p, h2, (c // 4), dS0)
        scan_done_tE = ctx.cnt["tE"]
    ctx.meta["xfm_vE"] = ctx.cnt["vE"]


def emit_mamba_zphase(ctx, t, s, p, h2, qh, dS0):
    ln_v = ctx.meta["ln_vE"]
    tq0 = h2 * 1024 + qh * 512
    ve = None
    def ve(build, extra=()):
        vq = ctx.cnt["vE"]
        return ctx.emit("vector", build, waits=[("vE", vq)] + list(extra), inc="vE")
    yq_v = ctx.cnt["vE"]
    zq_a = ctx.cnt["aE"]
    for m in range(16):
        sl = m % 2
        ctx.emit("gpsimd", lambda e, m=m, sl=sl: e.dma_start(
            out=s.wslab[sl].rearrange("p (kc m) -> p kc m", kc=8),
            in_=t.w_in[:, m * 128:(m + 1) * 128].rearrange(
                "(kc p) m -> p kc m", p=128)),
            waits=[("tE", ctx.meta.get("mm_wslab_free_%d" % sl, 0))],
            inc="dW", inc_by=16)
        dWs = ctx.cnt["dW"]
        b = m % 2
        for kc in range(8):
            w = ()
            if kc == 0:
                w = ctx.bank_acquire(b, [("dW", dWs), ("vE", yq_v)])
            ctx.emit("tensor", lambda e, sl=sl, kc=kc, b=b: e.matmul(
                p.b[b][:], s.wslab[sl][:, kc * 128:(kc + 1) * 128],
                s.x_ln[:, kc * T + tq0: kc * T + tq0 + 512],
                start=(kc == 0), stop=(kc == 7)), waits=w,
                inc="tE" if kc == 7 else None)
        ctx.meta["mm_wslab_free_%d" % sl] = ctx.cnt["tE"]
        ctx.bank_filled(b)
        zscr = s.cs_c[:, 0:512]
        aq = ctx.emit("scalar", lambda e, b=b, zscr=zscr: e.activation(
            out=zscr, in_=p.b[b][:], func=AF.Silu),
            waits=ctx.bank_read_waits(b, [("aE", ctx.meta.get("mm_z_aq", 0)),
                                          ("vE", ctx.meta.get("mm_z_vq", 0))]),
            inc="aE")
        ctx.bank_read_done(b, "aE")
        ctx.meta["mm_z_aq"] = aq
        yslice = s.y_q[:, m * 512:(m + 1) * 512]
        vq = ctx.emit("vector", lambda e, yslice=yslice, zscr=zscr: e.tensor_mul(
            out=yslice, in0=yslice, in1=zscr), waits=[("aE", aq)], inc="vE")
        ctx.meta["mm_z_vq"] = vq
        sqscr = s.cs_c[:, 512:1024]
        aq = ctx.emit("scalar", lambda e, yslice=yslice, sqscr=sqscr: e.activation(
            out=sqscr, in_=yslice, func=AF.Square), waits=[("vE", vq)], inc="aE")
        ctx.emit("tensor", lambda e, m=m, sqscr=sqscr: e.matmul(
            p.b[4][0:1, :], s.onescol_bf[:], sqscr,
            start=(m == 0), stop=(m == 15)),
            waits=ctx.bank_acquire(4, [("aE", aq)]) if m == 0 else [("aE", aq)],
            inc="tE")
    ctx.bank_filled(4)
    R = s.th_sc[0:1, 0:512]
    ve(lambda e, R=R: e.tensor_scalar(
        out=R, in0=p.b[4][0:1, :], scalar1=1.0 / 2048.0, scalar2=1e-5,
        op0=ALU.mult, op1=ALU.add),
       extra=ctx.bank_read_waits(4) + [("aE", zq_a)])
    ctx.bank_read_done(4, "vE")
    vq = ctx.cnt["vE"]
    aq = ctx.emit("scalar", lambda e, R=R: e.activation(out=R, in_=R, func=AF.Ln),
                  waits=[("vE", vq)], inc="aE")
    ve(lambda e, R=R: e.tensor_scalar(
        out=R, in0=R, scalar1=-0.5, scalar2=None, op0=ALU.mult),
       extra=[("aE", aq)])
    vq = ctx.cnt["vE"]
    aq = ctx.emit("scalar", lambda e, R=R: e.activation(out=R, in_=R, func=AF.Exp),
                  waits=[("vE", vq)], inc="aE")
    ctx.emit("tensor", lambda e, R=R: e.matmul(
        p.b[5][:], s.ones_f32[:], R, start=True, stop=True),
        waits=ctx.bank_acquire(5, [("aE", aq)]), inc="tE")
    ctx.bank_filled(5)
    for m in range(16):
        yslice = s.y_q[:, m * 512:(m + 1) * 512]
        ctx.emit("vector", lambda e, yslice=yslice: e.tensor_mul(
            out=yslice, in0=yslice, in1=p.b[5][:]),
            waits=ctx.bank_read_waits(5) if m == 0 else (), inc="vE")
        ctx.emit("vector", lambda e, yslice=yslice, m=m: e.tensor_scalar(
            out=yslice, in0=yslice, scalar1=s.nw_sb[:, m:m + 1], scalar2=None,
            op0=ALU.mult), inc="vE")
    ctx.bank_read_done(5, "vE")
    yn_v = ctx.cnt["vE"]
    for kc in range(16):
        sl = kc % 2
        ctx.emit("gpsimd", lambda e, kc=kc, sl=sl: e.dma_start(
            out=s.wslab[sl][:, 0:1024],
            in_=t.w_out[kc * 128:(kc + 1) * 128, :]),
            waits=[("tE", ctx.meta.get("mm_wslab_free_%d" % sl, 0))],
            inc="dW", inc_by=16)
        dWs = ctx.cnt["dW"]
        for mo in range(8):
            w = [("dW", dWs)]
            if kc == 0:
                w += ctx.bank_acquire(mo, [("vE", yn_v)])
            ctx.emit("tensor", lambda e, kc=kc, sl=sl, mo=mo: e.matmul(
                p.b[mo][:], s.wslab[sl][:, mo * 128:(mo + 1) * 128],
                s.y_q[:, kc * 512:(kc + 1) * 512],
                start=(kc == 0), stop=(kc == 15)),
                waits=w if (kc == 0 or mo == 0) else (),
                inc="tE" if (kc == 15 or mo == 7) else None)
        ctx.meta["mm_wslab_free_%d" % sl] = ctx.cnt["tE"]
    for mo in range(8):
        ctx.bank_filled(mo)
        dst = s.x_fm[:, mo * T + tq0: mo * T + tq0 + 512]
        ctx.emit("vector", lambda e, mo=mo, dst=dst: e.tensor_add(
            out=dst, in0=p.b[mo][:], in1=dst),
            waits=ctx.bank_read_waits(mo), inc="vE")
        ctx.bank_read_done(mo, "vE")
    ctx.meta["mm_yq_free"] = ctx.cnt["vE"]


def emit_attn(ctx, t, s, p):
    ln_v = ctx.meta["ln_vE"]
    ent_t, ent_v, ent_a = ctx.cnt["tE"], ctx.cnt["vE"], ctx.cnt.get("aE", 0)
    bar = [("tE", ent_t), ("vE", ent_v), ("aE", ent_a)]
    atc = s.atc
    ctx.emit("sync", lambda e: e.dma_start(out=s.maskband, in_=t.c_maskband),
             waits=bar, inc="dS", inc_by=16)
    ctx.emit("sync", lambda e: e.dma_start(out=s.ba_sb, in_=t.ba_cm),
             waits=bar, inc="dS", inc_by=16)
    ctx.emit("sync", lambda e: e.dma_start(out=s.bp_sb, in_=t.bp_cm),
             waits=bar, inc="dS", inc_by=16)
    dS0 = ctx.cnt["dS"]
    ctx.emit("vector", lambda e: e.tensor_scalar(
        out=s.qba_sb, in0=s.ba_sb[:, 0:8], scalar1=0.125, scalar2=None,
        op0=ALU.mult), waits=[("dS", dS0)] + bar, inc="vE")
    qba_v = ctx.cnt["vE"]
    for hg in range(4):
        for hh in range(4):
            h = hg * 4 + hh
            ctx.emit("gpsimd", lambda e, h=h, hh=hh: e.dma_start(
                out=s.wproj_sb[0:64, hh * 1024:(hh + 1) * 1024],
                in_=t.w_proj[h * 64:(h + 1) * 64, :]),
                waits=(bar + [("tE", ctx.meta.get("at_wproj_free", 0))]) if hh == 0
                else (), inc="dW", inc_by=16)
        dWp = ctx.cnt["dW"]
        for lc in range(6):
            m_abs = [2 * hg, 2 * hg + 1, 8 + 2 * hg, 9 + 2 * hg,
                     16 + 2 * hg, 17 + 2 * hg][lc]
            sl = lc % 2
            ctx.emit("gpsimd", lambda e, m_abs=m_abs, sl=sl: e.dma_start(
                out=s.wattn_sb[sl].rearrange("p (kc m) -> p kc m", kc=8),
                in_=t.w_attn[:, m_abs * 128:(m_abs + 1) * 128].rearrange(
                    "(kc p) m -> p kc m", p=128)),
                waits=[("tE", ctx.meta.get("at_wattn_free_%d" % sl, 0))] +
                (bar if hg == 0 and lc < 2 else []), inc="dW", inc_by=16)
            dWa = ctx.cnt["dW"]
            for nt in range(4):
                t0 = nt * 512
                b = (lc * 4 + nt) % 2
                for kc in range(8):
                    w = ()
                    if kc == 0:
                        w = ctx.bank_acquire(b, [("dW", dWa), ("vE", ln_v)])
                    ctx.emit("tensor", lambda e, sl=sl, kc=kc, t0=t0, b=b: e.matmul(
                        p.b[b][:], s.wattn_sb[sl][:, kc * 128:(kc + 1) * 128],
                        s.x_ln[:, kc * T + t0: kc * T + t0 + 512],
                        start=(kc == 0), stop=(kc == 7)), waits=w,
                        inc="tE" if kc == 7 else None)
                tq = ctx.cnt["tE"]
                if nt == 3:
                    ctx.meta["at_wattn_free_%d" % sl] = tq
                ctx.bank_filled(b)
                m_abs = [2 * hg, 2 * hg + 1, 8 + 2 * hg, 9 + 2 * hg,
                         16 + 2 * hg, 17 + 2 * hg][lc]
                wq = [("tE", ctx.meta.get("at_qkv_free", 0))] if (lc == 0 and
                                                                 nt == 0) else []
                if lc < 2:
                    ctx.emit("vector", lambda e, lc=lc, t0=t0, b=b, m_abs=m_abs:
                             e.tensor_scalar(
                                 out=s.qkv_g[:, lc * T + t0: lc * T + t0 + 512],
                                 in0=p.b[b][:], scalar1=0.125,
                                 scalar2=s.qba_sb[:, m_abs:m_abs + 1],
                                 op0=ALU.mult, op1=ALU.add),
                             waits=ctx.bank_read_waits(
                                 b, [("dS", dS0), ("vE", qba_v)] + wq), inc="vE")
                    ctx.bank_read_done(b, "vE")
                else:
                    ctx.emit("scalar", lambda e, lc=lc, t0=t0, b=b, m_abs=m_abs:
                             e.activation(
                                 out=s.qkv_g[:, lc * T + t0: lc * T + t0 + 512],
                                 in_=p.b[b][:], func=AF.Identity,
                                 bias=s.ba_sb[:, m_abs:m_abs + 1]),
                             waits=ctx.bank_read_waits(b, [("dS", dS0)] + wq),
                             inc="aE")
                    ctx.bank_read_done(b, "aE")
        qkv_a = ctx.cnt["aE"]
        qkv_v = ctx.cnt["vE"]
        for hh in range(4):
            base = (hh % 2) * 64
            qc = (hh // 2) * T
            kc_off = (2 + hh // 2) * T
            vc = (4 + hh // 2) * T
            p3bf = p.b[3].ap().bitcast(BF16)
            for tc in range(16):
                ctx.emit("tensor", lambda e, tc=tc, base=base, vc=vc, p3bf=p3bf:
                         e.transpose(p3bf[0:128, tc * 64:(tc + 1) * 64],
                                     s.qkv_g[base:base + 64,
                                             vc + tc * 128: vc + (tc + 1) * 128],
                                     s.eye_bf[base:base + 64, base:base + 64]),
                         waits=ctx.bank_acquire(3, [("aE", qkv_a),
                                                    ("vE", qkv_v)]) if tc == 0
                         else (), inc="tE" if tc == 15 else None)
            ctx.bank_filled(3)
            ctx.emit("vector", lambda e, p3bf=p3bf: e.tensor_copy(
                out=s.v_tm[:], in_=p3bf[0:128, 0:1024]),
                waits=ctx.bank_read_waits(3), inc="vE")
            ctx.bank_read_done(3, "vE")
            vtm_v = ctx.cnt["vE"]
            for qt in range(16):
                sl = qt % 2
                for nt in range(4):
                    b = 4 + nt
                    off = 2048 + nt * 512 - qt * 128
                    ctx.emit("tensor", lambda e, base=base, qc=qc, kc_off=kc_off,
                             qt=qt, nt=nt, b=b: e.matmul(
                                 p.b[b][:],
                                 s.qkv_g[base:base + 64,
                                         qc + qt * 128: qc + (qt + 1) * 128],
                                 s.qkv_g[base:base + 64,
                                         kc_off + nt * 512: kc_off + nt * 512 + 512],
                                 start=True, stop=False),
                             waits=ctx.bank_acquire(b, [("vE", vtm_v)] if qt == 0
                                                    and nt == 0 else ()), inc=None)
                    ctx.emit("tensor", lambda e, off=off, b=b: e.matmul(
                        p.b[b][:], s.eye_bf[:],
                        s.maskband[:, off:off + 512], start=False, stop=True),
                        inc="tE")
                    ctx.bank_filled(b)
                for nt in range(4):
                    b = 4 + nt
                    ctx.emit("vector", lambda e, nt=nt, b=b: e.reduce_max(
                        out=atc[:, nt:nt + 1], in_=p.b[b][:], axis=AX.X),
                        waits=ctx.bank_read_waits(b), inc="vE")
                vq = ctx.cnt["vE"]
                vq = ctx.emit("vector", lambda e: e.tensor_max(
                    out=atc[:, 4:5], in0=atc[:, 0:1], in1=atc[:, 1:2]),
                    waits=[("vE", vq)], inc="vE")
                vq = ctx.emit("vector", lambda e: e.tensor_max(
                    out=atc[:, 5:6], in0=atc[:, 2:3], in1=atc[:, 3:4]),
                    waits=[("vE", vq)], inc="vE")
                vq = ctx.emit("vector", lambda e: e.tensor_max(
                    out=atc[:, 4:5], in0=atc[:, 4:5], in1=atc[:, 5:6]),
                    waits=[("vE", vq)], inc="vE")
                vq = ctx.emit("vector", lambda e: e.tensor_scalar(
                    out=atc[:, 5:6], in0=atc[:, 4:5], scalar1=-1.0, scalar2=None,
                    op0=ALU.mult), waits=[("vE", vq)], inc="vE")
                for nt in range(4):
                    b = 4 + nt
                    ctx.emit("scalar", lambda e, nt=nt, b=b, sl=sl: e.activation(
                        out=s.attn_sb[sl][:, nt * 512:(nt + 1) * 512],
                        in_=p.b[b][:], func=AF.Exp, bias=atc[:, 5:6],
                        accum_out=atc[:, 6 + nt:7 + nt]),
                        waits=[("vE", vq),
                               ("vE", ctx.meta.get("at_attnsb_free_%d" % sl, 0))],
                        inc="aE")
                    ctx.bank_read_done(b, "aE")
                aq = ctx.cnt["aE"]
                vq = ctx.emit("vector", lambda e: e.reduce_sum(
                    out=atc[:, 10:11], in_=atc[:, 6:10], axis=AX.X),
                    waits=[("aE", aq)], inc="vE")
                vq = ctx.emit("vector", lambda e: e.reciprocal(
                    out=atc[:, 11:12], in_=atc[:, 10:11]), waits=[("vE", vq)],
                    inc="vE")
                for nt in range(4):
                    ctx.emit("vector", lambda e, nt=nt, sl=sl: e.tensor_scalar(
                        out=s.attn_sb[sl][:, nt * 512:(nt + 1) * 512],
                        in0=s.attn_sb[sl][:, nt * 512:(nt + 1) * 512],
                        scalar1=atc[:, 11:12], scalar2=None, op0=ALU.mult),
                        waits=[("vE", vq)] if nt == 0 else (), inc="vE")
                attn_v = ctx.cnt["vE"]
                for tc in range(16):
                    b = tc // 8
                    pbf = p.b[b].ap().bitcast(BF16)
                    ctx.emit("tensor", lambda e, tc=tc, b=b, pbf=pbf, sl=sl:
                             e.transpose(pbf[0:128, (tc % 8) * 128:
                                             (tc % 8) * 128 + 128],
                                         s.attn_sb[sl][:, tc * 128:(tc + 1) * 128],
                                         s.eye_bf[0:128, 0:128]),
                             waits=ctx.bank_acquire(b, [("vE", attn_v)])
                             if tc % 8 == 0 else (),
                             inc="tE" if tc % 8 == 7 else None)
                    if tc % 8 == 7:
                        ctx.bank_filled(b)
                for b in range(2):
                    pbf = p.b[b].ap().bitcast(BF16)
                    ctx.emit("vector", lambda e, b=b, pbf=pbf, sl=sl: e.tensor_copy(
                        out=s.attnT_sb[sl][:, b * 1024:(b + 1) * 1024],
                        in_=pbf[0:128, 0:1024]),
                        waits=ctx.bank_read_waits(b), inc="vE")
                    ctx.bank_read_done(b, "vE")
                ctx.meta["at_attnsb_free_%d" % sl] = ctx.cnt["vE"]
                attnT_v = ctx.cnt["vE"]
                for kc2 in range(16):
                    w = ()
                    if kc2 == 0:
                        w = ctx.bank_acquire(2, [("vE", attnT_v)])
                    ctx.emit("tensor", lambda e, kc2=kc2, sl=sl: e.matmul(
                        p.b[2][0:64, 0:128],
                        s.v_tm[:, kc2 * 64:(kc2 + 1) * 64],
                        s.attnT_sb[sl][:, kc2 * 128:(kc2 + 1) * 128],
                        start=(kc2 == 0), stop=(kc2 == 15)), waits=w,
                        inc="tE" if kc2 == 15 else None)
                ctx.bank_filled(2)
                ctx.emit("vector", lambda e, hh=hh, qt=qt: e.tensor_copy(
                    out=s.attnout[:, hh * T + qt * 128: hh * T + (qt + 1) * 128],
                    in_=p.b[2][0:64, 0:128]),
                    waits=ctx.bank_read_waits(2), inc="vE")
                ctx.bank_read_done(2, "vE")
        ao_v = ctx.cnt["vE"]
        ctx.meta["at_qkv_free"] = ctx.cnt["tE"]
        for mo in range(8):
            for nt in range(4):
                t0 = nt * 512
                b = (mo * 4 + nt) % 2
                for hh in range(4):
                    w = ()
                    if hh == 0:
                        w = ctx.bank_acquire(b, [("vE", ao_v), ("dW", dWp)])
                    ctx.emit("tensor", lambda e, hh=hh, mo=mo, t0=t0, b=b: e.matmul(
                        p.b[b][:],
                        s.wproj_sb[0:64, hh * 1024 + mo * 128:
                                   hh * 1024 + mo * 128 + 128],
                        s.attnout[:, hh * T + t0: hh * T + t0 + 512],
                        start=(hh == 0), stop=(hh == 3)), waits=w,
                        inc="tE" if hh == 3 else None)
                ctx.bank_filled(b)
                dst = s.x_fm[:, mo * T + t0: mo * T + t0 + 512]
                ctx.emit("vector", lambda e, b=b, dst=dst: e.tensor_add(
                    out=dst, in0=p.b[b][:], in1=dst),
                    waits=ctx.bank_read_waits(b), inc="vE")
                if hg == 0:
                    ctx.emit("vector", lambda e, mo=mo, dst=dst: e.tensor_scalar(
                        out=dst, in0=dst, scalar1=s.bp_sb[:, mo:mo + 1],
                        scalar2=None, op0=ALU.add), inc="vE")
                ctx.bank_read_done(b, "vE")
        ctx.meta["at_wproj_free"] = ctx.cnt["tE"]
    ctx.meta["xfm_vE"] = ctx.cnt["vE"]


def emit_lff(ctx, t, s, p):
    ln_v = ctx.meta["ln_vE"]
    ent_t, ent_v, ent_a = ctx.cnt["tE"], ctx.cnt["vE"], ctx.cnt.get("aE", 0)
    bar = [("tE", ent_t), ("vE", ent_v), ("aE", ent_a)]
    for dst, srcap in ((s.ld_sb, t.lora_down), (s.lu_sb, t.lora_up),
                       (s.upxt_sb, t.upxt), (s.dnaux_sb, t.down_aux),
                       (s.bl_sb, t.bl_cm)):
        ctx.emit("sync", lambda e, a=dst, b=srcap: e.dma_start(
            out=a if isinstance(a, bass.AP) else a[:], in_=b),
            waits=bar, inc="dS", inc_by=16)
    ctx.emit("gpsimd", lambda e: e.dma_start(
        out=s.l1w_bf.rearrange("p (kc o) -> p kc o", kc=8),
        in_=t.l1wt.rearrange("(kc p) o -> p kc o", p=128)),
        waits=bar, inc="dW", inc_by=16)
    dS0, dW0 = ctx.cnt["dS"], ctx.cnt["dW"]
    ctx.emit("vector", lambda e: e.tensor_scalar(
        out=s.hbl_sb[:], in0=s.bl_sb[:], scalar1=0.5, scalar2=None, op0=ALU.mult),
        waits=[("dS", dS0)] + bar, inc="vE")
    for n2 in range(2):
        ctx.emit("tensor", lambda e, n2=n2: e.matmul(
            p.b[2][0:4, :], s.lu_sb, s.upxt_sb[:, n2 * 512:(n2 + 1) * 512],
            start=True, stop=True),
            waits=ctx.bank_acquire(2, [("dS", dS0)]), inc="tE")
        ctx.bank_filled(2)
        vq = ctx.emit("vector", lambda e, n2=n2: e.tensor_copy(
            out=s.p1_sb[:, n2 * 512:(n2 + 1) * 512], in_=p.b[2][0:4, :]),
            waits=ctx.bank_read_waits(2), inc="vE")
        ctx.bank_read_done(2, "vE")
    for n2 in range(2):
        ctx.emit("tensor", lambda e, n2=n2: e.matmul(
            p.b[3][0:100, :], s.ld_sb, s.p1_sb[:, n2 * 512:(n2 + 1) * 512],
            start=True, stop=True),
            waits=ctx.bank_acquire(3, [("vE", vq)]), inc="tE")
        ctx.bank_filled(3)
        vq2 = ctx.emit("vector", lambda e, n2=n2: e.tensor_copy(
            out=s.p2_sb[:, n2 * 512:(n2 + 1) * 512], in_=p.b[3][0:100, :]),
            waits=ctx.bank_read_waits(3), inc="vE")
        ctx.bank_read_done(3, "vE")
    for kc in range(8):
        for n2 in range(2):
            b = 2 + (kc * 2 + n2) % 2
            ctx.emit("tensor", lambda e, kc=kc, n2=n2, b=b: e.matmul(
                p.b[b][:], s.dnaux_sb[:, kc * 128:(kc + 1) * 128],
                s.p2_sb[:, n2 * 512:(n2 + 1) * 512], start=True, stop=True),
                waits=ctx.bank_acquire(b, [("vE", vq2)]), inc="tE")
            ctx.bank_filled(b)
            ctx.emit("vector", lambda e, kc=kc, n2=n2, b=b: e.tensor_add(
                out=s.weff[:, kc * 1024 + n2 * 512: kc * 1024 + n2 * 512 + 512],
                in0=p.b[b][:],
                in1=s.l1w_bf[:, kc * 1024 + n2 * 512: kc * 1024 + n2 * 512 + 512]),
                waits=ctx.bank_read_waits(b, [("dW", dW0)]), inc="vE")
            ctx.bank_read_done(b, "vE")
    weff_v = ctx.cnt["vE"]
    for mo in range(8):
        for nt in range(4):
            t0 = nt * 512
            b = (mo * 4 + nt) % 2
            for kc in range(8):
                w = ()
                if kc == 0:
                    w = ctx.bank_acquire(b, [("vE", max(ln_v, weff_v))])
                ctx.emit("tensor", lambda e, mo=mo, kc=kc, t0=t0, b=b: e.matmul(
                    p.b[b][:],
                    s.weff[:, kc * 1024 + mo * 128: kc * 1024 + mo * 128 + 128],
                    s.x_ln[:, kc * T + t0: kc * T + t0 + 512],
                    start=(kc == 0), stop=(kc == 7)),
                    waits=w, inc="tE" if kc == 7 else None)
            ctx.bank_filled(b)
            dst = s.x_fm[:, mo * T + t0:mo * T + t0 + 512]
            ctx.emit("vector", lambda e, b=b, dst=dst: e.scalar_tensor_tensor(
                out=dst, in0=p.b[b][:], scalar=0.5, in1=dst,
                op0=ALU.mult, op1=ALU.add),
                waits=ctx.bank_read_waits(b), inc="vE")
            ctx.emit("vector", lambda e, mo=mo, dst=dst: e.tensor_scalar(
                out=dst, in0=dst, scalar1=s.hbl_sb[:, mo:mo + 1], scalar2=None,
                op0=ALU.add), inc="vE")
            ctx.bank_read_done(b, "vE")
    ctx.meta["xfm_vE"] = ctx.cnt["vE"]


def build_program(phases, taps=()):
    phi = 0
    nc = bass.Bass("TRN2", target_bir_lowering=False, debug=False)
    t = declare_tensors(nc, phases)
    s = alloc_sbuf(nc, phases)
    p = alloc_psum(nc)
    ctx = Ctx(nc)
    emit_load_x(ctx, t, s)
    emit_load_consts(ctx, t, s, phases)
    for ph in phases:
        emit_ln(ctx, t, s, p)
        if "ln" in taps:
            emit_dump(ctx, nc, "d_xln", s.x_ln[:], NCH * T, BF16)
        if ph == "moe":
            emit_router(ctx, t, s, p)
            if "comb" in taps:
                emit_dump(ctx, nc, "d_comb", s.comb_em[:], 2 * T, BF16)
            if "scr" in taps:
                emit_dump(ctx, nc, "d_scr", s.ar_f32[:, 12288:20480], 8192, F32)
            if "stop_before_moe" in taps:
                break
            emit_moe(ctx, t, s, p)
        elif ph == "mamba":
            emit_mamba(ctx, t, s, p)
        elif ph == "lff":
            emit_lff(ctx, t, s, p)
        elif ph == "attn":
            emit_attn(ctx, t, s, p)
            if "attn_dbg" in taps:
                emit_dump(ctx, nc, "d_qkv", s.qkv_g[:], 6 * T, BF16)
                emit_dump(ctx, nc, "d_ao", s.attnout, 4 * T, BF16, rows=64)
                emit_dump(ctx, nc, "d_attn", s.attn_sb[1][:], T, BF16)
        phi += 1
        if "stream" in taps:
            emit_dump(ctx, nc, "d_ph%d" % phi, s.x_fm[:], NCH * T, F32)
    emit_store_x(ctx, t, s)
    ctx.run()
    return nc



def _consts(phases):
    c = {}
    c["c_eye_bf"] = np.eye(128, dtype=BF)
    c["c_eye_f32"] = np.eye(128, dtype=np.float32)
    c["c_ones_bf"] = np.ones((128, 128), dtype=BF)
    c["c_ones_f32"] = np.ones((1, 128), dtype=np.float32)
    c["c_onescol_bf"] = np.ones((128, 1), dtype=BF)
    if "mamba" in phases:
        sidx = np.arange(128)[:, None]
        tidx = np.arange(128)[None, :]
        c["c_tri"] = (sidx <= tidx).astype(np.float32)
        c["c_minf"] = np.where(tidx >= sidx, 0.0, -1e30).astype(np.float32)
        invf = (1.0 / 10000.0 ** (np.arange(16, dtype=np.float64) / 16.0))
        c["c_invfreq"] = np.tile(invf, 32)[None, :].astype(np.float32)
    if "attn" in phases:
        v = np.arange(4224)[None, :] - 2048 - np.arange(128)[:, None]
        f = np.where(v > 0, -np.log1p(np.maximum(v, 0).astype(np.float64)), 0.0)
        f = np.where(v < 0, -1e30, f)
        c["c_maskband"] = f.astype(BF)
    return c


def _cm(vec, ncols):
    v = np.zeros(128 * ncols, dtype=np.float32)
    v[:len(vec)] = np.asarray(vec, dtype=np.float32)
    return np.ascontiguousarray(v.reshape(ncols, 128).T)


def _layout_weights(inp, phases):
    w = {}
    f32c = lambda a: np.ascontiguousarray(np.asarray(a, dtype=np.float32))
    if "moe" in phases:
        w["wgt"] = f32c(np.asarray(inp["L0_ffn_w_gate"]).transpose(0, 2, 1))
        w["wdt"] = f32c(np.asarray(inp["L0_ffn_w_down"]).transpose(0, 2, 1))
        w["wgst"] = f32c(np.asarray(inp["L0_ffn_w_gate_shared"]).T)
        w["wdst"] = f32c(np.asarray(inp["L0_ffn_w_down_shared"]).T)
        w["rwt"] = f32c(np.asarray(inp["L0_ffn_router_w"]).T)
        w["eb_cm"] = _cm(inp["L0_ffn_expert_bias"], 2)
    if "mamba" in phases:
        w["w_in"] = f32c(inp["L0_attn_W_in"])
        w["w_out"] = f32c(inp["L0_attn_W_out"])
        w["dt_bias"] = f32c(np.asarray(inp["L0_attn_dt_bias"]).reshape(1, 32))
        w["a_log"] = f32c(np.asarray(inp["L0_attn_A_log"]).reshape(1, 32))
        w["dskip_rep"] = f32c(np.broadcast_to(
            np.asarray(inp["L0_attn_D_skip"])[None, :], (128, 32)))
        w["nw_cm"] = _cm(inp["L0_attn_norm_weight"], 16)
    if "attn" in phases:
        w["w_attn"] = f32c(inp["L1_attn_w_attn"])
        w["w_proj"] = f32c(inp["L1_attn_w_proj"])
        w["ba_cm"] = _cm(inp["L1_attn_b_attn"], 24)
        w["bp_cm"] = _cm(inp["L1_attn_b_proj"], 8)
    if "lff" in phases:
        w["l1wt"] = f32c(np.asarray(inp["L1_ffn_W"]).T)
        w["lora_down"] = f32c(inp["L1_ffn_lora_down"])
        w["lora_up"] = f32c(inp["L1_ffn_lora_up"])
        w["down_aux"] = f32c(inp["L1_ffn_down_aux"])
        w["upxt"] = f32c(np.asarray(inp["L1_ffn_up_aux"]).T)
        w["bl_cm"] = _cm(inp["L1_ffn_b"], 8)
    return w


def run_phases(phases, xs, weights):
    nc = build_program(phases)
    consts = _consts(phases)
    in_maps = []
    for b in range(8):
        im = {"xin": xs[b]}
        im.update(consts)
        im.update(weights)
        in_maps.append(im)
    res = run_bass_kernel_spmd(nc, in_maps, core_ids=list(range(8)))
    return [np.asarray(res.results[b]["xout"]) for b in range(8)]


def kernel(**inputs):
    x = np.asarray(inputs["L0_attn_x"])
    xs = [np.ascontiguousarray(np.asarray(x[b], dtype=np.float32).T) for b in range(8)]
    for phases in PHASE_GROUPS:
        weights = _layout_weights(inputs, phases)
        xs = run_phases(phases, xs, weights)
    out = np.stack([xb.T for xb in xs])
    return np.ascontiguousarray(out).astype(np.float32)


# version 03
# speedup vs baseline: 1.0202x; 1.0202x over previous
import numpy as np
import ml_dtypes
import concourse.bass as bass
import concourse.mybir as mybir
from concourse.bass_utils import run_bass_kernel_spmd

F32 = mybir.dt.float32
BF16 = mybir.dt.bfloat16
AF = mybir.ActivationFunctionType
ALU = mybir.AluOpType
AX = mybir.AxisListType
BF = ml_dtypes.bfloat16

T = 2048
DM = 1024
NCH = 8
NE = 160
INTER = 512

PHASE_GROUPS = [["moe", "mamba", "moe", "lff", "attn", "lff"]]


class Ctx:

    def __init__(self, nc):
        self.nc = nc
        self.ops = []
        self.cnt = {}
        self.handles = {}
        self.meta = {}
        self.bank_r = [{"vE": 0, "aE": 0} for _ in range(8)]
        self.bank_w = [0 for _ in range(8)]

    def sem(self, name):
        if name not in self.handles:
            self.handles[name] = self.nc.alloc_semaphore(name)
            self.cnt[name] = 0
        return self.handles[name]

    def emit(self, eng, build, waits=(), inc=None, inc_by=1):
        waits = [(n, v) for (n, v) in waits if v > 0]
        for n, _ in waits:
            self.sem(n)
        tgt = None
        if inc is not None:
            self.sem(inc)
            self.cnt[inc] += inc_by
            tgt = self.cnt[inc]

        def fn(e, build=build, waits=tuple(waits), inc=inc, inc_by=inc_by):
            for n, v in waits:
                e.wait_ge(self.handles[n], v)
            ins = build(e)
            if inc is not None:
                ins.then_inc(self.handles[inc], inc_by)
        self.ops.append((eng, fn))
        return tgt

    def bank_acquire(self, b, extra=()):
        r = self.bank_r[b]
        return [("vE", r["vE"]), ("aE", r["aE"])] + list(extra)

    def bank_filled(self, b):
        self.bank_w[b] = self.cnt["tE"]

    def bank_read_waits(self, b, extra=()):
        return [("tE", self.bank_w[b])] + list(extra)

    def bank_read_done(self, b, eng):
        self.bank_r[b][eng] = self.cnt[eng]

    def run(self):
        nc = self.nc
        with nc.Block() as block:
            for eng_name in ("sync", "scalar", "tensor", "vector", "gpsimd"):
                ops = [f for (e, f) in self.ops if e == eng_name]

                def mk(ops):
                    def runner(engine):
                        for f in ops:
                            f(engine)
                    return runner
                getattr(block, eng_name)(mk(ops))


class NS:
    pass



def declare_tensors(nc, phases):
    t = NS()
    d = lambda name, shape, dtype=F32, kind="ExternalInput": nc.dram_tensor(
        name, list(shape), dtype, kind=kind).ap()
    t.xin = d("xin", [DM, T])
    t.xout = d("xout", [DM, T], kind="ExternalOutput")
    t.c_eye_bf = d("c_eye_bf", [128, 128], BF16)
    t.c_eye_f32 = d("c_eye_f32", [128, 128])
    t.c_ones_bf = d("c_ones_bf", [128, 128], BF16)
    t.c_ones_f32 = d("c_ones_f32", [1, 128])
    t.c_onescol_bf = d("c_onescol_bf", [128, 1], BF16)
    if "moe" in phases:
        t.wgt = d("wgt", [NE, DM, INTER])
        t.wdt = d("wdt", [NE, INTER, DM])
        t.wgst = d("wgst", [DM, INTER])
        t.wdst = d("wdst", [INTER, DM])
        t.rwt = d("rwt", [DM, NE])
        t.eb_cm = d("eb_cm", [128, 2])
    if "mamba" in phases:
        t.w_in = d("w_in", [DM, 4256])
        t.w_out = d("w_out", [2 * DM, DM])
        t.dt_bias = d("dt_bias", [1, 32])
        t.a_log = d("a_log", [1, 32])
        t.dskip_rep = d("dskip_rep", [128, 32])
        t.nw_cm = d("nw_cm", [128, 16])
        t.c_tri = d("c_tri", [128, 128])
        t.c_minf = d("c_minf", [128, 128])
        t.c_invfreq = d("c_invfreq", [1, 512])
    if "attn" in phases:
        t.w_attn = d("w_attn", [DM, 3 * DM])
        t.w_proj = d("w_proj", [DM, DM])
        t.ba_cm = d("ba_cm", [128, 24])
        t.bp_cm = d("bp_cm", [128, 8])
        t.c_maskband = d("c_maskband", [128, 4224], BF16)
    if "lff" in phases:
        t.l1wt = d("l1wt", [DM, DM])
        t.lora_down = d("lora_down", [4, 100])
        t.lora_up = d("lora_up", [50, 4])
        t.down_aux = d("down_aux", [100, DM])
        t.upxt = d("upxt", [50, DM])
        t.bl_cm = d("bl_cm", [128, 8])
    return t


def alloc_sbuf(nc, phases):
    s = NS()
    a = lambda name, shape, dtype: nc.alloc_sbuf_tensor(name, list(shape), dtype)
    s.x_fm = a("x_fm", [128, NCH * T], F32)
    s.x_ln = a("x_ln", [128, NCH * T], BF16)
    s.eye_bf = a("eye_bf", [128, 128], BF16)
    s.eye_f32 = a("eye_f32", [128, 128], F32)
    s.ones_bf = a("ones_bf", [128, 128], BF16)
    s.ones_f32 = a("ones_f32", [1, 128], F32)
    s.onescol_bf = a("onescol_bf", [128, 1], BF16)
    s.ln_rows = a("ln_rows", [1, 3 * 512], F32)
    s.ln_sq = a("ln_sq", [128, 2 * 512], BF16)
    s.ln_t = a("ln_t", [128, 2 * 512], F32)
    s.AR = 48384
    s.arena = a("arena", [128, s.AR], BF16)
    s.ar_bf = s.arena.ap()
    s.ar_f32 = s.arena.ap().bitcast(F32)
    if "moe" in phases:
        s.wg_sb = [s.ar_bf[:, 0:8192], s.ar_bf[:, 8192:16384]]
        s.wd_sb = s.ar_bf[:, 16384:24576]
        s.h1w = s.ar_bf[:, 24576:40960]
        s.comb_em = s.ar_bf[:, 40960:45056]
        s.crow = [s.ar_bf[32:33, 43008:45056], s.ar_bf[64:65, 43008:45056]]
        s.silu_sc = s.ar_f32[:, 22528:23552]
        s.rw_sb = s.ar_bf[:, 47104:48384]
        s.eb_sb = a("eb_sb", [128, 2], F32)
    if "mamba" in phases:
        s.xs_fmh = s.ar_bf[:, 0:16384]
        s.y_q = s.ar_bf[:, 16384:24576]
        s.bcdt = s.ar_bf[:, 24576:26624]
        s.wslab = [s.ar_bf[:, 26624:27648], s.ar_bf[:, 27648:28672]]
        s.cs_c = s.ar_bf[:, 28672:29696]
        s.bhat = s.ar_bf[:, 29696:31744]
        s.ctil = s.ar_bf[:, 31744:33792]
        s.cbar = s.ar_bf[:, 33792:35840]
        s.g_bf = s.ar_bf[0:64, 35840:37888]
        MF = 18944
        s.dt_tm = s.ar_f32[:, MF:MF + 512]
        s.cumdt = s.ar_f32[:, MF + 512:MF + 1024]
        s.L_tm = s.ar_f32[:, MF + 1024:MF + 1536]
        s.L_emc = s.ar_f32[0:32, MF + 1536:MF + 1664]
        s.th_sc = s.ar_f32[:, MF + 1664:MF + 2176]
        s.mrows = s.ar_f32[0:1, MF + 2176:MF + 2688]
        s.fmscr = s.ar_bf[:, 2 * (MF + 2688):2 * (MF + 2688) + 512]
        s.attnT_m = s.ar_bf[:, 2 * (MF + 2944):2 * (MF + 2944) + 128]
        s.g_f32 = s.ar_f32[0:64, MF + 3008:MF + 5056]
        s.dtb_bc = a("dtb_bc", [128, 32], F32)
        s.nea_bc = a("nea_bc", [128, 32], F32)
        s.tri_sb = a("tri_sb", [128, 128], F32)
        s.minf_sb = a("minf_sb", [128, 128], F32)
        s.invf_row = a("invf_row", [1, 512], F32)
        s.dtb_row = a("dtb_row", [1, 32], F32)
        s.alog_row = a("alog_row", [1, 32], F32)
        s.dskip_sb = a("dskip_sb", [128, 32], F32)
        s.nw_sb = a("nw_sb", [128, 16], F32)
    if "lff" in phases:
        s.weff = s.ar_bf[:, 0:8192]
        s.l1w_bf = s.ar_bf[:, 8192:16384]
        s.upxt_sb = s.ar_f32[0:50, 8192:9216]
        s.p1_sb = s.ar_f32[0:4, 9216:10240]
        s.p2_sb = s.ar_f32[0:100, 10240:11264]
        s.dnaux_sb = s.ar_f32[0:100, 11264:12288]
        s.ld_sb = s.ar_f32[0:4, 12288:12388]
        s.lu_sb = s.ar_f32[0:50, 12416:12420]
        s.bl_sb = a("bl_sb", [128, 8], F32)
        s.hbl_sb = a("hbl_sb", [128, 8], F32)
    if "attn" in phases:
        s.qkv_g = s.ar_bf[:, 0:12288]
        s.attnout = s.ar_bf[0:64, 12288:20480]
        s.maskband = s.ar_bf[:, 20480:24704]
        s.v_tm = s.ar_bf[:, 24704:25728]
        s.attn_sb = [s.ar_bf[:, 25728:27776], s.ar_bf[:, 27776:29824]]
        s.attnT_sb = [s.ar_bf[:, 29824:31872], s.ar_bf[:, 31872:33920]]
        s.wproj_sb = s.ar_bf[0:64, 33920:38016]
        s.wattn_sb = [s.ar_bf[:, 38016:39040], s.ar_bf[:, 39040:40064]]
        s.ba_sb = s.ar_f32[:, 24000:24024]
        s.bp_sb = s.ar_f32[:, 24024:24032]
        s.qba_sb = s.ar_f32[:, 24032:24040]
        s.atc = s.ar_f32[:, 24040:24064]
    return s


def alloc_psum(nc):
    p = NS()
    p.b = [nc.alloc_psum_tensor("pb%d" % i, [128, 512], F32) for i in range(8)]
    return p



def emit_load_x(ctx, t, s):
    src = t.xin.rearrange("(c p) t -> p c t", p=128)
    dst = s.x_fm.ap().rearrange("p (c t) -> p c t", c=NCH)
    ctx.emit("sync", lambda e: e.dma_start(out=dst, in_=src), inc="dS", inc_by=16)
    ctx.meta["x_dS"] = ctx.cnt["dS"]


def emit_load_consts(ctx, t, s, phases):
    pairs = [(s.eye_bf, t.c_eye_bf), (s.eye_f32, t.c_eye_f32),
             (s.ones_bf, t.c_ones_bf), (s.ones_f32, t.c_ones_f32),
             (s.onescol_bf, t.c_onescol_bf)]
    if "moe" in phases:
        pairs += [(s.eb_sb, t.eb_cm)]
    for dst, src in pairs:
        ctx.emit("sync", lambda e, d=dst, sr=src: e.dma_start(out=d[:], in_=sr),
                 inc="dS", inc_by=16)
    ctx.meta["consts_dS"] = ctx.cnt["dS"]
    ctx.meta["consts_dW"] = ctx.cnt.get("dW", 0)


def emit_ln(ctx, t, s, p):
    dep_v = ctx.meta.get("xfm_vE", 0)
    first_dma = [("dS", max(ctx.meta["x_dS"], ctx.meta["consts_dS"]))]
    R = s.ln_rows
    for nt in range(4):
        t0 = nt * 512
        for c in range(NCH):
            sq = s.ln_sq[:, (c % 2) * 512:(c % 2) * 512 + 512]
            w = [("tE", ctx.meta.get("ln_sq_tE_%d" % (c % 2), 0))]
            if c == 0 and nt == 0:
                w += first_dma + [("vE", dep_v)]
            ctx.emit("scalar", lambda e, c=c, sq=sq, t0=t0: e.activation(
                out=sq, in_=s.x_fm[:, c * T + t0:c * T + t0 + 512], func=AF.Square),
                waits=w, inc="aE")
            aq = ctx.cnt["aE"]
            wv = first_dma if (c == 0 and nt == 0) else ()
            ctx.emit("vector", lambda e, c=c, t0=t0: e.tensor_copy(
                out=s.x_ln[:, c * T + t0:c * T + t0 + 512],
                in_=s.x_fm[:, c * T + t0:c * T + t0 + 512]), waits=wv, inc="vE")
            vq = ctx.cnt["vE"]
            wsum = ctx.bank_acquire(4, [("vE", vq)]) if c == 0 else [("vE", vq)]
            ctx.emit("tensor", lambda e, c=c, t0=t0: e.matmul(
                p.b[4][0:1, :], s.onescol_bf[:],
                s.x_ln[:, c * T + t0:c * T + t0 + 512],
                start=(c == 0), stop=(c == NCH - 1)), waits=wsum,
                inc="tE" if c == NCH - 1 else None)
            wsq = ctx.bank_acquire(5, [("aE", aq)]) if c == 0 else [("aE", aq)]
            tq = ctx.emit("tensor", lambda e, c=c, sq=sq: e.matmul(
                p.b[5][0:1, :], s.onescol_bf[:], sq,
                start=(c == 0), stop=(c == NCH - 1)), waits=wsq, inc="tE")
            if tq is None:
                tq = ctx.cnt["tE"]
            ctx.meta["ln_sq_tE_%d" % (c % 2)] = tq
        ctx.bank_filled(4)
        ctx.bank_filled(5)
        ctx.emit("vector", lambda e: e.tensor_scalar(
            out=R[:, 0:512], in0=p.b[4][0:1, :], scalar1=1.0 / DM, scalar2=None,
            op0=ALU.mult), waits=ctx.bank_read_waits(4), inc="vE")
        ctx.emit("vector", lambda e: e.tensor_copy(out=R[:, 512:1024], in_=p.b[5][0:1, :]),
                 waits=ctx.bank_read_waits(5), inc="vE")
        ctx.bank_read_done(4, "vE")
        ctx.bank_read_done(5, "vE")
        ctx.emit("vector", lambda e: e.tensor_tensor(
            out=R[:, 1024:1536], in0=R[:, 0:512], in1=R[:, 0:512], op=ALU.mult),
            inc="vE")
        ctx.emit("vector", lambda e: e.scalar_tensor_tensor(
            out=R[:, 512:1024], in0=R[:, 512:1024], scalar=1.0 / DM,
            in1=R[:, 1024:1536], op0=ALU.mult, op1=ALU.subtract), inc="vE")
        vq = ctx.emit("vector", lambda e: e.tensor_scalar(
            out=R[:, 512:1024], in0=R[:, 512:1024], scalar1=1e-5, scalar2=None,
            op0=ALU.add), inc="vE")
        aq1 = ctx.emit("scalar", lambda e: e.activation(
            out=R[:, 512:1024], in_=R[:, 512:1024], func=AF.Ln),
            waits=[("vE", vq)], inc="aE")
        vq = ctx.emit("vector", lambda e: e.tensor_scalar(
            out=R[:, 512:1024], in0=R[:, 512:1024], scalar1=-0.5, scalar2=None,
            op0=ALU.mult), waits=[("aE", aq1)], inc="vE")
        aq = ctx.emit("scalar", lambda e: e.activation(
            out=R[:, 512:1024], in_=R[:, 512:1024], func=AF.Exp),
            waits=[("vE", vq)], inc="aE")
        ctx.emit("tensor", lambda e: e.matmul(
            p.b[6][:], s.ones_f32[:], R[:, 0:512], start=True, stop=True),
            waits=ctx.bank_acquire(6, [("aE", aq)]), inc="tE")
        ctx.bank_filled(6)
        ctx.emit("tensor", lambda e: e.matmul(
            p.b[7][:], s.ones_f32[:], R[:, 512:1024], start=True, stop=True),
            waits=ctx.bank_acquire(7), inc="tE")
        ctx.bank_filled(7)
        for c in range(NCH):
            sc = s.ln_t[:, (c % 2) * 512:(c % 2) * 512 + 512]
            w = ctx.bank_read_waits(6) + ctx.bank_read_waits(7) if c == 0 else ()
            ctx.emit("vector", lambda e, c=c, sc=sc, t0=t0: e.tensor_sub(
                out=sc, in0=s.x_fm[:, c * T + t0:c * T + t0 + 512], in1=p.b[6][:]),
                waits=w, inc="vE")
            ctx.emit("vector", lambda e, c=c, sc=sc, t0=t0: e.tensor_mul(
                out=s.x_ln[:, c * T + t0:c * T + t0 + 512], in0=sc, in1=p.b[7][:]),
                inc="vE")
        ctx.bank_read_done(6, "vE")
        ctx.bank_read_done(7, "vE")
    ctx.meta["ln_vE"] = ctx.cnt["vE"]



def emit_router(ctx, t, s, p):
    scr = s.ar_f32[:, 12288:20480]
    probs_em = lambda ch: scr[:, ch * T:(ch + 1) * T]
    ptm = lambda tc: scr[:, 2 * T + tc * 160: 2 * T + tc * 160 + 160]
    work = scr[:, 2 * T + 16 * 160: 2 * T + 16 * 160 + 160]
    msk = scr[:, 2 * T + 17 * 160: 2 * T + 18 * 160]
    cols = scr[:, 2 * T + 18 * 160: 2 * T + 18 * 160 + 8]
    ctm = lambda tc: s.wg_sb[0][:, tc * 160: tc * 160 + 160]
    ln_v = ctx.meta["ln_vE"]
    ctx.emit("gpsimd", lambda e: e.dma_start(
        out=s.rw_sb.rearrange("p (kc n) -> p kc n", kc=8),
        in_=t.rwt.rearrange("(kc p) n -> p kc n", p=128)),
        waits=[("tE", ctx.cnt["tE"]), ("vE", ctx.cnt["vE"]),
               ("aE", ctx.cnt.get("aE", 0))], inc="dW", inc_by=16)
    ctx.meta["rw_dW"] = ctx.cnt["dW"]
    for nt in range(4):
        t0 = nt * 512
        for part in range(2):
            M = 128 if part == 0 else 32
            b = 4 + part
            for kc in range(8):
                w = ctx.bank_acquire(b, [("vE", ln_v), ("dW", ctx.meta["rw_dW"])]
                                     ) if kc == 0 else ()
                ctx.emit("tensor", lambda e, kc=kc, part=part, M=M, t0=t0, b=b:
                         e.matmul(p.b[b][0:M, :],
                                  s.rw_sb[:, kc * NE + part * 128:
                                          kc * NE + part * 128 + M],
                                  s.x_ln[:, kc * T + t0: kc * T + t0 + 512],
                                  start=(kc == 0), stop=(kc == 7)),
                         waits=w, inc="tE" if kc == 7 else None)
            ctx.bank_filled(b)
            ctx.emit("scalar", lambda e, part=part, M=M, t0=t0, b=b:
                     e.activation(out=probs_em(part)[0:M, t0:t0 + 512],
                                  in_=p.b[b][0:M, :], func=AF.Sigmoid,
                                  bias=s.eb_sb[0:M, part:part + 1]),
                     waits=ctx.bank_read_waits(b), inc="aE")
            ctx.bank_read_done(b, "aE")
    aq_sig = ctx.cnt["aE"]
    for tc in range(16):
        for part in range(2):
            M = 128 if part == 0 else 32
            b = 6 + (tc * 2 + part) % 2
            ctx.emit("tensor", lambda e, tc=tc, part=part, M=M, b=b:
                     e.transpose(p.b[b][0:128, 0:M],
                                 probs_em(part)[0:M, tc * 128:(tc + 1) * 128],
                                 s.eye_f32[0:M, 0:M]),
                     waits=ctx.bank_acquire(b, [("aE", aq_sig)]), inc="tE")
            ctx.bank_filled(b)
            ctx.emit("vector", lambda e, tc=tc, part=part, M=M, b=b:
                     e.tensor_copy(out=ptm(tc)[:, part * 128:part * 128 + M],
                                   in_=p.b[b][0:128, 0:M]),
                     waits=ctx.bank_read_waits(b), inc="vE")
            ctx.bank_read_done(b, "vE")
    scol = scr[:, 2 * T + 18 * 160: 2 * T + 18 * 160 + 16]
    mxcol = scr[:, 2 * T + 18 * 160 + 16: 2 * T + 18 * 160 + 32]
    sccol = scr[:, 2 * T + 18 * 160 + 32: 2 * T + 18 * 160 + 48]
    def vemit(build):
        vq = ctx.cnt["vE"]
        return ctx.emit("vector", build, waits=[("vE", vq)], inc="vE")
    for tc in range(16):
        vemit(lambda e, tc=tc: e.tensor_copy(out=work, in_=ptm(tc)))
        for it in range(8):
            vemit(lambda e, tc=tc: e.reduce_max(
                out=mxcol[:, tc:tc + 1], in_=work, axis=AX.X))
            if it < 7:
                vemit(lambda e, tc=tc: e.tensor_scalar(
                    out=msk, in0=work, scalar1=mxcol[:, tc:tc + 1], scalar2=None,
                    op0=ALU.is_ge))
                vemit(lambda e: e.scalar_tensor_tensor(
                    out=work, in0=msk, scalar=-1e30, in1=work,
                    op0=ALU.mult, op1=ALU.add))
        vemit(lambda e, tc=tc: e.tensor_scalar(
            out=msk, in0=ptm(tc), scalar1=mxcol[:, tc:tc + 1], scalar2=None,
            op0=ALU.is_ge))
        vemit(lambda e, tc=tc: e.tensor_mul(out=msk, in0=msk, in1=ptm(tc)))
        vemit(lambda e, tc=tc: e.tensor_copy(out=ctm(tc), in_=msk))
        vemit(lambda e, tc=tc: e.reduce_sum(
            out=scol[:, tc:tc + 1], in_=msk, axis=AX.X))
    vemit(lambda e: e.tensor_scalar(
        out=scol, in0=scol, scalar1=1e-9, scalar2=None, op0=ALU.add))
    vemit(lambda e: e.reciprocal(out=sccol, in_=scol))
    vemit(lambda e: e.tensor_scalar(
        out=sccol, in0=sccol, scalar1=2.5, scalar2=None, op0=ALU.mult))
    for tc in range(16):
        vemit(lambda e, tc=tc: e.tensor_scalar(
            out=ctm(tc), in0=ctm(tc), scalar1=sccol[:, tc:tc + 1], scalar2=None,
            op0=ALU.mult))
    vq_c = ctx.cnt["vE"]
    for tc in range(16):
        for part in range(2):
            M = 128 if part == 0 else 32
            b = 6 + (tc * 2 + part) % 2
            pbf = p.b[b].ap().bitcast(BF16)
            ctx.emit("tensor", lambda e, tc=tc, part=part, M=M, pbf=pbf:
                     e.transpose(pbf[0:M, 0:128],
                                 ctm(tc)[:, part * 128:part * 128 + M],
                                 s.eye_bf[:]),
                     waits=ctx.bank_acquire(b, [("vE", vq_c)]), inc="tE")
            ctx.bank_filled(b)
            ctx.emit("vector", lambda e, tc=tc, part=part, M=M, pbf=pbf:
                     e.tensor_copy(
                         out=s.comb_em[0:M, part * T + tc * 128:
                                       part * T + (tc + 1) * 128],
                         in_=pbf[0:M, 0:128]),
                     waits=ctx.bank_read_waits(b), inc="vE")
            ctx.bank_read_done(b, "vE")
    ctx.meta["comb_vE"] = ctx.cnt["vE"]
    ctx.meta["router_tE"] = ctx.cnt["tE"]


def emit_moe(ctx, t, s, p):
    experts = [-1] + list(range(NE))
    groups = [experts[i:i + 2] for i in range(0, len(experts), 2)]
    comb_v = ctx.meta["comb_vE"]
    rt_t = ctx.meta["router_tE"]
    ln_v = ctx.meta["ln_vE"]
    wg_free = [rt_t, 0]
    crow_free = 0
    down_done = 0
    ntile = 0

    for g, grp in enumerate(groups):
        sl = g % 2
        dma_w0 = [("tE", wg_free[sl])]
        for j, e_id in enumerate(grp):
            wg_src = (t.wgst if e_id < 0 else t.wgt[e_id]).rearrange(
                "(kc p) i -> p kc i", p=128)
            wd_src = (t.wdst if e_id < 0 else t.wdt[e_id]).rearrange(
                "(kc p) o -> p kc o", p=128)
            wg_dst = s.wg_sb[sl][:, j * 4096:(j + 1) * 4096].rearrange(
                "p (kc i) -> p kc i", kc=8)
            wd_dst = s.wd_sb[:, j * 4096:(j + 1) * 4096].rearrange(
                "p (kc o) -> p kc o", kc=4)
            ctx.emit("gpsimd", lambda e, a=wg_dst, b=wg_src: e.dma_start(out=a, in_=b),
                     waits=dma_w0 if j == 0 else (), inc="dW", inc_by=16)
            ctx.emit("gpsimd", lambda e, a=wd_dst, b=wd_src: e.dma_start(out=a, in_=b),
                     waits=[("tE", down_done)] if j == 0 else (), inc="dW", inc_by=16)
            crow_dst = s.crow[j]
            w_cr = [("vE", comb_v)] + ([("tE", crow_free)] if j == 0 else [])
            if e_id < 0:
                ctx.emit("vector", lambda e, a=crow_dst: e.memset(a, 1.0),
                         waits=w_cr, inc="vE")
                ctx.meta["crow_memset_vE"] = ctx.cnt["vE"]
            else:
                ch, r = divmod(e_id, 128)
                csrc = s.comb_em[r:r + 1, ch * T:(ch + 1) * T]
                ctx.emit("sync", lambda e, a=crow_dst, b=csrc: e.dma_start(out=a, in_=b),
                         waits=w_cr, inc="dS", inc_by=16)
        dW_at = ctx.cnt["dW"]
        dS_at = ctx.cnt["dS"]
        for j, e_id in enumerate(grp):
            for nt in range(4):
                t0 = nt * 512
                cb = 2 + nt % 2
                ctx.emit("tensor", lambda e, j=j, t0=t0, cb=cb: e.matmul(
                    p.b[cb][:], s.ones_bf[32 * (j + 1):32 * (j + 1) + 1, :],
                    s.crow[j][:, t0:t0 + 512],
                    start=True, stop=True),
                    waits=ctx.bank_acquire(cb, [("dS", dS_at),
                          ("vE", ctx.meta.get("crow_memset_vE", 0))]), inc="tE")
                ctx.bank_filled(cb)
                for mi in range(4):
                    gb = (nt * 4 + mi) % 2
                    w = ctx.bank_acquire(gb)
                    if nt == 0 and mi == 0 and j == 0:
                        w += [("dW", dW_at), ("vE", ln_v)]
                    for kc in range(8):
                        ctx.emit("tensor", lambda e, sl=sl, j=j, mi=mi, kc=kc, t0=t0,
                                 gb=gb: e.matmul(
                                     p.b[gb][:],
                                     s.wg_sb[sl][:, j * 4096 + kc * 512 + mi * 128:
                                                 j * 4096 + kc * 512 + mi * 128 + 128],
                                     s.x_ln[:, kc * T + t0:kc * T + t0 + 512],
                                     start=(kc == 0), stop=(kc == 7)),
                                 waits=w if kc == 0 else (),
                                 inc="tE" if kc == 7 else None)
                    ctx.bank_filled(gb)
                    sj = (nt * 4 + mi) % 2
                    sc = s.silu_sc[:, sj * 512:(sj + 1) * 512]
                    ctx.emit("scalar", lambda e, gb=gb, sc=sc: e.activation(
                        out=sc, in_=p.b[gb][:], func=AF.Silu),
                        waits=ctx.bank_read_waits(
                            gb, [("vE", ctx.meta.get("silu_vE_%d" % sj, 0))]),
                        inc="aE")
                    ctx.bank_read_done(gb, "aE")
                    aq = ctx.cnt["aE"]
                    dst = s.h1w[:, (j * 4 + mi) * T + t0:(j * 4 + mi) * T + t0 + 512]
                    w2 = ctx.bank_read_waits(cb, [("aE", aq)])
                    if nt == 0 and mi == 0 and j == 0:
                        w2 += [("tE", down_done)]
                    vq = ctx.emit("vector", lambda e, sc=sc, dst=dst, cb=cb:
                                  e.tensor_tensor(out=dst, in0=sc, in1=p.b[cb][:],
                                                  op=ALU.mult),
                                  waits=w2, inc="vE")
                    ctx.meta["silu_vE_%d" % sj] = vq
                    ctx.bank_read_done(cb, "vE")
        h1w_v = ctx.cnt["vE"]
        wg_free[sl] = ctx.cnt["tE"]
        crow_free = ctx.cnt["tE"]
        for mo in range(8):
            for nt in range(4):
                t0 = nt * 512
                db = 4 + ntile % 2
                nmm = len(grp) * 4
                i = 0
                for j in range(len(grp)):
                    for kc in range(4):
                        first = (i == 0)
                        i += 1
                        w = ()
                        if first:
                            w = ctx.bank_acquire(db)
                            if mo == 0 and nt == 0:
                                w += [("vE", h1w_v)]
                        ctx.emit("tensor", lambda e, j=j, kc=kc, mo=mo, t0=t0,
                                 db=db, st=first, sp=(i == nmm): e.matmul(
                                     p.b[db][:],
                                     s.wd_sb[:, j * 4096 + kc * 1024 + mo * 128:
                                             j * 4096 + kc * 1024 + mo * 128 + 128],
                                     s.h1w[:, (j * 4 + kc) * T + t0:
                                           (j * 4 + kc) * T + t0 + 512],
                                     start=st, stop=sp),
                                 waits=w, inc="tE" if i == nmm else None)
                ctx.bank_filled(db)
                dst = s.x_fm[:, mo * T + t0:mo * T + t0 + 512]
                ctx.emit("vector", lambda e, db=db, dst=dst:
                         e.scalar_tensor_tensor(out=dst, in0=p.b[db][:], scalar=0.5,
                                                in1=dst, op0=ALU.mult, op1=ALU.add),
                         waits=ctx.bank_read_waits(db), inc="vE")
                ctx.bank_read_done(db, "vE")
                ntile += 1
        down_done = ctx.cnt["tE"]
    ctx.meta["xfm_vE"] = ctx.cnt["vE"]


def emit_dump(ctx, nc, name, ap, cols, dtype, rows=128):
    out = nc.dram_tensor(name, [rows, cols], dtype, kind="ExternalOutput").ap()
    ctx.emit("sync", lambda e: e.dma_start(out=out, in_=ap),
             waits=[("vE", ctx.cnt.get("vE", 0)), ("aE", ctx.cnt.get("aE", 0)),
                    ("tE", ctx.cnt.get("tE", 0)), ("dW", ctx.cnt.get("dW", 0))],
             inc="dS", inc_by=16)


def emit_store_x(ctx, t, s):
    src = s.x_fm.ap().rearrange("p (c t) -> p c t", c=NCH)
    dst = t.xout.rearrange("(c p) t -> p c t", p=128)
    ctx.emit("sync", lambda e: e.dma_start(out=dst, in_=src),
             waits=[("vE", ctx.cnt.get("vE", 0)), ("aE", ctx.cnt.get("aE", 0)),
                    ("tE", ctx.cnt.get("tE", 0))], inc="dS", inc_by=16)
    ctx.emit("sync", lambda e: e.nop(), waits=[("dS", ctx.cnt["dS"])])


def emit_mamba(ctx, t, s, p):
    ln_v = ctx.meta["ln_vE"]
    ent_t, ent_v, ent_a = ctx.cnt["tE"], ctx.cnt["vE"], ctx.cnt.get("aE", 0)
    bar = [("tE", ent_t), ("vE", ent_v), ("aE", ent_a)]
    PI = 3.14159265358979
    TWO_PI = 2 * PI
    INV_2PI = 1.0 / TWO_PI
    MAGIC = 1.5 * 2.0 ** 23

    def ve(build, extra=()):
        vq = ctx.cnt["vE"]
        return ctx.emit("vector", build, waits=[("vE", vq)] + list(extra), inc="vE")

    for dst, srcap in ((s.tri_sb, t.c_tri), (s.minf_sb, t.c_minf),
                       (s.invf_row, t.c_invfreq), (s.dtb_row, t.dt_bias),
                       (s.alog_row, t.a_log), (s.dskip_sb, t.dskip_rep),
                       (s.nw_sb, t.nw_cm)):
        ctx.emit("sync", lambda e, a=dst, b=srcap: e.dma_start(out=a[:], in_=b),
                 waits=bar, inc="dS", inc_by=16)
    dS0 = ctx.cnt["dS"]
    ctx.emit("tensor", lambda e: e.matmul(
        p.b[6][:, 0:32], s.ones_f32[:], s.dtb_row[:], start=True, stop=True),
        waits=ctx.bank_acquire(6, [("dS", dS0)] + bar), inc="tE")
    ctx.bank_filled(6)
    ctx.emit("vector", lambda e: e.tensor_copy(out=s.dtb_bc[:], in_=p.b[6][:, 0:32]),
             waits=ctx.bank_read_waits(6), inc="vE")
    ctx.bank_read_done(6, "vE")
    aq = ctx.emit("scalar", lambda e: e.activation(
        out=s.mrows[:, 0:32], in_=s.alog_row[:], func=AF.Exp),
        waits=[("dS", dS0)] + bar, inc="aE")
    ve(lambda e: e.tensor_scalar(out=s.mrows[:, 0:32], in0=s.mrows[:, 0:32],
                                 scalar1=-1.0, scalar2=None, op0=ALU.mult),
       extra=[("aE", aq)])
    vq = ctx.cnt["vE"]
    ctx.emit("tensor", lambda e: e.matmul(
        p.b[6][:, 0:32], s.ones_f32[:], s.mrows[:, 0:32], start=True, stop=True),
        waits=ctx.bank_acquire(6, [("vE", vq)]), inc="tE")
    ctx.bank_filled(6)
    ctx.emit("vector", lambda e: e.tensor_copy(out=s.nea_bc[:], in_=p.b[6][:, 0:32]),
             waits=ctx.bank_read_waits(6), inc="vE")
    ctx.bank_read_done(6, "vE")
    ve(lambda e: e.memset(s.g_f32, 0.0))
    ve(lambda e: e.memset(s.g_bf, 0.0))

    scan_done_tE = 0
    for h2 in range(2):
        th0 = h2 * 1024
        for mi, m in enumerate(range(16, 34)):
            sl = mi % 2
            mw = 128 if m < 33 else 32
            ctx.emit("gpsimd", lambda e, m=m, sl=sl, mw=mw: e.dma_start(
                out=s.wslab[sl][:, 0:8 * mw].rearrange("p (kc m) -> p kc m", kc=8),
                in_=t.w_in[:, m * 128: m * 128 + mw].rearrange(
                    "(kc p) m -> p kc m", p=128)),
                waits=[("tE", ctx.meta.get("mm_wslab_free_%d" % sl, 0))] +
                (bar if h2 == 0 and mi < 2 else []), inc="dW", inc_by=16)
            dWs = ctx.cnt["dW"]
            for nth in range(2):
                t0 = th0 + nth * 512
                b = (mi * 2 + nth) % 2
                for kc in range(8):
                    w = ()
                    if kc == 0:
                        w = ctx.bank_acquire(b, [("dW", dWs), ("vE", ln_v),
                                                 ("tE", scan_done_tE)])
                    ctx.emit("tensor", lambda e, sl=sl, kc=kc, t0=t0, b=b, mw=mw:
                             e.matmul(p.b[b][0:mw, 0:512],
                                      s.wslab[sl][:, kc * mw:(kc + 1) * mw],
                                      s.x_ln[:, kc * T + t0: kc * T + t0 + 512],
                                      start=(kc == 0), stop=(kc == 7)),
                             waits=w, inc="tE" if kc == 7 else None)
                tq = ctx.cnt["tE"]
                if nth == 1:
                    ctx.meta["mm_wslab_free_%d" % sl] = tq
                ctx.bank_filled(b)
                lt0 = nth * 512
                if m < 32:
                    dst = s.xs_fmh[:, (m - 16) * 1024 + lt0:(m - 16) * 1024 + lt0 + 512]
                    ctx.emit("scalar", lambda e, b=b, dst=dst: e.activation(
                        out=dst, in_=p.b[b][:], func=AF.Identity),
                        waits=ctx.bank_read_waits(b), inc="aE")
                    ctx.bank_read_done(b, "aE")
                elif m == 32:
                    dst = s.bcdt[:, lt0:lt0 + 512]
                    ctx.emit("scalar", lambda e, b=b, dst=dst: e.activation(
                        out=dst, in_=p.b[b][:], func=AF.Identity),
                        waits=ctx.bank_read_waits(b), inc="aE")
                    ctx.bank_read_done(b, "aE")
                else:
                    dst = s.bcdt[0:32, 1024 + lt0:1024 + lt0 + 512]
                    ctx.emit("scalar", lambda e, b=b, dst=dst: e.activation(
                        out=dst, in_=p.b[b][0:32, :], func=AF.Identity),
                        waits=ctx.bank_read_waits(b), inc="aE")
                    ctx.bank_read_done(b, "aE")
        inproj_a = ctx.cnt["aE"]
        for c in range(8):
            gc = h2 * 8 + c
            co = gc * 32
            pbf = p.b[7].ap().bitcast(BF16)
            ctx.emit("tensor", lambda e, c=c, pbf=pbf: e.transpose(
                pbf[0:128, 0:32],
                s.bcdt[0:32, 1024 + c * 128:1024 + (c + 1) * 128],
                s.eye_bf[0:32, 0:32]),
                waits=ctx.bank_acquire(7, [("aE", inproj_a)]), inc="tE")
            ctx.bank_filled(7)
            DT = s.dt_tm[:, co:co + 32]
            SC = s.th_sc[:, 0:32]
            ve(lambda e, DT=DT, pbf=pbf: e.tensor_copy(out=DT, in_=pbf[0:128, 0:32]),
               extra=ctx.bank_read_waits(7))
            ctx.bank_read_done(7, "vE")
            ve(lambda e, DT=DT: e.tensor_add(out=DT, in0=DT, in1=s.dtb_bc[:]))
            vq = ctx.cnt["vE"]
            aq = ctx.emit("scalar", lambda e, DT=DT, SC=SC: e.activation(
                out=SC, in_=DT, func=AF.Abs), waits=[("vE", vq)], inc="aE")
            ve(lambda e, SC=SC: e.tensor_scalar(
                out=SC, in0=SC, scalar1=-1.0, scalar2=None, op0=ALU.mult),
               extra=[("aE", aq)])
            aq = ctx.emit("scalar", lambda e, SC=SC: e.activation(
                out=SC, in_=SC, func=AF.Exp), waits=[("vE", ctx.cnt["vE"])], inc="aE")
            ve(lambda e, DT=DT: e.tensor_relu(out=s.th_sc[:, 32:64], in_=DT),
               extra=[("aE", aq)])
            ve(lambda e, SC=SC: e.tensor_scalar(
                out=SC, in0=SC, scalar1=1.0, scalar2=None, op0=ALU.add))
            aq = ctx.emit("scalar", lambda e, SC=SC: e.activation(
                out=SC, in_=SC, func=AF.Ln), waits=[("vE", ctx.cnt["vE"])], inc="aE")
            ve(lambda e, DT=DT, SC=SC: e.tensor_add(
                out=DT, in0=s.th_sc[:, 32:64], in1=SC), extra=[("aE", aq)])
            CU = s.cumdt[:, co:co + 32]
            vq = ctx.cnt["vE"]
            ctx.emit("tensor", lambda e, DT=DT: e.matmul(
                p.b[7][:, 0:32], s.tri_sb[:], DT, start=True, stop=(gc == 0)),
                waits=ctx.bank_acquire(7, [("vE", vq), ("dS", dS0)]), inc="tE")
            if gc > 0:
                ctx.emit("tensor", lambda e: e.matmul(
                    p.b[7][:, 0:32], s.ones_f32[:], s.mrows[:, 64:96],
                    start=False, stop=True),
                    waits=[("dS", ctx.meta.get("mm_carry_dS", 0))], inc="tE")
                ctx.meta["mm_carry_used"] = ctx.cnt["tE"]
            ctx.bank_filled(7)
            ve(lambda e, CU=CU: e.tensor_copy(out=CU, in_=p.b[7][:, 0:32]),
               extra=ctx.bank_read_waits(7))
            ctx.bank_read_done(7, "vE")
            LL = s.L_tm[:, co:co + 32]
            ve(lambda e, LL=LL, CU=CU: e.tensor_mul(out=LL, in0=CU, in1=s.nea_bc[:]))
            vq = ctx.cnt["vE"]
            ctx.emit("sync", lambda e, CU=CU: e.dma_start(
                out=s.mrows[:, 64:96], in_=CU[127:128, :]),
                waits=[("vE", vq), ("tE", ctx.meta.get("mm_carry_used", 0))],
                inc="dS", inc_by=16)
            ctx.meta["mm_carry_dS"] = ctx.cnt["dS"]
        prep_v = ctx.cnt["vE"]
        for c in range(8):
            gc = h2 * 8 + c
            co = gc * 32
            qlocal = c % 4
            ctx.emit("tensor", lambda e: e.matmul(
                p.b[6][:], s.ones_f32[:], s.invf_row[:], start=True, stop=True),
                waits=ctx.bank_acquire(6, [("vE", prep_v)]), inc="tE")
            ctx.bank_filled(6)
            cum_exp = s.cumdt[:, co:co + 32].rearrange("p (h one) -> p h one", one=1).broadcast_to(
                [128, 32, 16])
            TH = s.th_sc[:, 0:512]
            TH3 = TH.rearrange("p (h j) -> p h j", h=32)
            P63 = p.b[6].ap().rearrange("p (h j) -> p h j", h=32)
            ve(lambda e, cum_exp=cum_exp, TH3=TH3, P63=P63: e.tensor_mul(
                out=TH3, in0=cum_exp, in1=P63),
               extra=ctx.bank_read_waits(6))
            ctx.bank_read_done(6, "vE")
            CSF = s.cs_c.bitcast(F32)
            ve(lambda e, TH=TH, CSF=CSF: e.tensor_scalar(
                out=CSF, in0=TH, scalar1=INV_2PI, scalar2=MAGIC,
                op0=ALU.mult, op1=ALU.add),
               extra=[("aE", ctx.meta.get("mm_cs_aE", 0)),
                      ("tE", ctx.meta.get("mm_fm_used_1", 0))])
            ve(lambda e, CSF=CSF: e.tensor_scalar(
                out=CSF, in0=CSF, scalar1=MAGIC, scalar2=None,
                op0=ALU.subtract))
            ve(lambda e, TH=TH, CSF=CSF: e.scalar_tensor_tensor(
                out=TH, in0=CSF, scalar=-TWO_PI, in1=TH,
                op0=ALU.mult, op1=ALU.add))
            vq = ctx.cnt["vE"]
            aq = ctx.emit("scalar", lambda e, TH=TH: e.activation(
                out=s.cs_c[:, 512:1024], in_=TH, func=AF.Sin),
                waits=[("vE", vq)], inc="aE")
            aq2 = ctx.emit("scalar", lambda e, TH=TH: e.activation(
                out=TH, in_=TH, func=AF.Abs), waits=[("aE", aq),
                ("vE", ctx.cnt["vE"])], inc="aE")
            ve(lambda e, TH=TH: e.tensor_scalar(
                out=TH, in0=TH, scalar1=-1.0, scalar2=PI / 2,
                op0=ALU.mult, op1=ALU.add), extra=[("aE", aq2)])
            vq = ctx.cnt["vE"]
            aq_cs = ctx.emit("scalar", lambda e, TH=TH: e.activation(
                out=s.cs_c[:, 0:512], in_=TH, func=AF.Sin),
                waits=[("vE", vq)], inc="aE")
            ctx.meta["mm_cs_aE"] = aq_cs
            pbf7 = p.b[7].ap().bitcast(BF16)
            ctx.emit("tensor", lambda e, c=c, pbf7=pbf7: e.transpose(
                pbf7[0:128, 0:128], s.bcdt[:, c * 128:(c + 1) * 128],
                s.eye_bf[:]),
                waits=ctx.bank_acquire(7, [("aE", inproj_a)]), inc="tE")
            ctx.bank_filled(7)
            cosap = s.cs_c[:, 0:512].rearrange("p (h j) -> p h j", h=32)
            sinap = s.cs_c[:, 512:1024].rearrange("p (h j) -> p h j", h=32)
            Bev = pbf7[0:128, 0:64:2].rearrange("p (one j) -> p one j", one=1).broadcast_to(
                [128, 32, 32])
            Bod = pbf7[0:128, 1:64:2].rearrange("p (one j) -> p one j", one=1).broadcast_to(
                [128, 32, 32])
            Bnr = pbf7[0:128, 32:64].rearrange("p (one n) -> p one n", one=1).broadcast_to(
                [128, 32, 32])
            Cev = pbf7[0:128, 64:128:2].rearrange("p (one j) -> p one j", one=1).broadcast_to(
                [128, 32, 32])
            Cod = pbf7[0:128, 65:128:2].rearrange("p (one j) -> p one j", one=1).broadcast_to(
                [128, 32, 32])
            Cnr = pbf7[0:128, 96:128].rearrange("p (one n) -> p one n", one=1).broadcast_to(
                [128, 32, 32])
            for dst_t, ev, od in ((s.bhat, Bev, Bod), (s.ctil, Cev, Cod)):
                de = dst_t.rearrange("p (h n) -> p h n", h=32)[:, :, 0:31:2]
                do = dst_t.rearrange("p (h n) -> p h n", h=32)[:, :, 1:32:2]
                dn = dst_t.rearrange("p (h n) -> p h n", h=32)[:, :, 32:64]
                evj = ev[:, :, 0:16]
                odj = od[:, :, 0:16]
                scr2 = s.fmscr.rearrange("p (h j) -> p h j", h=32)
                cos3 = cosap
                sin3 = sinap
                ve(lambda e, de=de, evj=evj, cos3=cos3: e.tensor_tensor(
                    out=de, in0=cos3, in1=evj, op=ALU.mult),
                   extra=[("aE", aq_cs)] + ctx.bank_read_waits(7) +
                   [("tE", ctx.meta.get("mm_fmscr_used", 0))])
                ve(lambda e, scr2=scr2, odj=odj, sin3=sin3: e.tensor_tensor(
                    out=scr2, in0=sin3, in1=odj, op=ALU.mult))
                ve(lambda e, de=de, scr2=scr2: e.tensor_tensor(
                    out=de, in0=de, in1=scr2, op=ALU.add))
                ve(lambda e, do=do, odj=odj, cos3=cos3: e.tensor_tensor(
                    out=do, in0=cos3, in1=odj, op=ALU.mult))
                ve(lambda e, scr2=scr2, evj=evj, sin3=sin3: e.tensor_tensor(
                    out=scr2, in0=sin3, in1=evj, op=ALU.mult))
                ve(lambda e, do=do, scr2=scr2: e.tensor_tensor(
                    out=do, in0=do, in1=scr2, op=ALU.subtract))
                ve(lambda e, dn=dn, nr=(Bnr if dst_t is s.bhat else Cnr):
                   e.tensor_copy(out=dn, in_=nr))
            ctx.bank_read_done(7, "vE")
            dt_exp = s.dt_tm[:, co:co + 32].rearrange("p (h one) -> p h one", one=1).broadcast_to(
                [128, 32, 64])
            bhat3 = s.bhat.rearrange("p (h n) -> p h n", h=32)
            ve(lambda e, dt_exp=dt_exp, bhat3=bhat3: e.tensor_tensor(
                out=bhat3, in0=bhat3, in1=dt_exp, op=ALU.mult))
            if gc == 0:
                vq_pe = ve(lambda e: e.memset(s.mrows[:, 96:128], 0.0))
                ctx.meta["mm_pe_v"] = vq_pe
            else:
                pco = (gc - 1) * 32
                ctx.emit("sync", lambda e, pco=pco: e.dma_start(
                    out=s.mrows[:, 96:128], in_=s.L_tm[127:128, pco:pco + 32]),
                    waits=[("vE", prep_v), ("tE", ctx.meta.get("mm_rowbc_tE", 0))],
                    inc="dS", inc_by=16)
            ctx.emit("sync", lambda e, co=co: e.dma_start(
                out=s.mrows[:, 128:160], in_=s.L_tm[127:128, co:co + 32]),
                waits=[("vE", prep_v), ("tE", ctx.meta.get("mm_rowbc_tE", 0))],
                inc="dS", inc_by=16)
            dSr = ctx.cnt["dS"]
            ctx.emit("tensor", lambda e: e.matmul(
                p.b[6][:, 0:64], s.ones_f32[:],
                s.mrows[:, 96:160], start=True, stop=True),
                waits=ctx.bank_acquire(6, [("dS", dSr),
                      ("vE", ctx.meta.get("mm_pe_v", 0))]), inc="tE")
            ctx.bank_filled(6)
            ctx.meta["mm_rowbc_tE"] = ctx.cnt["tE"]
            COLF = s.th_sc[:, 0:32]
            FALL = s.th_sc[:, 32:64]
            ve(lambda e, COLF=COLF, co=co: e.tensor_sub(
                out=COLF, in0=s.L_tm[:, co:co + 32], in1=p.b[6][:, 0:32]),
               extra=ctx.bank_read_waits(6) + [("aE", aq_cs)])
            ve(lambda e, FALL=FALL, co=co: e.tensor_sub(
                out=FALL, in0=p.b[6][:, 32:64], in1=s.L_tm[:, co:co + 32]))
            ctx.bank_read_done(6, "vE")
            vq = ctx.cnt["vE"]
            aq = ctx.emit("scalar", lambda e, COLF=COLF: e.activation(
                out=COLF, in_=COLF, func=AF.Exp), waits=[("vE", vq)], inc="aE")
            aq = ctx.emit("scalar", lambda e, FALL=FALL: e.activation(
                out=FALL, in_=FALL, func=AF.Exp), inc="aE")
            colf_exp = COLF.rearrange("p (h one) -> p h one", one=1).broadcast_to([128, 32, 64])
            cbar3 = s.cbar.rearrange("p (h n) -> p h n", h=32)
            ctil3 = s.ctil.rearrange("p (h n) -> p h n", h=32)
            ve(lambda e, colf_exp=colf_exp, cbar3=cbar3, ctil3=ctil3:
               e.tensor_tensor(out=cbar3, in0=ctil3, in1=colf_exp, op=ALU.mult),
               extra=[("aE", aq)])
            ve(lambda e: e.tensor_sub(out=s.mrows[:, 160:192],
                                      in0=s.mrows[:, 128:160],
                                      in1=s.mrows[:, 96:128]),
               extra=[("dS", dSr)])
            vq = ctx.cnt["vE"]
            aq = ctx.emit("scalar", lambda e: e.activation(
                out=s.mrows[:, 160:192], in_=s.mrows[:, 160:192], func=AF.Exp),
                waits=[("vE", vq)], inc="aE")
            ctx.emit("tensor", lambda e: e.matmul(
                p.b[6][:, 64:96], s.ones_f32[:], s.mrows[:, 160:192],
                start=True, stop=True),
                waits=ctx.bank_acquire(6, [("aE", aq)]), inc="tE")
            ctx.bank_filled(6)
            GSC = s.th_sc[:, 64:96]
            ve(lambda e, GSC=GSC: e.tensor_copy(out=GSC[0:64, :],
                                                in_=p.b[6][0:64, 64:96]),
               extra=ctx.bank_read_waits(6))
            ctx.bank_read_done(6, "vE")
            ctx.emit("tensor", lambda e, co=co: e.transpose(
                p.b[6].ap()[0:32, 0:128], s.L_tm[:, co:co + 32], s.eye_f32[:]),
                waits=ctx.bank_acquire(6), inc="tE")
            ctx.bank_filled(6)
            ve(lambda e: e.tensor_copy(out=s.L_emc[:], in_=p.b[6].ap()[0:32, 0:128]),
               extra=ctx.bank_read_waits(6))
            ctx.bank_read_done(6, "vE")
            chunk_v = ctx.cnt["vE"]
            chunk_a = ctx.cnt["aE"]
            p2bf = [p.b[2].ap().bitcast(BF16), p.b[6].ap().bitcast(BF16)]
            DSBs = [s.th_sc[:, 128:256], s.th_sc[:, 256:384]]
            FMs = [s.fmscr[:, 0:448], s.cs_c[:, 0:448]]
            ATs = [s.attnT_m[:, 0:128], s.bcdt[:, 1024:1152]]
            ybank = [3, 0]
            stbank = [7, 1]
            scbank = [4, 5]
            trbank = [2, 6]
            headmeta = {}

            def stageA(h):
                sl = h % 2
                base = (h % 2) * 64
                tb = trbank[sl]
                pb = p2bf[sl]
                DSB = DSBs[sl]
                ctx.emit("sync", lambda e, h=h: e.dma_start(
                    out=s.mrows[:, 192 + sl * 128:320 + sl * 128],
                    in_=s.L_emc[h:h + 1, :]),
                    waits=[("vE", chunk_v),
                           ("tE", ctx.meta.get("mm_d_tE_%d" % sl, 0))],
                    inc="dS", inc_by=16)
                dSh = ctx.cnt["dS"]
                ctx.emit("tensor", lambda e, sl=sl: e.matmul(
                    p.b[tb][:, 0:128], s.ones_f32[:],
                    s.mrows[:, 192 + sl * 128:320 + sl * 128],
                    start=True, stop=True),
                    waits=ctx.bank_acquire(tb, [("dS", dSh)]), inc="tE")
                xs_src = s.xs_fmh[base:base + 64,
                                  (h // 2) * 1024 + c * 128:
                                  (h // 2) * 1024 + (c + 1) * 128]
                ctx.emit("tensor", lambda e, xs_src=xs_src, pb=pb, base=base:
                         e.transpose(pb[0:128, 256:320], xs_src,
                                     s.eye_bf[base:base + 64, base:base + 64]),
                         waits=[("vE", chunk_v)], inc=None)
                ctx.emit("tensor", lambda e, h=h, pb=pb: e.transpose(
                    pb[0:64, 320:448], s.bhat[:, h * 64:(h + 1) * 64],
                    s.eye_bf[:]), inc=None)
                ctx.emit("tensor", lambda e, h=h, pb=pb: e.transpose(
                    pb[0:64, 448:576], s.ctil[:, h * 64:(h + 1) * 64],
                    s.eye_bf[:]), inc=None)
                ctx.emit("tensor", lambda e, h=h, pb=pb: e.transpose(
                    pb[0:64, 576:704], s.cbar[:, h * 64:(h + 1) * 64],
                    s.eye_bf[:]), inc="tE")
                ctx.bank_filled(tb)
                ctx.meta["mm_d_tE_%d" % sl] = ctx.cnt["tE"]
                ve(lambda e, DSB=DSB, co=co, h=h, tb=tb: e.tensor_scalar(
                    out=DSB, in0=p.b[tb][:, 0:128],
                    scalar1=s.L_tm[:, co + h:co + h + 1], scalar2=None,
                    op0=ALU.subtract), extra=ctx.bank_read_waits(tb))
                ve(lambda e, DSB=DSB: e.tensor_add(out=DSB, in0=DSB,
                                                   in1=s.minf_sb[:]))
                vq = ctx.cnt["vE"]
                aqD = ctx.emit("scalar", lambda e, DSB=DSB: e.activation(
                    out=DSB, in_=DSB, func=AF.Exp), waits=[("vE", vq)], inc="aE")
                FM = FMs[sl]
                ve(lambda e, pb=pb, FM=FM: e.tensor_copy(
                    out=FM, in_=pb[0:128, 256:704]),
                   extra=ctx.bank_read_waits(tb) +
                   [("tE", ctx.meta.get("mm_fm_used_%d" % sl, 0))])
                ctx.bank_read_done(tb, "vE")
                fm_v = ctx.cnt["vE"]
                ve(lambda e, h=h, FALL=FALL: e.tensor_scalar(
                    out=s.bhat[:, h * 64:(h + 1) * 64],
                    in0=s.bhat[:, h * 64:(h + 1) * 64],
                    scalar1=FALL[:, h:h + 1], scalar2=None, op0=ALU.mult))
                headmeta[h] = (fm_v, ctx.cnt["vE"], aqD)

            def stageB(h):
                sl = h % 2
                base = (h % 2) * 64
                fm_v, bbar_v, aqD = headmeta[h]
                FM = FMs[sl]
                DSB = DSBs[sl]
                AT = ATs[sl]
                sb_ = scbank[sl]
                yb = ybank[sl]
                stb = stbank[sl]
                ctx.emit("tensor", lambda e, sb_=sb_, FM=FM: e.matmul(
                    p.b[sb_][:, 0:128], FM[0:64, 64:192],
                    FM[0:64, 192:320], start=True, stop=True),
                    waits=ctx.bank_acquire(sb_, [("vE", fm_v)]), inc="tE")
                ctx.bank_filled(sb_)
                ve(lambda e, sb_=sb_, DSB=DSB, AT=AT: e.tensor_tensor(
                    out=AT, in0=p.b[sb_][:, 0:128], in1=DSB, op=ALU.mult),
                   extra=ctx.bank_read_waits(sb_) +
                   [("aE", aqD), ("tE", ctx.meta.get("mm_at_used_%d" % sl, 0))])
                ctx.bank_read_done(sb_, "vE")
                ve(lambda e, h=h, AT=AT: e.scalar_tensor_tensor(
                    out=AT, in0=s.eye_bf[:],
                    scalar=s.dskip_sb[:, h:h + 1], in1=AT,
                    op0=ALU.mult, op1=ALU.add))
                at_v = ctx.cnt["vE"]
                ctx.emit("tensor", lambda e, h=h, FM=FM, yb=yb: e.matmul(
                    p.b[yb][0:64, 0:128], s.g_bf[:, h * 64:(h + 1) * 64],
                    FM[0:64, 320:448], start=True, stop=False),
                    waits=ctx.bank_acquire(yb, [("vE", at_v)]), inc=None)
                ctx.emit("tensor", lambda e, FM=FM, AT=AT, yb=yb: e.matmul(
                    p.b[yb][0:64, 0:128], FM[:, 0:64],
                    AT, start=False, stop=True), inc="tE")
                ctx.bank_filled(yb)
                ctx.meta["mm_at_used_%d" % sl] = ctx.cnt["tE"]
                ydst = s.y_q[base:base + 64,
                             (h // 2) * 512 + qlocal * 128:
                             (h // 2) * 512 + (qlocal + 1) * 128]
                ctx.emit("vector", lambda e, ydst=ydst, yb=yb: e.tensor_copy(
                    out=ydst, in_=p.b[yb][0:64, 0:128]),
                    waits=ctx.bank_read_waits(yb), inc="vE")
                ctx.bank_read_done(yb, "vE")
                ctx.emit("tensor", lambda e, h=h, FM=FM, stb=stb: e.matmul(
                    p.b[stb][0:64, 0:64], s.bhat[:, h * 64:(h + 1) * 64],
                    FM[:, 0:64], start=True, stop=True),
                    waits=ctx.bank_acquire(stb, [("vE", bbar_v)]), inc="tE")
                ctx.bank_filled(stb)
                ctx.meta["mm_fm_used_%d" % sl] = ctx.cnt["tE"]
                gs = s.g_f32[:, h * 64:(h + 1) * 64]
                ve(lambda e, gs=gs, GSC=GSC, h=h: e.tensor_scalar(
                    out=gs, in0=gs, scalar1=GSC[0:64, h:h + 1], scalar2=None,
                    op0=ALU.mult))
                ve(lambda e, gs=gs, stb=stb: e.tensor_add(
                    out=gs, in0=gs, in1=p.b[stb][0:64, 0:64]),
                   extra=ctx.bank_read_waits(stb))
                ctx.bank_read_done(stb, "vE")
                ve(lambda e, gs=gs, h=h: e.tensor_copy(
                    out=s.g_bf[:, h * 64:(h + 1) * 64], in_=gs))

            for h in range(33):
                if h < 32:
                    stageA(h)
                if h >= 1:
                    stageB(h - 1)
            if qlocal == 3:
                emit_mamba_zphase(ctx, t, s, p, h2, (c // 4), dS0)
        scan_done_tE = ctx.cnt["tE"]
    ctx.meta["xfm_vE"] = ctx.cnt["vE"]


def emit_mamba_zphase(ctx, t, s, p, h2, qh, dS0):
    ln_v = ctx.meta["ln_vE"]
    tq0 = h2 * 1024 + qh * 512
    ve = None
    def ve(build, extra=()):
        vq = ctx.cnt["vE"]
        return ctx.emit("vector", build, waits=[("vE", vq)] + list(extra), inc="vE")
    yq_v = ctx.cnt["vE"]
    zq_a = ctx.cnt["aE"]
    for m in range(16):
        sl = m % 2
        ctx.emit("gpsimd", lambda e, m=m, sl=sl: e.dma_start(
            out=s.wslab[sl].rearrange("p (kc m) -> p kc m", kc=8),
            in_=t.w_in[:, m * 128:(m + 1) * 128].rearrange(
                "(kc p) m -> p kc m", p=128)),
            waits=[("tE", ctx.meta.get("mm_wslab_free_%d" % sl, 0))],
            inc="dW", inc_by=16)
        dWs = ctx.cnt["dW"]
        b = m % 2
        for kc in range(8):
            w = ()
            if kc == 0:
                w = ctx.bank_acquire(b, [("dW", dWs), ("vE", yq_v)])
            ctx.emit("tensor", lambda e, sl=sl, kc=kc, b=b: e.matmul(
                p.b[b][:], s.wslab[sl][:, kc * 128:(kc + 1) * 128],
                s.x_ln[:, kc * T + tq0: kc * T + tq0 + 512],
                start=(kc == 0), stop=(kc == 7)), waits=w,
                inc="tE" if kc == 7 else None)
        ctx.meta["mm_wslab_free_%d" % sl] = ctx.cnt["tE"]
        ctx.bank_filled(b)
        zscr = s.cs_c[:, 0:512]
        aq = ctx.emit("scalar", lambda e, b=b, zscr=zscr: e.activation(
            out=zscr, in_=p.b[b][:], func=AF.Silu),
            waits=ctx.bank_read_waits(b, [("aE", ctx.meta.get("mm_z_aq", 0)),
                                          ("vE", ctx.meta.get("mm_z_vq", 0))]),
            inc="aE")
        ctx.bank_read_done(b, "aE")
        ctx.meta["mm_z_aq"] = aq
        yslice = s.y_q[:, m * 512:(m + 1) * 512]
        vq = ctx.emit("vector", lambda e, yslice=yslice, zscr=zscr: e.tensor_mul(
            out=yslice, in0=yslice, in1=zscr), waits=[("aE", aq)], inc="vE")
        ctx.meta["mm_z_vq"] = vq
        sqscr = s.cs_c[:, 512:1024]
        aq = ctx.emit("scalar", lambda e, yslice=yslice, sqscr=sqscr: e.activation(
            out=sqscr, in_=yslice, func=AF.Square), waits=[("vE", vq)], inc="aE")
        ctx.emit("tensor", lambda e, m=m, sqscr=sqscr: e.matmul(
            p.b[4][0:1, :], s.onescol_bf[:], sqscr,
            start=(m == 0), stop=(m == 15)),
            waits=ctx.bank_acquire(4, [("aE", aq)]) if m == 0 else [("aE", aq)],
            inc="tE")
    ctx.bank_filled(4)
    R = s.th_sc[0:1, 0:512]
    ve(lambda e, R=R: e.tensor_scalar(
        out=R, in0=p.b[4][0:1, :], scalar1=1.0 / 2048.0, scalar2=1e-5,
        op0=ALU.mult, op1=ALU.add),
       extra=ctx.bank_read_waits(4) + [("aE", zq_a)])
    ctx.bank_read_done(4, "vE")
    vq = ctx.cnt["vE"]
    aq = ctx.emit("scalar", lambda e, R=R: e.activation(out=R, in_=R, func=AF.Ln),
                  waits=[("vE", vq)], inc="aE")
    ve(lambda e, R=R: e.tensor_scalar(
        out=R, in0=R, scalar1=-0.5, scalar2=None, op0=ALU.mult),
       extra=[("aE", aq)])
    vq = ctx.cnt["vE"]
    aq = ctx.emit("scalar", lambda e, R=R: e.activation(out=R, in_=R, func=AF.Exp),
                  waits=[("vE", vq)], inc="aE")
    ctx.emit("tensor", lambda e, R=R: e.matmul(
        p.b[5][:], s.ones_f32[:], R, start=True, stop=True),
        waits=ctx.bank_acquire(5, [("aE", aq)]), inc="tE")
    ctx.bank_filled(5)
    for m in range(16):
        yslice = s.y_q[:, m * 512:(m + 1) * 512]
        ctx.emit("vector", lambda e, yslice=yslice: e.tensor_mul(
            out=yslice, in0=yslice, in1=p.b[5][:]),
            waits=ctx.bank_read_waits(5) if m == 0 else (), inc="vE")
        ctx.emit("vector", lambda e, yslice=yslice, m=m: e.tensor_scalar(
            out=yslice, in0=yslice, scalar1=s.nw_sb[:, m:m + 1], scalar2=None,
            op0=ALU.mult), inc="vE")
    ctx.bank_read_done(5, "vE")
    yn_v = ctx.cnt["vE"]
    for kc in range(16):
        sl = kc % 2
        ctx.emit("gpsimd", lambda e, kc=kc, sl=sl: e.dma_start(
            out=s.wslab[sl][:, 0:1024],
            in_=t.w_out[kc * 128:(kc + 1) * 128, :]),
            waits=[("tE", ctx.meta.get("mm_wslab_free_%d" % sl, 0))],
            inc="dW", inc_by=16)
        dWs = ctx.cnt["dW"]
        for mo in range(8):
            w = [("dW", dWs)]
            if kc == 0:
                w += ctx.bank_acquire(mo, [("vE", yn_v)])
            ctx.emit("tensor", lambda e, kc=kc, sl=sl, mo=mo: e.matmul(
                p.b[mo][:], s.wslab[sl][:, mo * 128:(mo + 1) * 128],
                s.y_q[:, kc * 512:(kc + 1) * 512],
                start=(kc == 0), stop=(kc == 15)),
                waits=w if (kc == 0 or mo == 0) else (),
                inc="tE" if (kc == 15 or mo == 7) else None)
        ctx.meta["mm_wslab_free_%d" % sl] = ctx.cnt["tE"]
    for mo in range(8):
        ctx.bank_filled(mo)
        dst = s.x_fm[:, mo * T + tq0: mo * T + tq0 + 512]
        ctx.emit("vector", lambda e, mo=mo, dst=dst: e.tensor_add(
            out=dst, in0=p.b[mo][:], in1=dst),
            waits=ctx.bank_read_waits(mo), inc="vE")
        ctx.bank_read_done(mo, "vE")
    ctx.meta["mm_yq_free"] = ctx.cnt["vE"]


def emit_attn(ctx, t, s, p):
    ln_v = ctx.meta["ln_vE"]
    ent_t, ent_v, ent_a = ctx.cnt["tE"], ctx.cnt["vE"], ctx.cnt.get("aE", 0)
    bar = [("tE", ent_t), ("vE", ent_v), ("aE", ent_a)]
    atc = s.atc
    ctx.emit("sync", lambda e: e.dma_start(out=s.maskband, in_=t.c_maskband),
             waits=bar, inc="dS", inc_by=16)
    ctx.emit("sync", lambda e: e.dma_start(out=s.ba_sb, in_=t.ba_cm),
             waits=bar, inc="dS", inc_by=16)
    ctx.emit("sync", lambda e: e.dma_start(out=s.bp_sb, in_=t.bp_cm),
             waits=bar, inc="dS", inc_by=16)
    dS0 = ctx.cnt["dS"]
    ctx.emit("vector", lambda e: e.tensor_scalar(
        out=s.qba_sb, in0=s.ba_sb[:, 0:8], scalar1=0.125, scalar2=None,
        op0=ALU.mult), waits=[("dS", dS0)] + bar, inc="vE")
    qba_v = ctx.cnt["vE"]
    for hg in range(4):
        for hh in range(4):
            h = hg * 4 + hh
            ctx.emit("gpsimd", lambda e, h=h, hh=hh: e.dma_start(
                out=s.wproj_sb[0:64, hh * 1024:(hh + 1) * 1024],
                in_=t.w_proj[h * 64:(h + 1) * 64, :]),
                waits=(bar + [("tE", ctx.meta.get("at_wproj_free", 0))]) if hh == 0
                else (), inc="dW", inc_by=16)
        dWp = ctx.cnt["dW"]
        for lc in range(6):
            m_abs = [2 * hg, 2 * hg + 1, 8 + 2 * hg, 9 + 2 * hg,
                     16 + 2 * hg, 17 + 2 * hg][lc]
            sl = lc % 2
            ctx.emit("gpsimd", lambda e, m_abs=m_abs, sl=sl: e.dma_start(
                out=s.wattn_sb[sl].rearrange("p (kc m) -> p kc m", kc=8),
                in_=t.w_attn[:, m_abs * 128:(m_abs + 1) * 128].rearrange(
                    "(kc p) m -> p kc m", p=128)),
                waits=[("tE", ctx.meta.get("at_wattn_free_%d" % sl, 0))] +
                (bar if hg == 0 and lc < 2 else []), inc="dW", inc_by=16)
            dWa = ctx.cnt["dW"]
            for nt in range(4):
                t0 = nt * 512
                b = (lc * 4 + nt) % 2
                for kc in range(8):
                    w = ()
                    if kc == 0:
                        w = ctx.bank_acquire(b, [("dW", dWa), ("vE", ln_v)])
                    ctx.emit("tensor", lambda e, sl=sl, kc=kc, t0=t0, b=b: e.matmul(
                        p.b[b][:], s.wattn_sb[sl][:, kc * 128:(kc + 1) * 128],
                        s.x_ln[:, kc * T + t0: kc * T + t0 + 512],
                        start=(kc == 0), stop=(kc == 7)), waits=w,
                        inc="tE" if kc == 7 else None)
                tq = ctx.cnt["tE"]
                if nt == 3:
                    ctx.meta["at_wattn_free_%d" % sl] = tq
                ctx.bank_filled(b)
                m_abs = [2 * hg, 2 * hg + 1, 8 + 2 * hg, 9 + 2 * hg,
                         16 + 2 * hg, 17 + 2 * hg][lc]
                wq = [("tE", ctx.meta.get("at_qkv_free", 0))] if (lc == 0 and
                                                                 nt == 0) else []
                if lc < 2:
                    ctx.emit("vector", lambda e, lc=lc, t0=t0, b=b, m_abs=m_abs:
                             e.tensor_scalar(
                                 out=s.qkv_g[:, lc * T + t0: lc * T + t0 + 512],
                                 in0=p.b[b][:], scalar1=0.125,
                                 scalar2=s.qba_sb[:, m_abs:m_abs + 1],
                                 op0=ALU.mult, op1=ALU.add),
                             waits=ctx.bank_read_waits(
                                 b, [("dS", dS0), ("vE", qba_v)] + wq), inc="vE")
                    ctx.bank_read_done(b, "vE")
                else:
                    ctx.emit("scalar", lambda e, lc=lc, t0=t0, b=b, m_abs=m_abs:
                             e.activation(
                                 out=s.qkv_g[:, lc * T + t0: lc * T + t0 + 512],
                                 in_=p.b[b][:], func=AF.Identity,
                                 bias=s.ba_sb[:, m_abs:m_abs + 1]),
                             waits=ctx.bank_read_waits(b, [("dS", dS0)] + wq),
                             inc="aE")
                    ctx.bank_read_done(b, "aE")
        qkv_a = ctx.cnt["aE"]
        qkv_v = ctx.cnt["vE"]
        for hh in range(4):
            base = (hh % 2) * 64
            qc = (hh // 2) * T
            kc_off = (2 + hh // 2) * T
            vc = (4 + hh // 2) * T
            p3bf = p.b[3].ap().bitcast(BF16)
            for tc in range(16):
                ctx.emit("tensor", lambda e, tc=tc, base=base, vc=vc, p3bf=p3bf:
                         e.transpose(p3bf[0:128, tc * 64:(tc + 1) * 64],
                                     s.qkv_g[base:base + 64,
                                             vc + tc * 128: vc + (tc + 1) * 128],
                                     s.eye_bf[base:base + 64, base:base + 64]),
                         waits=ctx.bank_acquire(3, [("aE", qkv_a),
                                                    ("vE", qkv_v)]) if tc == 0
                         else (), inc="tE" if tc == 15 else None)
            ctx.bank_filled(3)
            ctx.emit("vector", lambda e, p3bf=p3bf: e.tensor_copy(
                out=s.v_tm[:], in_=p3bf[0:128, 0:1024]),
                waits=ctx.bank_read_waits(3), inc="vE")
            ctx.bank_read_done(3, "vE")
            vtm_v = ctx.cnt["vE"]
            qmeta = {}

            def stageA(qt):
                sl = qt % 2
                ab = sl * 12
                for nt in range(4):
                    b = 4 + nt
                    off = 2048 + nt * 512 - qt * 128
                    ctx.emit("tensor", lambda e, base=base, qc=qc, kc_off=kc_off,
                             qt=qt, nt=nt, b=b: e.matmul(
                                 p.b[b][:],
                                 s.qkv_g[base:base + 64,
                                         qc + qt * 128: qc + (qt + 1) * 128],
                                 s.qkv_g[base:base + 64,
                                         kc_off + nt * 512: kc_off + nt * 512 + 512],
                                 start=True, stop=False),
                             waits=ctx.bank_acquire(b, [("vE", vtm_v)] if qt == 0
                                                    and nt == 0 else ()), inc=None)
                    ctx.emit("tensor", lambda e, off=off, b=b: e.matmul(
                        p.b[b][:], s.eye_bf[:],
                        s.maskband[:, off:off + 512], start=False, stop=True),
                        inc="tE")
                    ctx.bank_filled(b)
                for nt in range(4):
                    b = 4 + nt
                    ctx.emit("vector", lambda e, nt=nt, b=b, ab=ab: e.reduce_max(
                        out=atc[:, ab + nt:ab + nt + 1], in_=p.b[b][:], axis=AX.X),
                        waits=ctx.bank_read_waits(b) +
                        [("aE", qmeta.get(("exp", sl), 0))], inc="vE")
                vq = ctx.cnt["vE"]
                vq = ctx.emit("vector", lambda e, ab=ab: e.tensor_max(
                    out=atc[:, ab + 11:ab + 12], in0=atc[:, ab:ab + 1],
                    in1=atc[:, ab + 1:ab + 2]), waits=[("vE", vq)], inc="vE")
                vq = ctx.emit("vector", lambda e, ab=ab: e.tensor_max(
                    out=atc[:, ab + 4:ab + 5], in0=atc[:, ab + 2:ab + 3],
                    in1=atc[:, ab + 3:ab + 4]), waits=[("vE", vq)], inc="vE")
                vq = ctx.emit("vector", lambda e, ab=ab: e.tensor_max(
                    out=atc[:, ab + 11:ab + 12], in0=atc[:, ab + 11:ab + 12],
                    in1=atc[:, ab + 4:ab + 5]), waits=[("vE", vq)], inc="vE")
                vq = ctx.emit("vector", lambda e, ab=ab: e.tensor_scalar(
                    out=atc[:, ab + 4:ab + 5], in0=atc[:, ab + 11:ab + 12],
                    scalar1=-1.0, scalar2=None, op0=ALU.mult),
                    waits=[("vE", vq)], inc="vE")
                for nt in range(4):
                    b = 4 + nt
                    aq = ctx.emit("scalar", lambda e, nt=nt, b=b, sl=sl, ab=ab:
                                  e.activation(
                                      out=s.attn_sb[sl][:, nt * 512:(nt + 1) * 512],
                                      in_=p.b[b][:], func=AF.Exp,
                                      bias=atc[:, ab + 4:ab + 5],
                                      accum_out=atc[:, ab + 5 + nt:ab + 6 + nt]),
                                  waits=[("vE", vq),
                                         ("vE", qmeta.get(("sb_free", sl), 0))],
                                  inc="aE")
                    ctx.bank_read_done(b, "aE")
                qmeta[("exp", sl)] = ctx.cnt["aE"]
                aq = ctx.cnt["aE"]
                vq = ctx.emit("vector", lambda e, ab=ab: e.reduce_sum(
                    out=atc[:, ab + 9:ab + 10], in_=atc[:, ab + 5:ab + 9],
                    axis=AX.X), waits=[("aE", aq)], inc="vE")
                vq = ctx.emit("vector", lambda e, ab=ab: e.reciprocal(
                    out=atc[:, ab + 10:ab + 11], in_=atc[:, ab + 9:ab + 10]),
                    waits=[("vE", vq)], inc="vE")
                for nt in range(4):
                    ctx.emit("vector", lambda e, nt=nt, sl=sl, ab=ab: e.tensor_scalar(
                        out=s.attn_sb[sl][:, nt * 512:(nt + 1) * 512],
                        in0=s.attn_sb[sl][:, nt * 512:(nt + 1) * 512],
                        scalar1=atc[:, ab + 10:ab + 11], scalar2=None, op0=ALU.mult),
                        waits=[("vE", vq)] if nt == 0 else (), inc="vE")
                qmeta[("attn_v", qt)] = ctx.cnt["vE"]

            def stageB(qt):
                sl = qt % 2
                attn_v = qmeta[("attn_v", qt)]
                for tc in range(16):
                    b = tc // 8
                    pbf = p.b[b].ap().bitcast(BF16)
                    ctx.emit("tensor", lambda e, tc=tc, b=b, pbf=pbf, sl=sl:
                             e.transpose(pbf[0:128, (tc % 8) * 128:
                                             (tc % 8) * 128 + 128],
                                         s.attn_sb[sl][:, tc * 128:(tc + 1) * 128],
                                         s.eye_bf[0:128, 0:128]),
                             waits=ctx.bank_acquire(b, [("vE", attn_v)])
                             if tc % 8 == 0 else (),
                             inc="tE" if tc % 8 == 7 else None)
                    if tc % 8 == 7:
                        ctx.bank_filled(b)
                for b in range(2):
                    pbf = p.b[b].ap().bitcast(BF16)
                    ctx.emit("vector", lambda e, b=b, pbf=pbf, sl=sl: e.tensor_copy(
                        out=s.attnT_sb[sl][:, b * 1024:(b + 1) * 1024],
                        in_=pbf[0:128, 0:1024]),
                        waits=ctx.bank_read_waits(b), inc="vE")
                    ctx.bank_read_done(b, "vE")
                qmeta[("sb_free", sl)] = ctx.cnt["vE"]
                attnT_v = ctx.cnt["vE"]
                for kc2 in range(16):
                    w = ()
                    if kc2 == 0:
                        w = ctx.bank_acquire(2, [("vE", attnT_v)])
                    ctx.emit("tensor", lambda e, kc2=kc2, sl=sl: e.matmul(
                        p.b[2][0:64, 0:128],
                        s.v_tm[:, kc2 * 64:(kc2 + 1) * 64],
                        s.attnT_sb[sl][:, kc2 * 128:(kc2 + 1) * 128],
                        start=(kc2 == 0), stop=(kc2 == 15)), waits=w,
                        inc="tE" if kc2 == 15 else None)
                ctx.bank_filled(2)
                ctx.emit("vector", lambda e, hh=hh, qt=qt: e.tensor_copy(
                    out=s.attnout[:, hh * T + qt * 128: hh * T + (qt + 1) * 128],
                    in_=p.b[2][0:64, 0:128]),
                    waits=ctx.bank_read_waits(2), inc="vE")
                ctx.bank_read_done(2, "vE")

            for qt in range(17):
                if qt < 16:
                    stageA(qt)
                if qt >= 1:
                    stageB(qt - 1)
        ao_v = ctx.cnt["vE"]
        ctx.meta["at_qkv_free"] = ctx.cnt["tE"]
        for mo in range(8):
            for nt in range(4):
                t0 = nt * 512
                b = (mo * 4 + nt) % 2
                for hh in range(4):
                    w = ()
                    if hh == 0:
                        w = ctx.bank_acquire(b, [("vE", ao_v), ("dW", dWp)])
                    ctx.emit("tensor", lambda e, hh=hh, mo=mo, t0=t0, b=b: e.matmul(
                        p.b[b][:],
                        s.wproj_sb[0:64, hh * 1024 + mo * 128:
                                   hh * 1024 + mo * 128 + 128],
                        s.attnout[:, hh * T + t0: hh * T + t0 + 512],
                        start=(hh == 0), stop=(hh == 3)), waits=w,
                        inc="tE" if hh == 3 else None)
                ctx.bank_filled(b)
                dst = s.x_fm[:, mo * T + t0: mo * T + t0 + 512]
                ctx.emit("vector", lambda e, b=b, dst=dst: e.tensor_add(
                    out=dst, in0=p.b[b][:], in1=dst),
                    waits=ctx.bank_read_waits(b), inc="vE")
                if hg == 0:
                    ctx.emit("vector", lambda e, mo=mo, dst=dst: e.tensor_scalar(
                        out=dst, in0=dst, scalar1=s.bp_sb[:, mo:mo + 1],
                        scalar2=None, op0=ALU.add), inc="vE")
                ctx.bank_read_done(b, "vE")
        ctx.meta["at_wproj_free"] = ctx.cnt["tE"]
    ctx.meta["xfm_vE"] = ctx.cnt["vE"]


def emit_lff(ctx, t, s, p):
    ln_v = ctx.meta["ln_vE"]
    ent_t, ent_v, ent_a = ctx.cnt["tE"], ctx.cnt["vE"], ctx.cnt.get("aE", 0)
    bar = [("tE", ent_t), ("vE", ent_v), ("aE", ent_a)]
    for dst, srcap in ((s.ld_sb, t.lora_down), (s.lu_sb, t.lora_up),
                       (s.upxt_sb, t.upxt), (s.dnaux_sb, t.down_aux),
                       (s.bl_sb, t.bl_cm)):
        ctx.emit("sync", lambda e, a=dst, b=srcap: e.dma_start(
            out=a if isinstance(a, bass.AP) else a[:], in_=b),
            waits=bar, inc="dS", inc_by=16)
    ctx.emit("gpsimd", lambda e: e.dma_start(
        out=s.l1w_bf.rearrange("p (kc o) -> p kc o", kc=8),
        in_=t.l1wt.rearrange("(kc p) o -> p kc o", p=128)),
        waits=bar, inc="dW", inc_by=16)
    dS0, dW0 = ctx.cnt["dS"], ctx.cnt["dW"]
    ctx.emit("vector", lambda e: e.tensor_scalar(
        out=s.hbl_sb[:], in0=s.bl_sb[:], scalar1=0.5, scalar2=None, op0=ALU.mult),
        waits=[("dS", dS0)] + bar, inc="vE")
    for n2 in range(2):
        ctx.emit("tensor", lambda e, n2=n2: e.matmul(
            p.b[2][0:4, :], s.lu_sb, s.upxt_sb[:, n2 * 512:(n2 + 1) * 512],
            start=True, stop=True),
            waits=ctx.bank_acquire(2, [("dS", dS0)]), inc="tE")
        ctx.bank_filled(2)
        vq = ctx.emit("vector", lambda e, n2=n2: e.tensor_copy(
            out=s.p1_sb[:, n2 * 512:(n2 + 1) * 512], in_=p.b[2][0:4, :]),
            waits=ctx.bank_read_waits(2), inc="vE")
        ctx.bank_read_done(2, "vE")
    for n2 in range(2):
        ctx.emit("tensor", lambda e, n2=n2: e.matmul(
            p.b[3][0:100, :], s.ld_sb, s.p1_sb[:, n2 * 512:(n2 + 1) * 512],
            start=True, stop=True),
            waits=ctx.bank_acquire(3, [("vE", vq)]), inc="tE")
        ctx.bank_filled(3)
        vq2 = ctx.emit("vector", lambda e, n2=n2: e.tensor_copy(
            out=s.p2_sb[:, n2 * 512:(n2 + 1) * 512], in_=p.b[3][0:100, :]),
            waits=ctx.bank_read_waits(3), inc="vE")
        ctx.bank_read_done(3, "vE")
    for kc in range(8):
        for n2 in range(2):
            b = 2 + (kc * 2 + n2) % 2
            ctx.emit("tensor", lambda e, kc=kc, n2=n2, b=b: e.matmul(
                p.b[b][:], s.dnaux_sb[:, kc * 128:(kc + 1) * 128],
                s.p2_sb[:, n2 * 512:(n2 + 1) * 512], start=True, stop=True),
                waits=ctx.bank_acquire(b, [("vE", vq2)]), inc="tE")
            ctx.bank_filled(b)
            ctx.emit("vector", lambda e, kc=kc, n2=n2, b=b: e.tensor_add(
                out=s.weff[:, kc * 1024 + n2 * 512: kc * 1024 + n2 * 512 + 512],
                in0=p.b[b][:],
                in1=s.l1w_bf[:, kc * 1024 + n2 * 512: kc * 1024 + n2 * 512 + 512]),
                waits=ctx.bank_read_waits(b, [("dW", dW0)]), inc="vE")
            ctx.bank_read_done(b, "vE")
    weff_v = ctx.cnt["vE"]
    for mo in range(8):
        for nt in range(4):
            t0 = nt * 512
            b = (mo * 4 + nt) % 2
            for kc in range(8):
                w = ()
                if kc == 0:
                    w = ctx.bank_acquire(b, [("vE", max(ln_v, weff_v))])
                ctx.emit("tensor", lambda e, mo=mo, kc=kc, t0=t0, b=b: e.matmul(
                    p.b[b][:],
                    s.weff[:, kc * 1024 + mo * 128: kc * 1024 + mo * 128 + 128],
                    s.x_ln[:, kc * T + t0: kc * T + t0 + 512],
                    start=(kc == 0), stop=(kc == 7)),
                    waits=w, inc="tE" if kc == 7 else None)
            ctx.bank_filled(b)
            dst = s.x_fm[:, mo * T + t0:mo * T + t0 + 512]
            ctx.emit("vector", lambda e, b=b, dst=dst: e.scalar_tensor_tensor(
                out=dst, in0=p.b[b][:], scalar=0.5, in1=dst,
                op0=ALU.mult, op1=ALU.add),
                waits=ctx.bank_read_waits(b), inc="vE")
            ctx.emit("vector", lambda e, mo=mo, dst=dst: e.tensor_scalar(
                out=dst, in0=dst, scalar1=s.hbl_sb[:, mo:mo + 1], scalar2=None,
                op0=ALU.add), inc="vE")
            ctx.bank_read_done(b, "vE")
    ctx.meta["xfm_vE"] = ctx.cnt["vE"]


def build_program(phases, taps=()):
    phi = 0
    nc = bass.Bass("TRN2", target_bir_lowering=False, debug=False)
    t = declare_tensors(nc, phases)
    s = alloc_sbuf(nc, phases)
    p = alloc_psum(nc)
    ctx = Ctx(nc)
    emit_load_x(ctx, t, s)
    emit_load_consts(ctx, t, s, phases)
    for ph in phases:
        emit_ln(ctx, t, s, p)
        if "ln" in taps:
            emit_dump(ctx, nc, "d_xln", s.x_ln[:], NCH * T, BF16)
        if ph == "moe":
            emit_router(ctx, t, s, p)
            if "comb" in taps:
                emit_dump(ctx, nc, "d_comb", s.comb_em[:], 2 * T, BF16)
            if "scr" in taps:
                emit_dump(ctx, nc, "d_scr", s.ar_f32[:, 12288:20480], 8192, F32)
            if "stop_before_moe" in taps:
                break
            emit_moe(ctx, t, s, p)
        elif ph == "mamba":
            emit_mamba(ctx, t, s, p)
        elif ph == "lff":
            emit_lff(ctx, t, s, p)
        elif ph == "attn":
            emit_attn(ctx, t, s, p)
            if "attn_dbg" in taps:
                emit_dump(ctx, nc, "d_qkv", s.qkv_g[:], 6 * T, BF16)
                emit_dump(ctx, nc, "d_ao", s.attnout, 4 * T, BF16, rows=64)
                emit_dump(ctx, nc, "d_attn", s.attn_sb[1][:], T, BF16)
        phi += 1
        if "stream" in taps:
            emit_dump(ctx, nc, "d_ph%d" % phi, s.x_fm[:], NCH * T, F32)
    emit_store_x(ctx, t, s)
    ctx.run()
    return nc



def _consts(phases):
    c = {}
    c["c_eye_bf"] = np.eye(128, dtype=BF)
    c["c_eye_f32"] = np.eye(128, dtype=np.float32)
    c["c_ones_bf"] = np.ones((128, 128), dtype=BF)
    c["c_ones_f32"] = np.ones((1, 128), dtype=np.float32)
    c["c_onescol_bf"] = np.ones((128, 1), dtype=BF)
    if "mamba" in phases:
        sidx = np.arange(128)[:, None]
        tidx = np.arange(128)[None, :]
        c["c_tri"] = (sidx <= tidx).astype(np.float32)
        c["c_minf"] = np.where(tidx >= sidx, 0.0, -1e30).astype(np.float32)
        invf = (1.0 / 10000.0 ** (np.arange(16, dtype=np.float64) / 16.0))
        c["c_invfreq"] = np.tile(invf, 32)[None, :].astype(np.float32)
    if "attn" in phases:
        v = np.arange(4224)[None, :] - 2048 - np.arange(128)[:, None]
        f = np.where(v > 0, -np.log1p(np.maximum(v, 0).astype(np.float64)), 0.0)
        f = np.where(v < 0, -1e30, f)
        c["c_maskband"] = f.astype(BF)
    return c


def _cm(vec, ncols):
    v = np.zeros(128 * ncols, dtype=np.float32)
    v[:len(vec)] = np.asarray(vec, dtype=np.float32)
    return np.ascontiguousarray(v.reshape(ncols, 128).T)


def _layout_weights(inp, phases):
    w = {}
    f32c = lambda a: np.ascontiguousarray(np.asarray(a, dtype=np.float32))
    if "moe" in phases:
        w["wgt"] = f32c(np.asarray(inp["L0_ffn_w_gate"]).transpose(0, 2, 1))
        w["wdt"] = f32c(np.asarray(inp["L0_ffn_w_down"]).transpose(0, 2, 1))
        w["wgst"] = f32c(np.asarray(inp["L0_ffn_w_gate_shared"]).T)
        w["wdst"] = f32c(np.asarray(inp["L0_ffn_w_down_shared"]).T)
        w["rwt"] = f32c(np.asarray(inp["L0_ffn_router_w"]).T)
        w["eb_cm"] = _cm(inp["L0_ffn_expert_bias"], 2)
    if "mamba" in phases:
        w["w_in"] = f32c(inp["L0_attn_W_in"])
        w["w_out"] = f32c(inp["L0_attn_W_out"])
        w["dt_bias"] = f32c(np.asarray(inp["L0_attn_dt_bias"]).reshape(1, 32))
        w["a_log"] = f32c(np.asarray(inp["L0_attn_A_log"]).reshape(1, 32))
        w["dskip_rep"] = f32c(np.broadcast_to(
            np.asarray(inp["L0_attn_D_skip"])[None, :], (128, 32)))
        w["nw_cm"] = _cm(inp["L0_attn_norm_weight"], 16)
    if "attn" in phases:
        w["w_attn"] = f32c(inp["L1_attn_w_attn"])
        w["w_proj"] = f32c(inp["L1_attn_w_proj"])
        w["ba_cm"] = _cm(inp["L1_attn_b_attn"], 24)
        w["bp_cm"] = _cm(inp["L1_attn_b_proj"], 8)
    if "lff" in phases:
        w["l1wt"] = f32c(np.asarray(inp["L1_ffn_W"]).T)
        w["lora_down"] = f32c(inp["L1_ffn_lora_down"])
        w["lora_up"] = f32c(inp["L1_ffn_lora_up"])
        w["down_aux"] = f32c(inp["L1_ffn_down_aux"])
        w["upxt"] = f32c(np.asarray(inp["L1_ffn_up_aux"]).T)
        w["bl_cm"] = _cm(inp["L1_ffn_b"], 8)
    return w


def run_phases(phases, xs, weights):
    nc = build_program(phases)
    consts = _consts(phases)
    in_maps = []
    for b in range(8):
        im = {"xin": xs[b]}
        im.update(consts)
        im.update(weights)
        in_maps.append(im)
    res = run_bass_kernel_spmd(nc, in_maps, core_ids=list(range(8)))
    return [np.asarray(res.results[b]["xout"]) for b in range(8)]


def kernel(**inputs):
    x = np.asarray(inputs["L0_attn_x"])
    xs = [np.ascontiguousarray(np.asarray(x[b], dtype=np.float32).T) for b in range(8)]
    for phases in PHASE_GROUPS:
        weights = _layout_weights(inputs, phases)
        xs = run_phases(phases, xs, weights)
    out = np.stack([xb.T for xb in xs])
    return np.ascontiguousarray(out).astype(np.float32)


# version 04
# speedup vs baseline: 1.1193x; 1.0971x over previous
import numpy as np
import ml_dtypes
import concourse.bass as bass
import concourse.mybir as mybir
from concourse.bass_utils import run_bass_kernel_spmd

F32 = mybir.dt.float32
BF16 = mybir.dt.bfloat16
AF = mybir.ActivationFunctionType
ALU = mybir.AluOpType
AX = mybir.AxisListType
BF = ml_dtypes.bfloat16

T = 2048
DM = 1024
NCH = 8
NE = 160
INTER = 512

PHASE_GROUPS = [["moe", "mamba", "moe", "lff", "attn", "lff"]]


class Ctx:

    def __init__(self, nc):
        self.nc = nc
        self.ops = []
        self.cnt = {}
        self.handles = {}
        self.meta = {}
        self.bank_r = [{"vE": 0, "aE": 0} for _ in range(8)]
        self.bank_w = [0 for _ in range(8)]

    def sem(self, name):
        if name not in self.handles:
            self.handles[name] = self.nc.alloc_semaphore(name)
            self.cnt[name] = 0
        return self.handles[name]

    def emit(self, eng, build, waits=(), inc=None, inc_by=1):
        waits = [(n, v) for (n, v) in waits if v > 0]
        for n, _ in waits:
            self.sem(n)
        tgt = None
        if inc is not None:
            self.sem(inc)
            self.cnt[inc] += inc_by
            tgt = self.cnt[inc]

        def fn(e, build=build, waits=tuple(waits), inc=inc, inc_by=inc_by):
            for n, v in waits:
                e.wait_ge(self.handles[n], v)
            ins = build(e)
            if inc is not None:
                ins.then_inc(self.handles[inc], inc_by)
        self.ops.append((eng, fn))
        return tgt

    def bank_acquire(self, b, extra=()):
        r = self.bank_r[b]
        return [("vE", r["vE"]), ("aE", r["aE"])] + list(extra)

    def bank_filled(self, b):
        self.bank_w[b] = self.cnt["tE"]

    def bank_read_waits(self, b, extra=()):
        return [("tE", self.bank_w[b])] + list(extra)

    def bank_read_done(self, b, eng):
        self.bank_r[b][eng] = self.cnt[eng]

    def run(self):
        nc = self.nc
        with nc.Block() as block:
            for eng_name in ("sync", "scalar", "tensor", "vector", "gpsimd"):
                ops = [f for (e, f) in self.ops if e == eng_name]

                def mk(ops):
                    def runner(engine):
                        for f in ops:
                            f(engine)
                    return runner
                getattr(block, eng_name)(mk(ops))


class NS:
    pass



def declare_tensors(nc, phases):
    t = NS()
    d = lambda name, shape, dtype=F32, kind="ExternalInput": nc.dram_tensor(
        name, list(shape), dtype, kind=kind).ap()
    t.xin = d("xin", [DM, T])
    t.xout = d("xout", [DM, T], kind="ExternalOutput")
    t.c_eye_bf = d("c_eye_bf", [128, 128], BF16)
    t.c_eye_f32 = d("c_eye_f32", [128, 128])
    t.c_ones_bf = d("c_ones_bf", [128, 128], BF16)
    t.c_ones_f32 = d("c_ones_f32", [1, 128])
    t.c_onescol_bf = d("c_onescol_bf", [128, 1], BF16)
    if "moe" in phases:
        t.wgt = d("wgt", [NE, DM, INTER])
        t.wdt = d("wdt", [NE, INTER, DM])
        t.wgst = d("wgst", [DM, INTER])
        t.wdst = d("wdst", [INTER, DM])
        t.rwt = d("rwt", [DM, NE])
        t.eb_cm = d("eb_cm", [128, 2])
    if "mamba" in phases:
        t.w_in = d("w_in", [DM, 4256])
        t.w_out = d("w_out", [2 * DM, DM])
        t.dt_bias = d("dt_bias", [1, 32])
        t.a_log = d("a_log", [1, 32])
        t.dskip_rep = d("dskip_rep", [128, 32])
        t.nw_cm = d("nw_cm", [128, 16])
        t.c_tri = d("c_tri", [128, 128])
        t.c_minf = d("c_minf", [128, 128])
        t.c_invfreq = d("c_invfreq", [1, 512])
    if "attn" in phases:
        t.w_attn = d("w_attn", [DM, 3 * DM])
        t.w_proj = d("w_proj", [DM, DM])
        t.ba_cm = d("ba_cm", [128, 24])
        t.bp_cm = d("bp_cm", [128, 8])
        t.c_maskband = d("c_maskband", [128, 4224], BF16)
    if "lff" in phases:
        t.l1wt = d("l1wt", [DM, DM])
        t.lora_down = d("lora_down", [4, 100])
        t.lora_up = d("lora_up", [50, 4])
        t.down_aux = d("down_aux", [100, DM])
        t.upxt = d("upxt", [50, DM])
        t.bl_cm = d("bl_cm", [128, 8])
    return t


def alloc_sbuf(nc, phases):
    s = NS()
    a = lambda name, shape, dtype: nc.alloc_sbuf_tensor(name, list(shape), dtype)
    s.x_fm = a("x_fm", [128, NCH * T], F32)
    s.x_ln = a("x_ln", [128, NCH * T], BF16)
    s.eye_bf = a("eye_bf", [128, 128], BF16)
    s.eye_f32 = a("eye_f32", [128, 128], F32)
    s.ones_bf = a("ones_bf", [128, 128], BF16)
    s.ones_f32 = a("ones_f32", [1, 128], F32)
    s.onescol_bf = a("onescol_bf", [128, 1], BF16)
    s.ln_rows = a("ln_rows", [1, 3 * 512], F32)
    s.ln_sq = a("ln_sq", [128, 2 * 512], BF16)
    s.ln_t = a("ln_t", [128, 2 * 512], F32)
    s.AR = 48384
    s.arena = a("arena", [128, s.AR], BF16)
    s.ar_bf = s.arena.ap()
    s.ar_f32 = s.arena.ap().bitcast(F32)
    if "moe" in phases:
        s.wg_sb = [s.ar_bf[:, 0:8192], s.ar_bf[:, 8192:16384]]
        s.wd_sb = s.ar_bf[:, 16384:24576]
        s.h1w = s.ar_bf[:, 24576:40960]
        s.comb_em = s.ar_bf[:, 40960:45056]
        s.crow = [s.ar_bf[32:33, 43008:45056], s.ar_bf[64:65, 43008:45056]]
        s.silu_sc = s.ar_f32[:, 22528:23552]
        s.rw_sb = s.ar_bf[:, 47104:48384]
        s.eb_sb = a("eb_sb", [128, 2], F32)
    if "mamba" in phases:
        s.xs_fmh = s.ar_bf[:, 0:16384]
        s.y_q = s.ar_bf[:, 16384:24576]
        s.bcdt = s.ar_bf[:, 24576:26624]
        s.wslab = [s.ar_bf[:, 26624:27648], s.ar_bf[:, 27648:28672]]
        s.cs_c = s.ar_bf[:, 28672:29696]
        s.bhat = s.ar_bf[:, 29696:31744]
        s.ctil = s.ar_bf[:, 31744:33792]
        s.cbar = s.ar_bf[:, 33792:35840]
        s.g_bf = s.ar_bf[0:64, 35840:37888]
        MF = 18944
        s.dt_tm = s.ar_f32[:, MF:MF + 512]
        s.cumdt = s.ar_f32[:, MF + 512:MF + 1024]
        s.L_tm = s.ar_f32[:, MF + 1024:MF + 1536]
        s.L_emc = s.ar_f32[0:32, MF + 1536:MF + 1664]
        s.th_sc = s.ar_f32[:, MF + 1664:MF + 2176]
        s.mrows = s.ar_f32[0:1, MF + 2176:MF + 2688]
        s.fmscr = s.ar_bf[:, 2 * (MF + 2688):2 * (MF + 2688) + 512]
        s.attnT_m = s.ar_bf[:, 2 * (MF + 2944):2 * (MF + 2944) + 128]
        s.g_f32 = s.ar_f32[0:64, MF + 3008:MF + 5056]
        s.dtb_bc = a("dtb_bc", [128, 32], F32)
        s.nea_bc = a("nea_bc", [128, 32], F32)
        s.tri_sb = a("tri_sb", [128, 128], F32)
        s.minf_sb = a("minf_sb", [128, 128], F32)
        s.invf_row = a("invf_row", [1, 512], F32)
        s.dtb_row = a("dtb_row", [1, 32], F32)
        s.alog_row = a("alog_row", [1, 32], F32)
        s.dskip_sb = a("dskip_sb", [128, 32], F32)
        s.nw_sb = a("nw_sb", [128, 16], F32)
    if "lff" in phases:
        s.weff = s.ar_bf[:, 0:8192]
        s.l1w_bf = s.ar_bf[:, 8192:16384]
        s.upxt_sb = s.ar_f32[0:50, 8192:9216]
        s.p1_sb = s.ar_f32[0:4, 9216:10240]
        s.p2_sb = s.ar_f32[0:100, 10240:11264]
        s.dnaux_sb = s.ar_f32[0:100, 11264:12288]
        s.ld_sb = s.ar_f32[0:4, 12288:12388]
        s.lu_sb = s.ar_f32[0:50, 12416:12420]
        s.bl_sb = a("bl_sb", [128, 8], F32)
        s.hbl_sb = a("hbl_sb", [128, 8], F32)
    if "attn" in phases:
        s.qkv_g = s.ar_bf[:, 0:12288]
        s.attnout = s.ar_bf[0:64, 12288:20480]
        s.maskband = s.ar_bf[:, 20480:24704]
        s.v_tm = s.ar_bf[:, 24704:25728]
        s.attn_sb = [s.ar_bf[:, 25728:27776], s.ar_bf[:, 27776:29824]]
        s.attnT_sb = [s.ar_bf[:, 29824:31872], s.ar_bf[:, 31872:33920]]
        s.wproj_sb = s.ar_bf[0:64, 33920:38016]
        s.wattn_sb = [s.ar_bf[:, 38016:39040], s.ar_bf[:, 39040:40064]]
        s.ba_sb = s.ar_f32[:, 24000:24024]
        s.bp_sb = s.ar_f32[:, 24024:24032]
        s.qba_sb = s.ar_f32[:, 24032:24040]
        s.atc = s.ar_f32[:, 24040:24064]
    return s


def alloc_psum(nc):
    p = NS()
    p.b = [nc.alloc_psum_tensor("pb%d" % i, [128, 512], F32) for i in range(8)]
    return p



def emit_load_x(ctx, t, s):
    src = t.xin.rearrange("(c p) t -> p c t", p=128)
    dst = s.x_fm.ap().rearrange("p (c t) -> p c t", c=NCH)
    ctx.emit("sync", lambda e: e.dma_start(out=dst, in_=src), inc="dS", inc_by=16)
    ctx.meta["x_dS"] = ctx.cnt["dS"]


def emit_load_consts(ctx, t, s, phases):
    pairs = [(s.eye_bf, t.c_eye_bf), (s.eye_f32, t.c_eye_f32),
             (s.ones_bf, t.c_ones_bf), (s.ones_f32, t.c_ones_f32),
             (s.onescol_bf, t.c_onescol_bf)]
    if "moe" in phases:
        pairs += [(s.eb_sb, t.eb_cm)]
    for dst, src in pairs:
        ctx.emit("sync", lambda e, d=dst, sr=src: e.dma_start(out=d[:], in_=sr),
                 inc="dS", inc_by=16)
    ctx.meta["consts_dS"] = ctx.cnt["dS"]
    ctx.meta["consts_dW"] = ctx.cnt.get("dW", 0)


def emit_ln(ctx, t, s, p):
    dep_v = ctx.meta.get("xfm_vE", 0)
    first_dma = [("dS", max(ctx.meta["x_dS"], ctx.meta["consts_dS"]))]
    R = s.ln_rows
    for nt in range(4):
        t0 = nt * 512
        for c in range(NCH):
            sq = s.ln_sq[:, (c % 2) * 512:(c % 2) * 512 + 512]
            w = [("tE", ctx.meta.get("ln_sq_tE_%d" % (c % 2), 0))]
            if c == 0 and nt == 0:
                w += first_dma + [("vE", dep_v)]
            ctx.emit("scalar", lambda e, c=c, sq=sq, t0=t0: e.activation(
                out=sq, in_=s.x_fm[:, c * T + t0:c * T + t0 + 512], func=AF.Square),
                waits=w, inc="aE")
            aq = ctx.cnt["aE"]
            wv = first_dma if (c == 0 and nt == 0) else ()
            ctx.emit("vector", lambda e, c=c, t0=t0: e.tensor_copy(
                out=s.x_ln[:, c * T + t0:c * T + t0 + 512],
                in_=s.x_fm[:, c * T + t0:c * T + t0 + 512]), waits=wv, inc="vE")
            vq = ctx.cnt["vE"]
            wsum = ctx.bank_acquire(4, [("vE", vq)]) if c == 0 else [("vE", vq)]
            ctx.emit("tensor", lambda e, c=c, t0=t0: e.matmul(
                p.b[4][0:1, :], s.onescol_bf[:],
                s.x_ln[:, c * T + t0:c * T + t0 + 512],
                start=(c == 0), stop=(c == NCH - 1)), waits=wsum,
                inc="tE" if c == NCH - 1 else None)
            wsq = ctx.bank_acquire(5, [("aE", aq)]) if c == 0 else [("aE", aq)]
            tq = ctx.emit("tensor", lambda e, c=c, sq=sq: e.matmul(
                p.b[5][0:1, :], s.onescol_bf[:], sq,
                start=(c == 0), stop=(c == NCH - 1)), waits=wsq, inc="tE")
            if tq is None:
                tq = ctx.cnt["tE"]
            ctx.meta["ln_sq_tE_%d" % (c % 2)] = tq
        ctx.bank_filled(4)
        ctx.bank_filled(5)
        ctx.emit("vector", lambda e: e.tensor_scalar(
            out=R[:, 0:512], in0=p.b[4][0:1, :], scalar1=1.0 / DM, scalar2=None,
            op0=ALU.mult), waits=ctx.bank_read_waits(4), inc="vE")
        ctx.emit("vector", lambda e: e.tensor_copy(out=R[:, 512:1024], in_=p.b[5][0:1, :]),
                 waits=ctx.bank_read_waits(5), inc="vE")
        ctx.bank_read_done(4, "vE")
        ctx.bank_read_done(5, "vE")
        ctx.emit("vector", lambda e: e.tensor_tensor(
            out=R[:, 1024:1536], in0=R[:, 0:512], in1=R[:, 0:512], op=ALU.mult),
            inc="vE")
        ctx.emit("vector", lambda e: e.scalar_tensor_tensor(
            out=R[:, 512:1024], in0=R[:, 512:1024], scalar=1.0 / DM,
            in1=R[:, 1024:1536], op0=ALU.mult, op1=ALU.subtract), inc="vE")
        vq = ctx.emit("vector", lambda e: e.tensor_scalar(
            out=R[:, 512:1024], in0=R[:, 512:1024], scalar1=1e-5, scalar2=None,
            op0=ALU.add), inc="vE")
        aq1 = ctx.emit("scalar", lambda e: e.activation(
            out=R[:, 512:1024], in_=R[:, 512:1024], func=AF.Ln),
            waits=[("vE", vq)], inc="aE")
        vq = ctx.emit("vector", lambda e: e.tensor_scalar(
            out=R[:, 512:1024], in0=R[:, 512:1024], scalar1=-0.5, scalar2=None,
            op0=ALU.mult), waits=[("aE", aq1)], inc="vE")
        aq = ctx.emit("scalar", lambda e: e.activation(
            out=R[:, 512:1024], in_=R[:, 512:1024], func=AF.Exp),
            waits=[("vE", vq)], inc="aE")
        ctx.emit("tensor", lambda e: e.matmul(
            p.b[6][:], s.ones_f32[:], R[:, 0:512], start=True, stop=True),
            waits=ctx.bank_acquire(6, [("aE", aq)]), inc="tE")
        ctx.bank_filled(6)
        ctx.emit("tensor", lambda e: e.matmul(
            p.b[7][:], s.ones_f32[:], R[:, 512:1024], start=True, stop=True),
            waits=ctx.bank_acquire(7), inc="tE")
        ctx.bank_filled(7)
        for c in range(NCH):
            sc = s.ln_t[:, (c % 2) * 512:(c % 2) * 512 + 512]
            w = ctx.bank_read_waits(6) + ctx.bank_read_waits(7) if c == 0 else ()
            ctx.emit("vector", lambda e, c=c, sc=sc, t0=t0: e.tensor_sub(
                out=sc, in0=s.x_fm[:, c * T + t0:c * T + t0 + 512], in1=p.b[6][:]),
                waits=w, inc="vE")
            ctx.emit("vector", lambda e, c=c, sc=sc, t0=t0: e.tensor_mul(
                out=s.x_ln[:, c * T + t0:c * T + t0 + 512], in0=sc, in1=p.b[7][:]),
                inc="vE")
        ctx.bank_read_done(6, "vE")
        ctx.bank_read_done(7, "vE")
    ctx.meta["ln_vE"] = ctx.cnt["vE"]



def emit_router(ctx, t, s, p):
    scr = s.ar_f32[:, 12288:20480]
    probs_em = lambda ch: scr[:, ch * T:(ch + 1) * T]
    ptm = lambda tc: scr[:, 2 * T + tc * 160: 2 * T + tc * 160 + 160]
    work = scr[:, 2 * T + 16 * 160: 2 * T + 16 * 160 + 160]
    msk = scr[:, 2 * T + 17 * 160: 2 * T + 18 * 160]
    cols = scr[:, 2 * T + 18 * 160: 2 * T + 18 * 160 + 8]
    ctm = lambda tc: s.wg_sb[0][:, tc * 160: tc * 160 + 160]
    ln_v = ctx.meta["ln_vE"]
    ctx.emit("gpsimd", lambda e: e.dma_start(
        out=s.rw_sb.rearrange("p (kc n) -> p kc n", kc=8),
        in_=t.rwt.rearrange("(kc p) n -> p kc n", p=128)),
        waits=[("tE", ctx.cnt["tE"]), ("vE", ctx.cnt["vE"]),
               ("aE", ctx.cnt.get("aE", 0))], inc="dW", inc_by=16)
    ctx.meta["rw_dW"] = ctx.cnt["dW"]
    for nt in range(4):
        t0 = nt * 512
        for part in range(2):
            M = 128 if part == 0 else 32
            b = 4 + part
            for kc in range(8):
                w = ctx.bank_acquire(b, [("vE", ln_v), ("dW", ctx.meta["rw_dW"])]
                                     ) if kc == 0 else ()
                ctx.emit("tensor", lambda e, kc=kc, part=part, M=M, t0=t0, b=b:
                         e.matmul(p.b[b][0:M, :],
                                  s.rw_sb[:, kc * NE + part * 128:
                                          kc * NE + part * 128 + M],
                                  s.x_ln[:, kc * T + t0: kc * T + t0 + 512],
                                  start=(kc == 0), stop=(kc == 7)),
                         waits=w, inc="tE" if kc == 7 else None)
            ctx.bank_filled(b)
            ctx.emit("scalar", lambda e, part=part, M=M, t0=t0, b=b:
                     e.activation(out=probs_em(part)[0:M, t0:t0 + 512],
                                  in_=p.b[b][0:M, :], func=AF.Sigmoid,
                                  bias=s.eb_sb[0:M, part:part + 1]),
                     waits=ctx.bank_read_waits(b), inc="aE")
            ctx.bank_read_done(b, "aE")
    aq_sig = ctx.cnt["aE"]
    for tc in range(16):
        for part in range(2):
            M = 128 if part == 0 else 32
            b = 6 + (tc * 2 + part) % 2
            ctx.emit("tensor", lambda e, tc=tc, part=part, M=M, b=b:
                     e.transpose(p.b[b][0:128, 0:M],
                                 probs_em(part)[0:M, tc * 128:(tc + 1) * 128],
                                 s.eye_f32[0:M, 0:M]),
                     waits=ctx.bank_acquire(b, [("aE", aq_sig)]), inc="tE")
            ctx.bank_filled(b)
            ctx.emit("vector", lambda e, tc=tc, part=part, M=M, b=b:
                     e.tensor_copy(out=ptm(tc)[:, part * 128:part * 128 + M],
                                   in_=p.b[b][0:128, 0:M]),
                     waits=ctx.bank_read_waits(b), inc="vE")
            ctx.bank_read_done(b, "vE")
    scol = scr[:, 2 * T + 18 * 160: 2 * T + 18 * 160 + 16]
    mxcol = scr[:, 2 * T + 18 * 160 + 16: 2 * T + 18 * 160 + 32]
    sccol = scr[:, 2 * T + 18 * 160 + 32: 2 * T + 18 * 160 + 48]
    def vemit(build):
        vq = ctx.cnt["vE"]
        return ctx.emit("vector", build, waits=[("vE", vq)], inc="vE")
    for tc in range(16):
        vemit(lambda e, tc=tc: e.tensor_copy(out=work, in_=ptm(tc)))
        for it in range(8):
            vemit(lambda e, tc=tc: e.reduce_max(
                out=mxcol[:, tc:tc + 1], in_=work, axis=AX.X))
            if it < 7:
                vemit(lambda e, tc=tc: e.tensor_scalar(
                    out=msk, in0=work, scalar1=mxcol[:, tc:tc + 1], scalar2=None,
                    op0=ALU.is_ge))
                vemit(lambda e: e.scalar_tensor_tensor(
                    out=work, in0=msk, scalar=-1e30, in1=work,
                    op0=ALU.mult, op1=ALU.add))
        vemit(lambda e, tc=tc: e.tensor_scalar(
            out=msk, in0=ptm(tc), scalar1=mxcol[:, tc:tc + 1], scalar2=None,
            op0=ALU.is_ge))
        vemit(lambda e, tc=tc: e.tensor_mul(out=msk, in0=msk, in1=ptm(tc)))
        vemit(lambda e, tc=tc: e.tensor_copy(out=ctm(tc), in_=msk))
        vemit(lambda e, tc=tc: e.reduce_sum(
            out=scol[:, tc:tc + 1], in_=msk, axis=AX.X))
    vemit(lambda e: e.tensor_scalar(
        out=scol, in0=scol, scalar1=1e-9, scalar2=None, op0=ALU.add))
    vemit(lambda e: e.reciprocal(out=sccol, in_=scol))
    vemit(lambda e: e.tensor_scalar(
        out=sccol, in0=sccol, scalar1=2.5, scalar2=None, op0=ALU.mult))
    for tc in range(16):
        vemit(lambda e, tc=tc: e.tensor_scalar(
            out=ctm(tc), in0=ctm(tc), scalar1=sccol[:, tc:tc + 1], scalar2=None,
            op0=ALU.mult))
    vq_c = ctx.cnt["vE"]
    for tc in range(16):
        for part in range(2):
            M = 128 if part == 0 else 32
            b = 6 + (tc * 2 + part) % 2
            pbf = p.b[b].ap().bitcast(BF16)
            ctx.emit("tensor", lambda e, tc=tc, part=part, M=M, pbf=pbf:
                     e.transpose(pbf[0:M, 0:128],
                                 ctm(tc)[:, part * 128:part * 128 + M],
                                 s.eye_bf[:]),
                     waits=ctx.bank_acquire(b, [("vE", vq_c)]), inc="tE")
            ctx.bank_filled(b)
            ctx.emit("vector", lambda e, tc=tc, part=part, M=M, pbf=pbf:
                     e.tensor_copy(
                         out=s.comb_em[0:M, part * T + tc * 128:
                                       part * T + (tc + 1) * 128],
                         in_=pbf[0:M, 0:128]),
                     waits=ctx.bank_read_waits(b), inc="vE")
            ctx.bank_read_done(b, "vE")
    ctx.meta["comb_vE"] = ctx.cnt["vE"]
    ctx.meta["router_tE"] = ctx.cnt["tE"]


def emit_moe(ctx, t, s, p):
    experts = [-1] + list(range(NE))
    groups = [experts[i:i + 2] for i in range(0, len(experts), 2)]
    comb_v = ctx.meta["comb_vE"]
    rt_t = ctx.meta["router_tE"]
    ln_v = ctx.meta["ln_vE"]
    wg_free = [rt_t, 0]
    crow_free = 0
    down_done = 0
    ntile = 0

    for g, grp in enumerate(groups):
        sl = g % 2
        dma_w0 = [("tE", wg_free[sl])]
        for j, e_id in enumerate(grp):
            wg_src = (t.wgst if e_id < 0 else t.wgt[e_id]).rearrange(
                "(kc p) i -> p kc i", p=128)
            wg_dst = s.wg_sb[sl][:, j * 4096:(j + 1) * 4096].rearrange(
                "p (kc i) -> p kc i", kc=8)
            ctx.emit("gpsimd", lambda e, a=wg_dst, b=wg_src: e.dma_start(out=a, in_=b),
                     waits=dma_w0 if j == 0 else (), inc="dW", inc_by=16)
        dW_wg = ctx.cnt["dW"]
        for j, e_id in enumerate(grp):
            wd_src = (t.wdst if e_id < 0 else t.wdt[e_id]).rearrange(
                "(kc p) o -> p kc o", p=128)
            wd_dst = s.wd_sb[:, j * 4096:(j + 1) * 4096].rearrange(
                "p (kc o) -> p kc o", kc=4)
            ctx.emit("gpsimd", lambda e, a=wd_dst, b=wd_src: e.dma_start(out=a, in_=b),
                     waits=[("tE", down_done)] if j == 0 else (), inc="dW", inc_by=16)
            crow_dst = s.crow[j]
            w_cr = [("vE", comb_v)] + ([("tE", crow_free)] if j == 0 else [])
            if e_id < 0:
                ctx.emit("vector", lambda e, a=crow_dst: e.memset(a, 1.0),
                         waits=w_cr, inc="vE")
                ctx.meta["crow_memset_vE"] = ctx.cnt["vE"]
            else:
                ch, r = divmod(e_id, 128)
                csrc = s.comb_em[r:r + 1, ch * T:(ch + 1) * T]
                ctx.emit("sync", lambda e, a=crow_dst, b=csrc: e.dma_start(out=a, in_=b),
                         waits=w_cr, inc="dS", inc_by=16)
        dW_at = ctx.cnt["dW"]
        dS_at = ctx.cnt["dS"]
        for j, e_id in enumerate(grp):
            for nt in range(4):
                t0 = nt * 512
                cb = 2 + nt % 2
                ctx.emit("tensor", lambda e, j=j, t0=t0, cb=cb: e.matmul(
                    p.b[cb][:], s.ones_bf[32 * (j + 1):32 * (j + 1) + 1, :],
                    s.crow[j][:, t0:t0 + 512],
                    start=True, stop=True),
                    waits=ctx.bank_acquire(cb, [("dS", dS_at),
                          ("vE", ctx.meta.get("crow_memset_vE", 0))]), inc="tE")
                ctx.bank_filled(cb)
                for mi in range(4):
                    gb = (nt * 4 + mi) % 2
                    w = ctx.bank_acquire(gb)
                    if nt == 0 and mi == 0 and j == 0:
                        w += [("dW", dW_wg), ("vE", ln_v)]
                    for kc in range(8):
                        ctx.emit("tensor", lambda e, sl=sl, j=j, mi=mi, kc=kc, t0=t0,
                                 gb=gb: e.matmul(
                                     p.b[gb][:],
                                     s.wg_sb[sl][:, j * 4096 + kc * 512 + mi * 128:
                                                 j * 4096 + kc * 512 + mi * 128 + 128],
                                     s.x_ln[:, kc * T + t0:kc * T + t0 + 512],
                                     start=(kc == 0), stop=(kc == 7)),
                                 waits=w if kc == 0 else (),
                                 inc="tE" if kc == 7 else None)
                    ctx.bank_filled(gb)
                    sj = (nt * 4 + mi) % 2
                    sc = s.silu_sc[:, sj * 512:(sj + 1) * 512]
                    ctx.emit("scalar", lambda e, gb=gb, sc=sc: e.activation(
                        out=sc, in_=p.b[gb][:], func=AF.Silu),
                        waits=ctx.bank_read_waits(
                            gb, [("vE", ctx.meta.get("silu_vE_%d" % sj, 0))]),
                        inc="aE")
                    ctx.bank_read_done(gb, "aE")
                    aq = ctx.cnt["aE"]
                    dst = s.h1w[:, (j * 4 + mi) * T + t0:(j * 4 + mi) * T + t0 + 512]
                    w2 = ctx.bank_read_waits(cb, [("aE", aq)])
                    if nt == 0 and mi == 0 and j == 0:
                        w2 += [("tE", down_done)]
                    vq = ctx.emit("vector", lambda e, sc=sc, dst=dst, cb=cb:
                                  e.tensor_tensor(out=dst, in0=sc, in1=p.b[cb][:],
                                                  op=ALU.mult),
                                  waits=w2, inc="vE")
                    ctx.meta["silu_vE_%d" % sj] = vq
                    ctx.bank_read_done(cb, "vE")
        h1w_v = ctx.cnt["vE"]
        wg_free[sl] = ctx.cnt["tE"]
        crow_free = ctx.cnt["tE"]
        for mo in range(8):
            for nt in range(4):
                t0 = nt * 512
                db = 4 + ntile % 2
                nmm = len(grp) * 4
                i = 0
                for j in range(len(grp)):
                    for kc in range(4):
                        first = (i == 0)
                        i += 1
                        w = ()
                        if first:
                            w = ctx.bank_acquire(db)
                            if mo == 0 and nt == 0:
                                w += [("vE", h1w_v), ("dW", dW_at)]
                        ctx.emit("tensor", lambda e, j=j, kc=kc, mo=mo, t0=t0,
                                 db=db, st=first, sp=(i == nmm): e.matmul(
                                     p.b[db][:],
                                     s.wd_sb[:, j * 4096 + kc * 1024 + mo * 128:
                                             j * 4096 + kc * 1024 + mo * 128 + 128],
                                     s.h1w[:, (j * 4 + kc) * T + t0:
                                           (j * 4 + kc) * T + t0 + 512],
                                     start=st, stop=sp),
                                 waits=w, inc="tE" if i == nmm else None)
                ctx.bank_filled(db)
                dst = s.x_fm[:, mo * T + t0:mo * T + t0 + 512]
                ctx.emit("vector", lambda e, db=db, dst=dst:
                         e.scalar_tensor_tensor(out=dst, in0=p.b[db][:], scalar=0.5,
                                                in1=dst, op0=ALU.mult, op1=ALU.add),
                         waits=ctx.bank_read_waits(db), inc="vE")
                ctx.bank_read_done(db, "vE")
                ntile += 1
        down_done = ctx.cnt["tE"]
    ctx.meta["xfm_vE"] = ctx.cnt["vE"]


def emit_dump(ctx, nc, name, ap, cols, dtype, rows=128):
    out = nc.dram_tensor(name, [rows, cols], dtype, kind="ExternalOutput").ap()
    ctx.emit("sync", lambda e: e.dma_start(out=out, in_=ap),
             waits=[("vE", ctx.cnt.get("vE", 0)), ("aE", ctx.cnt.get("aE", 0)),
                    ("tE", ctx.cnt.get("tE", 0)), ("dW", ctx.cnt.get("dW", 0))],
             inc="dS", inc_by=16)


def emit_store_x(ctx, t, s):
    src = s.x_fm.ap().rearrange("p (c t) -> p c t", c=NCH)
    dst = t.xout.rearrange("(c p) t -> p c t", p=128)
    ctx.emit("sync", lambda e: e.dma_start(out=dst, in_=src),
             waits=[("vE", ctx.cnt.get("vE", 0)), ("aE", ctx.cnt.get("aE", 0)),
                    ("tE", ctx.cnt.get("tE", 0))], inc="dS", inc_by=16)
    ctx.emit("sync", lambda e: e.nop(), waits=[("dS", ctx.cnt["dS"])])


def emit_mamba(ctx, t, s, p):
    ln_v = ctx.meta["ln_vE"]
    ent_t, ent_v, ent_a = ctx.cnt["tE"], ctx.cnt["vE"], ctx.cnt.get("aE", 0)
    bar = [("tE", ent_t), ("vE", ent_v), ("aE", ent_a)]
    PI = 3.14159265358979
    TWO_PI = 2 * PI
    INV_2PI = 1.0 / TWO_PI
    MAGIC = 1.5 * 2.0 ** 23

    def ve(build, extra=()):
        vq = ctx.cnt["vE"]
        return ctx.emit("vector", build, waits=[("vE", vq)] + list(extra), inc="vE")

    for dst, srcap in ((s.tri_sb, t.c_tri), (s.minf_sb, t.c_minf),
                       (s.invf_row, t.c_invfreq), (s.dtb_row, t.dt_bias),
                       (s.alog_row, t.a_log), (s.dskip_sb, t.dskip_rep),
                       (s.nw_sb, t.nw_cm)):
        ctx.emit("sync", lambda e, a=dst, b=srcap: e.dma_start(out=a[:], in_=b),
                 waits=bar, inc="dS", inc_by=16)
    dS0 = ctx.cnt["dS"]
    ctx.emit("tensor", lambda e: e.matmul(
        p.b[6][:, 0:32], s.ones_f32[:], s.dtb_row[:], start=True, stop=True),
        waits=ctx.bank_acquire(6, [("dS", dS0)] + bar), inc="tE")
    ctx.bank_filled(6)
    ctx.emit("vector", lambda e: e.tensor_copy(out=s.dtb_bc[:], in_=p.b[6][:, 0:32]),
             waits=ctx.bank_read_waits(6), inc="vE")
    ctx.bank_read_done(6, "vE")
    aq = ctx.emit("scalar", lambda e: e.activation(
        out=s.mrows[:, 0:32], in_=s.alog_row[:], func=AF.Exp),
        waits=[("dS", dS0)] + bar, inc="aE")
    ve(lambda e: e.tensor_scalar(out=s.mrows[:, 0:32], in0=s.mrows[:, 0:32],
                                 scalar1=-1.0, scalar2=None, op0=ALU.mult),
       extra=[("aE", aq)])
    vq = ctx.cnt["vE"]
    ctx.emit("tensor", lambda e: e.matmul(
        p.b[6][:, 0:32], s.ones_f32[:], s.mrows[:, 0:32], start=True, stop=True),
        waits=ctx.bank_acquire(6, [("vE", vq)]), inc="tE")
    ctx.bank_filled(6)
    ctx.emit("vector", lambda e: e.tensor_copy(out=s.nea_bc[:], in_=p.b[6][:, 0:32]),
             waits=ctx.bank_read_waits(6), inc="vE")
    ctx.bank_read_done(6, "vE")
    ve(lambda e: e.memset(s.g_f32, 0.0))
    ve(lambda e: e.memset(s.g_bf, 0.0))

    scan_done_tE = 0
    for h2 in range(2):
        th0 = h2 * 1024
        for mi, m in enumerate(range(16, 34)):
            sl = mi % 2
            mw = 128 if m < 33 else 32
            ctx.emit("gpsimd", lambda e, m=m, sl=sl, mw=mw: e.dma_start(
                out=s.wslab[sl][:, 0:8 * mw].rearrange("p (kc m) -> p kc m", kc=8),
                in_=t.w_in[:, m * 128: m * 128 + mw].rearrange(
                    "(kc p) m -> p kc m", p=128)),
                waits=[("tE", ctx.meta.get("mm_wslab_free_%d" % sl, 0))] +
                (bar if h2 == 0 and mi < 2 else []), inc="dW", inc_by=16)
            dWs = ctx.cnt["dW"]
            for nth in range(2):
                t0 = th0 + nth * 512
                b = (mi * 2 + nth) % 2
                for kc in range(8):
                    w = ()
                    if kc == 0:
                        w = ctx.bank_acquire(b, [("dW", dWs), ("vE", ln_v),
                                                 ("tE", scan_done_tE)])
                    ctx.emit("tensor", lambda e, sl=sl, kc=kc, t0=t0, b=b, mw=mw:
                             e.matmul(p.b[b][0:mw, 0:512],
                                      s.wslab[sl][:, kc * mw:(kc + 1) * mw],
                                      s.x_ln[:, kc * T + t0: kc * T + t0 + 512],
                                      start=(kc == 0), stop=(kc == 7)),
                             waits=w, inc="tE" if kc == 7 else None)
                tq = ctx.cnt["tE"]
                if nth == 1:
                    ctx.meta["mm_wslab_free_%d" % sl] = tq
                ctx.bank_filled(b)
                lt0 = nth * 512
                if m < 32:
                    dst = s.xs_fmh[:, (m - 16) * 1024 + lt0:(m - 16) * 1024 + lt0 + 512]
                    ctx.emit("scalar", lambda e, b=b, dst=dst: e.activation(
                        out=dst, in_=p.b[b][:], func=AF.Identity),
                        waits=ctx.bank_read_waits(b), inc="aE")
                    ctx.bank_read_done(b, "aE")
                elif m == 32:
                    dst = s.bcdt[:, lt0:lt0 + 512]
                    ctx.emit("scalar", lambda e, b=b, dst=dst: e.activation(
                        out=dst, in_=p.b[b][:], func=AF.Identity),
                        waits=ctx.bank_read_waits(b), inc="aE")
                    ctx.bank_read_done(b, "aE")
                else:
                    dst = s.bcdt[0:32, 1024 + lt0:1024 + lt0 + 512]
                    ctx.emit("scalar", lambda e, b=b, dst=dst: e.activation(
                        out=dst, in_=p.b[b][0:32, :], func=AF.Identity),
                        waits=ctx.bank_read_waits(b), inc="aE")
                    ctx.bank_read_done(b, "aE")
        inproj_a = ctx.cnt["aE"]
        for c in range(8):
            gc = h2 * 8 + c
            co = gc * 32
            pbf = p.b[7].ap().bitcast(BF16)
            ctx.emit("tensor", lambda e, c=c, pbf=pbf: e.transpose(
                pbf[0:128, 0:32],
                s.bcdt[0:32, 1024 + c * 128:1024 + (c + 1) * 128],
                s.eye_bf[0:32, 0:32]),
                waits=ctx.bank_acquire(7, [("aE", inproj_a)]), inc="tE")
            ctx.bank_filled(7)
            DT = s.dt_tm[:, co:co + 32]
            SC = s.th_sc[:, 0:32]
            ve(lambda e, DT=DT, pbf=pbf: e.tensor_copy(out=DT, in_=pbf[0:128, 0:32]),
               extra=ctx.bank_read_waits(7))
            ctx.bank_read_done(7, "vE")
            ve(lambda e, DT=DT: e.tensor_add(out=DT, in0=DT, in1=s.dtb_bc[:]))
            vq = ctx.cnt["vE"]
            aq = ctx.emit("scalar", lambda e, DT=DT, SC=SC: e.activation(
                out=SC, in_=DT, func=AF.Abs), waits=[("vE", vq)], inc="aE")
            ve(lambda e, SC=SC: e.tensor_scalar(
                out=SC, in0=SC, scalar1=-1.0, scalar2=None, op0=ALU.mult),
               extra=[("aE", aq)])
            aq = ctx.emit("scalar", lambda e, SC=SC: e.activation(
                out=SC, in_=SC, func=AF.Exp), waits=[("vE", ctx.cnt["vE"])], inc="aE")
            ve(lambda e, DT=DT: e.tensor_relu(out=s.th_sc[:, 32:64], in_=DT),
               extra=[("aE", aq)])
            ve(lambda e, SC=SC: e.tensor_scalar(
                out=SC, in0=SC, scalar1=1.0, scalar2=None, op0=ALU.add))
            aq = ctx.emit("scalar", lambda e, SC=SC: e.activation(
                out=SC, in_=SC, func=AF.Ln), waits=[("vE", ctx.cnt["vE"])], inc="aE")
            ve(lambda e, DT=DT, SC=SC: e.tensor_add(
                out=DT, in0=s.th_sc[:, 32:64], in1=SC), extra=[("aE", aq)])
            CU = s.cumdt[:, co:co + 32]
            vq = ctx.cnt["vE"]
            ctx.emit("tensor", lambda e, DT=DT: e.matmul(
                p.b[7][:, 0:32], s.tri_sb[:], DT, start=True, stop=(gc == 0)),
                waits=ctx.bank_acquire(7, [("vE", vq), ("dS", dS0)]), inc="tE")
            if gc > 0:
                ctx.emit("tensor", lambda e: e.matmul(
                    p.b[7][:, 0:32], s.ones_f32[:], s.mrows[:, 64:96],
                    start=False, stop=True),
                    waits=[("dS", ctx.meta.get("mm_carry_dS", 0))], inc="tE")
                ctx.meta["mm_carry_used"] = ctx.cnt["tE"]
            ctx.bank_filled(7)
            ve(lambda e, CU=CU: e.tensor_copy(out=CU, in_=p.b[7][:, 0:32]),
               extra=ctx.bank_read_waits(7))
            ctx.bank_read_done(7, "vE")
            LL = s.L_tm[:, co:co + 32]
            ve(lambda e, LL=LL, CU=CU: e.tensor_mul(out=LL, in0=CU, in1=s.nea_bc[:]))
            vq = ctx.cnt["vE"]
            ctx.emit("sync", lambda e, CU=CU: e.dma_start(
                out=s.mrows[:, 64:96], in_=CU[127:128, :]),
                waits=[("vE", vq), ("tE", ctx.meta.get("mm_carry_used", 0))],
                inc="dS", inc_by=16)
            ctx.meta["mm_carry_dS"] = ctx.cnt["dS"]
        prep_v = ctx.cnt["vE"]
        for c in range(8):
            gc = h2 * 8 + c
            co = gc * 32
            qlocal = c % 4
            ctx.emit("tensor", lambda e: e.matmul(
                p.b[6][:], s.ones_f32[:], s.invf_row[:], start=True, stop=True),
                waits=ctx.bank_acquire(6, [("vE", prep_v)]), inc="tE")
            ctx.bank_filled(6)
            cum_exp = s.cumdt[:, co:co + 32].rearrange("p (h one) -> p h one", one=1).broadcast_to(
                [128, 32, 16])
            TH = s.th_sc[:, 0:512]
            TH3 = TH.rearrange("p (h j) -> p h j", h=32)
            P63 = p.b[6].ap().rearrange("p (h j) -> p h j", h=32)
            ve(lambda e, cum_exp=cum_exp, TH3=TH3, P63=P63: e.tensor_mul(
                out=TH3, in0=cum_exp, in1=P63),
               extra=ctx.bank_read_waits(6))
            ctx.bank_read_done(6, "vE")
            CSF = s.cs_c.bitcast(F32)
            ve(lambda e, TH=TH, CSF=CSF: e.tensor_scalar(
                out=CSF, in0=TH, scalar1=INV_2PI, scalar2=MAGIC,
                op0=ALU.mult, op1=ALU.add),
               extra=[("aE", ctx.meta.get("mm_cs_aE", 0)),
                      ("tE", ctx.meta.get("mm_fm_used_1", 0))])
            ve(lambda e, CSF=CSF: e.tensor_scalar(
                out=CSF, in0=CSF, scalar1=MAGIC, scalar2=None,
                op0=ALU.subtract))
            ve(lambda e, TH=TH, CSF=CSF: e.scalar_tensor_tensor(
                out=TH, in0=CSF, scalar=-TWO_PI, in1=TH,
                op0=ALU.mult, op1=ALU.add))
            vq = ctx.cnt["vE"]
            aq = ctx.emit("scalar", lambda e, TH=TH: e.activation(
                out=s.cs_c[:, 512:1024], in_=TH, func=AF.Sin),
                waits=[("vE", vq)], inc="aE")
            aq2 = ctx.emit("scalar", lambda e, TH=TH: e.activation(
                out=TH, in_=TH, func=AF.Abs), waits=[("aE", aq),
                ("vE", ctx.cnt["vE"])], inc="aE")
            ve(lambda e, TH=TH: e.tensor_scalar(
                out=TH, in0=TH, scalar1=-1.0, scalar2=PI / 2,
                op0=ALU.mult, op1=ALU.add), extra=[("aE", aq2)])
            vq = ctx.cnt["vE"]
            aq_cs = ctx.emit("scalar", lambda e, TH=TH: e.activation(
                out=s.cs_c[:, 0:512], in_=TH, func=AF.Sin),
                waits=[("vE", vq)], inc="aE")
            ctx.meta["mm_cs_aE"] = aq_cs
            pbf7 = p.b[7].ap().bitcast(BF16)
            ctx.emit("tensor", lambda e, c=c, pbf7=pbf7: e.transpose(
                pbf7[0:128, 0:128], s.bcdt[:, c * 128:(c + 1) * 128],
                s.eye_bf[:]),
                waits=ctx.bank_acquire(7, [("aE", inproj_a)]), inc="tE")
            ctx.bank_filled(7)
            cosap = s.cs_c[:, 0:512].rearrange("p (h j) -> p h j", h=32)
            sinap = s.cs_c[:, 512:1024].rearrange("p (h j) -> p h j", h=32)
            Bev = pbf7[0:128, 0:64:2].rearrange("p (one j) -> p one j", one=1).broadcast_to(
                [128, 32, 32])
            Bod = pbf7[0:128, 1:64:2].rearrange("p (one j) -> p one j", one=1).broadcast_to(
                [128, 32, 32])
            Bnr = pbf7[0:128, 32:64].rearrange("p (one n) -> p one n", one=1).broadcast_to(
                [128, 32, 32])
            Cev = pbf7[0:128, 64:128:2].rearrange("p (one j) -> p one j", one=1).broadcast_to(
                [128, 32, 32])
            Cod = pbf7[0:128, 65:128:2].rearrange("p (one j) -> p one j", one=1).broadcast_to(
                [128, 32, 32])
            Cnr = pbf7[0:128, 96:128].rearrange("p (one n) -> p one n", one=1).broadcast_to(
                [128, 32, 32])
            for dst_t, ev, od in ((s.bhat, Bev, Bod), (s.ctil, Cev, Cod)):
                de = dst_t.rearrange("p (h n) -> p h n", h=32)[:, :, 0:31:2]
                do = dst_t.rearrange("p (h n) -> p h n", h=32)[:, :, 1:32:2]
                dn = dst_t.rearrange("p (h n) -> p h n", h=32)[:, :, 32:64]
                evj = ev[:, :, 0:16]
                odj = od[:, :, 0:16]
                scr2 = s.fmscr.rearrange("p (h j) -> p h j", h=32)
                cos3 = cosap
                sin3 = sinap
                ve(lambda e, de=de, evj=evj, cos3=cos3: e.tensor_tensor(
                    out=de, in0=cos3, in1=evj, op=ALU.mult),
                   extra=[("aE", aq_cs)] + ctx.bank_read_waits(7) +
                   [("tE", ctx.meta.get("mm_fmscr_used", 0))])
                ve(lambda e, scr2=scr2, odj=odj, sin3=sin3: e.tensor_tensor(
                    out=scr2, in0=sin3, in1=odj, op=ALU.mult))
                ve(lambda e, de=de, scr2=scr2: e.tensor_tensor(
                    out=de, in0=de, in1=scr2, op=ALU.add))
                ve(lambda e, do=do, odj=odj, cos3=cos3: e.tensor_tensor(
                    out=do, in0=cos3, in1=odj, op=ALU.mult))
                ve(lambda e, scr2=scr2, evj=evj, sin3=sin3: e.tensor_tensor(
                    out=scr2, in0=sin3, in1=evj, op=ALU.mult))
                ve(lambda e, do=do, scr2=scr2: e.tensor_tensor(
                    out=do, in0=do, in1=scr2, op=ALU.subtract))
                ve(lambda e, dn=dn, nr=(Bnr if dst_t is s.bhat else Cnr):
                   e.tensor_copy(out=dn, in_=nr))
            ctx.bank_read_done(7, "vE")
            dt_exp = s.dt_tm[:, co:co + 32].rearrange("p (h one) -> p h one", one=1).broadcast_to(
                [128, 32, 64])
            bhat3 = s.bhat.rearrange("p (h n) -> p h n", h=32)
            ve(lambda e, dt_exp=dt_exp, bhat3=bhat3: e.tensor_tensor(
                out=bhat3, in0=bhat3, in1=dt_exp, op=ALU.mult))
            if gc == 0:
                vq_pe = ve(lambda e: e.memset(s.mrows[:, 96:128], 0.0))
                ctx.meta["mm_pe_v"] = vq_pe
            else:
                pco = (gc - 1) * 32
                ctx.emit("sync", lambda e, pco=pco: e.dma_start(
                    out=s.mrows[:, 96:128], in_=s.L_tm[127:128, pco:pco + 32]),
                    waits=[("vE", prep_v), ("tE", ctx.meta.get("mm_rowbc_tE", 0))],
                    inc="dS", inc_by=16)
            ctx.emit("sync", lambda e, co=co: e.dma_start(
                out=s.mrows[:, 128:160], in_=s.L_tm[127:128, co:co + 32]),
                waits=[("vE", prep_v), ("tE", ctx.meta.get("mm_rowbc_tE", 0))],
                inc="dS", inc_by=16)
            dSr = ctx.cnt["dS"]
            ctx.emit("tensor", lambda e: e.matmul(
                p.b[6][:, 0:64], s.ones_f32[:],
                s.mrows[:, 96:160], start=True, stop=True),
                waits=ctx.bank_acquire(6, [("dS", dSr),
                      ("vE", ctx.meta.get("mm_pe_v", 0))]), inc="tE")
            ctx.bank_filled(6)
            ctx.meta["mm_rowbc_tE"] = ctx.cnt["tE"]
            COLF = s.th_sc[:, 0:32]
            FALL = s.th_sc[:, 32:64]
            ve(lambda e, COLF=COLF, co=co: e.tensor_sub(
                out=COLF, in0=s.L_tm[:, co:co + 32], in1=p.b[6][:, 0:32]),
               extra=ctx.bank_read_waits(6) + [("aE", aq_cs)])
            ve(lambda e, FALL=FALL, co=co: e.tensor_sub(
                out=FALL, in0=p.b[6][:, 32:64], in1=s.L_tm[:, co:co + 32]))
            ctx.bank_read_done(6, "vE")
            vq = ctx.cnt["vE"]
            aq = ctx.emit("scalar", lambda e, COLF=COLF: e.activation(
                out=COLF, in_=COLF, func=AF.Exp), waits=[("vE", vq)], inc="aE")
            aq = ctx.emit("scalar", lambda e, FALL=FALL: e.activation(
                out=FALL, in_=FALL, func=AF.Exp), inc="aE")
            colf_exp = COLF.rearrange("p (h one) -> p h one", one=1).broadcast_to([128, 32, 64])
            cbar3 = s.cbar.rearrange("p (h n) -> p h n", h=32)
            ctil3 = s.ctil.rearrange("p (h n) -> p h n", h=32)
            ve(lambda e, colf_exp=colf_exp, cbar3=cbar3, ctil3=ctil3:
               e.tensor_tensor(out=cbar3, in0=ctil3, in1=colf_exp, op=ALU.mult),
               extra=[("aE", aq)])
            ve(lambda e: e.tensor_sub(out=s.mrows[:, 160:192],
                                      in0=s.mrows[:, 128:160],
                                      in1=s.mrows[:, 96:128]),
               extra=[("dS", dSr)])
            vq = ctx.cnt["vE"]
            aq = ctx.emit("scalar", lambda e: e.activation(
                out=s.mrows[:, 160:192], in_=s.mrows[:, 160:192], func=AF.Exp),
                waits=[("vE", vq)], inc="aE")
            ctx.emit("tensor", lambda e: e.matmul(
                p.b[6][:, 64:96], s.ones_f32[:], s.mrows[:, 160:192],
                start=True, stop=True),
                waits=ctx.bank_acquire(6, [("aE", aq)]), inc="tE")
            ctx.bank_filled(6)
            GSC = s.th_sc[:, 64:96]
            ve(lambda e, GSC=GSC: e.tensor_copy(out=GSC[0:64, :],
                                                in_=p.b[6][0:64, 64:96]),
               extra=ctx.bank_read_waits(6))
            ctx.bank_read_done(6, "vE")
            ctx.emit("tensor", lambda e, co=co: e.transpose(
                p.b[6].ap()[0:32, 0:128], s.L_tm[:, co:co + 32], s.eye_f32[:]),
                waits=ctx.bank_acquire(6), inc="tE")
            ctx.bank_filled(6)
            ve(lambda e: e.tensor_copy(out=s.L_emc[:], in_=p.b[6].ap()[0:32, 0:128]),
               extra=ctx.bank_read_waits(6))
            ctx.bank_read_done(6, "vE")
            chunk_v = ctx.cnt["vE"]
            chunk_a = ctx.cnt["aE"]
            p2bf = [p.b[2].ap().bitcast(BF16), p.b[6].ap().bitcast(BF16)]
            DSBs = [s.th_sc[:, 128:256], s.th_sc[:, 256:384]]
            FMs = [s.fmscr[:, 0:448], s.cs_c[:, 0:448]]
            ATs = [s.attnT_m[:, 0:128], s.bcdt[:, 1024:1152]]
            ybank = [3, 0]
            stbank = [7, 1]
            scbank = [4, 5]
            trbank = [2, 6]
            headmeta = {}

            def stageA(h):
                sl = h % 2
                base = (h % 2) * 64
                tb = trbank[sl]
                pb = p2bf[sl]
                DSB = DSBs[sl]
                ctx.emit("sync", lambda e, h=h: e.dma_start(
                    out=s.mrows[:, 192 + sl * 128:320 + sl * 128],
                    in_=s.L_emc[h:h + 1, :]),
                    waits=[("vE", chunk_v),
                           ("tE", ctx.meta.get("mm_d_tE_%d" % sl, 0))],
                    inc="dS", inc_by=16)
                dSh = ctx.cnt["dS"]
                ctx.emit("tensor", lambda e, sl=sl: e.matmul(
                    p.b[tb][:, 0:128], s.ones_f32[:],
                    s.mrows[:, 192 + sl * 128:320 + sl * 128],
                    start=True, stop=True),
                    waits=ctx.bank_acquire(tb, [("dS", dSh)]), inc="tE")
                xs_src = s.xs_fmh[base:base + 64,
                                  (h // 2) * 1024 + c * 128:
                                  (h // 2) * 1024 + (c + 1) * 128]
                ctx.emit("tensor", lambda e, xs_src=xs_src, pb=pb, base=base:
                         e.transpose(pb[0:128, 256:320], xs_src,
                                     s.eye_bf[base:base + 64, base:base + 64]),
                         waits=[("vE", chunk_v)], inc=None)
                ctx.emit("tensor", lambda e, h=h, pb=pb: e.transpose(
                    pb[0:64, 320:448], s.bhat[:, h * 64:(h + 1) * 64],
                    s.eye_bf[:]), inc=None)
                ctx.emit("tensor", lambda e, h=h, pb=pb: e.transpose(
                    pb[0:64, 448:576], s.ctil[:, h * 64:(h + 1) * 64],
                    s.eye_bf[:]), inc=None)
                ctx.emit("tensor", lambda e, h=h, pb=pb: e.transpose(
                    pb[0:64, 576:704], s.cbar[:, h * 64:(h + 1) * 64],
                    s.eye_bf[:]), inc="tE")
                ctx.bank_filled(tb)
                ctx.meta["mm_d_tE_%d" % sl] = ctx.cnt["tE"]
                ve(lambda e, DSB=DSB, co=co, h=h, tb=tb: e.tensor_scalar(
                    out=DSB, in0=p.b[tb][:, 0:128],
                    scalar1=s.L_tm[:, co + h:co + h + 1], scalar2=None,
                    op0=ALU.subtract), extra=ctx.bank_read_waits(tb))
                ve(lambda e, DSB=DSB: e.tensor_add(out=DSB, in0=DSB,
                                                   in1=s.minf_sb[:]))
                vq = ctx.cnt["vE"]
                aqD = ctx.emit("scalar", lambda e, DSB=DSB: e.activation(
                    out=DSB, in_=DSB, func=AF.Exp), waits=[("vE", vq)], inc="aE")
                FM = FMs[sl]
                fm_v = ctx.emit("scalar", lambda e, pb=pb, FM=FM: e.copy(
                    out=FM, in_=pb[0:128, 256:704]),
                    waits=ctx.bank_read_waits(tb) +
                    [("tE", ctx.meta.get("mm_fm_used_%d" % sl, 0))], inc="aE")
                ctx.bank_read_done(tb, "aE")
                ve(lambda e, h=h, FALL=FALL: e.tensor_scalar(
                    out=s.bhat[:, h * 64:(h + 1) * 64],
                    in0=s.bhat[:, h * 64:(h + 1) * 64],
                    scalar1=FALL[:, h:h + 1], scalar2=None, op0=ALU.mult))
                headmeta[h] = (fm_v, ctx.cnt["vE"], aqD)

            def stageB(h):
                sl = h % 2
                base = (h % 2) * 64
                fm_v, bbar_v, aqD = headmeta[h]
                FM = FMs[sl]
                DSB = DSBs[sl]
                AT = ATs[sl]
                sb_ = scbank[sl]
                yb = ybank[sl]
                stb = stbank[sl]
                ctx.emit("tensor", lambda e, sb_=sb_, FM=FM: e.matmul(
                    p.b[sb_][:, 0:128], FM[0:64, 64:192],
                    FM[0:64, 192:320], start=True, stop=True),
                    waits=ctx.bank_acquire(sb_, [("aE", fm_v)]), inc="tE")
                ctx.bank_filled(sb_)
                ve(lambda e, sb_=sb_, DSB=DSB, AT=AT: e.tensor_tensor(
                    out=AT, in0=p.b[sb_][:, 0:128], in1=DSB, op=ALU.mult),
                   extra=ctx.bank_read_waits(sb_) +
                   [("aE", aqD), ("tE", ctx.meta.get("mm_at_used_%d" % sl, 0))])
                ctx.bank_read_done(sb_, "vE")
                ve(lambda e, h=h, AT=AT: e.scalar_tensor_tensor(
                    out=AT, in0=s.eye_bf[:],
                    scalar=s.dskip_sb[:, h:h + 1], in1=AT,
                    op0=ALU.mult, op1=ALU.add))
                at_v = ctx.cnt["vE"]
                ctx.emit("tensor", lambda e, h=h, FM=FM, yb=yb: e.matmul(
                    p.b[yb][0:64, 0:128], s.g_bf[:, h * 64:(h + 1) * 64],
                    FM[0:64, 320:448], start=True, stop=False),
                    waits=ctx.bank_acquire(yb, [("vE", at_v),
                          ("aE", ctx.meta.get("mm_gcast_a", 0))]), inc=None)
                ctx.emit("tensor", lambda e, FM=FM, AT=AT, yb=yb: e.matmul(
                    p.b[yb][0:64, 0:128], FM[:, 0:64],
                    AT, start=False, stop=True), inc="tE")
                ctx.bank_filled(yb)
                ctx.meta["mm_at_used_%d" % sl] = ctx.cnt["tE"]
                ydst = s.y_q[base:base + 64,
                             (h // 2) * 512 + qlocal * 128:
                             (h // 2) * 512 + (qlocal + 1) * 128]
                ctx.emit("scalar", lambda e, ydst=ydst, yb=yb: e.copy(
                    out=ydst, in_=p.b[yb][0:64, 0:128]),
                    waits=ctx.bank_read_waits(yb), inc="aE")
                ctx.bank_read_done(yb, "aE")
                ctx.emit("tensor", lambda e, h=h, FM=FM, stb=stb: e.matmul(
                    p.b[stb][0:64, 0:64], s.bhat[:, h * 64:(h + 1) * 64],
                    FM[:, 0:64], start=True, stop=True),
                    waits=ctx.bank_acquire(stb, [("vE", bbar_v)]), inc="tE")
                ctx.bank_filled(stb)
                ctx.meta["mm_fm_used_%d" % sl] = ctx.cnt["tE"]
                gs = s.g_f32[:, h * 64:(h + 1) * 64]
                ve(lambda e, gs=gs, GSC=GSC, h=h: e.tensor_scalar(
                    out=gs, in0=gs, scalar1=GSC[0:64, h:h + 1], scalar2=None,
                    op0=ALU.mult))
                ve(lambda e, gs=gs, stb=stb: e.tensor_add(
                    out=gs, in0=gs, in1=p.b[stb][0:64, 0:64]),
                   extra=ctx.bank_read_waits(stb))
                ctx.bank_read_done(stb, "vE")
                vq = ctx.cnt["vE"]
                ctx.emit("scalar", lambda e, gs=gs, h=h: e.copy(
                    out=s.g_bf[:, h * 64:(h + 1) * 64], in_=gs),
                    waits=[("vE", vq)], inc="aE")
                ctx.meta["mm_gcast_a"] = ctx.cnt["aE"]

            for h in range(33):
                if h < 32:
                    stageA(h)
                if h >= 1:
                    stageB(h - 1)
            if qlocal == 3:
                emit_mamba_zphase(ctx, t, s, p, h2, (c // 4), dS0)
        scan_done_tE = ctx.cnt["tE"]
    ctx.meta["xfm_vE"] = ctx.cnt["vE"]


def emit_mamba_zphase(ctx, t, s, p, h2, qh, dS0):
    ln_v = ctx.meta["ln_vE"]
    tq0 = h2 * 1024 + qh * 512
    ve = None
    def ve(build, extra=()):
        vq = ctx.cnt["vE"]
        return ctx.emit("vector", build, waits=[("vE", vq)] + list(extra), inc="vE")
    yq_v = ctx.cnt["vE"]
    zq_a = ctx.cnt["aE"]
    for m in range(16):
        sl = m % 2
        ctx.emit("gpsimd", lambda e, m=m, sl=sl: e.dma_start(
            out=s.wslab[sl].rearrange("p (kc m) -> p kc m", kc=8),
            in_=t.w_in[:, m * 128:(m + 1) * 128].rearrange(
                "(kc p) m -> p kc m", p=128)),
            waits=[("tE", ctx.meta.get("mm_wslab_free_%d" % sl, 0))],
            inc="dW", inc_by=16)
        dWs = ctx.cnt["dW"]
        b = m % 2
        for kc in range(8):
            w = ()
            if kc == 0:
                w = ctx.bank_acquire(b, [("dW", dWs), ("vE", yq_v)])
            ctx.emit("tensor", lambda e, sl=sl, kc=kc, b=b: e.matmul(
                p.b[b][:], s.wslab[sl][:, kc * 128:(kc + 1) * 128],
                s.x_ln[:, kc * T + tq0: kc * T + tq0 + 512],
                start=(kc == 0), stop=(kc == 7)), waits=w,
                inc="tE" if kc == 7 else None)
        ctx.meta["mm_wslab_free_%d" % sl] = ctx.cnt["tE"]
        ctx.bank_filled(b)
        zscr = s.cs_c[:, 0:512]
        aq = ctx.emit("scalar", lambda e, b=b, zscr=zscr: e.activation(
            out=zscr, in_=p.b[b][:], func=AF.Silu),
            waits=ctx.bank_read_waits(b, [("aE", ctx.meta.get("mm_z_aq", 0)),
                                          ("vE", ctx.meta.get("mm_z_vq", 0))]),
            inc="aE")
        ctx.bank_read_done(b, "aE")
        ctx.meta["mm_z_aq"] = aq
        yslice = s.y_q[:, m * 512:(m + 1) * 512]
        vq = ctx.emit("vector", lambda e, yslice=yslice, zscr=zscr: e.tensor_mul(
            out=yslice, in0=yslice, in1=zscr), waits=[("aE", aq)], inc="vE")
        ctx.meta["mm_z_vq"] = vq
        sqscr = s.cs_c[:, 512:1024]
        aq = ctx.emit("scalar", lambda e, yslice=yslice, sqscr=sqscr: e.activation(
            out=sqscr, in_=yslice, func=AF.Square), waits=[("vE", vq)], inc="aE")
        ctx.emit("tensor", lambda e, m=m, sqscr=sqscr: e.matmul(
            p.b[4][0:1, :], s.onescol_bf[:], sqscr,
            start=(m == 0), stop=(m == 15)),
            waits=ctx.bank_acquire(4, [("aE", aq)]) if m == 0 else [("aE", aq)],
            inc="tE")
    ctx.bank_filled(4)
    R = s.th_sc[0:1, 0:512]
    ve(lambda e, R=R: e.tensor_scalar(
        out=R, in0=p.b[4][0:1, :], scalar1=1.0 / 2048.0, scalar2=1e-5,
        op0=ALU.mult, op1=ALU.add),
       extra=ctx.bank_read_waits(4) + [("aE", zq_a)])
    ctx.bank_read_done(4, "vE")
    vq = ctx.cnt["vE"]
    aq = ctx.emit("scalar", lambda e, R=R: e.activation(out=R, in_=R, func=AF.Ln),
                  waits=[("vE", vq)], inc="aE")
    ve(lambda e, R=R: e.tensor_scalar(
        out=R, in0=R, scalar1=-0.5, scalar2=None, op0=ALU.mult),
       extra=[("aE", aq)])
    vq = ctx.cnt["vE"]
    aq = ctx.emit("scalar", lambda e, R=R: e.activation(out=R, in_=R, func=AF.Exp),
                  waits=[("vE", vq)], inc="aE")
    ctx.emit("tensor", lambda e, R=R: e.matmul(
        p.b[5][:], s.ones_f32[:], R, start=True, stop=True),
        waits=ctx.bank_acquire(5, [("aE", aq)]), inc="tE")
    ctx.bank_filled(5)
    for m in range(16):
        yslice = s.y_q[:, m * 512:(m + 1) * 512]
        ctx.emit("vector", lambda e, yslice=yslice: e.tensor_mul(
            out=yslice, in0=yslice, in1=p.b[5][:]),
            waits=ctx.bank_read_waits(5) if m == 0 else (), inc="vE")
        ctx.emit("vector", lambda e, yslice=yslice, m=m: e.tensor_scalar(
            out=yslice, in0=yslice, scalar1=s.nw_sb[:, m:m + 1], scalar2=None,
            op0=ALU.mult), inc="vE")
    ctx.bank_read_done(5, "vE")
    yn_v = ctx.cnt["vE"]
    for kc in range(16):
        sl = kc % 2
        ctx.emit("gpsimd", lambda e, kc=kc, sl=sl: e.dma_start(
            out=s.wslab[sl][:, 0:1024],
            in_=t.w_out[kc * 128:(kc + 1) * 128, :]),
            waits=[("tE", ctx.meta.get("mm_wslab_free_%d" % sl, 0))],
            inc="dW", inc_by=16)
        dWs = ctx.cnt["dW"]
        for mo in range(8):
            w = [("dW", dWs)]
            if kc == 0:
                w += ctx.bank_acquire(mo, [("vE", yn_v)])
            ctx.emit("tensor", lambda e, kc=kc, sl=sl, mo=mo: e.matmul(
                p.b[mo][:], s.wslab[sl][:, mo * 128:(mo + 1) * 128],
                s.y_q[:, kc * 512:(kc + 1) * 512],
                start=(kc == 0), stop=(kc == 15)),
                waits=w if (kc == 0 or mo == 0) else (),
                inc="tE" if (kc == 15 or mo == 7) else None)
        ctx.meta["mm_wslab_free_%d" % sl] = ctx.cnt["tE"]
    for mo in range(8):
        ctx.bank_filled(mo)
        dst = s.x_fm[:, mo * T + tq0: mo * T + tq0 + 512]
        ctx.emit("vector", lambda e, mo=mo, dst=dst: e.tensor_add(
            out=dst, in0=p.b[mo][:], in1=dst),
            waits=ctx.bank_read_waits(mo), inc="vE")
        ctx.bank_read_done(mo, "vE")
    ctx.meta["mm_yq_free"] = ctx.cnt["vE"]


def emit_attn(ctx, t, s, p):
    ln_v = ctx.meta["ln_vE"]
    ent_t, ent_v, ent_a = ctx.cnt["tE"], ctx.cnt["vE"], ctx.cnt.get("aE", 0)
    bar = [("tE", ent_t), ("vE", ent_v), ("aE", ent_a)]
    atc = s.atc
    ctx.emit("sync", lambda e: e.dma_start(out=s.maskband, in_=t.c_maskband),
             waits=bar, inc="dS", inc_by=16)
    ctx.emit("sync", lambda e: e.dma_start(out=s.ba_sb, in_=t.ba_cm),
             waits=bar, inc="dS", inc_by=16)
    ctx.emit("sync", lambda e: e.dma_start(out=s.bp_sb, in_=t.bp_cm),
             waits=bar, inc="dS", inc_by=16)
    dS0 = ctx.cnt["dS"]
    ctx.emit("vector", lambda e: e.tensor_scalar(
        out=s.qba_sb, in0=s.ba_sb[:, 0:8], scalar1=0.125, scalar2=None,
        op0=ALU.mult), waits=[("dS", dS0)] + bar, inc="vE")
    qba_v = ctx.cnt["vE"]
    for hg in range(4):
        for hh in range(4):
            h = hg * 4 + hh
            ctx.emit("gpsimd", lambda e, h=h, hh=hh: e.dma_start(
                out=s.wproj_sb[0:64, hh * 1024:(hh + 1) * 1024],
                in_=t.w_proj[h * 64:(h + 1) * 64, :]),
                waits=(bar + [("tE", ctx.meta.get("at_wproj_free", 0))]) if hh == 0
                else (), inc="dW", inc_by=16)
        dWp = ctx.cnt["dW"]
        for lc in range(6):
            m_abs = [2 * hg, 2 * hg + 1, 8 + 2 * hg, 9 + 2 * hg,
                     16 + 2 * hg, 17 + 2 * hg][lc]
            sl = lc % 2
            ctx.emit("gpsimd", lambda e, m_abs=m_abs, sl=sl: e.dma_start(
                out=s.wattn_sb[sl].rearrange("p (kc m) -> p kc m", kc=8),
                in_=t.w_attn[:, m_abs * 128:(m_abs + 1) * 128].rearrange(
                    "(kc p) m -> p kc m", p=128)),
                waits=[("tE", ctx.meta.get("at_wattn_free_%d" % sl, 0))] +
                (bar if hg == 0 and lc < 2 else []), inc="dW", inc_by=16)
            dWa = ctx.cnt["dW"]
            for nt in range(4):
                t0 = nt * 512
                b = (lc * 4 + nt) % 2
                for kc in range(8):
                    w = ()
                    if kc == 0:
                        w = ctx.bank_acquire(b, [("dW", dWa), ("vE", ln_v)])
                    ctx.emit("tensor", lambda e, sl=sl, kc=kc, t0=t0, b=b: e.matmul(
                        p.b[b][:], s.wattn_sb[sl][:, kc * 128:(kc + 1) * 128],
                        s.x_ln[:, kc * T + t0: kc * T + t0 + 512],
                        start=(kc == 0), stop=(kc == 7)), waits=w,
                        inc="tE" if kc == 7 else None)
                tq = ctx.cnt["tE"]
                if nt == 3:
                    ctx.meta["at_wattn_free_%d" % sl] = tq
                ctx.bank_filled(b)
                m_abs = [2 * hg, 2 * hg + 1, 8 + 2 * hg, 9 + 2 * hg,
                         16 + 2 * hg, 17 + 2 * hg][lc]
                wq = [("tE", ctx.meta.get("at_qkv_free", 0))] if (lc == 0 and
                                                                 nt == 0) else []
                if lc < 2:
                    ctx.emit("vector", lambda e, lc=lc, t0=t0, b=b, m_abs=m_abs:
                             e.tensor_scalar(
                                 out=s.qkv_g[:, lc * T + t0: lc * T + t0 + 512],
                                 in0=p.b[b][:], scalar1=0.125,
                                 scalar2=s.qba_sb[:, m_abs:m_abs + 1],
                                 op0=ALU.mult, op1=ALU.add),
                             waits=ctx.bank_read_waits(
                                 b, [("dS", dS0), ("vE", qba_v)] + wq), inc="vE")
                    ctx.bank_read_done(b, "vE")
                else:
                    ctx.emit("scalar", lambda e, lc=lc, t0=t0, b=b, m_abs=m_abs:
                             e.activation(
                                 out=s.qkv_g[:, lc * T + t0: lc * T + t0 + 512],
                                 in_=p.b[b][:], func=AF.Identity,
                                 bias=s.ba_sb[:, m_abs:m_abs + 1]),
                             waits=ctx.bank_read_waits(b, [("dS", dS0)] + wq),
                             inc="aE")
                    ctx.bank_read_done(b, "aE")
        qkv_a = ctx.cnt["aE"]
        qkv_v = ctx.cnt["vE"]
        for hh in range(4):
            base = (hh % 2) * 64
            qc = (hh // 2) * T
            kc_off = (2 + hh // 2) * T
            vc = (4 + hh // 2) * T
            p3bf = p.b[3].ap().bitcast(BF16)
            for tc in range(16):
                ctx.emit("tensor", lambda e, tc=tc, base=base, vc=vc, p3bf=p3bf:
                         e.transpose(p3bf[0:128, tc * 64:(tc + 1) * 64],
                                     s.qkv_g[base:base + 64,
                                             vc + tc * 128: vc + (tc + 1) * 128],
                                     s.eye_bf[base:base + 64, base:base + 64]),
                         waits=ctx.bank_acquire(3, [("aE", qkv_a),
                                                    ("vE", qkv_v)]) if tc == 0
                         else (), inc="tE" if tc == 15 else None)
            ctx.bank_filled(3)
            ctx.emit("vector", lambda e, p3bf=p3bf: e.tensor_copy(
                out=s.v_tm[:], in_=p3bf[0:128, 0:1024]),
                waits=ctx.bank_read_waits(3), inc="vE")
            ctx.bank_read_done(3, "vE")
            vtm_v = ctx.cnt["vE"]
            qmeta = {}

            def stageA(qt):
                sl = qt % 2
                ab = sl * 12
                for nt in range(4):
                    b = 4 + nt
                    off = 2048 + nt * 512 - qt * 128
                    ctx.emit("tensor", lambda e, base=base, qc=qc, kc_off=kc_off,
                             qt=qt, nt=nt, b=b: e.matmul(
                                 p.b[b][:],
                                 s.qkv_g[base:base + 64,
                                         qc + qt * 128: qc + (qt + 1) * 128],
                                 s.qkv_g[base:base + 64,
                                         kc_off + nt * 512: kc_off + nt * 512 + 512],
                                 start=True, stop=False),
                             waits=ctx.bank_acquire(b, [("vE", vtm_v)] if qt == 0
                                                    and nt == 0 else ()), inc=None)
                    ctx.emit("tensor", lambda e, off=off, b=b: e.matmul(
                        p.b[b][:], s.eye_bf[:],
                        s.maskband[:, off:off + 512], start=False, stop=True),
                        inc="tE")
                    ctx.bank_filled(b)
                for nt in range(4):
                    b = 4 + nt
                    ctx.emit("vector", lambda e, nt=nt, b=b, ab=ab: e.reduce_max(
                        out=atc[:, ab + nt:ab + nt + 1], in_=p.b[b][:], axis=AX.X),
                        waits=ctx.bank_read_waits(b) +
                        [("aE", qmeta.get(("exp", sl), 0))], inc="vE")
                vq = ctx.cnt["vE"]
                vq = ctx.emit("vector", lambda e, ab=ab: e.tensor_max(
                    out=atc[:, ab + 11:ab + 12], in0=atc[:, ab:ab + 1],
                    in1=atc[:, ab + 1:ab + 2]), waits=[("vE", vq)], inc="vE")
                vq = ctx.emit("vector", lambda e, ab=ab: e.tensor_max(
                    out=atc[:, ab + 4:ab + 5], in0=atc[:, ab + 2:ab + 3],
                    in1=atc[:, ab + 3:ab + 4]), waits=[("vE", vq)], inc="vE")
                vq = ctx.emit("vector", lambda e, ab=ab: e.tensor_max(
                    out=atc[:, ab + 11:ab + 12], in0=atc[:, ab + 11:ab + 12],
                    in1=atc[:, ab + 4:ab + 5]), waits=[("vE", vq)], inc="vE")
                vq = ctx.emit("vector", lambda e, ab=ab: e.tensor_scalar(
                    out=atc[:, ab + 4:ab + 5], in0=atc[:, ab + 11:ab + 12],
                    scalar1=-1.0, scalar2=None, op0=ALU.mult),
                    waits=[("vE", vq)], inc="vE")
                for nt in range(4):
                    b = 4 + nt
                    aq = ctx.emit("scalar", lambda e, nt=nt, b=b, sl=sl, ab=ab:
                                  e.activation(
                                      out=s.attn_sb[sl][:, nt * 512:(nt + 1) * 512],
                                      in_=p.b[b][:], func=AF.Exp,
                                      bias=atc[:, ab + 4:ab + 5],
                                      accum_out=atc[:, ab + 5 + nt:ab + 6 + nt]),
                                  waits=[("vE", vq),
                                         ("vE", qmeta.get(("sb_free", sl), 0))],
                                  inc="aE")
                    ctx.bank_read_done(b, "aE")
                qmeta[("exp", sl)] = ctx.cnt["aE"]
                aq = ctx.cnt["aE"]
                vq = ctx.emit("vector", lambda e, ab=ab: e.reduce_sum(
                    out=atc[:, ab + 9:ab + 10], in_=atc[:, ab + 5:ab + 9],
                    axis=AX.X), waits=[("aE", aq)], inc="vE")
                vq = ctx.emit("vector", lambda e, ab=ab: e.reciprocal(
                    out=atc[:, ab + 10:ab + 11], in_=atc[:, ab + 9:ab + 10]),
                    waits=[("vE", vq)], inc="vE")
                for nt in range(4):
                    ctx.emit("vector", lambda e, nt=nt, sl=sl, ab=ab: e.tensor_scalar(
                        out=s.attn_sb[sl][:, nt * 512:(nt + 1) * 512],
                        in0=s.attn_sb[sl][:, nt * 512:(nt + 1) * 512],
                        scalar1=atc[:, ab + 10:ab + 11], scalar2=None, op0=ALU.mult),
                        waits=[("vE", vq)] if nt == 0 else (), inc="vE")
                qmeta[("attn_v", qt)] = ctx.cnt["vE"]

            def stageB(qt):
                sl = qt % 2
                attn_v = qmeta[("attn_v", qt)]
                for tc in range(16):
                    b = tc // 8
                    pbf = p.b[b].ap().bitcast(BF16)
                    ctx.emit("tensor", lambda e, tc=tc, b=b, pbf=pbf, sl=sl:
                             e.transpose(pbf[0:128, (tc % 8) * 128:
                                             (tc % 8) * 128 + 128],
                                         s.attn_sb[sl][:, tc * 128:(tc + 1) * 128],
                                         s.eye_bf[0:128, 0:128]),
                             waits=ctx.bank_acquire(b, [("vE", attn_v)])
                             if tc % 8 == 0 else (),
                             inc="tE" if tc % 8 == 7 else None)
                    if tc % 8 == 7:
                        ctx.bank_filled(b)
                for b in range(2):
                    pbf = p.b[b].ap().bitcast(BF16)
                    ctx.emit("vector", lambda e, b=b, pbf=pbf, sl=sl: e.tensor_copy(
                        out=s.attnT_sb[sl][:, b * 1024:(b + 1) * 1024],
                        in_=pbf[0:128, 0:1024]),
                        waits=ctx.bank_read_waits(b), inc="vE")
                    ctx.bank_read_done(b, "vE")
                qmeta[("sb_free", sl)] = ctx.cnt["vE"]
                attnT_v = ctx.cnt["vE"]
                for kc2 in range(16):
                    w = ()
                    if kc2 == 0:
                        w = ctx.bank_acquire(2, [("vE", attnT_v)])
                    ctx.emit("tensor", lambda e, kc2=kc2, sl=sl: e.matmul(
                        p.b[2][0:64, 0:128],
                        s.v_tm[:, kc2 * 64:(kc2 + 1) * 64],
                        s.attnT_sb[sl][:, kc2 * 128:(kc2 + 1) * 128],
                        start=(kc2 == 0), stop=(kc2 == 15)), waits=w,
                        inc="tE" if kc2 == 15 else None)
                ctx.bank_filled(2)
                ctx.emit("vector", lambda e, hh=hh, qt=qt: e.tensor_copy(
                    out=s.attnout[:, hh * T + qt * 128: hh * T + (qt + 1) * 128],
                    in_=p.b[2][0:64, 0:128]),
                    waits=ctx.bank_read_waits(2), inc="vE")
                ctx.bank_read_done(2, "vE")

            for qt in range(17):
                if qt < 16:
                    stageA(qt)
                if qt >= 1:
                    stageB(qt - 1)
        ao_v = ctx.cnt["vE"]
        ctx.meta["at_qkv_free"] = ctx.cnt["tE"]
        for mo in range(8):
            for nt in range(4):
                t0 = nt * 512
                b = (mo * 4 + nt) % 2
                for hh in range(4):
                    w = ()
                    if hh == 0:
                        w = ctx.bank_acquire(b, [("vE", ao_v), ("dW", dWp)])
                    ctx.emit("tensor", lambda e, hh=hh, mo=mo, t0=t0, b=b: e.matmul(
                        p.b[b][:],
                        s.wproj_sb[0:64, hh * 1024 + mo * 128:
                                   hh * 1024 + mo * 128 + 128],
                        s.attnout[:, hh * T + t0: hh * T + t0 + 512],
                        start=(hh == 0), stop=(hh == 3)), waits=w,
                        inc="tE" if hh == 3 else None)
                ctx.bank_filled(b)
                dst = s.x_fm[:, mo * T + t0: mo * T + t0 + 512]
                ctx.emit("vector", lambda e, b=b, dst=dst: e.tensor_add(
                    out=dst, in0=p.b[b][:], in1=dst),
                    waits=ctx.bank_read_waits(b), inc="vE")
                if hg == 0:
                    ctx.emit("vector", lambda e, mo=mo, dst=dst: e.tensor_scalar(
                        out=dst, in0=dst, scalar1=s.bp_sb[:, mo:mo + 1],
                        scalar2=None, op0=ALU.add), inc="vE")
                ctx.bank_read_done(b, "vE")
        ctx.meta["at_wproj_free"] = ctx.cnt["tE"]
    ctx.meta["xfm_vE"] = ctx.cnt["vE"]


def emit_lff(ctx, t, s, p):
    ln_v = ctx.meta["ln_vE"]
    ent_t, ent_v, ent_a = ctx.cnt["tE"], ctx.cnt["vE"], ctx.cnt.get("aE", 0)
    bar = [("tE", ent_t), ("vE", ent_v), ("aE", ent_a)]
    for dst, srcap in ((s.ld_sb, t.lora_down), (s.lu_sb, t.lora_up),
                       (s.upxt_sb, t.upxt), (s.dnaux_sb, t.down_aux),
                       (s.bl_sb, t.bl_cm)):
        ctx.emit("sync", lambda e, a=dst, b=srcap: e.dma_start(
            out=a if isinstance(a, bass.AP) else a[:], in_=b),
            waits=bar, inc="dS", inc_by=16)
    ctx.emit("gpsimd", lambda e: e.dma_start(
        out=s.l1w_bf.rearrange("p (kc o) -> p kc o", kc=8),
        in_=t.l1wt.rearrange("(kc p) o -> p kc o", p=128)),
        waits=bar, inc="dW", inc_by=16)
    dS0, dW0 = ctx.cnt["dS"], ctx.cnt["dW"]
    ctx.emit("vector", lambda e: e.tensor_scalar(
        out=s.hbl_sb[:], in0=s.bl_sb[:], scalar1=0.5, scalar2=None, op0=ALU.mult),
        waits=[("dS", dS0)] + bar, inc="vE")
    for n2 in range(2):
        ctx.emit("tensor", lambda e, n2=n2: e.matmul(
            p.b[2][0:4, :], s.lu_sb, s.upxt_sb[:, n2 * 512:(n2 + 1) * 512],
            start=True, stop=True),
            waits=ctx.bank_acquire(2, [("dS", dS0)]), inc="tE")
        ctx.bank_filled(2)
        vq = ctx.emit("vector", lambda e, n2=n2: e.tensor_copy(
            out=s.p1_sb[:, n2 * 512:(n2 + 1) * 512], in_=p.b[2][0:4, :]),
            waits=ctx.bank_read_waits(2), inc="vE")
        ctx.bank_read_done(2, "vE")
    for n2 in range(2):
        ctx.emit("tensor", lambda e, n2=n2: e.matmul(
            p.b[3][0:100, :], s.ld_sb, s.p1_sb[:, n2 * 512:(n2 + 1) * 512],
            start=True, stop=True),
            waits=ctx.bank_acquire(3, [("vE", vq)]), inc="tE")
        ctx.bank_filled(3)
        vq2 = ctx.emit("vector", lambda e, n2=n2: e.tensor_copy(
            out=s.p2_sb[:, n2 * 512:(n2 + 1) * 512], in_=p.b[3][0:100, :]),
            waits=ctx.bank_read_waits(3), inc="vE")
        ctx.bank_read_done(3, "vE")
    for kc in range(8):
        for n2 in range(2):
            b = 2 + (kc * 2 + n2) % 2
            ctx.emit("tensor", lambda e, kc=kc, n2=n2, b=b: e.matmul(
                p.b[b][:], s.dnaux_sb[:, kc * 128:(kc + 1) * 128],
                s.p2_sb[:, n2 * 512:(n2 + 1) * 512], start=True, stop=True),
                waits=ctx.bank_acquire(b, [("vE", vq2)]), inc="tE")
            ctx.bank_filled(b)
            ctx.emit("vector", lambda e, kc=kc, n2=n2, b=b: e.tensor_add(
                out=s.weff[:, kc * 1024 + n2 * 512: kc * 1024 + n2 * 512 + 512],
                in0=p.b[b][:],
                in1=s.l1w_bf[:, kc * 1024 + n2 * 512: kc * 1024 + n2 * 512 + 512]),
                waits=ctx.bank_read_waits(b, [("dW", dW0)]), inc="vE")
            ctx.bank_read_done(b, "vE")
    weff_v = ctx.cnt["vE"]
    for mo in range(8):
        for nt in range(4):
            t0 = nt * 512
            b = (mo * 4 + nt) % 2
            for kc in range(8):
                w = ()
                if kc == 0:
                    w = ctx.bank_acquire(b, [("vE", max(ln_v, weff_v))])
                ctx.emit("tensor", lambda e, mo=mo, kc=kc, t0=t0, b=b: e.matmul(
                    p.b[b][:],
                    s.weff[:, kc * 1024 + mo * 128: kc * 1024 + mo * 128 + 128],
                    s.x_ln[:, kc * T + t0: kc * T + t0 + 512],
                    start=(kc == 0), stop=(kc == 7)),
                    waits=w, inc="tE" if kc == 7 else None)
            ctx.bank_filled(b)
            dst = s.x_fm[:, mo * T + t0:mo * T + t0 + 512]
            ctx.emit("vector", lambda e, b=b, dst=dst: e.scalar_tensor_tensor(
                out=dst, in0=p.b[b][:], scalar=0.5, in1=dst,
                op0=ALU.mult, op1=ALU.add),
                waits=ctx.bank_read_waits(b), inc="vE")
            ctx.emit("vector", lambda e, mo=mo, dst=dst: e.tensor_scalar(
                out=dst, in0=dst, scalar1=s.hbl_sb[:, mo:mo + 1], scalar2=None,
                op0=ALU.add), inc="vE")
            ctx.bank_read_done(b, "vE")
    ctx.meta["xfm_vE"] = ctx.cnt["vE"]


def build_program(phases, taps=()):
    phi = 0
    nc = bass.Bass("TRN2", target_bir_lowering=False, debug=False)
    t = declare_tensors(nc, phases)
    s = alloc_sbuf(nc, phases)
    p = alloc_psum(nc)
    ctx = Ctx(nc)
    emit_load_x(ctx, t, s)
    emit_load_consts(ctx, t, s, phases)
    for ph in phases:
        emit_ln(ctx, t, s, p)
        if "ln" in taps:
            emit_dump(ctx, nc, "d_xln", s.x_ln[:], NCH * T, BF16)
        if ph == "moe":
            emit_router(ctx, t, s, p)
            if "comb" in taps:
                emit_dump(ctx, nc, "d_comb", s.comb_em[:], 2 * T, BF16)
            if "scr" in taps:
                emit_dump(ctx, nc, "d_scr", s.ar_f32[:, 12288:20480], 8192, F32)
            if "stop_before_moe" in taps:
                break
            emit_moe(ctx, t, s, p)
        elif ph == "mamba":
            emit_mamba(ctx, t, s, p)
        elif ph == "lff":
            emit_lff(ctx, t, s, p)
        elif ph == "attn":
            emit_attn(ctx, t, s, p)
            if "attn_dbg" in taps:
                emit_dump(ctx, nc, "d_qkv", s.qkv_g[:], 6 * T, BF16)
                emit_dump(ctx, nc, "d_ao", s.attnout, 4 * T, BF16, rows=64)
                emit_dump(ctx, nc, "d_attn", s.attn_sb[1][:], T, BF16)
        phi += 1
        if "stream" in taps:
            emit_dump(ctx, nc, "d_ph%d" % phi, s.x_fm[:], NCH * T, F32)
    emit_store_x(ctx, t, s)
    ctx.run()
    return nc



def _consts(phases):
    c = {}
    c["c_eye_bf"] = np.eye(128, dtype=BF)
    c["c_eye_f32"] = np.eye(128, dtype=np.float32)
    c["c_ones_bf"] = np.ones((128, 128), dtype=BF)
    c["c_ones_f32"] = np.ones((1, 128), dtype=np.float32)
    c["c_onescol_bf"] = np.ones((128, 1), dtype=BF)
    if "mamba" in phases:
        sidx = np.arange(128)[:, None]
        tidx = np.arange(128)[None, :]
        c["c_tri"] = (sidx <= tidx).astype(np.float32)
        c["c_minf"] = np.where(tidx >= sidx, 0.0, -1e30).astype(np.float32)
        invf = (1.0 / 10000.0 ** (np.arange(16, dtype=np.float64) / 16.0))
        c["c_invfreq"] = np.tile(invf, 32)[None, :].astype(np.float32)
    if "attn" in phases:
        v = np.arange(4224)[None, :] - 2048 - np.arange(128)[:, None]
        f = np.where(v > 0, -np.log1p(np.maximum(v, 0).astype(np.float64)), 0.0)
        f = np.where(v < 0, -1e30, f)
        c["c_maskband"] = f.astype(BF)
    return c


def _cm(vec, ncols):
    v = np.zeros(128 * ncols, dtype=np.float32)
    v[:len(vec)] = np.asarray(vec, dtype=np.float32)
    return np.ascontiguousarray(v.reshape(ncols, 128).T)


def _layout_weights(inp, phases):
    w = {}
    f32c = lambda a: np.ascontiguousarray(np.asarray(a, dtype=np.float32))
    if "moe" in phases:
        w["wgt"] = f32c(np.asarray(inp["L0_ffn_w_gate"]).transpose(0, 2, 1))
        w["wdt"] = f32c(np.asarray(inp["L0_ffn_w_down"]).transpose(0, 2, 1))
        w["wgst"] = f32c(np.asarray(inp["L0_ffn_w_gate_shared"]).T)
        w["wdst"] = f32c(np.asarray(inp["L0_ffn_w_down_shared"]).T)
        w["rwt"] = f32c(np.asarray(inp["L0_ffn_router_w"]).T)
        w["eb_cm"] = _cm(inp["L0_ffn_expert_bias"], 2)
    if "mamba" in phases:
        w["w_in"] = f32c(inp["L0_attn_W_in"])
        w["w_out"] = f32c(inp["L0_attn_W_out"])
        w["dt_bias"] = f32c(np.asarray(inp["L0_attn_dt_bias"]).reshape(1, 32))
        w["a_log"] = f32c(np.asarray(inp["L0_attn_A_log"]).reshape(1, 32))
        w["dskip_rep"] = f32c(np.broadcast_to(
            np.asarray(inp["L0_attn_D_skip"])[None, :], (128, 32)))
        w["nw_cm"] = _cm(inp["L0_attn_norm_weight"], 16)
    if "attn" in phases:
        w["w_attn"] = f32c(inp["L1_attn_w_attn"])
        w["w_proj"] = f32c(inp["L1_attn_w_proj"])
        w["ba_cm"] = _cm(inp["L1_attn_b_attn"], 24)
        w["bp_cm"] = _cm(inp["L1_attn_b_proj"], 8)
    if "lff" in phases:
        w["l1wt"] = f32c(np.asarray(inp["L1_ffn_W"]).T)
        w["lora_down"] = f32c(inp["L1_ffn_lora_down"])
        w["lora_up"] = f32c(inp["L1_ffn_lora_up"])
        w["down_aux"] = f32c(inp["L1_ffn_down_aux"])
        w["upxt"] = f32c(np.asarray(inp["L1_ffn_up_aux"]).T)
        w["bl_cm"] = _cm(inp["L1_ffn_b"], 8)
    return w


def run_phases(phases, xs, weights):
    nc = build_program(phases)
    consts = _consts(phases)
    in_maps = []
    for b in range(8):
        im = {"xin": xs[b]}
        im.update(consts)
        im.update(weights)
        in_maps.append(im)
    res = run_bass_kernel_spmd(nc, in_maps, core_ids=list(range(8)))
    return [np.asarray(res.results[b]["xout"]) for b in range(8)]


def kernel(**inputs):
    x = np.asarray(inputs["L0_attn_x"])
    xs = [np.ascontiguousarray(np.asarray(x[b], dtype=np.float32).T) for b in range(8)]
    for phases in PHASE_GROUPS:
        weights = _layout_weights(inputs, phases)
        xs = run_phases(phases, xs, weights)
    out = np.stack([xb.T for xb in xs])
    return np.ascontiguousarray(out).astype(np.float32)


# version 05
# speedup vs baseline: 1.2452x; 1.1125x over previous
import numpy as np
import ml_dtypes
import concourse.bass as bass
import concourse.mybir as mybir
from concourse.bass_utils import run_bass_kernel_spmd

F32 = mybir.dt.float32
BF16 = mybir.dt.bfloat16
AF = mybir.ActivationFunctionType
ALU = mybir.AluOpType
AX = mybir.AxisListType
BF = ml_dtypes.bfloat16

T = 2048
DM = 1024
NCH = 8
NE = 160
INTER = 512

PHASE_GROUPS = [["moe", "mamba", "moe", "lff", "attn", "lff"]]


class Ctx:

    def __init__(self, nc):
        self.nc = nc
        self.ops = []
        self.cnt = {}
        self.handles = {}
        self.meta = {}
        self.bank_r = [{"vE": 0, "aE": 0} for _ in range(8)]
        self.bank_w = [0 for _ in range(8)]

    def sem(self, name):
        if name not in self.handles:
            self.handles[name] = self.nc.alloc_semaphore(name)
            self.cnt[name] = 0
        return self.handles[name]

    def emit(self, eng, build, waits=(), inc=None, inc_by=1):
        waits = [(n, v) for (n, v) in waits if v > 0]
        for n, _ in waits:
            self.sem(n)
        tgt = None
        if inc is not None:
            self.sem(inc)
            self.cnt[inc] += inc_by
            tgt = self.cnt[inc]

        def fn(e, build=build, waits=tuple(waits), inc=inc, inc_by=inc_by):
            for n, v in waits:
                e.wait_ge(self.handles[n], v)
            ins = build(e)
            if inc is not None:
                ins.then_inc(self.handles[inc], inc_by)
        self.ops.append((eng, fn))
        return tgt

    def bank_acquire(self, b, extra=()):
        r = self.bank_r[b]
        return [("vE", r["vE"]), ("aE", r["aE"])] + list(extra)

    def bank_filled(self, b):
        self.bank_w[b] = self.cnt["tE"]

    def bank_read_waits(self, b, extra=()):
        return [("tE", self.bank_w[b])] + list(extra)

    def bank_read_done(self, b, eng):
        self.bank_r[b][eng] = self.cnt[eng]

    def run(self):
        nc = self.nc
        with nc.Block() as block:
            for eng_name in ("sync", "scalar", "tensor", "vector", "gpsimd"):
                ops = [f for (e, f) in self.ops if e == eng_name]

                def mk(ops):
                    def runner(engine):
                        for f in ops:
                            f(engine)
                    return runner
                getattr(block, eng_name)(mk(ops))


class NS:
    pass



def declare_tensors(nc, phases):
    t = NS()
    d = lambda name, shape, dtype=F32, kind="ExternalInput": nc.dram_tensor(
        name, list(shape), dtype, kind=kind).ap()
    t.xin = d("xin", [DM, T])
    t.xout = d("xout", [DM, T], kind="ExternalOutput")
    t.c_eye_bf = d("c_eye_bf", [128, 128], BF16)
    t.c_eye_f32 = d("c_eye_f32", [128, 128])
    t.c_ones_bf = d("c_ones_bf", [128, 128], BF16)
    t.c_ones_f32 = d("c_ones_f32", [1, 128])
    t.c_onescol_bf = d("c_onescol_bf", [128, 1], BF16)
    if "moe" in phases:
        t.wgt = d("wgt", [NE, DM, INTER])
        t.wdt = d("wdt", [NE, INTER, DM])
        t.wgst = d("wgst", [DM, INTER])
        t.wdst = d("wdst", [INTER, DM])
        t.rwt = d("rwt", [DM, NE])
        t.eb_cm = d("eb_cm", [128, 2])
    if "mamba" in phases:
        t.w_in = d("w_in", [DM, 4256])
        t.w_out = d("w_out", [2 * DM, DM])
        t.dt_bias = d("dt_bias", [1, 32])
        t.a_log = d("a_log", [1, 32])
        t.dskip_rep = d("dskip_rep", [128, 32])
        t.nw_cm = d("nw_cm", [128, 16])
        t.c_tri = d("c_tri", [128, 128])
        t.c_minf = d("c_minf", [128, 128])
        t.c_invfreq = d("c_invfreq", [1, 512])
    if "attn" in phases:
        t.w_attn = d("w_attn", [DM, 3 * DM])
        t.w_proj = d("w_proj", [DM, DM])
        t.ba_cm = d("ba_cm", [128, 24])
        t.bp_cm = d("bp_cm", [128, 8])
        t.c_maskband = d("c_maskband", [128, 4224], BF16)
    if "lff" in phases:
        t.l1wt = d("l1wt", [DM, DM])
        t.lora_down = d("lora_down", [4, 100])
        t.lora_up = d("lora_up", [50, 4])
        t.down_aux = d("down_aux", [100, DM])
        t.upxt = d("upxt", [50, DM])
        t.bl_cm = d("bl_cm", [128, 8])
    return t


def alloc_sbuf(nc, phases):
    s = NS()
    a = lambda name, shape, dtype: nc.alloc_sbuf_tensor(name, list(shape), dtype)
    s.x_fm = a("x_fm", [128, NCH * T], F32)
    s.x_ln = a("x_ln", [128, NCH * T], BF16)
    s.eye_bf = a("eye_bf", [128, 128], BF16)
    s.eye_f32 = a("eye_f32", [128, 128], F32)
    s.ones_bf = a("ones_bf", [128, 128], BF16)
    s.ones_f32 = a("ones_f32", [1, 128], F32)
    s.onescol_bf = a("onescol_bf", [128, 1], BF16)
    s.ln_rows = a("ln_rows", [1, 3 * 512], F32)
    s.ln_sq = a("ln_sq", [128, 2 * 512], BF16)
    s.ln_t = a("ln_t", [128, 2 * 512], F32)
    s.AR = 48384
    s.arena = a("arena", [128, s.AR], BF16)
    s.ar_bf = s.arena.ap()
    s.ar_f32 = s.arena.ap().bitcast(F32)
    if "moe" in phases:
        s.wg_sb = [s.ar_bf[:, 0:8192], s.ar_bf[:, 8192:16384]]
        s.wd_sb = s.ar_bf[:, 16384:24576]
        s.h1w = s.ar_bf[:, 24576:40960]
        s.comb_em = s.ar_bf[:, 40960:45056]
        s.crow = [s.ar_bf[32:33, 43008:45056], s.ar_bf[64:65, 43008:45056]]
        s.silu_sc = s.ar_f32[:, 22528:23552]
        s.rw_sb = s.ar_bf[:, 47104:48384]
        s.eb_sb = a("eb_sb", [128, 2], F32)
    if "mamba" in phases:
        s.xs_fmh = s.ar_bf[:, 0:16384]
        s.y_q = s.ar_bf[:, 16384:24576]
        s.bcdt = s.ar_bf[:, 24576:26624]
        s.wslab = [s.ar_bf[:, 26624:27648], s.ar_bf[:, 27648:28672]]
        s.cs_c = s.ar_bf[:, 28672:29696]
        s.bhat = s.ar_bf[:, 29696:31744]
        s.ctil = s.ar_bf[:, 31744:33792]
        s.cbar = s.ar_bf[:, 33792:35840]
        s.g_bf = s.ar_bf[0:64, 35840:37888]
        MF = 18944
        s.dt_tm = s.ar_f32[:, MF:MF + 512]
        s.cumdt = s.ar_f32[:, MF + 512:MF + 1024]
        s.L_tm = s.ar_f32[:, MF + 1024:MF + 1536]
        s.L_emc = s.ar_f32[0:32, MF + 1536:MF + 1664]
        s.th_sc = s.ar_f32[:, MF + 1664:MF + 2176]
        s.mrows = s.ar_f32[0:1, MF + 2176:MF + 2688]
        s.fmscr = s.ar_bf[:, 2 * (MF + 2688):2 * (MF + 2688) + 512]
        s.attnT_m = s.ar_bf[:, 2 * (MF + 2944):2 * (MF + 2944) + 128]
        s.g_f32 = s.ar_f32[0:64, MF + 3008:MF + 5056]
        s.dtb_bc = a("dtb_bc", [128, 32], F32)
        s.nea_bc = a("nea_bc", [128, 32], F32)
        s.tri_sb = a("tri_sb", [128, 128], F32)
        s.minf_sb = a("minf_sb", [128, 128], F32)
        s.invf_row = a("invf_row", [1, 512], F32)
        s.dtb_row = a("dtb_row", [1, 32], F32)
        s.alog_row = a("alog_row", [1, 32], F32)
        s.dskip_sb = a("dskip_sb", [128, 32], F32)
        s.nw_sb = a("nw_sb", [128, 16], F32)
    if "lff" in phases:
        s.weff = s.ar_bf[:, 0:8192]
        s.l1w_bf = s.ar_bf[:, 8192:16384]
        s.upxt_sb = s.ar_f32[0:50, 8192:9216]
        s.p1_sb = s.ar_f32[0:4, 9216:10240]
        s.p2_sb = s.ar_f32[0:100, 10240:11264]
        s.dnaux_sb = s.ar_f32[0:100, 11264:12288]
        s.ld_sb = s.ar_f32[0:4, 12288:12388]
        s.lu_sb = s.ar_f32[0:50, 12416:12420]
        s.bl_sb = a("bl_sb", [128, 8], F32)
        s.hbl_sb = a("hbl_sb", [128, 8], F32)
    if "attn" in phases:
        s.qkv_g = s.ar_bf[:, 0:12288]
        s.attnout = s.ar_bf[0:64, 12288:20480]
        s.maskband = s.ar_bf[:, 20480:24704]
        s.v_tm = s.ar_bf[:, 24704:25728]
        s.attn_sb = [s.ar_bf[:, 25728:27776], s.ar_bf[:, 27776:29824]]
        s.attnT_sb = [s.ar_bf[:, 29824:31872], s.ar_bf[:, 31872:33920]]
        s.wproj_sb = s.ar_bf[0:64, 33920:38016]
        s.wattn_sb = [s.ar_bf[:, 38016:39040], s.ar_bf[:, 39040:40064]]
        s.ba_sb = s.ar_f32[:, 24000:24024]
        s.bp_sb = s.ar_f32[:, 24024:24032]
        s.qba_sb = s.ar_f32[:, 24032:24040]
        s.atc = s.ar_f32[:, 24040:24064]
    return s


def alloc_psum(nc):
    p = NS()
    p.b = [nc.alloc_psum_tensor("pb%d" % i, [128, 512], F32) for i in range(8)]
    return p



def emit_load_x(ctx, t, s):
    src = t.xin.rearrange("(c p) t -> p c t", p=128)
    dst = s.x_fm.ap().rearrange("p (c t) -> p c t", c=NCH)
    ctx.emit("sync", lambda e: e.dma_start(out=dst, in_=src), inc="dS", inc_by=16)
    ctx.meta["x_dS"] = ctx.cnt["dS"]


def emit_load_consts(ctx, t, s, phases):
    pairs = [(s.eye_bf, t.c_eye_bf), (s.eye_f32, t.c_eye_f32),
             (s.ones_bf, t.c_ones_bf), (s.ones_f32, t.c_ones_f32),
             (s.onescol_bf, t.c_onescol_bf)]
    if "moe" in phases:
        pairs += [(s.eb_sb, t.eb_cm)]
    for dst, src in pairs:
        ctx.emit("sync", lambda e, d=dst, sr=src: e.dma_start(out=d[:], in_=sr),
                 inc="dS", inc_by=16)
    ctx.meta["consts_dS"] = ctx.cnt["dS"]
    ctx.meta["consts_dW"] = ctx.cnt.get("dW", 0)


def emit_ln(ctx, t, s, p):
    dep_v = ctx.meta.get("xfm_vE", 0)
    first_dma = [("dS", max(ctx.meta["x_dS"], ctx.meta["consts_dS"]))]
    R = s.ln_rows
    for nt in range(4):
        t0 = nt * 512
        for c in range(NCH):
            sq = s.ln_sq[:, (c % 2) * 512:(c % 2) * 512 + 512]
            w = [("tE", ctx.meta.get("ln_sq_tE_%d" % (c % 2), 0))]
            if c == 0 and nt == 0:
                w += first_dma + [("vE", dep_v)]
            ctx.emit("scalar", lambda e, c=c, sq=sq, t0=t0: e.activation(
                out=sq, in_=s.x_fm[:, c * T + t0:c * T + t0 + 512], func=AF.Square),
                waits=w, inc="aE")
            aq = ctx.cnt["aE"]
            wv = first_dma if (c == 0 and nt == 0) else ()
            ctx.emit("vector", lambda e, c=c, t0=t0: e.tensor_copy(
                out=s.x_ln[:, c * T + t0:c * T + t0 + 512],
                in_=s.x_fm[:, c * T + t0:c * T + t0 + 512]), waits=wv, inc="vE")
            vq = ctx.cnt["vE"]
            wsum = ctx.bank_acquire(4, [("vE", vq)]) if c == 0 else [("vE", vq)]
            ctx.emit("tensor", lambda e, c=c, t0=t0: e.matmul(
                p.b[4][0:1, :], s.onescol_bf[:],
                s.x_ln[:, c * T + t0:c * T + t0 + 512],
                start=(c == 0), stop=(c == NCH - 1)), waits=wsum,
                inc="tE" if c == NCH - 1 else None)
            wsq = ctx.bank_acquire(5, [("aE", aq)]) if c == 0 else [("aE", aq)]
            tq = ctx.emit("tensor", lambda e, c=c, sq=sq: e.matmul(
                p.b[5][0:1, :], s.onescol_bf[:], sq,
                start=(c == 0), stop=(c == NCH - 1)), waits=wsq, inc="tE")
            if tq is None:
                tq = ctx.cnt["tE"]
            ctx.meta["ln_sq_tE_%d" % (c % 2)] = tq
        ctx.bank_filled(4)
        ctx.bank_filled(5)
        ctx.emit("vector", lambda e: e.tensor_scalar(
            out=R[:, 0:512], in0=p.b[4][0:1, :], scalar1=1.0 / DM, scalar2=None,
            op0=ALU.mult), waits=ctx.bank_read_waits(4), inc="vE")
        ctx.emit("vector", lambda e: e.tensor_copy(out=R[:, 512:1024], in_=p.b[5][0:1, :]),
                 waits=ctx.bank_read_waits(5), inc="vE")
        ctx.bank_read_done(4, "vE")
        ctx.bank_read_done(5, "vE")
        ctx.emit("vector", lambda e: e.tensor_tensor(
            out=R[:, 1024:1536], in0=R[:, 0:512], in1=R[:, 0:512], op=ALU.mult),
            inc="vE")
        ctx.emit("vector", lambda e: e.scalar_tensor_tensor(
            out=R[:, 512:1024], in0=R[:, 512:1024], scalar=1.0 / DM,
            in1=R[:, 1024:1536], op0=ALU.mult, op1=ALU.subtract), inc="vE")
        vq = ctx.emit("vector", lambda e: e.tensor_scalar(
            out=R[:, 512:1024], in0=R[:, 512:1024], scalar1=1e-5, scalar2=None,
            op0=ALU.add), inc="vE")
        aq1 = ctx.emit("scalar", lambda e: e.activation(
            out=R[:, 512:1024], in_=R[:, 512:1024], func=AF.Ln),
            waits=[("vE", vq)], inc="aE")
        vq = ctx.emit("vector", lambda e: e.tensor_scalar(
            out=R[:, 512:1024], in0=R[:, 512:1024], scalar1=-0.5, scalar2=None,
            op0=ALU.mult), waits=[("aE", aq1)], inc="vE")
        aq = ctx.emit("scalar", lambda e: e.activation(
            out=R[:, 512:1024], in_=R[:, 512:1024], func=AF.Exp),
            waits=[("vE", vq)], inc="aE")
        ctx.emit("tensor", lambda e: e.matmul(
            p.b[6][:], s.ones_f32[:], R[:, 0:512], start=True, stop=True),
            waits=ctx.bank_acquire(6, [("aE", aq)]), inc="tE")
        ctx.bank_filled(6)
        ctx.emit("tensor", lambda e: e.matmul(
            p.b[7][:], s.ones_f32[:], R[:, 512:1024], start=True, stop=True),
            waits=ctx.bank_acquire(7), inc="tE")
        ctx.bank_filled(7)
        for c in range(NCH):
            sc = s.ln_t[:, (c % 2) * 512:(c % 2) * 512 + 512]
            w = ctx.bank_read_waits(6) + ctx.bank_read_waits(7) if c == 0 else ()
            ctx.emit("vector", lambda e, c=c, sc=sc, t0=t0: e.tensor_sub(
                out=sc, in0=s.x_fm[:, c * T + t0:c * T + t0 + 512], in1=p.b[6][:]),
                waits=w, inc="vE")
            ctx.emit("vector", lambda e, c=c, sc=sc, t0=t0: e.tensor_mul(
                out=s.x_ln[:, c * T + t0:c * T + t0 + 512], in0=sc, in1=p.b[7][:]),
                inc="vE")
        ctx.bank_read_done(6, "vE")
        ctx.bank_read_done(7, "vE")
    ctx.meta["ln_vE"] = ctx.cnt["vE"]



def emit_router(ctx, t, s, p):
    scr = s.ar_f32[:, 12288:20480]
    probs_em = lambda ch: scr[:, ch * T:(ch + 1) * T]
    ptm = lambda tc: scr[:, 2 * T + tc * 160: 2 * T + tc * 160 + 160]
    work = scr[:, 2 * T + 16 * 160: 2 * T + 16 * 160 + 160]
    msk = scr[:, 2 * T + 17 * 160: 2 * T + 18 * 160]
    cols = scr[:, 2 * T + 18 * 160: 2 * T + 18 * 160 + 8]
    ctm = lambda tc: s.wg_sb[0][:, tc * 160: tc * 160 + 160]
    ln_v = ctx.meta["ln_vE"]
    ctx.emit("gpsimd", lambda e: e.dma_start(
        out=s.rw_sb.rearrange("p (kc n) -> p kc n", kc=8),
        in_=t.rwt.rearrange("(kc p) n -> p kc n", p=128)),
        waits=[("tE", ctx.cnt["tE"]), ("vE", ctx.cnt["vE"]),
               ("aE", ctx.cnt.get("aE", 0))], inc="dW", inc_by=16)
    ctx.meta["rw_dW"] = ctx.cnt["dW"]
    for nt in range(4):
        t0 = nt * 512
        for part in range(2):
            M = 128 if part == 0 else 32
            b = 4 + part
            for kc in range(8):
                w = ctx.bank_acquire(b, [("vE", ln_v), ("dW", ctx.meta["rw_dW"])]
                                     ) if kc == 0 else ()
                ctx.emit("tensor", lambda e, kc=kc, part=part, M=M, t0=t0, b=b:
                         e.matmul(p.b[b][0:M, :],
                                  s.rw_sb[:, kc * NE + part * 128:
                                          kc * NE + part * 128 + M],
                                  s.x_ln[:, kc * T + t0: kc * T + t0 + 512],
                                  start=(kc == 0), stop=(kc == 7)),
                         waits=w, inc="tE" if kc == 7 else None)
            ctx.bank_filled(b)
            ctx.emit("scalar", lambda e, part=part, M=M, t0=t0, b=b:
                     e.activation(out=probs_em(part)[0:M, t0:t0 + 512],
                                  in_=p.b[b][0:M, :], func=AF.Sigmoid,
                                  bias=s.eb_sb[0:M, part:part + 1]),
                     waits=ctx.bank_read_waits(b), inc="aE")
            ctx.bank_read_done(b, "aE")
    aq_sig = ctx.cnt["aE"]
    for tc in range(16):
        for part in range(2):
            M = 128 if part == 0 else 32
            b = 6 + (tc * 2 + part) % 2
            ctx.emit("tensor", lambda e, tc=tc, part=part, M=M, b=b:
                     e.transpose(p.b[b][0:128, 0:M],
                                 probs_em(part)[0:M, tc * 128:(tc + 1) * 128],
                                 s.eye_f32[0:M, 0:M]),
                     waits=ctx.bank_acquire(b, [("aE", aq_sig)]), inc="tE")
            ctx.bank_filled(b)
            ctx.emit("vector", lambda e, tc=tc, part=part, M=M, b=b:
                     e.tensor_copy(out=ptm(tc)[:, part * 128:part * 128 + M],
                                   in_=p.b[b][0:128, 0:M]),
                     waits=ctx.bank_read_waits(b), inc="vE")
            ctx.bank_read_done(b, "vE")
    scol = scr[:, 2 * T + 18 * 160: 2 * T + 18 * 160 + 16]
    mxcol = scr[:, 2 * T + 18 * 160 + 16: 2 * T + 18 * 160 + 32]
    sccol = scr[:, 2 * T + 18 * 160 + 32: 2 * T + 18 * 160 + 48]
    def vemit(build):
        vq = ctx.cnt["vE"]
        return ctx.emit("vector", build, waits=[("vE", vq)], inc="vE")
    for tc in range(16):
        vemit(lambda e, tc=tc: e.tensor_copy(out=work, in_=ptm(tc)))
        for it in range(8):
            vemit(lambda e, tc=tc: e.reduce_max(
                out=mxcol[:, tc:tc + 1], in_=work, axis=AX.X))
            if it < 7:
                vemit(lambda e, tc=tc: e.tensor_scalar(
                    out=msk, in0=work, scalar1=mxcol[:, tc:tc + 1], scalar2=None,
                    op0=ALU.is_ge))
                vemit(lambda e: e.scalar_tensor_tensor(
                    out=work, in0=msk, scalar=-1e30, in1=work,
                    op0=ALU.mult, op1=ALU.add))
        vemit(lambda e, tc=tc: e.tensor_scalar(
            out=msk, in0=ptm(tc), scalar1=mxcol[:, tc:tc + 1], scalar2=None,
            op0=ALU.is_ge))
        vemit(lambda e, tc=tc: e.tensor_mul(out=msk, in0=msk, in1=ptm(tc)))
        vemit(lambda e, tc=tc: e.tensor_copy(out=ctm(tc), in_=msk))
        vemit(lambda e, tc=tc: e.reduce_sum(
            out=scol[:, tc:tc + 1], in_=msk, axis=AX.X))
    vemit(lambda e: e.tensor_scalar(
        out=scol, in0=scol, scalar1=1e-9, scalar2=None, op0=ALU.add))
    vemit(lambda e: e.reciprocal(out=sccol, in_=scol))
    vemit(lambda e: e.tensor_scalar(
        out=sccol, in0=sccol, scalar1=2.5, scalar2=None, op0=ALU.mult))
    for tc in range(16):
        vemit(lambda e, tc=tc: e.tensor_scalar(
            out=ctm(tc), in0=ctm(tc), scalar1=sccol[:, tc:tc + 1], scalar2=None,
            op0=ALU.mult))
    vq_c = ctx.cnt["vE"]
    for tc in range(16):
        for part in range(2):
            M = 128 if part == 0 else 32
            b = 6 + (tc * 2 + part) % 2
            pbf = p.b[b].ap().bitcast(BF16)
            ctx.emit("tensor", lambda e, tc=tc, part=part, M=M, pbf=pbf:
                     e.transpose(pbf[0:M, 0:128],
                                 ctm(tc)[:, part * 128:part * 128 + M],
                                 s.eye_bf[:]),
                     waits=ctx.bank_acquire(b, [("vE", vq_c)]), inc="tE")
            ctx.bank_filled(b)
            ctx.emit("vector", lambda e, tc=tc, part=part, M=M, pbf=pbf:
                     e.tensor_copy(
                         out=s.comb_em[0:M, part * T + tc * 128:
                                       part * T + (tc + 1) * 128],
                         in_=pbf[0:M, 0:128]),
                     waits=ctx.bank_read_waits(b), inc="vE")
            ctx.bank_read_done(b, "vE")
    ctx.meta["comb_vE"] = ctx.cnt["vE"]
    ctx.meta["router_tE"] = ctx.cnt["tE"]


def emit_moe(ctx, t, s, p):
    experts = [-1] + list(range(NE))
    groups = [experts[i:i + 2] for i in range(0, len(experts), 2)]
    comb_v = ctx.meta["comb_vE"]
    rt_t = ctx.meta["router_tE"]
    ln_v = ctx.meta["ln_vE"]
    wg_free = [rt_t, 0]
    crow_free = 0
    down_done = 0
    ntile = 0

    for g, grp in enumerate(groups):
        sl = g % 2
        dma_w0 = [("tE", wg_free[sl])]
        for j, e_id in enumerate(grp):
            wg_src = (t.wgst if e_id < 0 else t.wgt[e_id]).rearrange(
                "(kc p) i -> p kc i", p=128)
            wg_dst = s.wg_sb[sl][:, j * 4096:(j + 1) * 4096].rearrange(
                "p (kc i) -> p kc i", kc=8)
            ctx.emit("gpsimd", lambda e, a=wg_dst, b=wg_src: e.dma_start(out=a, in_=b),
                     waits=dma_w0 if j == 0 else (), inc="dW", inc_by=16)
        dW_wg = ctx.cnt["dW"]
        for j, e_id in enumerate(grp):
            wd_src = (t.wdst if e_id < 0 else t.wdt[e_id]).rearrange(
                "(kc p) o -> p kc o", p=128)
            wd_dst = s.wd_sb[:, j * 4096:(j + 1) * 4096].rearrange(
                "p (kc o) -> p kc o", kc=4)
            ctx.emit("gpsimd", lambda e, a=wd_dst, b=wd_src: e.dma_start(out=a, in_=b),
                     waits=[("tE", down_done)] if j == 0 else (), inc="dW", inc_by=16)
            crow_dst = s.crow[j]
            w_cr = [("vE", comb_v)] + ([("tE", crow_free)] if j == 0 else [])
            if e_id < 0:
                ctx.emit("vector", lambda e, a=crow_dst: e.memset(a, 1.0),
                         waits=w_cr, inc="vE")
                ctx.meta["crow_memset_vE"] = ctx.cnt["vE"]
            else:
                ch, r = divmod(e_id, 128)
                csrc = s.comb_em[r:r + 1, ch * T:(ch + 1) * T]
                ctx.emit("sync", lambda e, a=crow_dst, b=csrc: e.dma_start(out=a, in_=b),
                         waits=w_cr, inc="dS", inc_by=16)
        dW_at = ctx.cnt["dW"]
        dS_at = ctx.cnt["dS"]
        for j, e_id in enumerate(grp):
            for nt in range(4):
                t0 = nt * 512
                cb = 2 + nt % 2
                ctx.emit("tensor", lambda e, j=j, t0=t0, cb=cb: e.matmul(
                    p.b[cb][:], s.ones_bf[32 * (j + 1):32 * (j + 1) + 1, :],
                    s.crow[j][:, t0:t0 + 512],
                    start=True, stop=True),
                    waits=ctx.bank_acquire(cb, [("dS", dS_at),
                          ("vE", ctx.meta.get("crow_memset_vE", 0))]), inc="tE")
                ctx.bank_filled(cb)
                for mi in range(4):
                    gb = (nt * 4 + mi) % 2
                    w = ctx.bank_acquire(gb)
                    if nt == 0 and mi == 0 and j == 0:
                        w += [("dW", dW_wg), ("vE", ln_v)]
                    for kc in range(8):
                        ctx.emit("tensor", lambda e, sl=sl, j=j, mi=mi, kc=kc, t0=t0,
                                 gb=gb: e.matmul(
                                     p.b[gb][:],
                                     s.wg_sb[sl][:, j * 4096 + kc * 512 + mi * 128:
                                                 j * 4096 + kc * 512 + mi * 128 + 128],
                                     s.x_ln[:, kc * T + t0:kc * T + t0 + 512],
                                     start=(kc == 0), stop=(kc == 7)),
                                 waits=w if kc == 0 else (),
                                 inc="tE" if kc == 7 else None)
                    ctx.bank_filled(gb)
                    sj = (nt * 4 + mi) % 2
                    sc = s.silu_sc[:, sj * 512:(sj + 1) * 512]
                    ctx.emit("scalar", lambda e, gb=gb, sc=sc: e.activation(
                        out=sc, in_=p.b[gb][:], func=AF.Silu),
                        waits=ctx.bank_read_waits(
                            gb, [("vE", ctx.meta.get("silu_vE_%d" % sj, 0))]),
                        inc="aE")
                    ctx.bank_read_done(gb, "aE")
                    aq = ctx.cnt["aE"]
                    dst = s.h1w[:, (j * 4 + mi) * T + t0:(j * 4 + mi) * T + t0 + 512]
                    w2 = ctx.bank_read_waits(cb, [("aE", aq)])
                    if nt == 0 and mi == 0 and j == 0:
                        w2 += [("tE", down_done)]
                    vq = ctx.emit("vector", lambda e, sc=sc, dst=dst, cb=cb:
                                  e.tensor_tensor(out=dst, in0=sc, in1=p.b[cb][:],
                                                  op=ALU.mult),
                                  waits=w2, inc="vE")
                    ctx.meta["silu_vE_%d" % sj] = vq
                    ctx.bank_read_done(cb, "vE")
        h1w_v = ctx.cnt["vE"]
        wg_free[sl] = ctx.cnt["tE"]
        crow_free = ctx.cnt["tE"]
        for mo in range(8):
            for nt in range(4):
                t0 = nt * 512
                db = 4 + ntile % 2
                nmm = len(grp) * 4
                i = 0
                for j in range(len(grp)):
                    for kc in range(4):
                        first = (i == 0)
                        i += 1
                        w = ()
                        if first:
                            w = ctx.bank_acquire(db)
                            if mo == 0 and nt == 0:
                                w += [("vE", h1w_v), ("dW", dW_at)]
                        ctx.emit("tensor", lambda e, j=j, kc=kc, mo=mo, t0=t0,
                                 db=db, st=first, sp=(i == nmm): e.matmul(
                                     p.b[db][:],
                                     s.wd_sb[:, j * 4096 + kc * 1024 + mo * 128:
                                             j * 4096 + kc * 1024 + mo * 128 + 128],
                                     s.h1w[:, (j * 4 + kc) * T + t0:
                                           (j * 4 + kc) * T + t0 + 512],
                                     start=st, stop=sp),
                                 waits=w, inc="tE" if i == nmm else None)
                ctx.bank_filled(db)
                dst = s.x_fm[:, mo * T + t0:mo * T + t0 + 512]
                ctx.emit("vector", lambda e, db=db, dst=dst:
                         e.scalar_tensor_tensor(out=dst, in0=p.b[db][:], scalar=0.5,
                                                in1=dst, op0=ALU.mult, op1=ALU.add),
                         waits=ctx.bank_read_waits(db), inc="vE")
                ctx.bank_read_done(db, "vE")
                ntile += 1
        down_done = ctx.cnt["tE"]
    ctx.meta["xfm_vE"] = ctx.cnt["vE"]


def emit_dump(ctx, nc, name, ap, cols, dtype, rows=128):
    out = nc.dram_tensor(name, [rows, cols], dtype, kind="ExternalOutput").ap()
    ctx.emit("sync", lambda e: e.dma_start(out=out, in_=ap),
             waits=[("vE", ctx.cnt.get("vE", 0)), ("aE", ctx.cnt.get("aE", 0)),
                    ("tE", ctx.cnt.get("tE", 0)), ("dW", ctx.cnt.get("dW", 0))],
             inc="dS", inc_by=16)


def emit_store_x(ctx, t, s):
    src = s.x_fm.ap().rearrange("p (c t) -> p c t", c=NCH)
    dst = t.xout.rearrange("(c p) t -> p c t", p=128)
    ctx.emit("sync", lambda e: e.dma_start(out=dst, in_=src),
             waits=[("vE", ctx.cnt.get("vE", 0)), ("aE", ctx.cnt.get("aE", 0)),
                    ("tE", ctx.cnt.get("tE", 0))], inc="dS", inc_by=16)
    ctx.emit("sync", lambda e: e.nop(), waits=[("dS", ctx.cnt["dS"])])


def emit_mamba(ctx, t, s, p):
    ln_v = ctx.meta["ln_vE"]
    ent_t, ent_v, ent_a = ctx.cnt["tE"], ctx.cnt["vE"], ctx.cnt.get("aE", 0)
    bar = [("tE", ent_t), ("vE", ent_v), ("aE", ent_a)]
    PI = 3.14159265358979
    TWO_PI = 2 * PI
    INV_2PI = 1.0 / TWO_PI
    MAGIC = 1.5 * 2.0 ** 23

    def ve(build, extra=()):
        vq = ctx.cnt["vE"]
        return ctx.emit("vector", build, waits=[("vE", vq)] + list(extra), inc="vE")

    for dst, srcap in ((s.tri_sb, t.c_tri), (s.minf_sb, t.c_minf),
                       (s.invf_row, t.c_invfreq), (s.dtb_row, t.dt_bias),
                       (s.alog_row, t.a_log), (s.dskip_sb, t.dskip_rep),
                       (s.nw_sb, t.nw_cm)):
        ctx.emit("sync", lambda e, a=dst, b=srcap: e.dma_start(out=a[:], in_=b),
                 waits=bar, inc="dS", inc_by=16)
    dS0 = ctx.cnt["dS"]
    ctx.emit("tensor", lambda e: e.matmul(
        p.b[6][:, 0:32], s.ones_f32[:], s.dtb_row[:], start=True, stop=True),
        waits=ctx.bank_acquire(6, [("dS", dS0)] + bar), inc="tE")
    ctx.bank_filled(6)
    ctx.emit("vector", lambda e: e.tensor_copy(out=s.dtb_bc[:], in_=p.b[6][:, 0:32]),
             waits=ctx.bank_read_waits(6), inc="vE")
    ctx.bank_read_done(6, "vE")
    aq = ctx.emit("scalar", lambda e: e.activation(
        out=s.mrows[:, 0:32], in_=s.alog_row[:], func=AF.Exp),
        waits=[("dS", dS0)] + bar, inc="aE")
    ve(lambda e: e.tensor_scalar(out=s.mrows[:, 0:32], in0=s.mrows[:, 0:32],
                                 scalar1=-1.0, scalar2=None, op0=ALU.mult),
       extra=[("aE", aq)])
    vq = ctx.cnt["vE"]
    ctx.emit("tensor", lambda e: e.matmul(
        p.b[6][:, 0:32], s.ones_f32[:], s.mrows[:, 0:32], start=True, stop=True),
        waits=ctx.bank_acquire(6, [("vE", vq)]), inc="tE")
    ctx.bank_filled(6)
    ctx.emit("vector", lambda e: e.tensor_copy(out=s.nea_bc[:], in_=p.b[6][:, 0:32]),
             waits=ctx.bank_read_waits(6), inc="vE")
    ctx.bank_read_done(6, "vE")
    ve(lambda e: e.memset(s.g_f32, 0.0))
    ve(lambda e: e.memset(s.g_bf, 0.0))

    scan_done_tE = 0
    for h2 in range(2):
        th0 = h2 * 1024
        for mi, m in enumerate(range(16, 34)):
            sl = mi % 2
            mw = 128 if m < 33 else 32
            ctx.emit("gpsimd", lambda e, m=m, sl=sl, mw=mw: e.dma_start(
                out=s.wslab[sl][:, 0:8 * mw].rearrange("p (kc m) -> p kc m", kc=8),
                in_=t.w_in[:, m * 128: m * 128 + mw].rearrange(
                    "(kc p) m -> p kc m", p=128)),
                waits=[("tE", ctx.meta.get("mm_wslab_free_%d" % sl, 0))] +
                (bar if h2 == 0 and mi < 2 else []), inc="dW", inc_by=16)
            dWs = ctx.cnt["dW"]
            for nth in range(2):
                t0 = th0 + nth * 512
                b = (mi * 2 + nth) % 2
                for kc in range(8):
                    w = ()
                    if kc == 0:
                        w = ctx.bank_acquire(b, [("dW", dWs), ("vE", ln_v),
                                                 ("tE", scan_done_tE)])
                    ctx.emit("tensor", lambda e, sl=sl, kc=kc, t0=t0, b=b, mw=mw:
                             e.matmul(p.b[b][0:mw, 0:512],
                                      s.wslab[sl][:, kc * mw:(kc + 1) * mw],
                                      s.x_ln[:, kc * T + t0: kc * T + t0 + 512],
                                      start=(kc == 0), stop=(kc == 7)),
                             waits=w, inc="tE" if kc == 7 else None)
                tq = ctx.cnt["tE"]
                if nth == 1:
                    ctx.meta["mm_wslab_free_%d" % sl] = tq
                ctx.bank_filled(b)
                lt0 = nth * 512
                if m < 32:
                    dst = s.xs_fmh[:, (m - 16) * 1024 + lt0:(m - 16) * 1024 + lt0 + 512]
                    ctx.emit("scalar", lambda e, b=b, dst=dst: e.activation(
                        out=dst, in_=p.b[b][:], func=AF.Identity),
                        waits=ctx.bank_read_waits(b), inc="aE")
                    ctx.bank_read_done(b, "aE")
                elif m == 32:
                    dst = s.bcdt[:, lt0:lt0 + 512]
                    ctx.emit("scalar", lambda e, b=b, dst=dst: e.activation(
                        out=dst, in_=p.b[b][:], func=AF.Identity),
                        waits=ctx.bank_read_waits(b), inc="aE")
                    ctx.bank_read_done(b, "aE")
                else:
                    dst = s.bcdt[0:32, 1024 + lt0:1024 + lt0 + 512]
                    ctx.emit("scalar", lambda e, b=b, dst=dst: e.activation(
                        out=dst, in_=p.b[b][0:32, :], func=AF.Identity),
                        waits=ctx.bank_read_waits(b), inc="aE")
                    ctx.bank_read_done(b, "aE")
        inproj_a = ctx.cnt["aE"]
        for c in range(8):
            gc = h2 * 8 + c
            co = gc * 32
            pbf = p.b[7].ap().bitcast(BF16)
            ctx.emit("tensor", lambda e, c=c, pbf=pbf: e.transpose(
                pbf[0:128, 0:32],
                s.bcdt[0:32, 1024 + c * 128:1024 + (c + 1) * 128],
                s.eye_bf[0:32, 0:32]),
                waits=ctx.bank_acquire(7, [("aE", inproj_a)]), inc="tE")
            ctx.bank_filled(7)
            DT = s.dt_tm[:, co:co + 32]
            SC = s.th_sc[:, 0:32]
            ve(lambda e, DT=DT, pbf=pbf: e.tensor_copy(out=DT, in_=pbf[0:128, 0:32]),
               extra=ctx.bank_read_waits(7))
            ctx.bank_read_done(7, "vE")
            ve(lambda e, DT=DT: e.tensor_add(out=DT, in0=DT, in1=s.dtb_bc[:]))
            vq = ctx.cnt["vE"]
            aq = ctx.emit("scalar", lambda e, DT=DT, SC=SC: e.activation(
                out=SC, in_=DT, func=AF.Abs), waits=[("vE", vq)], inc="aE")
            ve(lambda e, SC=SC: e.tensor_scalar(
                out=SC, in0=SC, scalar1=-1.0, scalar2=None, op0=ALU.mult),
               extra=[("aE", aq)])
            aq = ctx.emit("scalar", lambda e, SC=SC: e.activation(
                out=SC, in_=SC, func=AF.Exp), waits=[("vE", ctx.cnt["vE"])], inc="aE")
            ve(lambda e, DT=DT: e.tensor_relu(out=s.th_sc[:, 32:64], in_=DT),
               extra=[("aE", aq)])
            ve(lambda e, SC=SC: e.tensor_scalar(
                out=SC, in0=SC, scalar1=1.0, scalar2=None, op0=ALU.add))
            aq = ctx.emit("scalar", lambda e, SC=SC: e.activation(
                out=SC, in_=SC, func=AF.Ln), waits=[("vE", ctx.cnt["vE"])], inc="aE")
            ve(lambda e, DT=DT, SC=SC: e.tensor_add(
                out=DT, in0=s.th_sc[:, 32:64], in1=SC), extra=[("aE", aq)])
            CU = s.cumdt[:, co:co + 32]
            vq = ctx.cnt["vE"]
            ctx.emit("tensor", lambda e, DT=DT: e.matmul(
                p.b[7][:, 0:32], s.tri_sb[:], DT, start=True, stop=(gc == 0)),
                waits=ctx.bank_acquire(7, [("vE", vq), ("dS", dS0)]), inc="tE")
            if gc > 0:
                ctx.emit("tensor", lambda e: e.matmul(
                    p.b[7][:, 0:32], s.ones_f32[:], s.mrows[:, 64:96],
                    start=False, stop=True),
                    waits=[("dS", ctx.meta.get("mm_carry_dS", 0))], inc="tE")
                ctx.meta["mm_carry_used"] = ctx.cnt["tE"]
            ctx.bank_filled(7)
            ve(lambda e, CU=CU: e.tensor_copy(out=CU, in_=p.b[7][:, 0:32]),
               extra=ctx.bank_read_waits(7))
            ctx.bank_read_done(7, "vE")
            LL = s.L_tm[:, co:co + 32]
            ve(lambda e, LL=LL, CU=CU: e.tensor_mul(out=LL, in0=CU, in1=s.nea_bc[:]))
            vq = ctx.cnt["vE"]
            ctx.emit("sync", lambda e, CU=CU: e.dma_start(
                out=s.mrows[:, 64:96], in_=CU[127:128, :]),
                waits=[("vE", vq), ("tE", ctx.meta.get("mm_carry_used", 0))],
                inc="dS", inc_by=16)
            ctx.meta["mm_carry_dS"] = ctx.cnt["dS"]
        prep_v = ctx.cnt["vE"]
        for c in range(8):
            gc = h2 * 8 + c
            co = gc * 32
            qlocal = c % 4
            ctx.emit("tensor", lambda e: e.matmul(
                p.b[6][:], s.ones_f32[:], s.invf_row[:], start=True, stop=True),
                waits=ctx.bank_acquire(6, [("vE", prep_v)]), inc="tE")
            ctx.bank_filled(6)
            cum_exp = s.cumdt[:, co:co + 32].rearrange("p (h one) -> p h one", one=1).broadcast_to(
                [128, 32, 16])
            TH = s.th_sc[:, 0:512]
            TH3 = TH.rearrange("p (h j) -> p h j", h=32)
            P63 = p.b[6].ap().rearrange("p (h j) -> p h j", h=32)
            ve(lambda e, cum_exp=cum_exp, TH3=TH3, P63=P63: e.tensor_mul(
                out=TH3, in0=cum_exp, in1=P63),
               extra=ctx.bank_read_waits(6))
            ctx.bank_read_done(6, "vE")
            CSF = s.cs_c.bitcast(F32)
            ve(lambda e, TH=TH, CSF=CSF: e.tensor_scalar(
                out=CSF, in0=TH, scalar1=INV_2PI, scalar2=MAGIC,
                op0=ALU.mult, op1=ALU.add),
               extra=[("aE", ctx.meta.get("mm_cs_aE", 0)),
                      ("tE", ctx.meta.get("mm_fm_used_1", 0))])
            ve(lambda e, CSF=CSF: e.tensor_scalar(
                out=CSF, in0=CSF, scalar1=MAGIC, scalar2=None,
                op0=ALU.subtract))
            ve(lambda e, TH=TH, CSF=CSF: e.scalar_tensor_tensor(
                out=TH, in0=CSF, scalar=-TWO_PI, in1=TH,
                op0=ALU.mult, op1=ALU.add))
            vq = ctx.cnt["vE"]
            aq = ctx.emit("scalar", lambda e, TH=TH: e.activation(
                out=s.cs_c[:, 512:1024], in_=TH, func=AF.Sin),
                waits=[("vE", vq)], inc="aE")
            aq2 = ctx.emit("scalar", lambda e, TH=TH: e.activation(
                out=TH, in_=TH, func=AF.Abs), waits=[("aE", aq),
                ("vE", ctx.cnt["vE"])], inc="aE")
            ve(lambda e, TH=TH: e.tensor_scalar(
                out=TH, in0=TH, scalar1=-1.0, scalar2=PI / 2,
                op0=ALU.mult, op1=ALU.add), extra=[("aE", aq2)])
            vq = ctx.cnt["vE"]
            aq_cs = ctx.emit("scalar", lambda e, TH=TH: e.activation(
                out=s.cs_c[:, 0:512], in_=TH, func=AF.Sin),
                waits=[("vE", vq)], inc="aE")
            ctx.meta["mm_cs_aE"] = aq_cs
            pbf7 = p.b[7].ap().bitcast(BF16)
            ctx.emit("tensor", lambda e, c=c, pbf7=pbf7: e.transpose(
                pbf7[0:128, 0:128], s.bcdt[:, c * 128:(c + 1) * 128],
                s.eye_bf[:]),
                waits=ctx.bank_acquire(7, [("aE", inproj_a)]), inc="tE")
            ctx.bank_filled(7)
            cosap = s.cs_c[:, 0:512].rearrange("p (h j) -> p h j", h=32)
            sinap = s.cs_c[:, 512:1024].rearrange("p (h j) -> p h j", h=32)
            Bev = pbf7[0:128, 0:64:2].rearrange("p (one j) -> p one j", one=1).broadcast_to(
                [128, 32, 32])
            Bod = pbf7[0:128, 1:64:2].rearrange("p (one j) -> p one j", one=1).broadcast_to(
                [128, 32, 32])
            Bnr = pbf7[0:128, 32:64].rearrange("p (one n) -> p one n", one=1).broadcast_to(
                [128, 32, 32])
            Cev = pbf7[0:128, 64:128:2].rearrange("p (one j) -> p one j", one=1).broadcast_to(
                [128, 32, 32])
            Cod = pbf7[0:128, 65:128:2].rearrange("p (one j) -> p one j", one=1).broadcast_to(
                [128, 32, 32])
            Cnr = pbf7[0:128, 96:128].rearrange("p (one n) -> p one n", one=1).broadcast_to(
                [128, 32, 32])
            for dst_t, ev, od in ((s.bhat, Bev, Bod), (s.ctil, Cev, Cod)):
                de = dst_t.rearrange("p (h n) -> p h n", h=32)[:, :, 0:31:2]
                do = dst_t.rearrange("p (h n) -> p h n", h=32)[:, :, 1:32:2]
                dn = dst_t.rearrange("p (h n) -> p h n", h=32)[:, :, 32:64]
                evj = ev[:, :, 0:16]
                odj = od[:, :, 0:16]
                scr2 = s.fmscr.rearrange("p (h j) -> p h j", h=32)
                cos3 = cosap
                sin3 = sinap
                ve(lambda e, de=de, evj=evj, cos3=cos3: e.tensor_tensor(
                    out=de, in0=cos3, in1=evj, op=ALU.mult),
                   extra=[("aE", aq_cs)] + ctx.bank_read_waits(7) +
                   [("tE", ctx.meta.get("mm_fmscr_used", 0))])
                ve(lambda e, scr2=scr2, odj=odj, sin3=sin3: e.tensor_tensor(
                    out=scr2, in0=sin3, in1=odj, op=ALU.mult))
                ve(lambda e, de=de, scr2=scr2: e.tensor_tensor(
                    out=de, in0=de, in1=scr2, op=ALU.add))
                ve(lambda e, do=do, odj=odj, cos3=cos3: e.tensor_tensor(
                    out=do, in0=cos3, in1=odj, op=ALU.mult))
                ve(lambda e, scr2=scr2, evj=evj, sin3=sin3: e.tensor_tensor(
                    out=scr2, in0=sin3, in1=evj, op=ALU.mult))
                ve(lambda e, do=do, scr2=scr2: e.tensor_tensor(
                    out=do, in0=do, in1=scr2, op=ALU.subtract))
                ve(lambda e, dn=dn, nr=(Bnr if dst_t is s.bhat else Cnr):
                   e.tensor_copy(out=dn, in_=nr))
            ctx.bank_read_done(7, "vE")
            dt_exp = s.dt_tm[:, co:co + 32].rearrange("p (h one) -> p h one", one=1).broadcast_to(
                [128, 32, 64])
            bhat3 = s.bhat.rearrange("p (h n) -> p h n", h=32)
            ve(lambda e, dt_exp=dt_exp, bhat3=bhat3: e.tensor_tensor(
                out=bhat3, in0=bhat3, in1=dt_exp, op=ALU.mult))
            if gc == 0:
                vq_pe = ve(lambda e: e.memset(s.mrows[:, 96:128], 0.0))
                ctx.meta["mm_pe_v"] = vq_pe
            else:
                pco = (gc - 1) * 32
                ctx.emit("sync", lambda e, pco=pco: e.dma_start(
                    out=s.mrows[:, 96:128], in_=s.L_tm[127:128, pco:pco + 32]),
                    waits=[("vE", prep_v), ("tE", ctx.meta.get("mm_rowbc_tE", 0))],
                    inc="dS", inc_by=16)
            ctx.emit("sync", lambda e, co=co: e.dma_start(
                out=s.mrows[:, 128:160], in_=s.L_tm[127:128, co:co + 32]),
                waits=[("vE", prep_v), ("tE", ctx.meta.get("mm_rowbc_tE", 0))],
                inc="dS", inc_by=16)
            dSr = ctx.cnt["dS"]
            ctx.emit("tensor", lambda e: e.matmul(
                p.b[6][:, 0:64], s.ones_f32[:],
                s.mrows[:, 96:160], start=True, stop=True),
                waits=ctx.bank_acquire(6, [("dS", dSr),
                      ("vE", ctx.meta.get("mm_pe_v", 0))]), inc="tE")
            ctx.bank_filled(6)
            ctx.meta["mm_rowbc_tE"] = ctx.cnt["tE"]
            COLF = s.th_sc[:, 0:32]
            FALL = s.th_sc[:, 32:64]
            ve(lambda e, COLF=COLF, co=co: e.tensor_sub(
                out=COLF, in0=s.L_tm[:, co:co + 32], in1=p.b[6][:, 0:32]),
               extra=ctx.bank_read_waits(6) + [("aE", aq_cs)])
            ve(lambda e, FALL=FALL, co=co: e.tensor_sub(
                out=FALL, in0=p.b[6][:, 32:64], in1=s.L_tm[:, co:co + 32]))
            ctx.bank_read_done(6, "vE")
            vq = ctx.cnt["vE"]
            aq = ctx.emit("scalar", lambda e, COLF=COLF: e.activation(
                out=COLF, in_=COLF, func=AF.Exp), waits=[("vE", vq)], inc="aE")
            aq = ctx.emit("scalar", lambda e, FALL=FALL: e.activation(
                out=FALL, in_=FALL, func=AF.Exp), inc="aE")
            colf_exp = COLF.rearrange("p (h one) -> p h one", one=1).broadcast_to([128, 32, 64])
            cbar3 = s.cbar.rearrange("p (h n) -> p h n", h=32)
            ctil3 = s.ctil.rearrange("p (h n) -> p h n", h=32)
            ve(lambda e, colf_exp=colf_exp, cbar3=cbar3, ctil3=ctil3:
               e.tensor_tensor(out=cbar3, in0=ctil3, in1=colf_exp, op=ALU.mult),
               extra=[("aE", aq)])
            ve(lambda e: e.tensor_sub(out=s.mrows[:, 160:192],
                                      in0=s.mrows[:, 128:160],
                                      in1=s.mrows[:, 96:128]),
               extra=[("dS", dSr)])
            vq = ctx.cnt["vE"]
            aq = ctx.emit("scalar", lambda e: e.activation(
                out=s.mrows[:, 160:192], in_=s.mrows[:, 160:192], func=AF.Exp),
                waits=[("vE", vq)], inc="aE")
            ctx.emit("tensor", lambda e: e.matmul(
                p.b[6][:, 64:96], s.ones_f32[:], s.mrows[:, 160:192],
                start=True, stop=True),
                waits=ctx.bank_acquire(6, [("aE", aq)]), inc="tE")
            ctx.bank_filled(6)
            GSC = s.th_sc[:, 64:96]
            ve(lambda e, GSC=GSC: e.tensor_copy(out=GSC[0:64, :],
                                                in_=p.b[6][0:64, 64:96]),
               extra=ctx.bank_read_waits(6))
            ctx.bank_read_done(6, "vE")
            ctx.emit("tensor", lambda e, co=co: e.transpose(
                p.b[6].ap()[0:32, 0:128], s.L_tm[:, co:co + 32], s.eye_f32[:]),
                waits=ctx.bank_acquire(6), inc="tE")
            ctx.bank_filled(6)
            ve(lambda e: e.tensor_copy(out=s.L_emc[:], in_=p.b[6].ap()[0:32, 0:128]),
               extra=ctx.bank_read_waits(6))
            ctx.bank_read_done(6, "vE")
            chunk_v = ctx.cnt["vE"]
            chunk_a = ctx.cnt["aE"]
            p2bf = [p.b[2].ap().bitcast(BF16), p.b[6].ap().bitcast(BF16)]
            DSBs = [s.th_sc[:, 128:256], s.th_sc[:, 256:384]]
            FMs = [s.fmscr[:, 0:448], s.cs_c[:, 0:448]]
            ATs = [s.attnT_m[:, 0:128], s.bcdt[:, 1024:1152]]
            ybank = [3, 0]
            stbank = [7, 1]
            scbank = [4, 5]
            trbank = [2, 6]
            headmeta = {}

            def stageA(h):
                sl = h % 2
                base = (h % 2) * 64
                tb = trbank[sl]
                pb = p2bf[sl]
                DSB = DSBs[sl]
                ctx.emit("sync", lambda e, h=h: e.dma_start(
                    out=s.mrows[:, 192 + sl * 128:320 + sl * 128],
                    in_=s.L_emc[h:h + 1, :]),
                    waits=[("vE", chunk_v),
                           ("tE", ctx.meta.get("mm_d_tE_%d" % sl, 0))],
                    inc="dS", inc_by=16)
                dSh = ctx.cnt["dS"]
                ctx.emit("tensor", lambda e, sl=sl: e.matmul(
                    p.b[tb][:, 0:128], s.ones_f32[:],
                    s.mrows[:, 192 + sl * 128:320 + sl * 128],
                    start=True, stop=True),
                    waits=ctx.bank_acquire(tb, [("dS", dSh)]), inc="tE")
                xs_src = s.xs_fmh[base:base + 64,
                                  (h // 2) * 1024 + c * 128:
                                  (h // 2) * 1024 + (c + 1) * 128]
                ctx.emit("tensor", lambda e, xs_src=xs_src, pb=pb, base=base:
                         e.transpose(pb[0:128, 256:320], xs_src,
                                     s.eye_bf[base:base + 64, base:base + 64]),
                         waits=[("vE", chunk_v)], inc=None)
                ctx.emit("tensor", lambda e, h=h, pb=pb: e.transpose(
                    pb[0:64, 320:448], s.bhat[:, h * 64:(h + 1) * 64],
                    s.eye_bf[:]), inc=None)
                ctx.emit("tensor", lambda e, h=h, pb=pb: e.transpose(
                    pb[0:64, 448:576], s.ctil[:, h * 64:(h + 1) * 64],
                    s.eye_bf[:]), inc=None)
                ctx.emit("tensor", lambda e, h=h, pb=pb: e.transpose(
                    pb[0:64, 576:704], s.cbar[:, h * 64:(h + 1) * 64],
                    s.eye_bf[:]), inc="tE")
                ctx.bank_filled(tb)
                ctx.meta["mm_d_tE_%d" % sl] = ctx.cnt["tE"]
                ve(lambda e, DSB=DSB, co=co, h=h, tb=tb: e.tensor_scalar(
                    out=DSB, in0=p.b[tb][:, 0:128],
                    scalar1=s.L_tm[:, co + h:co + h + 1], scalar2=None,
                    op0=ALU.subtract), extra=ctx.bank_read_waits(tb))
                ve(lambda e, DSB=DSB: e.tensor_add(out=DSB, in0=DSB,
                                                   in1=s.minf_sb[:]))
                vq = ctx.cnt["vE"]
                aqD = ctx.emit("scalar", lambda e, DSB=DSB: e.activation(
                    out=DSB, in_=DSB, func=AF.Exp), waits=[("vE", vq)], inc="aE")
                FM = FMs[sl]
                fm_v = ctx.emit("scalar", lambda e, pb=pb, FM=FM: e.copy(
                    out=FM, in_=pb[0:128, 256:704]),
                    waits=ctx.bank_read_waits(tb) +
                    [("tE", ctx.meta.get("mm_fm_used_%d" % sl, 0))], inc="aE")
                ctx.bank_read_done(tb, "aE")
                ve(lambda e, h=h, FALL=FALL: e.tensor_scalar(
                    out=s.bhat[:, h * 64:(h + 1) * 64],
                    in0=s.bhat[:, h * 64:(h + 1) * 64],
                    scalar1=FALL[:, h:h + 1], scalar2=None, op0=ALU.mult))
                headmeta[h] = (fm_v, ctx.cnt["vE"], aqD)

            def stageB(h):
                sl = h % 2
                base = (h % 2) * 64
                fm_v, bbar_v, aqD = headmeta[h]
                FM = FMs[sl]
                DSB = DSBs[sl]
                AT = ATs[sl]
                sb_ = scbank[sl]
                yb = ybank[sl]
                stb = stbank[sl]
                ctx.emit("tensor", lambda e, sb_=sb_, FM=FM: e.matmul(
                    p.b[sb_][:, 0:128], FM[0:64, 64:192],
                    FM[0:64, 192:320], start=True, stop=True),
                    waits=ctx.bank_acquire(sb_, [("aE", fm_v)]), inc="tE")
                ctx.bank_filled(sb_)
                ve(lambda e, sb_=sb_, DSB=DSB, AT=AT: e.tensor_tensor(
                    out=AT, in0=p.b[sb_][:, 0:128], in1=DSB, op=ALU.mult),
                   extra=ctx.bank_read_waits(sb_) +
                   [("aE", aqD), ("tE", ctx.meta.get("mm_at_used_%d" % sl, 0))])
                ctx.bank_read_done(sb_, "vE")
                ve(lambda e, h=h, AT=AT: e.scalar_tensor_tensor(
                    out=AT, in0=s.eye_bf[:],
                    scalar=s.dskip_sb[:, h:h + 1], in1=AT,
                    op0=ALU.mult, op1=ALU.add))
                at_v = ctx.cnt["vE"]
                ctx.emit("tensor", lambda e, h=h, FM=FM, yb=yb: e.matmul(
                    p.b[yb][0:64, 0:128], s.g_bf[:, h * 64:(h + 1) * 64],
                    FM[0:64, 320:448], start=True, stop=False),
                    waits=ctx.bank_acquire(yb, [("vE", at_v),
                          ("aE", ctx.meta.get("mm_gcast_a", 0))]), inc=None)
                ctx.emit("tensor", lambda e, FM=FM, AT=AT, yb=yb: e.matmul(
                    p.b[yb][0:64, 0:128], FM[:, 0:64],
                    AT, start=False, stop=True), inc="tE")
                ctx.bank_filled(yb)
                ctx.meta["mm_at_used_%d" % sl] = ctx.cnt["tE"]
                ydst = s.y_q[base:base + 64,
                             (h // 2) * 512 + qlocal * 128:
                             (h // 2) * 512 + (qlocal + 1) * 128]
                ctx.emit("scalar", lambda e, ydst=ydst, yb=yb: e.copy(
                    out=ydst, in_=p.b[yb][0:64, 0:128]),
                    waits=ctx.bank_read_waits(yb), inc="aE")
                ctx.bank_read_done(yb, "aE")
                ctx.emit("tensor", lambda e, h=h, FM=FM, stb=stb: e.matmul(
                    p.b[stb][0:64, 0:64], s.bhat[:, h * 64:(h + 1) * 64],
                    FM[:, 0:64], start=True, stop=True),
                    waits=ctx.bank_acquire(stb, [("vE", bbar_v)]), inc="tE")
                ctx.bank_filled(stb)
                ctx.meta["mm_fm_used_%d" % sl] = ctx.cnt["tE"]
                gs = s.g_f32[:, h * 64:(h + 1) * 64]
                ve(lambda e, gs=gs, GSC=GSC, h=h: e.tensor_scalar(
                    out=gs, in0=gs, scalar1=GSC[0:64, h:h + 1], scalar2=None,
                    op0=ALU.mult))
                ve(lambda e, gs=gs, stb=stb: e.tensor_add(
                    out=gs, in0=gs, in1=p.b[stb][0:64, 0:64]),
                   extra=ctx.bank_read_waits(stb))
                ctx.bank_read_done(stb, "vE")
                vq = ctx.cnt["vE"]
                ctx.emit("scalar", lambda e, gs=gs, h=h: e.copy(
                    out=s.g_bf[:, h * 64:(h + 1) * 64], in_=gs),
                    waits=[("vE", vq)], inc="aE")
                ctx.meta["mm_gcast_a"] = ctx.cnt["aE"]

            for h in range(33):
                if h < 32:
                    stageA(h)
                if h >= 1:
                    stageB(h - 1)
            if qlocal == 3:
                emit_mamba_zphase(ctx, t, s, p, h2, (c // 4), dS0)
        scan_done_tE = ctx.cnt["tE"]
    ctx.meta["xfm_vE"] = ctx.cnt["vE"]


def emit_mamba_zphase(ctx, t, s, p, h2, qh, dS0):
    ln_v = ctx.meta["ln_vE"]
    tq0 = h2 * 1024 + qh * 512
    ve = None
    def ve(build, extra=()):
        vq = ctx.cnt["vE"]
        return ctx.emit("vector", build, waits=[("vE", vq)] + list(extra), inc="vE")
    yq_v = ctx.cnt["vE"]
    zq_a = ctx.cnt["aE"]
    for m in range(16):
        sl = m % 2
        ctx.emit("gpsimd", lambda e, m=m, sl=sl: e.dma_start(
            out=s.wslab[sl].rearrange("p (kc m) -> p kc m", kc=8),
            in_=t.w_in[:, m * 128:(m + 1) * 128].rearrange(
                "(kc p) m -> p kc m", p=128)),
            waits=[("tE", ctx.meta.get("mm_wslab_free_%d" % sl, 0))],
            inc="dW", inc_by=16)
        dWs = ctx.cnt["dW"]
        b = m % 2
        for kc in range(8):
            w = ()
            if kc == 0:
                w = ctx.bank_acquire(b, [("dW", dWs), ("vE", yq_v)])
            ctx.emit("tensor", lambda e, sl=sl, kc=kc, b=b: e.matmul(
                p.b[b][:], s.wslab[sl][:, kc * 128:(kc + 1) * 128],
                s.x_ln[:, kc * T + tq0: kc * T + tq0 + 512],
                start=(kc == 0), stop=(kc == 7)), waits=w,
                inc="tE" if kc == 7 else None)
        ctx.meta["mm_wslab_free_%d" % sl] = ctx.cnt["tE"]
        ctx.bank_filled(b)
        zscr = s.cs_c[:, 0:512]
        aq = ctx.emit("scalar", lambda e, b=b, zscr=zscr: e.activation(
            out=zscr, in_=p.b[b][:], func=AF.Silu),
            waits=ctx.bank_read_waits(b, [("aE", ctx.meta.get("mm_z_aq", 0)),
                                          ("vE", ctx.meta.get("mm_z_vq", 0))]),
            inc="aE")
        ctx.bank_read_done(b, "aE")
        ctx.meta["mm_z_aq"] = aq
        yslice = s.y_q[:, m * 512:(m + 1) * 512]
        vq = ctx.emit("vector", lambda e, yslice=yslice, zscr=zscr: e.tensor_mul(
            out=yslice, in0=yslice, in1=zscr), waits=[("aE", aq)], inc="vE")
        ctx.meta["mm_z_vq"] = vq
        sqscr = s.cs_c[:, 512:1024]
        aq = ctx.emit("scalar", lambda e, yslice=yslice, sqscr=sqscr: e.activation(
            out=sqscr, in_=yslice, func=AF.Square), waits=[("vE", vq)], inc="aE")
        ctx.emit("tensor", lambda e, m=m, sqscr=sqscr: e.matmul(
            p.b[4][0:1, :], s.onescol_bf[:], sqscr,
            start=(m == 0), stop=(m == 15)),
            waits=ctx.bank_acquire(4, [("aE", aq)]) if m == 0 else [("aE", aq)],
            inc="tE")
    ctx.bank_filled(4)
    R = s.th_sc[0:1, 0:512]
    ve(lambda e, R=R: e.tensor_scalar(
        out=R, in0=p.b[4][0:1, :], scalar1=1.0 / 2048.0, scalar2=1e-5,
        op0=ALU.mult, op1=ALU.add),
       extra=ctx.bank_read_waits(4) + [("aE", zq_a)])
    ctx.bank_read_done(4, "vE")
    vq = ctx.cnt["vE"]
    aq = ctx.emit("scalar", lambda e, R=R: e.activation(out=R, in_=R, func=AF.Ln),
                  waits=[("vE", vq)], inc="aE")
    ve(lambda e, R=R: e.tensor_scalar(
        out=R, in0=R, scalar1=-0.5, scalar2=None, op0=ALU.mult),
       extra=[("aE", aq)])
    vq = ctx.cnt["vE"]
    aq = ctx.emit("scalar", lambda e, R=R: e.activation(out=R, in_=R, func=AF.Exp),
                  waits=[("vE", vq)], inc="aE")
    ctx.emit("tensor", lambda e, R=R: e.matmul(
        p.b[5][:], s.ones_f32[:], R, start=True, stop=True),
        waits=ctx.bank_acquire(5, [("aE", aq)]), inc="tE")
    ctx.bank_filled(5)
    for m in range(16):
        yslice = s.y_q[:, m * 512:(m + 1) * 512]
        ctx.emit("vector", lambda e, yslice=yslice: e.tensor_mul(
            out=yslice, in0=yslice, in1=p.b[5][:]),
            waits=ctx.bank_read_waits(5) if m == 0 else (), inc="vE")
        ctx.emit("vector", lambda e, yslice=yslice, m=m: e.tensor_scalar(
            out=yslice, in0=yslice, scalar1=s.nw_sb[:, m:m + 1], scalar2=None,
            op0=ALU.mult), inc="vE")
    ctx.bank_read_done(5, "vE")
    yn_v = ctx.cnt["vE"]
    for kc in range(16):
        sl = kc % 2
        ctx.emit("gpsimd", lambda e, kc=kc, sl=sl: e.dma_start(
            out=s.wslab[sl][:, 0:1024],
            in_=t.w_out[kc * 128:(kc + 1) * 128, :]),
            waits=[("tE", ctx.meta.get("mm_wslab_free_%d" % sl, 0))],
            inc="dW", inc_by=16)
        dWs = ctx.cnt["dW"]
        for mo in range(8):
            w = [("dW", dWs)]
            if kc == 0:
                w += ctx.bank_acquire(mo, [("vE", yn_v)])
            ctx.emit("tensor", lambda e, kc=kc, sl=sl, mo=mo: e.matmul(
                p.b[mo][:], s.wslab[sl][:, mo * 128:(mo + 1) * 128],
                s.y_q[:, kc * 512:(kc + 1) * 512],
                start=(kc == 0), stop=(kc == 15)),
                waits=w if (kc == 0 or mo == 0) else (),
                inc="tE" if (kc == 15 or mo == 7) else None)
        ctx.meta["mm_wslab_free_%d" % sl] = ctx.cnt["tE"]
    for mo in range(8):
        ctx.bank_filled(mo)
        dst = s.x_fm[:, mo * T + tq0: mo * T + tq0 + 512]
        ctx.emit("vector", lambda e, mo=mo, dst=dst: e.tensor_add(
            out=dst, in0=p.b[mo][:], in1=dst),
            waits=ctx.bank_read_waits(mo), inc="vE")
        ctx.bank_read_done(mo, "vE")
    ctx.meta["mm_yq_free"] = ctx.cnt["vE"]


def emit_attn(ctx, t, s, p):
    ln_v = ctx.meta["ln_vE"]
    ent_t, ent_v, ent_a = ctx.cnt["tE"], ctx.cnt["vE"], ctx.cnt.get("aE", 0)
    bar = [("tE", ent_t), ("vE", ent_v), ("aE", ent_a)]
    atc = s.atc
    ctx.emit("sync", lambda e: e.dma_start(out=s.maskband, in_=t.c_maskband),
             waits=bar, inc="dS", inc_by=16)
    ctx.emit("sync", lambda e: e.dma_start(out=s.ba_sb, in_=t.ba_cm),
             waits=bar, inc="dS", inc_by=16)
    ctx.emit("sync", lambda e: e.dma_start(out=s.bp_sb, in_=t.bp_cm),
             waits=bar, inc="dS", inc_by=16)
    dS0 = ctx.cnt["dS"]
    ctx.emit("vector", lambda e: e.tensor_scalar(
        out=s.qba_sb, in0=s.ba_sb[:, 0:8], scalar1=0.125, scalar2=None,
        op0=ALU.mult), waits=[("dS", dS0)] + bar, inc="vE")
    qba_v = ctx.cnt["vE"]
    for hg in range(4):
        for hh in range(4):
            h = hg * 4 + hh
            ctx.emit("gpsimd", lambda e, h=h, hh=hh: e.dma_start(
                out=s.wproj_sb[0:64, hh * 1024:(hh + 1) * 1024],
                in_=t.w_proj[h * 64:(h + 1) * 64, :]),
                waits=(bar + [("tE", ctx.meta.get("at_wproj_free", 0))]) if hh == 0
                else (), inc="dW", inc_by=16)
        dWp = ctx.cnt["dW"]
        for lc in range(6):
            m_abs = [2 * hg, 2 * hg + 1, 8 + 2 * hg, 9 + 2 * hg,
                     16 + 2 * hg, 17 + 2 * hg][lc]
            sl = lc % 2
            ctx.emit("gpsimd", lambda e, m_abs=m_abs, sl=sl: e.dma_start(
                out=s.wattn_sb[sl].rearrange("p (kc m) -> p kc m", kc=8),
                in_=t.w_attn[:, m_abs * 128:(m_abs + 1) * 128].rearrange(
                    "(kc p) m -> p kc m", p=128)),
                waits=[("tE", ctx.meta.get("at_wattn_free_%d" % sl, 0))] +
                (bar if hg == 0 and lc < 2 else []), inc="dW", inc_by=16)
            dWa = ctx.cnt["dW"]
            for nt in range(4):
                t0 = nt * 512
                b = (lc * 4 + nt) % 2
                for kc in range(8):
                    w = ()
                    if kc == 0:
                        w = ctx.bank_acquire(b, [("dW", dWa), ("vE", ln_v)])
                    ctx.emit("tensor", lambda e, sl=sl, kc=kc, t0=t0, b=b: e.matmul(
                        p.b[b][:], s.wattn_sb[sl][:, kc * 128:(kc + 1) * 128],
                        s.x_ln[:, kc * T + t0: kc * T + t0 + 512],
                        start=(kc == 0), stop=(kc == 7)), waits=w,
                        inc="tE" if kc == 7 else None)
                tq = ctx.cnt["tE"]
                if nt == 3:
                    ctx.meta["at_wattn_free_%d" % sl] = tq
                ctx.bank_filled(b)
                m_abs = [2 * hg, 2 * hg + 1, 8 + 2 * hg, 9 + 2 * hg,
                         16 + 2 * hg, 17 + 2 * hg][lc]
                wq = [("tE", ctx.meta.get("at_qkv_free", 0))] if (lc == 0 and
                                                                 nt == 0) else []
                if lc < 2:
                    ctx.emit("vector", lambda e, lc=lc, t0=t0, b=b, m_abs=m_abs:
                             e.tensor_scalar(
                                 out=s.qkv_g[:, lc * T + t0: lc * T + t0 + 512],
                                 in0=p.b[b][:], scalar1=0.125,
                                 scalar2=s.qba_sb[:, m_abs:m_abs + 1],
                                 op0=ALU.mult, op1=ALU.add),
                             waits=ctx.bank_read_waits(
                                 b, [("dS", dS0), ("vE", qba_v)] + wq), inc="vE")
                    ctx.bank_read_done(b, "vE")
                else:
                    ctx.emit("scalar", lambda e, lc=lc, t0=t0, b=b, m_abs=m_abs:
                             e.activation(
                                 out=s.qkv_g[:, lc * T + t0: lc * T + t0 + 512],
                                 in_=p.b[b][:], func=AF.Identity,
                                 bias=s.ba_sb[:, m_abs:m_abs + 1]),
                             waits=ctx.bank_read_waits(b, [("dS", dS0)] + wq),
                             inc="aE")
                    ctx.bank_read_done(b, "aE")
        qkv_a = ctx.cnt["aE"]
        qkv_v = ctx.cnt["vE"]
        for hh in range(4):
            base = (hh % 2) * 64
            qc = (hh // 2) * T
            kc_off = (2 + hh // 2) * T
            vc = (4 + hh // 2) * T
            p3bf = p.b[3].ap().bitcast(BF16)
            for tc in range(16):
                ctx.emit("tensor", lambda e, tc=tc, base=base, vc=vc, p3bf=p3bf:
                         e.transpose(p3bf[0:128, tc * 64:(tc + 1) * 64],
                                     s.qkv_g[base:base + 64,
                                             vc + tc * 128: vc + (tc + 1) * 128],
                                     s.eye_bf[base:base + 64, base:base + 64]),
                         waits=ctx.bank_acquire(3, [("aE", qkv_a),
                                                    ("vE", qkv_v)]) if tc == 0
                         else (), inc="tE" if tc == 15 else None)
            ctx.bank_filled(3)
            ctx.emit("vector", lambda e, p3bf=p3bf: e.tensor_copy(
                out=s.v_tm[:], in_=p3bf[0:128, 0:1024]),
                waits=ctx.bank_read_waits(3), inc="vE")
            ctx.bank_read_done(3, "vE")
            vtm_v = ctx.cnt["vE"]
            qmeta = {}

            def stageA(qt):
                sl = qt % 2
                ab = sl * 12
                for nt in range(4):
                    b = 4 + nt
                    off = 2048 + nt * 512 - qt * 128
                    ctx.emit("tensor", lambda e, base=base, qc=qc, kc_off=kc_off,
                             qt=qt, nt=nt, b=b: e.matmul(
                                 p.b[b][:],
                                 s.qkv_g[base:base + 64,
                                         qc + qt * 128: qc + (qt + 1) * 128],
                                 s.qkv_g[base:base + 64,
                                         kc_off + nt * 512: kc_off + nt * 512 + 512],
                                 start=True, stop=False),
                             waits=ctx.bank_acquire(b, [("vE", vtm_v)] if qt == 0
                                                    and nt == 0 else ()), inc=None)
                    ctx.emit("tensor", lambda e, off=off, b=b: e.matmul(
                        p.b[b][:], s.eye_bf[:],
                        s.maskband[:, off:off + 512], start=False, stop=True),
                        inc="tE")
                    ctx.bank_filled(b)
                for nt in range(4):
                    b = 4 + nt
                    ctx.emit("vector", lambda e, nt=nt, b=b, ab=ab: e.reduce_max(
                        out=atc[:, ab + nt:ab + nt + 1], in_=p.b[b][:], axis=AX.X),
                        waits=ctx.bank_read_waits(b) +
                        [("aE", qmeta.get(("exp", sl), 0))], inc="vE")
                vq = ctx.cnt["vE"]
                vq = ctx.emit("vector", lambda e, ab=ab: e.tensor_max(
                    out=atc[:, ab + 11:ab + 12], in0=atc[:, ab:ab + 1],
                    in1=atc[:, ab + 1:ab + 2]), waits=[("vE", vq)], inc="vE")
                vq = ctx.emit("vector", lambda e, ab=ab: e.tensor_max(
                    out=atc[:, ab + 4:ab + 5], in0=atc[:, ab + 2:ab + 3],
                    in1=atc[:, ab + 3:ab + 4]), waits=[("vE", vq)], inc="vE")
                vq = ctx.emit("vector", lambda e, ab=ab: e.tensor_max(
                    out=atc[:, ab + 11:ab + 12], in0=atc[:, ab + 11:ab + 12],
                    in1=atc[:, ab + 4:ab + 5]), waits=[("vE", vq)], inc="vE")
                vq = ctx.emit("vector", lambda e, ab=ab: e.tensor_scalar(
                    out=atc[:, ab + 4:ab + 5], in0=atc[:, ab + 11:ab + 12],
                    scalar1=-1.0, scalar2=None, op0=ALU.mult),
                    waits=[("vE", vq)], inc="vE")
                for nt in range(4):
                    b = 4 + nt
                    aq = ctx.emit("scalar", lambda e, nt=nt, b=b, sl=sl, ab=ab:
                                  e.activation(
                                      out=s.attn_sb[sl][:, nt * 512:(nt + 1) * 512],
                                      in_=p.b[b][:], func=AF.Exp,
                                      bias=atc[:, ab + 4:ab + 5],
                                      accum_out=atc[:, ab + 5 + nt:ab + 6 + nt]),
                                  waits=[("vE", vq),
                                         ("tE", qmeta.get(("tr_done", sl), 0))],
                                  inc="aE")
                    ctx.bank_read_done(b, "aE")
                qmeta[("exp", sl)] = ctx.cnt["aE"]
                aq = ctx.cnt["aE"]
                vq = ctx.emit("vector", lambda e, ab=ab: e.reduce_sum(
                    out=atc[:, ab + 9:ab + 10], in_=atc[:, ab + 5:ab + 9],
                    axis=AX.X), waits=[("aE", aq)], inc="vE")
                vq = ctx.emit("vector", lambda e, ab=ab: e.reciprocal(
                    out=atc[:, ab + 10:ab + 11], in_=atc[:, ab + 9:ab + 10]),
                    waits=[("vE", vq)], inc="vE")
                for nt in range(4):
                    ctx.emit("vector", lambda e, nt=nt, sl=sl, ab=ab: e.tensor_scalar(
                        out=s.attn_sb[sl][:, nt * 512:(nt + 1) * 512],
                        in0=s.attn_sb[sl][:, nt * 512:(nt + 1) * 512],
                        scalar1=atc[:, ab + 10:ab + 11], scalar2=None, op0=ALU.mult),
                        waits=[("vE", vq)] if nt == 0 else (), inc="vE")
                qmeta[("attn_v", qt)] = ctx.cnt["vE"]

            def stageB1(qt):
                sl = qt % 2
                attn_v = qmeta[("attn_v", qt)]
                for tc in range(16):
                    b = tc // 8
                    pbf = p.b[b].ap().bitcast(BF16)
                    ctx.emit("tensor", lambda e, tc=tc, b=b, pbf=pbf, sl=sl:
                             e.transpose(pbf[0:128, (tc % 8) * 128:
                                             (tc % 8) * 128 + 128],
                                         s.attn_sb[sl][:, tc * 128:(tc + 1) * 128],
                                         s.eye_bf[0:128, 0:128]),
                             waits=ctx.bank_acquire(b, [("vE", attn_v)])
                             if tc % 8 == 0 else (),
                             inc="tE" if tc % 8 == 7 else None)
                    if tc % 8 == 7:
                        ctx.bank_filled(b)
                for b in range(2):
                    pbf = p.b[b].ap().bitcast(BF16)
                    ctx.emit("scalar", lambda e, b=b, pbf=pbf, sl=sl: e.copy(
                        out=s.attnT_sb[sl][:, b * 1024:(b + 1) * 1024],
                        in_=pbf[0:128, 0:1024]),
                        waits=ctx.bank_read_waits(b), inc="aE")
                    ctx.bank_read_done(b, "aE")
                qmeta[("tr_done", sl)] = ctx.cnt["tE"]
                qmeta[("attnT_a", qt)] = ctx.cnt["aE"]

            def stageB2(qt):
                sl = qt % 2
                attnT_a = qmeta[("attnT_a", qt)]
                for kc2 in range(16):
                    w = ()
                    if kc2 == 0:
                        w = ctx.bank_acquire(2, [("aE", attnT_a)])
                    ctx.emit("tensor", lambda e, kc2=kc2, sl=sl: e.matmul(
                        p.b[2][0:64, 0:128],
                        s.v_tm[:, kc2 * 64:(kc2 + 1) * 64],
                        s.attnT_sb[sl][:, kc2 * 128:(kc2 + 1) * 128],
                        start=(kc2 == 0), stop=(kc2 == 15)), waits=w,
                        inc="tE" if kc2 == 15 else None)
                ctx.bank_filled(2)
                ctx.emit("scalar", lambda e, hh=hh, qt=qt: e.copy(
                    out=s.attnout[:, hh * T + qt * 128: hh * T + (qt + 1) * 128],
                    in_=p.b[2][0:64, 0:128]),
                    waits=ctx.bank_read_waits(2), inc="aE")
                ctx.bank_read_done(2, "aE")

            for qt in range(18):
                if qt < 16:
                    stageA(qt)
                if 1 <= qt < 17:
                    stageB1(qt - 1)
                if qt >= 2:
                    stageB2(qt - 2)
        ao_v = ctx.cnt["vE"]
        ao_a = ctx.cnt["aE"]
        ctx.meta["at_qkv_free"] = ctx.cnt["tE"]
        for mo in range(8):
            for nt in range(4):
                t0 = nt * 512
                b = (mo * 4 + nt) % 2
                for hh in range(4):
                    w = ()
                    if hh == 0:
                        w = ctx.bank_acquire(b, [("vE", ao_v), ("aE", ao_a),
                                                 ("dW", dWp)])
                    ctx.emit("tensor", lambda e, hh=hh, mo=mo, t0=t0, b=b: e.matmul(
                        p.b[b][:],
                        s.wproj_sb[0:64, hh * 1024 + mo * 128:
                                   hh * 1024 + mo * 128 + 128],
                        s.attnout[:, hh * T + t0: hh * T + t0 + 512],
                        start=(hh == 0), stop=(hh == 3)), waits=w,
                        inc="tE" if hh == 3 else None)
                ctx.bank_filled(b)
                dst = s.x_fm[:, mo * T + t0: mo * T + t0 + 512]
                ctx.emit("vector", lambda e, b=b, dst=dst: e.tensor_add(
                    out=dst, in0=p.b[b][:], in1=dst),
                    waits=ctx.bank_read_waits(b), inc="vE")
                if hg == 0:
                    ctx.emit("vector", lambda e, mo=mo, dst=dst: e.tensor_scalar(
                        out=dst, in0=dst, scalar1=s.bp_sb[:, mo:mo + 1],
                        scalar2=None, op0=ALU.add), inc="vE")
                ctx.bank_read_done(b, "vE")
        ctx.meta["at_wproj_free"] = ctx.cnt["tE"]
    ctx.meta["xfm_vE"] = ctx.cnt["vE"]


def emit_lff(ctx, t, s, p):
    ln_v = ctx.meta["ln_vE"]
    ent_t, ent_v, ent_a = ctx.cnt["tE"], ctx.cnt["vE"], ctx.cnt.get("aE", 0)
    bar = [("tE", ent_t), ("vE", ent_v), ("aE", ent_a)]
    for dst, srcap in ((s.ld_sb, t.lora_down), (s.lu_sb, t.lora_up),
                       (s.upxt_sb, t.upxt), (s.dnaux_sb, t.down_aux),
                       (s.bl_sb, t.bl_cm)):
        ctx.emit("sync", lambda e, a=dst, b=srcap: e.dma_start(
            out=a if isinstance(a, bass.AP) else a[:], in_=b),
            waits=bar, inc="dS", inc_by=16)
    ctx.emit("gpsimd", lambda e: e.dma_start(
        out=s.l1w_bf.rearrange("p (kc o) -> p kc o", kc=8),
        in_=t.l1wt.rearrange("(kc p) o -> p kc o", p=128)),
        waits=bar, inc="dW", inc_by=16)
    dS0, dW0 = ctx.cnt["dS"], ctx.cnt["dW"]
    ctx.emit("vector", lambda e: e.tensor_scalar(
        out=s.hbl_sb[:], in0=s.bl_sb[:], scalar1=0.5, scalar2=None, op0=ALU.mult),
        waits=[("dS", dS0)] + bar, inc="vE")
    for n2 in range(2):
        ctx.emit("tensor", lambda e, n2=n2: e.matmul(
            p.b[2][0:4, :], s.lu_sb, s.upxt_sb[:, n2 * 512:(n2 + 1) * 512],
            start=True, stop=True),
            waits=ctx.bank_acquire(2, [("dS", dS0)]), inc="tE")
        ctx.bank_filled(2)
        vq = ctx.emit("vector", lambda e, n2=n2: e.tensor_copy(
            out=s.p1_sb[:, n2 * 512:(n2 + 1) * 512], in_=p.b[2][0:4, :]),
            waits=ctx.bank_read_waits(2), inc="vE")
        ctx.bank_read_done(2, "vE")
    for n2 in range(2):
        ctx.emit("tensor", lambda e, n2=n2: e.matmul(
            p.b[3][0:100, :], s.ld_sb, s.p1_sb[:, n2 * 512:(n2 + 1) * 512],
            start=True, stop=True),
            waits=ctx.bank_acquire(3, [("vE", vq)]), inc="tE")
        ctx.bank_filled(3)
        vq2 = ctx.emit("vector", lambda e, n2=n2: e.tensor_copy(
            out=s.p2_sb[:, n2 * 512:(n2 + 1) * 512], in_=p.b[3][0:100, :]),
            waits=ctx.bank_read_waits(3), inc="vE")
        ctx.bank_read_done(3, "vE")
    for kc in range(8):
        for n2 in range(2):
            b = 2 + (kc * 2 + n2) % 2
            ctx.emit("tensor", lambda e, kc=kc, n2=n2, b=b: e.matmul(
                p.b[b][:], s.dnaux_sb[:, kc * 128:(kc + 1) * 128],
                s.p2_sb[:, n2 * 512:(n2 + 1) * 512], start=True, stop=True),
                waits=ctx.bank_acquire(b, [("vE", vq2)]), inc="tE")
            ctx.bank_filled(b)
            ctx.emit("vector", lambda e, kc=kc, n2=n2, b=b: e.tensor_add(
                out=s.weff[:, kc * 1024 + n2 * 512: kc * 1024 + n2 * 512 + 512],
                in0=p.b[b][:],
                in1=s.l1w_bf[:, kc * 1024 + n2 * 512: kc * 1024 + n2 * 512 + 512]),
                waits=ctx.bank_read_waits(b, [("dW", dW0)]), inc="vE")
            ctx.bank_read_done(b, "vE")
    weff_v = ctx.cnt["vE"]
    for mo in range(8):
        for nt in range(4):
            t0 = nt * 512
            b = (mo * 4 + nt) % 2
            for kc in range(8):
                w = ()
                if kc == 0:
                    w = ctx.bank_acquire(b, [("vE", max(ln_v, weff_v))])
                ctx.emit("tensor", lambda e, mo=mo, kc=kc, t0=t0, b=b: e.matmul(
                    p.b[b][:],
                    s.weff[:, kc * 1024 + mo * 128: kc * 1024 + mo * 128 + 128],
                    s.x_ln[:, kc * T + t0: kc * T + t0 + 512],
                    start=(kc == 0), stop=(kc == 7)),
                    waits=w, inc="tE" if kc == 7 else None)
            ctx.bank_filled(b)
            dst = s.x_fm[:, mo * T + t0:mo * T + t0 + 512]
            ctx.emit("vector", lambda e, b=b, dst=dst: e.scalar_tensor_tensor(
                out=dst, in0=p.b[b][:], scalar=0.5, in1=dst,
                op0=ALU.mult, op1=ALU.add),
                waits=ctx.bank_read_waits(b), inc="vE")
            ctx.emit("vector", lambda e, mo=mo, dst=dst: e.tensor_scalar(
                out=dst, in0=dst, scalar1=s.hbl_sb[:, mo:mo + 1], scalar2=None,
                op0=ALU.add), inc="vE")
            ctx.bank_read_done(b, "vE")
    ctx.meta["xfm_vE"] = ctx.cnt["vE"]


def build_program(phases, taps=()):
    phi = 0
    nc = bass.Bass("TRN2", target_bir_lowering=False, debug=False)
    t = declare_tensors(nc, phases)
    s = alloc_sbuf(nc, phases)
    p = alloc_psum(nc)
    ctx = Ctx(nc)
    emit_load_x(ctx, t, s)
    emit_load_consts(ctx, t, s, phases)
    for ph in phases:
        emit_ln(ctx, t, s, p)
        if "ln" in taps:
            emit_dump(ctx, nc, "d_xln", s.x_ln[:], NCH * T, BF16)
        if ph == "moe":
            emit_router(ctx, t, s, p)
            if "comb" in taps:
                emit_dump(ctx, nc, "d_comb", s.comb_em[:], 2 * T, BF16)
            if "scr" in taps:
                emit_dump(ctx, nc, "d_scr", s.ar_f32[:, 12288:20480], 8192, F32)
            if "stop_before_moe" in taps:
                break
            emit_moe(ctx, t, s, p)
        elif ph == "mamba":
            emit_mamba(ctx, t, s, p)
        elif ph == "lff":
            emit_lff(ctx, t, s, p)
        elif ph == "attn":
            emit_attn(ctx, t, s, p)
            if "attn_dbg" in taps:
                emit_dump(ctx, nc, "d_qkv", s.qkv_g[:], 6 * T, BF16)
                emit_dump(ctx, nc, "d_ao", s.attnout, 4 * T, BF16, rows=64)
                emit_dump(ctx, nc, "d_attn", s.attn_sb[1][:], T, BF16)
        phi += 1
        if "stream" in taps:
            emit_dump(ctx, nc, "d_ph%d" % phi, s.x_fm[:], NCH * T, F32)
    emit_store_x(ctx, t, s)
    ctx.run()
    return nc



def _consts(phases):
    c = {}
    c["c_eye_bf"] = np.eye(128, dtype=BF)
    c["c_eye_f32"] = np.eye(128, dtype=np.float32)
    c["c_ones_bf"] = np.ones((128, 128), dtype=BF)
    c["c_ones_f32"] = np.ones((1, 128), dtype=np.float32)
    c["c_onescol_bf"] = np.ones((128, 1), dtype=BF)
    if "mamba" in phases:
        sidx = np.arange(128)[:, None]
        tidx = np.arange(128)[None, :]
        c["c_tri"] = (sidx <= tidx).astype(np.float32)
        c["c_minf"] = np.where(tidx >= sidx, 0.0, -1e30).astype(np.float32)
        invf = (1.0 / 10000.0 ** (np.arange(16, dtype=np.float64) / 16.0))
        c["c_invfreq"] = np.tile(invf, 32)[None, :].astype(np.float32)
    if "attn" in phases:
        v = np.arange(4224)[None, :] - 2048 - np.arange(128)[:, None]
        f = np.where(v > 0, -np.log1p(np.maximum(v, 0).astype(np.float64)), 0.0)
        f = np.where(v < 0, -1e30, f)
        c["c_maskband"] = f.astype(BF)
    return c


def _cm(vec, ncols):
    v = np.zeros(128 * ncols, dtype=np.float32)
    v[:len(vec)] = np.asarray(vec, dtype=np.float32)
    return np.ascontiguousarray(v.reshape(ncols, 128).T)


def _layout_weights(inp, phases):
    w = {}
    f32c = lambda a: np.ascontiguousarray(np.asarray(a, dtype=np.float32))
    if "moe" in phases:
        w["wgt"] = f32c(np.asarray(inp["L0_ffn_w_gate"]).transpose(0, 2, 1))
        w["wdt"] = f32c(np.asarray(inp["L0_ffn_w_down"]).transpose(0, 2, 1))
        w["wgst"] = f32c(np.asarray(inp["L0_ffn_w_gate_shared"]).T)
        w["wdst"] = f32c(np.asarray(inp["L0_ffn_w_down_shared"]).T)
        w["rwt"] = f32c(np.asarray(inp["L0_ffn_router_w"]).T)
        w["eb_cm"] = _cm(inp["L0_ffn_expert_bias"], 2)
    if "mamba" in phases:
        w["w_in"] = f32c(inp["L0_attn_W_in"])
        w["w_out"] = f32c(inp["L0_attn_W_out"])
        w["dt_bias"] = f32c(np.asarray(inp["L0_attn_dt_bias"]).reshape(1, 32))
        w["a_log"] = f32c(np.asarray(inp["L0_attn_A_log"]).reshape(1, 32))
        w["dskip_rep"] = f32c(np.broadcast_to(
            np.asarray(inp["L0_attn_D_skip"])[None, :], (128, 32)))
        w["nw_cm"] = _cm(inp["L0_attn_norm_weight"], 16)
    if "attn" in phases:
        w["w_attn"] = f32c(inp["L1_attn_w_attn"])
        w["w_proj"] = f32c(inp["L1_attn_w_proj"])
        w["ba_cm"] = _cm(inp["L1_attn_b_attn"], 24)
        w["bp_cm"] = _cm(inp["L1_attn_b_proj"], 8)
    if "lff" in phases:
        w["l1wt"] = f32c(np.asarray(inp["L1_ffn_W"]).T)
        w["lora_down"] = f32c(inp["L1_ffn_lora_down"])
        w["lora_up"] = f32c(inp["L1_ffn_lora_up"])
        w["down_aux"] = f32c(inp["L1_ffn_down_aux"])
        w["upxt"] = f32c(np.asarray(inp["L1_ffn_up_aux"]).T)
        w["bl_cm"] = _cm(inp["L1_ffn_b"], 8)
    return w


def run_phases(phases, xs, weights):
    nc = build_program(phases)
    consts = _consts(phases)
    in_maps = []
    for b in range(8):
        im = {"xin": xs[b]}
        im.update(consts)
        im.update(weights)
        in_maps.append(im)
    res = run_bass_kernel_spmd(nc, in_maps, core_ids=list(range(8)))
    return [np.asarray(res.results[b]["xout"]) for b in range(8)]


def kernel(**inputs):
    x = np.asarray(inputs["L0_attn_x"])
    xs = [np.ascontiguousarray(np.asarray(x[b], dtype=np.float32).T) for b in range(8)]
    for phases in PHASE_GROUPS:
        weights = _layout_weights(inputs, phases)
        xs = run_phases(phases, xs, weights)
    out = np.stack([xb.T for xb in xs])
    return np.ascontiguousarray(out).astype(np.float32)


# version 06 (final)
# speedup vs baseline: 1.2626x; 1.0140x over previous
import numpy as np
import ml_dtypes
import concourse.bass as bass
import concourse.mybir as mybir
from concourse.bass_utils import run_bass_kernel_spmd

F32 = mybir.dt.float32
BF16 = mybir.dt.bfloat16
AF = mybir.ActivationFunctionType
ALU = mybir.AluOpType
AX = mybir.AxisListType
BF = ml_dtypes.bfloat16

T = 2048
DM = 1024
NCH = 8
NE = 160
INTER = 512

PHASE_GROUPS = [["moe", "mamba", "moe", "lff", "attn", "lff"]]


class Ctx:

    def __init__(self, nc):
        self.nc = nc
        self.ops = []
        self.cnt = {}
        self.handles = {}
        self.meta = {}
        self.bank_r = [{"vE": 0, "aE": 0} for _ in range(8)]
        self.bank_w = [0 for _ in range(8)]

    def sem(self, name):
        if name not in self.handles:
            self.handles[name] = self.nc.alloc_semaphore(name)
            self.cnt[name] = 0
        return self.handles[name]

    def emit(self, eng, build, waits=(), inc=None, inc_by=1):
        waits = [(n, v) for (n, v) in waits if v > 0]
        for n, _ in waits:
            self.sem(n)
        tgt = None
        if inc is not None:
            self.sem(inc)
            self.cnt[inc] += inc_by
            tgt = self.cnt[inc]

        def fn(e, build=build, waits=tuple(waits), inc=inc, inc_by=inc_by):
            for n, v in waits:
                e.wait_ge(self.handles[n], v)
            ins = build(e)
            if inc is not None:
                ins.then_inc(self.handles[inc], inc_by)
        self.ops.append((eng, fn))
        return tgt

    def bank_acquire(self, b, extra=()):
        r = self.bank_r[b]
        return [("vE", r["vE"]), ("aE", r["aE"])] + list(extra)

    def bank_filled(self, b):
        self.bank_w[b] = self.cnt["tE"]

    def bank_read_waits(self, b, extra=()):
        return [("tE", self.bank_w[b])] + list(extra)

    def bank_read_done(self, b, eng):
        self.bank_r[b][eng] = self.cnt[eng]

    def run(self):
        nc = self.nc
        with nc.Block() as block:
            for eng_name in ("sync", "scalar", "tensor", "vector", "gpsimd"):
                ops = [f for (e, f) in self.ops if e == eng_name]

                def mk(ops):
                    def runner(engine):
                        for f in ops:
                            f(engine)
                    return runner
                getattr(block, eng_name)(mk(ops))


class NS:
    pass



def declare_tensors(nc, phases):
    t = NS()
    d = lambda name, shape, dtype=F32, kind="ExternalInput": nc.dram_tensor(
        name, list(shape), dtype, kind=kind).ap()
    t.xin = d("xin", [DM, T])
    t.xout = d("xout", [DM, T], kind="ExternalOutput")
    t.c_eye_bf = d("c_eye_bf", [128, 128], BF16)
    t.c_eye_f32 = d("c_eye_f32", [128, 128])
    t.c_ones_bf = d("c_ones_bf", [128, 128], BF16)
    t.c_ones_f32 = d("c_ones_f32", [1, 128])
    t.c_onescol_bf = d("c_onescol_bf", [128, 1], BF16)
    if "moe" in phases:
        t.wgt = d("wgt", [NE, DM, INTER])
        t.wdt = d("wdt", [NE, INTER, DM])
        t.wgst = d("wgst", [DM, INTER])
        t.wdst = d("wdst", [INTER, DM])
        t.rwt = d("rwt", [DM, NE])
        t.eb_cm = d("eb_cm", [128, 2])
    if "mamba" in phases:
        t.w_in = d("w_in", [DM, 4256])
        t.w_out = d("w_out", [2 * DM, DM])
        t.dt_bias = d("dt_bias", [1, 32])
        t.a_log = d("a_log", [1, 32])
        t.dskip_rep = d("dskip_rep", [128, 32])
        t.nw_cm = d("nw_cm", [128, 16])
        t.c_tri = d("c_tri", [128, 128])
        t.c_minf = d("c_minf", [128, 128])
        t.c_invfreq = d("c_invfreq", [1, 512])
    if "attn" in phases:
        t.w_attn = d("w_attn", [DM, 3 * DM])
        t.w_proj = d("w_proj", [DM, DM])
        t.ba_cm = d("ba_cm", [128, 24])
        t.bp_cm = d("bp_cm", [128, 8])
        t.c_maskband = d("c_maskband", [128, 4224], BF16)
    if "lff" in phases:
        t.l1wt = d("l1wt", [DM, DM])
        t.lora_down = d("lora_down", [4, 100])
        t.lora_up = d("lora_up", [50, 4])
        t.down_aux = d("down_aux", [100, DM])
        t.upxt = d("upxt", [50, DM])
        t.bl_cm = d("bl_cm", [128, 8])
    return t


def alloc_sbuf(nc, phases):
    s = NS()
    a = lambda name, shape, dtype: nc.alloc_sbuf_tensor(name, list(shape), dtype)
    s.x_fm = a("x_fm", [128, NCH * T], F32)
    s.x_ln = a("x_ln", [128, NCH * T], BF16)
    s.eye_bf = a("eye_bf", [128, 128], BF16)
    s.eye_f32 = a("eye_f32", [128, 128], F32)
    s.ones_bf = a("ones_bf", [128, 128], BF16)
    s.ones_f32 = a("ones_f32", [1, 128], F32)
    s.onescol_bf = a("onescol_bf", [128, 1], BF16)
    s.ln_rows = a("ln_rows", [1, 3 * 512], F32)
    s.ln_sq = a("ln_sq", [128, 2 * 512], BF16)
    s.ln_t = a("ln_t", [128, 2 * 512], F32)
    s.AR = 48384
    s.arena = a("arena", [128, s.AR], BF16)
    s.ar_bf = s.arena.ap()
    s.ar_f32 = s.arena.ap().bitcast(F32)
    if "moe" in phases:
        s.wg_sb = [s.ar_bf[:, 0:8192], s.ar_bf[:, 8192:16384]]
        s.wd_sb = s.ar_bf[:, 16384:24576]
        s.h1w = s.ar_bf[:, 24576:40960]
        s.comb_em = s.ar_bf[:, 40960:45056]
        s.crow = [s.ar_bf[32:33, 43008:45056], s.ar_bf[64:65, 43008:45056]]
        s.silu_sc = s.ar_f32[:, 22528:23552]
        s.rw_sb = s.ar_bf[:, 47104:48384]
        s.eb_sb = a("eb_sb", [128, 2], F32)
    if "mamba" in phases:
        s.xs_fmh = s.ar_bf[:, 0:16384]
        s.y_q = s.ar_bf[:, 16384:24576]
        s.bcdt = s.ar_bf[:, 24576:26624]
        s.wslab = [s.ar_bf[:, 26624:27648], s.ar_bf[:, 27648:28672]]
        s.cs_c = s.ar_bf[:, 28672:29696]
        s.bhat = s.ar_bf[:, 29696:31744]
        s.ctil = s.ar_bf[:, 31744:33792]
        s.cbar = s.ar_bf[:, 33792:35840]
        s.g_bf = s.ar_bf[0:64, 35840:37888]
        MF = 18944
        s.dt_tm = s.ar_f32[:, MF:MF + 512]
        s.cumdt = s.ar_f32[:, MF + 512:MF + 1024]
        s.L_tm = s.ar_f32[:, MF + 1024:MF + 1536]
        s.L_emc = s.ar_f32[0:32, MF + 1536:MF + 1664]
        s.th_sc = s.ar_f32[:, MF + 1664:MF + 2176]
        s.mrows = s.ar_f32[0:1, MF + 2176:MF + 2688]
        s.fmscr = s.ar_bf[:, 2 * (MF + 2688):2 * (MF + 2688) + 512]
        s.attnT_m = s.ar_bf[:, 2 * (MF + 2944):2 * (MF + 2944) + 128]
        s.g_f32 = s.ar_f32[0:64, MF + 3008:MF + 5056]
        s.dtb_bc = a("dtb_bc", [128, 32], F32)
        s.nea_bc = a("nea_bc", [128, 32], F32)
        s.tri_sb = a("tri_sb", [128, 128], F32)
        s.minf_sb = a("minf_sb", [128, 128], F32)
        s.invf_row = a("invf_row", [1, 512], F32)
        s.dtb_row = a("dtb_row", [1, 32], F32)
        s.alog_row = a("alog_row", [1, 32], F32)
        s.dskip_sb = a("dskip_sb", [128, 32], F32)
        s.nw_sb = a("nw_sb", [128, 16], F32)
    if "lff" in phases:
        s.weff = s.ar_bf[:, 0:8192]
        s.l1w_bf = s.ar_bf[:, 8192:16384]
        s.upxt_sb = s.ar_f32[0:50, 8192:9216]
        s.p1_sb = s.ar_f32[0:4, 9216:10240]
        s.p2_sb = s.ar_f32[0:100, 10240:11264]
        s.dnaux_sb = s.ar_f32[0:100, 11264:12288]
        s.ld_sb = s.ar_f32[0:4, 12288:12388]
        s.lu_sb = s.ar_f32[0:50, 12416:12420]
        s.bl_sb = a("bl_sb", [128, 8], F32)
        s.hbl_sb = a("hbl_sb", [128, 8], F32)
    if "attn" in phases:
        s.qkv_g = s.ar_bf[:, 0:12288]
        s.attnout = s.ar_bf[0:64, 12288:20480]
        s.maskband = s.ar_bf[:, 20480:24704]
        s.v_tm = s.ar_bf[:, 24704:25728]
        s.attn_sb = [s.ar_bf[:, 25728:27776], s.ar_bf[:, 27776:29824]]
        s.attnT_sb = [s.ar_bf[:, 29824:31872], s.ar_bf[:, 31872:33920]]
        s.wproj_sb = s.ar_bf[0:64, 33920:38016]
        s.wattn_sb = [s.ar_bf[:, 38016:39040], s.ar_bf[:, 39040:40064]]
        s.ba_sb = s.ar_f32[:, 24000:24024]
        s.bp_sb = s.ar_f32[:, 24024:24032]
        s.qba_sb = s.ar_f32[:, 24032:24040]
        s.atc = s.ar_f32[:, 24040:24064]
    return s


def alloc_psum(nc):
    p = NS()
    p.b = [nc.alloc_psum_tensor("pb%d" % i, [128, 512], F32) for i in range(8)]
    return p



def emit_load_x(ctx, t, s):
    src = t.xin.rearrange("(c p) t -> p c t", p=128)
    dst = s.x_fm.ap().rearrange("p (c t) -> p c t", c=NCH)
    ctx.emit("sync", lambda e: e.dma_start(out=dst, in_=src), inc="dS", inc_by=16)
    ctx.meta["x_dS"] = ctx.cnt["dS"]


def emit_load_consts(ctx, t, s, phases):
    pairs = [(s.eye_bf, t.c_eye_bf), (s.eye_f32, t.c_eye_f32),
             (s.ones_bf, t.c_ones_bf), (s.ones_f32, t.c_ones_f32),
             (s.onescol_bf, t.c_onescol_bf)]
    if "moe" in phases:
        pairs += [(s.eb_sb, t.eb_cm)]
    for dst, src in pairs:
        ctx.emit("sync", lambda e, d=dst, sr=src: e.dma_start(out=d[:], in_=sr),
                 inc="dS", inc_by=16)
    ctx.meta["consts_dS"] = ctx.cnt["dS"]
    ctx.meta["consts_dW"] = ctx.cnt.get("dW", 0)


def emit_ln(ctx, t, s, p):
    dep_v = ctx.meta.get("xfm_vE", 0)
    first_dma = [("dS", max(ctx.meta["x_dS"], ctx.meta["consts_dS"]))]
    R = s.ln_rows
    for nt in range(4):
        t0 = nt * 512
        for c in range(NCH):
            sq = s.ln_sq[:, (c % 2) * 512:(c % 2) * 512 + 512]
            w = [("tE", ctx.meta.get("ln_sq_tE_%d" % (c % 2), 0))]
            if c == 0 and nt == 0:
                w += first_dma + [("vE", dep_v)]
            ctx.emit("scalar", lambda e, c=c, sq=sq, t0=t0: e.activation(
                out=sq, in_=s.x_fm[:, c * T + t0:c * T + t0 + 512], func=AF.Square),
                waits=w, inc="aE")
            aq = ctx.cnt["aE"]
            wv = first_dma if (c == 0 and nt == 0) else ()
            ctx.emit("vector", lambda e, c=c, t0=t0: e.tensor_copy(
                out=s.x_ln[:, c * T + t0:c * T + t0 + 512],
                in_=s.x_fm[:, c * T + t0:c * T + t0 + 512]), waits=wv, inc="vE")
            vq = ctx.cnt["vE"]
            wsum = ctx.bank_acquire(4, [("vE", vq)]) if c == 0 else [("vE", vq)]
            ctx.emit("tensor", lambda e, c=c, t0=t0: e.matmul(
                p.b[4][0:1, :], s.onescol_bf[:],
                s.x_ln[:, c * T + t0:c * T + t0 + 512],
                start=(c == 0), stop=(c == NCH - 1)), waits=wsum,
                inc="tE" if c == NCH - 1 else None)
            wsq = ctx.bank_acquire(5, [("aE", aq)]) if c == 0 else [("aE", aq)]
            tq = ctx.emit("tensor", lambda e, c=c, sq=sq: e.matmul(
                p.b[5][0:1, :], s.onescol_bf[:], sq,
                start=(c == 0), stop=(c == NCH - 1)), waits=wsq, inc="tE")
            if tq is None:
                tq = ctx.cnt["tE"]
            ctx.meta["ln_sq_tE_%d" % (c % 2)] = tq
        ctx.bank_filled(4)
        ctx.bank_filled(5)
        ctx.emit("vector", lambda e: e.tensor_scalar(
            out=R[:, 0:512], in0=p.b[4][0:1, :], scalar1=1.0 / DM, scalar2=None,
            op0=ALU.mult), waits=ctx.bank_read_waits(4), inc="vE")
        ctx.emit("vector", lambda e: e.tensor_copy(out=R[:, 512:1024], in_=p.b[5][0:1, :]),
                 waits=ctx.bank_read_waits(5), inc="vE")
        ctx.bank_read_done(4, "vE")
        ctx.bank_read_done(5, "vE")
        ctx.emit("vector", lambda e: e.tensor_tensor(
            out=R[:, 1024:1536], in0=R[:, 0:512], in1=R[:, 0:512], op=ALU.mult),
            inc="vE")
        ctx.emit("vector", lambda e: e.scalar_tensor_tensor(
            out=R[:, 512:1024], in0=R[:, 512:1024], scalar=1.0 / DM,
            in1=R[:, 1024:1536], op0=ALU.mult, op1=ALU.subtract), inc="vE")
        vq = ctx.emit("vector", lambda e: e.tensor_scalar(
            out=R[:, 512:1024], in0=R[:, 512:1024], scalar1=1e-5, scalar2=None,
            op0=ALU.add), inc="vE")
        aq1 = ctx.emit("scalar", lambda e: e.activation(
            out=R[:, 512:1024], in_=R[:, 512:1024], func=AF.Ln),
            waits=[("vE", ctx.cnt["vE"])], inc="aE")
        vq = ctx.emit("vector", lambda e: e.tensor_scalar(
            out=R[:, 512:1024], in0=R[:, 512:1024], scalar1=-0.5, scalar2=None,
            op0=ALU.mult), waits=[("aE", aq1)], inc="vE")
        aq = ctx.emit("scalar", lambda e: e.activation(
            out=R[:, 512:1024], in_=R[:, 512:1024], func=AF.Exp),
            waits=[("vE", ctx.cnt["vE"])], inc="aE")
        ctx.emit("tensor", lambda e: e.matmul(
            p.b[6][:], s.ones_f32[:], R[:, 0:512], start=True, stop=True),
            waits=ctx.bank_acquire(6, [("aE", aq), ("vE", ctx.cnt["vE"])]),
            inc="tE")
        ctx.bank_filled(6)
        ctx.emit("tensor", lambda e: e.matmul(
            p.b[7][:], s.ones_f32[:], R[:, 512:1024], start=True, stop=True),
            waits=ctx.bank_acquire(7), inc="tE")
        ctx.bank_filled(7)
        for c in range(NCH):
            sc = s.ln_t[:, (c % 2) * 512:(c % 2) * 512 + 512]
            w = ctx.bank_read_waits(6) + ctx.bank_read_waits(7) if c == 0 else ()
            ctx.emit("vector", lambda e, c=c, sc=sc, t0=t0: e.tensor_sub(
                out=sc, in0=s.x_fm[:, c * T + t0:c * T + t0 + 512], in1=p.b[6][:]),
                waits=w, inc="vE")
            ctx.emit("vector", lambda e, c=c, sc=sc, t0=t0: e.tensor_mul(
                out=s.x_ln[:, c * T + t0:c * T + t0 + 512], in0=sc, in1=p.b[7][:]),
                inc="vE")
        ctx.bank_read_done(6, "vE")
        ctx.bank_read_done(7, "vE")
    ctx.meta["ln_vE"] = ctx.cnt["vE"]



def emit_router(ctx, t, s, p):
    scr = s.ar_f32[:, 12288:20480]
    probs_em = lambda ch: scr[:, ch * T:(ch + 1) * T]
    ptm = lambda tc: scr[:, 2 * T + tc * 160: 2 * T + tc * 160 + 160]
    work = scr[:, 2 * T + 16 * 160: 2 * T + 16 * 160 + 160]
    msk = scr[:, 2 * T + 17 * 160: 2 * T + 18 * 160]
    cols = scr[:, 2 * T + 18 * 160: 2 * T + 18 * 160 + 8]
    ctm = lambda tc: s.wg_sb[0][:, tc * 160: tc * 160 + 160]
    ln_v = ctx.meta["ln_vE"]
    ctx.emit("gpsimd", lambda e: e.dma_start(
        out=s.rw_sb.rearrange("p (kc n) -> p kc n", kc=8),
        in_=t.rwt.rearrange("(kc p) n -> p kc n", p=128)),
        waits=[("tE", ctx.cnt["tE"]), ("vE", ctx.cnt["vE"]),
               ("aE", ctx.cnt.get("aE", 0))], inc="dW", inc_by=16)
    ctx.meta["rw_dW"] = ctx.cnt["dW"]
    for nt in range(4):
        t0 = nt * 512
        for part in range(2):
            M = 128 if part == 0 else 32
            b = 4 + part
            for kc in range(8):
                w = ctx.bank_acquire(b, [("vE", ln_v), ("dW", ctx.meta["rw_dW"])]
                                     ) if kc == 0 else ()
                ctx.emit("tensor", lambda e, kc=kc, part=part, M=M, t0=t0, b=b:
                         e.matmul(p.b[b][0:M, :],
                                  s.rw_sb[:, kc * NE + part * 128:
                                          kc * NE + part * 128 + M],
                                  s.x_ln[:, kc * T + t0: kc * T + t0 + 512],
                                  start=(kc == 0), stop=(kc == 7)),
                         waits=w, inc="tE" if kc == 7 else None)
            ctx.bank_filled(b)
            ctx.emit("scalar", lambda e, part=part, M=M, t0=t0, b=b:
                     e.activation(out=probs_em(part)[0:M, t0:t0 + 512],
                                  in_=p.b[b][0:M, :], func=AF.Sigmoid,
                                  bias=s.eb_sb[0:M, part:part + 1]),
                     waits=ctx.bank_read_waits(b), inc="aE")
            ctx.bank_read_done(b, "aE")
    aq_sig = ctx.cnt["aE"]
    for tc in range(16):
        for part in range(2):
            M = 128 if part == 0 else 32
            b = 6 + (tc * 2 + part) % 2
            ctx.emit("tensor", lambda e, tc=tc, part=part, M=M, b=b:
                     e.transpose(p.b[b][0:128, 0:M],
                                 probs_em(part)[0:M, tc * 128:(tc + 1) * 128],
                                 s.eye_f32[0:M, 0:M]),
                     waits=ctx.bank_acquire(b, [("aE", aq_sig)]), inc="tE")
            ctx.bank_filled(b)
            ctx.emit("vector", lambda e, tc=tc, part=part, M=M, b=b:
                     e.tensor_copy(out=ptm(tc)[:, part * 128:part * 128 + M],
                                   in_=p.b[b][0:128, 0:M]),
                     waits=ctx.bank_read_waits(b), inc="vE")
            ctx.bank_read_done(b, "vE")
    scol = scr[:, 2 * T + 18 * 160: 2 * T + 18 * 160 + 16]
    mxcol = scr[:, 2 * T + 18 * 160 + 16: 2 * T + 18 * 160 + 32]
    sccol = scr[:, 2 * T + 18 * 160 + 32: 2 * T + 18 * 160 + 48]
    def vemit(build):
        vq = ctx.cnt["vE"]
        return ctx.emit("vector", build, waits=[("vE", vq)], inc="vE")
    for tc in range(16):
        vemit(lambda e, tc=tc: e.tensor_copy(out=work, in_=ptm(tc)))
        for it in range(8):
            vemit(lambda e, tc=tc: e.reduce_max(
                out=mxcol[:, tc:tc + 1], in_=work, axis=AX.X))
            if it < 7:
                vemit(lambda e, tc=tc: e.tensor_scalar(
                    out=msk, in0=work, scalar1=mxcol[:, tc:tc + 1], scalar2=None,
                    op0=ALU.is_ge))
                vemit(lambda e: e.scalar_tensor_tensor(
                    out=work, in0=msk, scalar=-1e30, in1=work,
                    op0=ALU.mult, op1=ALU.add))
        vemit(lambda e, tc=tc: e.tensor_scalar(
            out=msk, in0=ptm(tc), scalar1=mxcol[:, tc:tc + 1], scalar2=None,
            op0=ALU.is_ge))
        vemit(lambda e, tc=tc: e.tensor_mul(out=msk, in0=msk, in1=ptm(tc)))
        vemit(lambda e, tc=tc: e.tensor_copy(out=ctm(tc), in_=msk))
        vemit(lambda e, tc=tc: e.reduce_sum(
            out=scol[:, tc:tc + 1], in_=msk, axis=AX.X))
    vemit(lambda e: e.tensor_scalar(
        out=scol, in0=scol, scalar1=1e-9, scalar2=None, op0=ALU.add))
    vemit(lambda e: e.reciprocal(out=sccol, in_=scol))
    vemit(lambda e: e.tensor_scalar(
        out=sccol, in0=sccol, scalar1=2.5, scalar2=None, op0=ALU.mult))
    for tc in range(16):
        vemit(lambda e, tc=tc: e.tensor_scalar(
            out=ctm(tc), in0=ctm(tc), scalar1=sccol[:, tc:tc + 1], scalar2=None,
            op0=ALU.mult))
    vq_c = ctx.cnt["vE"]
    for tc in range(16):
        for part in range(2):
            M = 128 if part == 0 else 32
            b = 6 + (tc * 2 + part) % 2
            pbf = p.b[b].ap().bitcast(BF16)
            ctx.emit("tensor", lambda e, tc=tc, part=part, M=M, pbf=pbf:
                     e.transpose(pbf[0:M, 0:128],
                                 ctm(tc)[:, part * 128:part * 128 + M],
                                 s.eye_bf[:]),
                     waits=ctx.bank_acquire(b, [("vE", vq_c)]), inc="tE")
            ctx.bank_filled(b)
            ctx.emit("vector", lambda e, tc=tc, part=part, M=M, pbf=pbf:
                     e.tensor_copy(
                         out=s.comb_em[0:M, part * T + tc * 128:
                                       part * T + (tc + 1) * 128],
                         in_=pbf[0:M, 0:128]),
                     waits=ctx.bank_read_waits(b), inc="vE")
            ctx.bank_read_done(b, "vE")
    ctx.meta["comb_vE"] = ctx.cnt["vE"]
    ctx.meta["router_tE"] = ctx.cnt["tE"]


def emit_moe(ctx, t, s, p):
    experts = [-1] + list(range(NE))
    groups = [experts[i:i + 2] for i in range(0, len(experts), 2)]
    comb_v = ctx.meta["comb_vE"]
    rt_t = ctx.meta["router_tE"]
    ln_v = ctx.meta["ln_vE"]
    wg_free = [rt_t, 0]
    crow_free = 0
    down_done = 0
    ntile = 0

    for g, grp in enumerate(groups):
        sl = g % 2
        dma_w0 = [("tE", wg_free[sl])]
        for j, e_id in enumerate(grp):
            wg_src = (t.wgst if e_id < 0 else t.wgt[e_id]).rearrange(
                "(kc p) i -> p kc i", p=128)
            wg_dst = s.wg_sb[sl][:, j * 4096:(j + 1) * 4096].rearrange(
                "p (kc i) -> p kc i", kc=8)
            ctx.emit("gpsimd", lambda e, a=wg_dst, b=wg_src: e.dma_start(out=a, in_=b),
                     waits=dma_w0 if j == 0 else (), inc="dW", inc_by=16)
        dW_wg = ctx.cnt["dW"]
        for j, e_id in enumerate(grp):
            wd_src = (t.wdst if e_id < 0 else t.wdt[e_id]).rearrange(
                "(kc p) o -> p kc o", p=128)
            wd_dst = s.wd_sb[:, j * 4096:(j + 1) * 4096].rearrange(
                "p (kc o) -> p kc o", kc=4)
            ctx.emit("gpsimd", lambda e, a=wd_dst, b=wd_src: e.dma_start(out=a, in_=b),
                     waits=[("tE", down_done)] if j == 0 else (), inc="dW", inc_by=16)
            crow_dst = s.crow[j]
            w_cr = [("vE", comb_v)] + ([("tE", crow_free)] if j == 0 else [])
            if e_id < 0:
                ctx.emit("vector", lambda e, a=crow_dst: e.memset(a, 1.0),
                         waits=w_cr, inc="vE")
                ctx.meta["crow_memset_vE"] = ctx.cnt["vE"]
            else:
                ch, r = divmod(e_id, 128)
                csrc = s.comb_em[r:r + 1, ch * T:(ch + 1) * T]
                ctx.emit("sync", lambda e, a=crow_dst, b=csrc: e.dma_start(out=a, in_=b),
                         waits=w_cr, inc="dS", inc_by=16)
        dW_at = ctx.cnt["dW"]
        dS_at = ctx.cnt["dS"]
        for j, e_id in enumerate(grp):
            for nt in range(4):
                t0 = nt * 512
                cb = 2 + nt % 2
                ctx.emit("tensor", lambda e, j=j, t0=t0, cb=cb: e.matmul(
                    p.b[cb][:], s.ones_bf[32 * (j + 1):32 * (j + 1) + 1, :],
                    s.crow[j][:, t0:t0 + 512],
                    start=True, stop=True),
                    waits=ctx.bank_acquire(cb, [("dS", dS_at),
                          ("vE", ctx.meta.get("crow_memset_vE", 0))]), inc="tE")
                ctx.bank_filled(cb)
                for mi in range(4):
                    gb = (nt * 4 + mi) % 2
                    w = ctx.bank_acquire(gb)
                    if nt == 0 and mi == 0 and j == 0:
                        w += [("dW", dW_wg), ("vE", ln_v)]
                    for kc in range(8):
                        ctx.emit("tensor", lambda e, sl=sl, j=j, mi=mi, kc=kc, t0=t0,
                                 gb=gb: e.matmul(
                                     p.b[gb][:],
                                     s.wg_sb[sl][:, j * 4096 + kc * 512 + mi * 128:
                                                 j * 4096 + kc * 512 + mi * 128 + 128],
                                     s.x_ln[:, kc * T + t0:kc * T + t0 + 512],
                                     start=(kc == 0), stop=(kc == 7)),
                                 waits=w if kc == 0 else (),
                                 inc="tE" if kc == 7 else None)
                    ctx.bank_filled(gb)
                    sj = (nt * 4 + mi) % 2
                    sc = s.silu_sc[:, sj * 512:(sj + 1) * 512]
                    ctx.emit("scalar", lambda e, gb=gb, sc=sc: e.activation(
                        out=sc, in_=p.b[gb][:], func=AF.Silu),
                        waits=ctx.bank_read_waits(
                            gb, [("vE", ctx.meta.get("silu_vE_%d" % sj, 0))]),
                        inc="aE")
                    ctx.bank_read_done(gb, "aE")
                    aq = ctx.cnt["aE"]
                    dst = s.h1w[:, (j * 4 + mi) * T + t0:(j * 4 + mi) * T + t0 + 512]
                    w2 = ctx.bank_read_waits(cb, [("aE", aq)])
                    if nt == 0 and mi == 0 and j == 0:
                        w2 += [("tE", down_done)]
                    vq = ctx.emit("vector", lambda e, sc=sc, dst=dst, cb=cb:
                                  e.tensor_tensor(out=dst, in0=sc, in1=p.b[cb][:],
                                                  op=ALU.mult),
                                  waits=w2, inc="vE")
                    ctx.meta["silu_vE_%d" % sj] = vq
                    ctx.bank_read_done(cb, "vE")
        h1w_v = ctx.cnt["vE"]
        wg_free[sl] = ctx.cnt["tE"]
        crow_free = ctx.cnt["tE"]
        for mo in range(8):
            for nt in range(4):
                t0 = nt * 512
                db = 4 + ntile % 2
                nmm = len(grp) * 4
                i = 0
                for j in range(len(grp)):
                    for kc in range(4):
                        first = (i == 0)
                        i += 1
                        w = ()
                        if first:
                            w = ctx.bank_acquire(db)
                            if mo == 0 and nt == 0:
                                w += [("vE", h1w_v), ("dW", dW_at)]
                        ctx.emit("tensor", lambda e, j=j, kc=kc, mo=mo, t0=t0,
                                 db=db, st=first, sp=(i == nmm): e.matmul(
                                     p.b[db][:],
                                     s.wd_sb[:, j * 4096 + kc * 1024 + mo * 128:
                                             j * 4096 + kc * 1024 + mo * 128 + 128],
                                     s.h1w[:, (j * 4 + kc) * T + t0:
                                           (j * 4 + kc) * T + t0 + 512],
                                     start=st, stop=sp),
                                 waits=w, inc="tE" if i == nmm else None)
                ctx.bank_filled(db)
                dst = s.x_fm[:, mo * T + t0:mo * T + t0 + 512]
                ctx.emit("vector", lambda e, db=db, dst=dst:
                         e.scalar_tensor_tensor(out=dst, in0=p.b[db][:], scalar=0.5,
                                                in1=dst, op0=ALU.mult, op1=ALU.add),
                         waits=ctx.bank_read_waits(db), inc="vE")
                ctx.bank_read_done(db, "vE")
                ntile += 1
        down_done = ctx.cnt["tE"]
    ctx.meta["xfm_vE"] = ctx.cnt["vE"]


def emit_dump(ctx, nc, name, ap, cols, dtype, rows=128):
    out = nc.dram_tensor(name, [rows, cols], dtype, kind="ExternalOutput").ap()
    ctx.emit("sync", lambda e: e.dma_start(out=out, in_=ap),
             waits=[("vE", ctx.cnt.get("vE", 0)), ("aE", ctx.cnt.get("aE", 0)),
                    ("tE", ctx.cnt.get("tE", 0)), ("dW", ctx.cnt.get("dW", 0))],
             inc="dS", inc_by=16)


def emit_store_x(ctx, t, s):
    src = s.x_fm.ap().rearrange("p (c t) -> p c t", c=NCH)
    dst = t.xout.rearrange("(c p) t -> p c t", p=128)
    ctx.emit("sync", lambda e: e.dma_start(out=dst, in_=src),
             waits=[("vE", ctx.cnt.get("vE", 0)), ("aE", ctx.cnt.get("aE", 0)),
                    ("tE", ctx.cnt.get("tE", 0))], inc="dS", inc_by=16)
    ctx.emit("sync", lambda e: e.nop(), waits=[("dS", ctx.cnt["dS"])])


def emit_mamba(ctx, t, s, p):
    ln_v = ctx.meta["ln_vE"]
    ent_t, ent_v, ent_a = ctx.cnt["tE"], ctx.cnt["vE"], ctx.cnt.get("aE", 0)
    bar = [("tE", ent_t), ("vE", ent_v), ("aE", ent_a)]
    PI = 3.14159265358979
    TWO_PI = 2 * PI
    INV_2PI = 1.0 / TWO_PI
    MAGIC = 1.5 * 2.0 ** 23

    def ve(build, extra=()):
        vq = ctx.cnt["vE"]
        return ctx.emit("vector", build, waits=[("vE", vq)] + list(extra), inc="vE")

    for dst, srcap in ((s.tri_sb, t.c_tri), (s.minf_sb, t.c_minf),
                       (s.invf_row, t.c_invfreq), (s.dtb_row, t.dt_bias),
                       (s.alog_row, t.a_log), (s.dskip_sb, t.dskip_rep),
                       (s.nw_sb, t.nw_cm)):
        ctx.emit("sync", lambda e, a=dst, b=srcap: e.dma_start(out=a[:], in_=b),
                 waits=bar, inc="dS", inc_by=16)
    dS0 = ctx.cnt["dS"]
    ctx.emit("tensor", lambda e: e.matmul(
        p.b[6][:, 0:32], s.ones_f32[:], s.dtb_row[:], start=True, stop=True),
        waits=ctx.bank_acquire(6, [("dS", dS0)] + bar), inc="tE")
    ctx.bank_filled(6)
    ctx.emit("vector", lambda e: e.tensor_copy(out=s.dtb_bc[:], in_=p.b[6][:, 0:32]),
             waits=ctx.bank_read_waits(6), inc="vE")
    ctx.bank_read_done(6, "vE")
    aq = ctx.emit("scalar", lambda e: e.activation(
        out=s.mrows[:, 0:32], in_=s.alog_row[:], func=AF.Exp),
        waits=[("dS", dS0)] + bar, inc="aE")
    ve(lambda e: e.tensor_scalar(out=s.mrows[:, 0:32], in0=s.mrows[:, 0:32],
                                 scalar1=-1.0, scalar2=None, op0=ALU.mult),
       extra=[("aE", aq)])
    vq = ctx.cnt["vE"]
    ctx.emit("tensor", lambda e: e.matmul(
        p.b[6][:, 0:32], s.ones_f32[:], s.mrows[:, 0:32], start=True, stop=True),
        waits=ctx.bank_acquire(6, [("vE", vq)]), inc="tE")
    ctx.bank_filled(6)
    ctx.emit("vector", lambda e: e.tensor_copy(out=s.nea_bc[:], in_=p.b[6][:, 0:32]),
             waits=ctx.bank_read_waits(6), inc="vE")
    ctx.bank_read_done(6, "vE")
    ve(lambda e: e.memset(s.g_f32, 0.0))
    ve(lambda e: e.memset(s.g_bf, 0.0))

    scan_done_tE = 0
    for h2 in range(2):
        th0 = h2 * 1024
        for mi, m in enumerate(range(16, 34)):
            sl = mi % 2
            mw = 128 if m < 33 else 32
            ctx.emit("gpsimd", lambda e, m=m, sl=sl, mw=mw: e.dma_start(
                out=s.wslab[sl][:, 0:8 * mw].rearrange("p (kc m) -> p kc m", kc=8),
                in_=t.w_in[:, m * 128: m * 128 + mw].rearrange(
                    "(kc p) m -> p kc m", p=128)),
                waits=[("tE", ctx.meta.get("mm_wslab_free_%d" % sl, 0))] +
                (bar if h2 == 0 and mi < 2 else []), inc="dW", inc_by=16)
            dWs = ctx.cnt["dW"]
            for nth in range(2):
                t0 = th0 + nth * 512
                b = (mi * 2 + nth) % 2
                for kc in range(8):
                    w = ()
                    if kc == 0:
                        w = ctx.bank_acquire(b, [("dW", dWs), ("vE", ln_v),
                                                 ("tE", scan_done_tE)])
                    ctx.emit("tensor", lambda e, sl=sl, kc=kc, t0=t0, b=b, mw=mw:
                             e.matmul(p.b[b][0:mw, 0:512],
                                      s.wslab[sl][:, kc * mw:(kc + 1) * mw],
                                      s.x_ln[:, kc * T + t0: kc * T + t0 + 512],
                                      start=(kc == 0), stop=(kc == 7)),
                             waits=w, inc="tE" if kc == 7 else None)
                tq = ctx.cnt["tE"]
                if nth == 1:
                    ctx.meta["mm_wslab_free_%d" % sl] = tq
                ctx.bank_filled(b)
                lt0 = nth * 512
                if m < 32:
                    dst = s.xs_fmh[:, (m - 16) * 1024 + lt0:(m - 16) * 1024 + lt0 + 512]
                    ctx.emit("scalar", lambda e, b=b, dst=dst: e.activation(
                        out=dst, in_=p.b[b][:], func=AF.Identity),
                        waits=ctx.bank_read_waits(b), inc="aE")
                    ctx.bank_read_done(b, "aE")
                elif m == 32:
                    dst = s.bcdt[:, lt0:lt0 + 512]
                    ctx.emit("scalar", lambda e, b=b, dst=dst: e.activation(
                        out=dst, in_=p.b[b][:], func=AF.Identity),
                        waits=ctx.bank_read_waits(b), inc="aE")
                    ctx.bank_read_done(b, "aE")
                else:
                    dst = s.bcdt[0:32, 1024 + lt0:1024 + lt0 + 512]
                    ctx.emit("scalar", lambda e, b=b, dst=dst: e.activation(
                        out=dst, in_=p.b[b][0:32, :], func=AF.Identity),
                        waits=ctx.bank_read_waits(b), inc="aE")
                    ctx.bank_read_done(b, "aE")
        inproj_a = ctx.cnt["aE"]
        for c in range(8):
            gc = h2 * 8 + c
            co = gc * 32
            pbf = p.b[7].ap().bitcast(BF16)
            ctx.emit("tensor", lambda e, c=c, pbf=pbf: e.transpose(
                pbf[0:128, 0:32],
                s.bcdt[0:32, 1024 + c * 128:1024 + (c + 1) * 128],
                s.eye_bf[0:32, 0:32]),
                waits=ctx.bank_acquire(7, [("aE", inproj_a)]), inc="tE")
            ctx.bank_filled(7)
            DT = s.dt_tm[:, co:co + 32]
            SC = s.th_sc[:, 0:32]
            ve(lambda e, DT=DT, pbf=pbf: e.tensor_copy(out=DT, in_=pbf[0:128, 0:32]),
               extra=ctx.bank_read_waits(7))
            ctx.bank_read_done(7, "vE")
            ve(lambda e, DT=DT: e.tensor_add(out=DT, in0=DT, in1=s.dtb_bc[:]))
            vq = ctx.cnt["vE"]
            aq = ctx.emit("scalar", lambda e, DT=DT, SC=SC: e.activation(
                out=SC, in_=DT, func=AF.Abs), waits=[("vE", vq)], inc="aE")
            ve(lambda e, SC=SC: e.tensor_scalar(
                out=SC, in0=SC, scalar1=-1.0, scalar2=None, op0=ALU.mult),
               extra=[("aE", aq)])
            aq = ctx.emit("scalar", lambda e, SC=SC: e.activation(
                out=SC, in_=SC, func=AF.Exp), waits=[("vE", ctx.cnt["vE"])], inc="aE")
            ve(lambda e, DT=DT: e.tensor_relu(out=s.th_sc[:, 32:64], in_=DT),
               extra=[("aE", aq)])
            ve(lambda e, SC=SC: e.tensor_scalar(
                out=SC, in0=SC, scalar1=1.0, scalar2=None, op0=ALU.add))
            aq = ctx.emit("scalar", lambda e, SC=SC: e.activation(
                out=SC, in_=SC, func=AF.Ln), waits=[("vE", ctx.cnt["vE"])], inc="aE")
            ve(lambda e, DT=DT, SC=SC: e.tensor_add(
                out=DT, in0=s.th_sc[:, 32:64], in1=SC), extra=[("aE", aq)])
            CU = s.cumdt[:, co:co + 32]
            vq = ctx.cnt["vE"]
            ctx.emit("tensor", lambda e, DT=DT: e.matmul(
                p.b[7][:, 0:32], s.tri_sb[:], DT, start=True, stop=(gc == 0)),
                waits=ctx.bank_acquire(7, [("vE", vq), ("dS", dS0)]), inc="tE")
            if gc > 0:
                ctx.emit("tensor", lambda e: e.matmul(
                    p.b[7][:, 0:32], s.ones_f32[:], s.mrows[:, 64:96],
                    start=False, stop=True),
                    waits=[("dS", ctx.meta.get("mm_carry_dS", 0))], inc="tE")
                ctx.meta["mm_carry_used"] = ctx.cnt["tE"]
            ctx.bank_filled(7)
            ve(lambda e, CU=CU: e.tensor_copy(out=CU, in_=p.b[7][:, 0:32]),
               extra=ctx.bank_read_waits(7))
            ctx.bank_read_done(7, "vE")
            LL = s.L_tm[:, co:co + 32]
            ve(lambda e, LL=LL, CU=CU: e.tensor_mul(out=LL, in0=CU, in1=s.nea_bc[:]))
            vq = ctx.cnt["vE"]
            ctx.emit("sync", lambda e, CU=CU: e.dma_start(
                out=s.mrows[:, 64:96], in_=CU[127:128, :]),
                waits=[("vE", vq), ("tE", ctx.meta.get("mm_carry_used", 0))],
                inc="dS", inc_by=16)
            ctx.meta["mm_carry_dS"] = ctx.cnt["dS"]
        prep_v = ctx.cnt["vE"]
        for c in range(8):
            gc = h2 * 8 + c
            co = gc * 32
            qlocal = c % 4
            ctx.emit("tensor", lambda e: e.matmul(
                p.b[6][:], s.ones_f32[:], s.invf_row[:], start=True, stop=True),
                waits=ctx.bank_acquire(6, [("vE", prep_v)]), inc="tE")
            ctx.bank_filled(6)
            cum_exp = s.cumdt[:, co:co + 32].rearrange("p (h one) -> p h one", one=1).broadcast_to(
                [128, 32, 16])
            TH = s.th_sc[:, 0:512]
            TH3 = TH.rearrange("p (h j) -> p h j", h=32)
            P63 = p.b[6].ap().rearrange("p (h j) -> p h j", h=32)
            ve(lambda e, cum_exp=cum_exp, TH3=TH3, P63=P63: e.tensor_mul(
                out=TH3, in0=cum_exp, in1=P63),
               extra=ctx.bank_read_waits(6))
            ctx.bank_read_done(6, "vE")
            CSF = s.cs_c.bitcast(F32)
            ve(lambda e, TH=TH, CSF=CSF: e.tensor_scalar(
                out=CSF, in0=TH, scalar1=INV_2PI, scalar2=MAGIC,
                op0=ALU.mult, op1=ALU.add),
               extra=[("aE", ctx.meta.get("mm_cs_aE", 0)),
                      ("tE", ctx.meta.get("mm_fm_used_1", 0))])
            ve(lambda e, CSF=CSF: e.tensor_scalar(
                out=CSF, in0=CSF, scalar1=MAGIC, scalar2=None,
                op0=ALU.subtract))
            ve(lambda e, TH=TH, CSF=CSF: e.scalar_tensor_tensor(
                out=TH, in0=CSF, scalar=-TWO_PI, in1=TH,
                op0=ALU.mult, op1=ALU.add))
            vq = ctx.cnt["vE"]
            aq = ctx.emit("scalar", lambda e, TH=TH: e.activation(
                out=s.cs_c[:, 512:1024], in_=TH, func=AF.Sin),
                waits=[("vE", vq)], inc="aE")
            aq2 = ctx.emit("scalar", lambda e, TH=TH: e.activation(
                out=TH, in_=TH, func=AF.Abs), waits=[("aE", aq),
                ("vE", ctx.cnt["vE"])], inc="aE")
            ve(lambda e, TH=TH: e.tensor_scalar(
                out=TH, in0=TH, scalar1=-1.0, scalar2=PI / 2,
                op0=ALU.mult, op1=ALU.add), extra=[("aE", aq2)])
            vq = ctx.cnt["vE"]
            aq_cs = ctx.emit("scalar", lambda e, TH=TH: e.activation(
                out=s.cs_c[:, 0:512], in_=TH, func=AF.Sin),
                waits=[("vE", vq)], inc="aE")
            ctx.meta["mm_cs_aE"] = aq_cs
            pbf7 = p.b[7].ap().bitcast(BF16)
            ctx.emit("tensor", lambda e, c=c, pbf7=pbf7: e.transpose(
                pbf7[0:128, 0:128], s.bcdt[:, c * 128:(c + 1) * 128],
                s.eye_bf[:]),
                waits=ctx.bank_acquire(7, [("aE", inproj_a)]), inc="tE")
            ctx.bank_filled(7)
            cosap = s.cs_c[:, 0:512].rearrange("p (h j) -> p h j", h=32)
            sinap = s.cs_c[:, 512:1024].rearrange("p (h j) -> p h j", h=32)
            Bev = pbf7[0:128, 0:64:2].rearrange("p (one j) -> p one j", one=1).broadcast_to(
                [128, 32, 32])
            Bod = pbf7[0:128, 1:64:2].rearrange("p (one j) -> p one j", one=1).broadcast_to(
                [128, 32, 32])
            Bnr = pbf7[0:128, 32:64].rearrange("p (one n) -> p one n", one=1).broadcast_to(
                [128, 32, 32])
            Cev = pbf7[0:128, 64:128:2].rearrange("p (one j) -> p one j", one=1).broadcast_to(
                [128, 32, 32])
            Cod = pbf7[0:128, 65:128:2].rearrange("p (one j) -> p one j", one=1).broadcast_to(
                [128, 32, 32])
            Cnr = pbf7[0:128, 96:128].rearrange("p (one n) -> p one n", one=1).broadcast_to(
                [128, 32, 32])
            for dst_t, ev, od in ((s.bhat, Bev, Bod), (s.ctil, Cev, Cod)):
                de = dst_t.rearrange("p (h n) -> p h n", h=32)[:, :, 0:31:2]
                do = dst_t.rearrange("p (h n) -> p h n", h=32)[:, :, 1:32:2]
                dn = dst_t.rearrange("p (h n) -> p h n", h=32)[:, :, 32:64]
                evj = ev[:, :, 0:16]
                odj = od[:, :, 0:16]
                scr2 = s.fmscr.rearrange("p (h j) -> p h j", h=32)
                cos3 = cosap
                sin3 = sinap
                ve(lambda e, de=de, evj=evj, cos3=cos3: e.tensor_tensor(
                    out=de, in0=cos3, in1=evj, op=ALU.mult),
                   extra=[("aE", aq_cs)] + ctx.bank_read_waits(7) +
                   [("tE", ctx.meta.get("mm_fmscr_used", 0))])
                ve(lambda e, scr2=scr2, odj=odj, sin3=sin3: e.tensor_tensor(
                    out=scr2, in0=sin3, in1=odj, op=ALU.mult))
                ve(lambda e, de=de, scr2=scr2: e.tensor_tensor(
                    out=de, in0=de, in1=scr2, op=ALU.add))
                ve(lambda e, do=do, odj=odj, cos3=cos3: e.tensor_tensor(
                    out=do, in0=cos3, in1=odj, op=ALU.mult))
                ve(lambda e, scr2=scr2, evj=evj, sin3=sin3: e.tensor_tensor(
                    out=scr2, in0=sin3, in1=evj, op=ALU.mult))
                ve(lambda e, do=do, scr2=scr2: e.tensor_tensor(
                    out=do, in0=do, in1=scr2, op=ALU.subtract))
                ve(lambda e, dn=dn, nr=(Bnr if dst_t is s.bhat else Cnr):
                   e.tensor_copy(out=dn, in_=nr))
            ctx.bank_read_done(7, "vE")
            dt_exp = s.dt_tm[:, co:co + 32].rearrange("p (h one) -> p h one", one=1).broadcast_to(
                [128, 32, 64])
            bhat3 = s.bhat.rearrange("p (h n) -> p h n", h=32)
            ve(lambda e, dt_exp=dt_exp, bhat3=bhat3: e.tensor_tensor(
                out=bhat3, in0=bhat3, in1=dt_exp, op=ALU.mult))
            if gc == 0:
                vq_pe = ve(lambda e: e.memset(s.mrows[:, 96:128], 0.0))
                ctx.meta["mm_pe_v"] = vq_pe
            else:
                pco = (gc - 1) * 32
                ctx.emit("sync", lambda e, pco=pco: e.dma_start(
                    out=s.mrows[:, 96:128], in_=s.L_tm[127:128, pco:pco + 32]),
                    waits=[("vE", prep_v), ("tE", ctx.meta.get("mm_rowbc_tE", 0))],
                    inc="dS", inc_by=16)
            ctx.emit("sync", lambda e, co=co: e.dma_start(
                out=s.mrows[:, 128:160], in_=s.L_tm[127:128, co:co + 32]),
                waits=[("vE", prep_v), ("tE", ctx.meta.get("mm_rowbc_tE", 0))],
                inc="dS", inc_by=16)
            dSr = ctx.cnt["dS"]
            ctx.emit("tensor", lambda e: e.matmul(
                p.b[6][:, 0:64], s.ones_f32[:],
                s.mrows[:, 96:160], start=True, stop=True),
                waits=ctx.bank_acquire(6, [("dS", dSr),
                      ("vE", ctx.meta.get("mm_pe_v", 0))]), inc="tE")
            ctx.bank_filled(6)
            ctx.meta["mm_rowbc_tE"] = ctx.cnt["tE"]
            COLF = s.th_sc[:, 0:32]
            FALL = s.th_sc[:, 32:64]
            ve(lambda e, COLF=COLF, co=co: e.tensor_sub(
                out=COLF, in0=s.L_tm[:, co:co + 32], in1=p.b[6][:, 0:32]),
               extra=ctx.bank_read_waits(6) + [("aE", aq_cs)])
            ve(lambda e, FALL=FALL, co=co: e.tensor_sub(
                out=FALL, in0=p.b[6][:, 32:64], in1=s.L_tm[:, co:co + 32]))
            ctx.bank_read_done(6, "vE")
            vq = ctx.cnt["vE"]
            aq = ctx.emit("scalar", lambda e, COLF=COLF: e.activation(
                out=COLF, in_=COLF, func=AF.Exp), waits=[("vE", vq)], inc="aE")
            aq = ctx.emit("scalar", lambda e, FALL=FALL: e.activation(
                out=FALL, in_=FALL, func=AF.Exp), inc="aE")
            colf_exp = COLF.rearrange("p (h one) -> p h one", one=1).broadcast_to([128, 32, 64])
            cbar3 = s.cbar.rearrange("p (h n) -> p h n", h=32)
            ctil3 = s.ctil.rearrange("p (h n) -> p h n", h=32)
            ve(lambda e, colf_exp=colf_exp, cbar3=cbar3, ctil3=ctil3:
               e.tensor_tensor(out=cbar3, in0=ctil3, in1=colf_exp, op=ALU.mult),
               extra=[("aE", aq)])
            ve(lambda e: e.tensor_sub(out=s.mrows[:, 160:192],
                                      in0=s.mrows[:, 128:160],
                                      in1=s.mrows[:, 96:128]),
               extra=[("dS", dSr)])
            vq = ctx.cnt["vE"]
            aq = ctx.emit("scalar", lambda e: e.activation(
                out=s.mrows[:, 160:192], in_=s.mrows[:, 160:192], func=AF.Exp),
                waits=[("vE", vq)], inc="aE")
            ctx.emit("tensor", lambda e: e.matmul(
                p.b[6][:, 64:96], s.ones_f32[:], s.mrows[:, 160:192],
                start=True, stop=True),
                waits=ctx.bank_acquire(6, [("aE", aq)]), inc="tE")
            ctx.bank_filled(6)
            GSC = s.th_sc[:, 64:96]
            ve(lambda e, GSC=GSC: e.tensor_copy(out=GSC[0:64, :],
                                                in_=p.b[6][0:64, 64:96]),
               extra=ctx.bank_read_waits(6))
            ctx.bank_read_done(6, "vE")
            ctx.emit("tensor", lambda e, co=co: e.transpose(
                p.b[6].ap()[0:32, 0:128], s.L_tm[:, co:co + 32], s.eye_f32[:]),
                waits=ctx.bank_acquire(6), inc="tE")
            ctx.bank_filled(6)
            ve(lambda e: e.tensor_copy(out=s.L_emc[:], in_=p.b[6].ap()[0:32, 0:128]),
               extra=ctx.bank_read_waits(6))
            ctx.bank_read_done(6, "vE")
            chunk_v = ctx.cnt["vE"]
            chunk_a = ctx.cnt["aE"]
            p2bf = [p.b[2].ap().bitcast(BF16), p.b[6].ap().bitcast(BF16)]
            DSBs = [s.th_sc[:, 128:256], s.th_sc[:, 256:384]]
            FMs = [s.fmscr[:, 0:448], s.cs_c[:, 0:448]]
            ATs = [s.attnT_m[:, 0:128], s.bcdt[:, 1024:1152]]
            ybank = [3, 0]
            stbank = [7, 1]
            scbank = [4, 5]
            trbank = [2, 6]
            headmeta = {}

            def stageA(h):
                sl = h % 2
                base = (h % 2) * 64
                tb = trbank[sl]
                pb = p2bf[sl]
                DSB = DSBs[sl]
                ctx.emit("sync", lambda e, h=h: e.dma_start(
                    out=s.mrows[:, 192 + sl * 128:320 + sl * 128],
                    in_=s.L_emc[h:h + 1, :]),
                    waits=[("vE", chunk_v),
                           ("tE", ctx.meta.get("mm_d_tE_%d" % sl, 0))],
                    inc="dS", inc_by=16)
                dSh = ctx.cnt["dS"]
                ctx.emit("tensor", lambda e, sl=sl: e.matmul(
                    p.b[tb][:, 0:128], s.ones_f32[:],
                    s.mrows[:, 192 + sl * 128:320 + sl * 128],
                    start=True, stop=True),
                    waits=ctx.bank_acquire(tb, [("dS", dSh)]), inc="tE")
                xs_src = s.xs_fmh[base:base + 64,
                                  (h // 2) * 1024 + c * 128:
                                  (h // 2) * 1024 + (c + 1) * 128]
                ctx.emit("tensor", lambda e, xs_src=xs_src, pb=pb, base=base:
                         e.transpose(pb[0:128, 256:320], xs_src,
                                     s.eye_bf[base:base + 64, base:base + 64]),
                         waits=[("vE", chunk_v)], inc=None)
                ctx.emit("tensor", lambda e, h=h, pb=pb: e.transpose(
                    pb[0:64, 320:448], s.bhat[:, h * 64:(h + 1) * 64],
                    s.eye_bf[:]), inc=None)
                ctx.emit("tensor", lambda e, h=h, pb=pb: e.transpose(
                    pb[0:64, 448:576], s.ctil[:, h * 64:(h + 1) * 64],
                    s.eye_bf[:]), inc=None)
                ctx.emit("tensor", lambda e, h=h, pb=pb: e.transpose(
                    pb[0:64, 576:704], s.cbar[:, h * 64:(h + 1) * 64],
                    s.eye_bf[:]), inc="tE")
                ctx.bank_filled(tb)
                ctx.meta["mm_d_tE_%d" % sl] = ctx.cnt["tE"]
                ve(lambda e, DSB=DSB, co=co, h=h, tb=tb: e.tensor_scalar(
                    out=DSB, in0=p.b[tb][:, 0:128],
                    scalar1=s.L_tm[:, co + h:co + h + 1], scalar2=None,
                    op0=ALU.subtract), extra=ctx.bank_read_waits(tb))
                ve(lambda e, DSB=DSB: e.tensor_add(out=DSB, in0=DSB,
                                                   in1=s.minf_sb[:]))
                vq = ctx.cnt["vE"]
                aqD = ctx.emit("scalar", lambda e, DSB=DSB: e.activation(
                    out=DSB, in_=DSB, func=AF.Exp), waits=[("vE", vq)], inc="aE")
                FM = FMs[sl]
                fm_v = ctx.emit("scalar", lambda e, pb=pb, FM=FM: e.copy(
                    out=FM, in_=pb[0:128, 256:704]),
                    waits=ctx.bank_read_waits(tb) +
                    [("tE", ctx.meta.get("mm_fm_used_%d" % sl, 0))], inc="aE")
                ctx.bank_read_done(tb, "aE")
                ve(lambda e, h=h, FALL=FALL: e.tensor_scalar(
                    out=s.bhat[:, h * 64:(h + 1) * 64],
                    in0=s.bhat[:, h * 64:(h + 1) * 64],
                    scalar1=FALL[:, h:h + 1], scalar2=None, op0=ALU.mult))
                headmeta[h] = (fm_v, ctx.cnt["vE"], aqD)

            def stageB(h):
                sl = h % 2
                base = (h % 2) * 64
                fm_v, bbar_v, aqD = headmeta[h]
                FM = FMs[sl]
                DSB = DSBs[sl]
                AT = ATs[sl]
                sb_ = scbank[sl]
                yb = ybank[sl]
                stb = stbank[sl]
                ctx.emit("tensor", lambda e, sb_=sb_, FM=FM: e.matmul(
                    p.b[sb_][:, 0:128], FM[0:64, 64:192],
                    FM[0:64, 192:320], start=True, stop=True),
                    waits=ctx.bank_acquire(sb_, [("aE", fm_v)]), inc="tE")
                ctx.bank_filled(sb_)
                ve(lambda e, sb_=sb_, DSB=DSB, AT=AT: e.tensor_tensor(
                    out=AT, in0=p.b[sb_][:, 0:128], in1=DSB, op=ALU.mult),
                   extra=ctx.bank_read_waits(sb_) +
                   [("aE", aqD), ("tE", ctx.meta.get("mm_at_used_%d" % sl, 0))])
                ctx.bank_read_done(sb_, "vE")
                ve(lambda e, h=h, AT=AT: e.scalar_tensor_tensor(
                    out=AT, in0=s.eye_bf[:],
                    scalar=s.dskip_sb[:, h:h + 1], in1=AT,
                    op0=ALU.mult, op1=ALU.add))
                at_v = ctx.cnt["vE"]
                ctx.emit("tensor", lambda e, h=h, FM=FM, yb=yb: e.matmul(
                    p.b[yb][0:64, 0:128], s.g_bf[:, h * 64:(h + 1) * 64],
                    FM[0:64, 320:448], start=True, stop=False),
                    waits=ctx.bank_acquire(yb, [("vE", at_v),
                          ("aE", ctx.meta.get("mm_gcast_a", 0))]), inc=None)
                ctx.emit("tensor", lambda e, FM=FM, AT=AT, yb=yb: e.matmul(
                    p.b[yb][0:64, 0:128], FM[:, 0:64],
                    AT, start=False, stop=True), inc="tE")
                ctx.bank_filled(yb)
                ctx.meta["mm_at_used_%d" % sl] = ctx.cnt["tE"]
                ydst = s.y_q[base:base + 64,
                             (h // 2) * 512 + qlocal * 128:
                             (h // 2) * 512 + (qlocal + 1) * 128]
                ctx.emit("scalar", lambda e, ydst=ydst, yb=yb: e.copy(
                    out=ydst, in_=p.b[yb][0:64, 0:128]),
                    waits=ctx.bank_read_waits(yb), inc="aE")
                ctx.bank_read_done(yb, "aE")
                ctx.emit("tensor", lambda e, h=h, FM=FM, stb=stb: e.matmul(
                    p.b[stb][0:64, 0:64], s.bhat[:, h * 64:(h + 1) * 64],
                    FM[:, 0:64], start=True, stop=True),
                    waits=ctx.bank_acquire(stb, [("vE", bbar_v)]), inc="tE")
                ctx.bank_filled(stb)
                ctx.meta["mm_fm_used_%d" % sl] = ctx.cnt["tE"]
                gs = s.g_f32[:, h * 64:(h + 1) * 64]
                ve(lambda e, gs=gs, GSC=GSC, h=h: e.tensor_scalar(
                    out=gs, in0=gs, scalar1=GSC[0:64, h:h + 1], scalar2=None,
                    op0=ALU.mult))
                ve(lambda e, gs=gs, stb=stb: e.tensor_add(
                    out=gs, in0=gs, in1=p.b[stb][0:64, 0:64]),
                   extra=ctx.bank_read_waits(stb))
                ctx.bank_read_done(stb, "vE")
                vq = ctx.cnt["vE"]
                ctx.emit("scalar", lambda e, gs=gs, h=h: e.copy(
                    out=s.g_bf[:, h * 64:(h + 1) * 64], in_=gs),
                    waits=[("vE", vq)], inc="aE")
                ctx.meta["mm_gcast_a"] = ctx.cnt["aE"]

            for h in range(33):
                if h < 32:
                    stageA(h)
                if h >= 1:
                    stageB(h - 1)
            if qlocal == 3:
                emit_mamba_zphase(ctx, t, s, p, h2, (c // 4), dS0)
        scan_done_tE = ctx.cnt["tE"]
    ctx.meta["xfm_vE"] = ctx.cnt["vE"]


def emit_mamba_zphase(ctx, t, s, p, h2, qh, dS0):
    ln_v = ctx.meta["ln_vE"]
    tq0 = h2 * 1024 + qh * 512
    ve = None
    def ve(build, extra=()):
        vq = ctx.cnt["vE"]
        return ctx.emit("vector", build, waits=[("vE", vq)] + list(extra), inc="vE")
    yq_v = ctx.cnt["vE"]
    zq_a = ctx.cnt["aE"]
    for m in range(16):
        sl = m % 2
        ctx.emit("gpsimd", lambda e, m=m, sl=sl: e.dma_start(
            out=s.wslab[sl].rearrange("p (kc m) -> p kc m", kc=8),
            in_=t.w_in[:, m * 128:(m + 1) * 128].rearrange(
                "(kc p) m -> p kc m", p=128)),
            waits=[("tE", ctx.meta.get("mm_wslab_free_%d" % sl, 0))],
            inc="dW", inc_by=16)
        dWs = ctx.cnt["dW"]
        b = m % 2
        for kc in range(8):
            w = ()
            if kc == 0:
                w = ctx.bank_acquire(b, [("dW", dWs), ("vE", yq_v)])
            ctx.emit("tensor", lambda e, sl=sl, kc=kc, b=b: e.matmul(
                p.b[b][:], s.wslab[sl][:, kc * 128:(kc + 1) * 128],
                s.x_ln[:, kc * T + tq0: kc * T + tq0 + 512],
                start=(kc == 0), stop=(kc == 7)), waits=w,
                inc="tE" if kc == 7 else None)
        ctx.meta["mm_wslab_free_%d" % sl] = ctx.cnt["tE"]
        ctx.bank_filled(b)
        zscr = s.cs_c[:, 0:512]
        aq = ctx.emit("scalar", lambda e, b=b, zscr=zscr: e.activation(
            out=zscr, in_=p.b[b][:], func=AF.Silu),
            waits=ctx.bank_read_waits(b, [("aE", ctx.meta.get("mm_z_aq", 0)),
                                          ("vE", ctx.meta.get("mm_z_vq", 0))]),
            inc="aE")
        ctx.bank_read_done(b, "aE")
        ctx.meta["mm_z_aq"] = aq
        yslice = s.y_q[:, m * 512:(m + 1) * 512]
        vq = ctx.emit("vector", lambda e, yslice=yslice, zscr=zscr: e.tensor_mul(
            out=yslice, in0=yslice, in1=zscr), waits=[("aE", aq)], inc="vE")
        ctx.meta["mm_z_vq"] = vq
        sqscr = s.cs_c[:, 512:1024]
        aq = ctx.emit("scalar", lambda e, yslice=yslice, sqscr=sqscr: e.activation(
            out=sqscr, in_=yslice, func=AF.Square), waits=[("vE", vq)], inc="aE")
        ctx.emit("tensor", lambda e, m=m, sqscr=sqscr: e.matmul(
            p.b[4][0:1, :], s.onescol_bf[:], sqscr,
            start=(m == 0), stop=(m == 15)),
            waits=ctx.bank_acquire(4, [("aE", aq)]) if m == 0 else [("aE", aq)],
            inc="tE")
    ctx.bank_filled(4)
    R = s.th_sc[0:1, 0:512]
    ve(lambda e, R=R: e.tensor_scalar(
        out=R, in0=p.b[4][0:1, :], scalar1=1.0 / 2048.0, scalar2=1e-5,
        op0=ALU.mult, op1=ALU.add),
       extra=ctx.bank_read_waits(4) + [("aE", zq_a)])
    ctx.bank_read_done(4, "vE")
    vq = ctx.cnt["vE"]
    aq = ctx.emit("scalar", lambda e, R=R: e.activation(out=R, in_=R, func=AF.Ln),
                  waits=[("vE", vq)], inc="aE")
    ve(lambda e, R=R: e.tensor_scalar(
        out=R, in0=R, scalar1=-0.5, scalar2=None, op0=ALU.mult),
       extra=[("aE", aq)])
    vq = ctx.cnt["vE"]
    aq = ctx.emit("scalar", lambda e, R=R: e.activation(out=R, in_=R, func=AF.Exp),
                  waits=[("vE", vq)], inc="aE")
    ctx.emit("tensor", lambda e, R=R: e.matmul(
        p.b[5][:], s.ones_f32[:], R, start=True, stop=True),
        waits=ctx.bank_acquire(5, [("aE", aq)]), inc="tE")
    ctx.bank_filled(5)
    for m in range(16):
        yslice = s.y_q[:, m * 512:(m + 1) * 512]
        ctx.emit("vector", lambda e, yslice=yslice: e.tensor_mul(
            out=yslice, in0=yslice, in1=p.b[5][:]),
            waits=ctx.bank_read_waits(5) if m == 0 else (), inc="vE")
        ctx.emit("vector", lambda e, yslice=yslice, m=m: e.tensor_scalar(
            out=yslice, in0=yslice, scalar1=s.nw_sb[:, m:m + 1], scalar2=None,
            op0=ALU.mult), inc="vE")
    ctx.bank_read_done(5, "vE")
    yn_v = ctx.cnt["vE"]
    for kc in range(16):
        sl = kc % 2
        ctx.emit("gpsimd", lambda e, kc=kc, sl=sl: e.dma_start(
            out=s.wslab[sl][:, 0:1024],
            in_=t.w_out[kc * 128:(kc + 1) * 128, :]),
            waits=[("tE", ctx.meta.get("mm_wslab_free_%d" % sl, 0))],
            inc="dW", inc_by=16)
        dWs = ctx.cnt["dW"]
        for mo in range(8):
            w = [("dW", dWs)]
            if kc == 0:
                w += ctx.bank_acquire(mo, [("vE", yn_v)])
            ctx.emit("tensor", lambda e, kc=kc, sl=sl, mo=mo: e.matmul(
                p.b[mo][:], s.wslab[sl][:, mo * 128:(mo + 1) * 128],
                s.y_q[:, kc * 512:(kc + 1) * 512],
                start=(kc == 0), stop=(kc == 15)),
                waits=w if (kc == 0 or mo == 0) else (),
                inc="tE" if (kc == 15 or mo == 7) else None)
        ctx.meta["mm_wslab_free_%d" % sl] = ctx.cnt["tE"]
    for mo in range(8):
        ctx.bank_filled(mo)
        dst = s.x_fm[:, mo * T + tq0: mo * T + tq0 + 512]
        ctx.emit("vector", lambda e, mo=mo, dst=dst: e.tensor_add(
            out=dst, in0=p.b[mo][:], in1=dst),
            waits=ctx.bank_read_waits(mo), inc="vE")
        ctx.bank_read_done(mo, "vE")
    ctx.meta["mm_yq_free"] = ctx.cnt["vE"]


def emit_attn(ctx, t, s, p):
    ln_v = ctx.meta["ln_vE"]
    ent_t, ent_v, ent_a = ctx.cnt["tE"], ctx.cnt["vE"], ctx.cnt.get("aE", 0)
    bar = [("tE", ent_t), ("vE", ent_v), ("aE", ent_a)]
    atc = s.atc
    ctx.emit("sync", lambda e: e.dma_start(out=s.maskband, in_=t.c_maskband),
             waits=bar, inc="dS", inc_by=16)
    ctx.emit("sync", lambda e: e.dma_start(out=s.ba_sb, in_=t.ba_cm),
             waits=bar, inc="dS", inc_by=16)
    ctx.emit("sync", lambda e: e.dma_start(out=s.bp_sb, in_=t.bp_cm),
             waits=bar, inc="dS", inc_by=16)
    dS0 = ctx.cnt["dS"]
    ctx.emit("vector", lambda e: e.tensor_scalar(
        out=s.qba_sb, in0=s.ba_sb[:, 0:8], scalar1=0.125, scalar2=None,
        op0=ALU.mult), waits=[("dS", dS0)] + bar, inc="vE")
    qba_v = ctx.cnt["vE"]
    for hg in range(4):
        for hh in range(4):
            h = hg * 4 + hh
            ctx.emit("gpsimd", lambda e, h=h, hh=hh: e.dma_start(
                out=s.wproj_sb[0:64, hh * 1024:(hh + 1) * 1024],
                in_=t.w_proj[h * 64:(h + 1) * 64, :]),
                waits=(bar + [("tE", ctx.meta.get("at_wproj_free", 0))]) if hh == 0
                else (), inc="dW", inc_by=16)
        dWp = ctx.cnt["dW"]
        for lc in range(6):
            m_abs = [2 * hg, 2 * hg + 1, 8 + 2 * hg, 9 + 2 * hg,
                     16 + 2 * hg, 17 + 2 * hg][lc]
            sl = lc % 2
            ctx.emit("gpsimd", lambda e, m_abs=m_abs, sl=sl: e.dma_start(
                out=s.wattn_sb[sl].rearrange("p (kc m) -> p kc m", kc=8),
                in_=t.w_attn[:, m_abs * 128:(m_abs + 1) * 128].rearrange(
                    "(kc p) m -> p kc m", p=128)),
                waits=[("tE", ctx.meta.get("at_wattn_free_%d" % sl, 0))] +
                (bar if hg == 0 and lc < 2 else []), inc="dW", inc_by=16)
            dWa = ctx.cnt["dW"]
            for nt in range(4):
                t0 = nt * 512
                b = (lc * 4 + nt) % 2
                for kc in range(8):
                    w = ()
                    if kc == 0:
                        w = ctx.bank_acquire(b, [("dW", dWa), ("vE", ln_v)])
                    ctx.emit("tensor", lambda e, sl=sl, kc=kc, t0=t0, b=b: e.matmul(
                        p.b[b][:], s.wattn_sb[sl][:, kc * 128:(kc + 1) * 128],
                        s.x_ln[:, kc * T + t0: kc * T + t0 + 512],
                        start=(kc == 0), stop=(kc == 7)), waits=w,
                        inc="tE" if kc == 7 else None)
                tq = ctx.cnt["tE"]
                if nt == 3:
                    ctx.meta["at_wattn_free_%d" % sl] = tq
                ctx.bank_filled(b)
                m_abs = [2 * hg, 2 * hg + 1, 8 + 2 * hg, 9 + 2 * hg,
                         16 + 2 * hg, 17 + 2 * hg][lc]
                wq = [("tE", ctx.meta.get("at_qkv_free", 0))] if (lc == 0 and
                                                                 nt == 0) else []
                if lc < 2:
                    ctx.emit("vector", lambda e, lc=lc, t0=t0, b=b, m_abs=m_abs:
                             e.tensor_scalar(
                                 out=s.qkv_g[:, lc * T + t0: lc * T + t0 + 512],
                                 in0=p.b[b][:], scalar1=0.125,
                                 scalar2=s.qba_sb[:, m_abs:m_abs + 1],
                                 op0=ALU.mult, op1=ALU.add),
                             waits=ctx.bank_read_waits(
                                 b, [("dS", dS0), ("vE", qba_v)] + wq), inc="vE")
                    ctx.bank_read_done(b, "vE")
                else:
                    ctx.emit("scalar", lambda e, lc=lc, t0=t0, b=b, m_abs=m_abs:
                             e.activation(
                                 out=s.qkv_g[:, lc * T + t0: lc * T + t0 + 512],
                                 in_=p.b[b][:], func=AF.Identity,
                                 bias=s.ba_sb[:, m_abs:m_abs + 1]),
                             waits=ctx.bank_read_waits(b, [("dS", dS0)] + wq),
                             inc="aE")
                    ctx.bank_read_done(b, "aE")
        qkv_a = ctx.cnt["aE"]
        qkv_v = ctx.cnt["vE"]
        for hh in range(4):
            base = (hh % 2) * 64
            qc = (hh // 2) * T
            kc_off = (2 + hh // 2) * T
            vc = (4 + hh // 2) * T
            p3bf = p.b[3].ap().bitcast(BF16)
            for tc in range(16):
                ctx.emit("tensor", lambda e, tc=tc, base=base, vc=vc, p3bf=p3bf:
                         e.transpose(p3bf[0:128, tc * 64:(tc + 1) * 64],
                                     s.qkv_g[base:base + 64,
                                             vc + tc * 128: vc + (tc + 1) * 128],
                                     s.eye_bf[base:base + 64, base:base + 64]),
                         waits=ctx.bank_acquire(3, [("aE", qkv_a),
                                                    ("vE", qkv_v)]) if tc == 0
                         else (), inc="tE" if tc == 15 else None)
            ctx.bank_filled(3)
            ctx.emit("vector", lambda e, p3bf=p3bf: e.tensor_copy(
                out=s.v_tm[:], in_=p3bf[0:128, 0:1024]),
                waits=ctx.bank_read_waits(3), inc="vE")
            ctx.bank_read_done(3, "vE")
            vtm_v = ctx.cnt["vE"]
            qmeta = {}

            def stageA(qt):
                sl = qt % 2
                ab = sl * 12
                for nt in range(4):
                    b = 4 + nt
                    off = 2048 + nt * 512 - qt * 128
                    ctx.emit("tensor", lambda e, base=base, qc=qc, kc_off=kc_off,
                             qt=qt, nt=nt, b=b: e.matmul(
                                 p.b[b][:],
                                 s.qkv_g[base:base + 64,
                                         qc + qt * 128: qc + (qt + 1) * 128],
                                 s.qkv_g[base:base + 64,
                                         kc_off + nt * 512: kc_off + nt * 512 + 512],
                                 start=True, stop=False),
                             waits=ctx.bank_acquire(b, [("vE", vtm_v)] if qt == 0
                                                    and nt == 0 else ()), inc=None)
                    ctx.emit("tensor", lambda e, off=off, b=b: e.matmul(
                        p.b[b][:], s.eye_bf[:],
                        s.maskband[:, off:off + 512], start=False, stop=True),
                        inc="tE")
                    ctx.bank_filled(b)
                for nt in range(4):
                    b = 4 + nt
                    ctx.emit("vector", lambda e, nt=nt, b=b, ab=ab: e.reduce_max(
                        out=atc[:, ab + nt:ab + nt + 1], in_=p.b[b][:], axis=AX.X),
                        waits=ctx.bank_read_waits(b) +
                        [("aE", qmeta.get(("exp", sl), 0))], inc="vE")
                vq = ctx.cnt["vE"]
                vq = ctx.emit("vector", lambda e, ab=ab: e.tensor_max(
                    out=atc[:, ab + 11:ab + 12], in0=atc[:, ab:ab + 1],
                    in1=atc[:, ab + 1:ab + 2]), waits=[("vE", vq)], inc="vE")
                vq = ctx.emit("vector", lambda e, ab=ab: e.tensor_max(
                    out=atc[:, ab + 4:ab + 5], in0=atc[:, ab + 2:ab + 3],
                    in1=atc[:, ab + 3:ab + 4]), waits=[("vE", vq)], inc="vE")
                vq = ctx.emit("vector", lambda e, ab=ab: e.tensor_max(
                    out=atc[:, ab + 11:ab + 12], in0=atc[:, ab + 11:ab + 12],
                    in1=atc[:, ab + 4:ab + 5]), waits=[("vE", vq)], inc="vE")
                vq = ctx.emit("vector", lambda e, ab=ab: e.tensor_scalar(
                    out=atc[:, ab + 4:ab + 5], in0=atc[:, ab + 11:ab + 12],
                    scalar1=-1.0, scalar2=None, op0=ALU.mult),
                    waits=[("vE", vq)], inc="vE")
                for nt in range(4):
                    b = 4 + nt
                    aq = ctx.emit("scalar", lambda e, nt=nt, b=b, sl=sl, ab=ab:
                                  e.activation(
                                      out=s.attn_sb[sl][:, nt * 512:(nt + 1) * 512],
                                      in_=p.b[b][:], func=AF.Exp,
                                      bias=atc[:, ab + 4:ab + 5],
                                      accum_out=atc[:, ab + 5 + nt:ab + 6 + nt]),
                                  waits=[("vE", vq),
                                         ("tE", qmeta.get(("tr_done", sl), 0))],
                                  inc="aE")
                    ctx.bank_read_done(b, "aE")
                qmeta[("exp", sl)] = ctx.cnt["aE"]
                aq = ctx.cnt["aE"]
                vq = ctx.emit("vector", lambda e, ab=ab: e.reduce_sum(
                    out=atc[:, ab + 9:ab + 10], in_=atc[:, ab + 5:ab + 9],
                    axis=AX.X), waits=[("aE", aq)], inc="vE")
                vq = ctx.emit("vector", lambda e, ab=ab: e.reciprocal(
                    out=atc[:, ab + 10:ab + 11], in_=atc[:, ab + 9:ab + 10]),
                    waits=[("vE", vq)], inc="vE")
                for nt in range(4):
                    ctx.emit("vector", lambda e, nt=nt, sl=sl, ab=ab: e.tensor_scalar(
                        out=s.attn_sb[sl][:, nt * 512:(nt + 1) * 512],
                        in0=s.attn_sb[sl][:, nt * 512:(nt + 1) * 512],
                        scalar1=atc[:, ab + 10:ab + 11], scalar2=None, op0=ALU.mult),
                        waits=[("vE", vq)] if nt == 0 else (), inc="vE")
                qmeta[("attn_v", qt)] = ctx.cnt["vE"]

            def stageB1(qt):
                sl = qt % 2
                attn_v = qmeta[("attn_v", qt)]
                for tc in range(16):
                    b = tc // 8
                    pbf = p.b[b].ap().bitcast(BF16)
                    ctx.emit("tensor", lambda e, tc=tc, b=b, pbf=pbf, sl=sl:
                             e.transpose(pbf[0:128, (tc % 8) * 128:
                                             (tc % 8) * 128 + 128],
                                         s.attn_sb[sl][:, tc * 128:(tc + 1) * 128],
                                         s.eye_bf[0:128, 0:128]),
                             waits=ctx.bank_acquire(b, [("vE", attn_v)])
                             if tc % 8 == 0 else (),
                             inc="tE" if tc % 8 == 7 else None)
                    if tc % 8 == 7:
                        ctx.bank_filled(b)
                for b in range(2):
                    pbf = p.b[b].ap().bitcast(BF16)
                    ctx.emit("scalar", lambda e, b=b, pbf=pbf, sl=sl: e.copy(
                        out=s.attnT_sb[sl][:, b * 1024:(b + 1) * 1024],
                        in_=pbf[0:128, 0:1024]),
                        waits=ctx.bank_read_waits(b), inc="aE")
                    ctx.bank_read_done(b, "aE")
                qmeta[("tr_done", sl)] = ctx.cnt["tE"]
                qmeta[("attnT_a", qt)] = ctx.cnt["aE"]

            def stageB2(qt):
                sl = qt % 2
                attnT_a = qmeta[("attnT_a", qt)]
                for kc2 in range(16):
                    w = ()
                    if kc2 == 0:
                        w = ctx.bank_acquire(2, [("aE", attnT_a)])
                    ctx.emit("tensor", lambda e, kc2=kc2, sl=sl: e.matmul(
                        p.b[2][0:64, 0:128],
                        s.v_tm[:, kc2 * 64:(kc2 + 1) * 64],
                        s.attnT_sb[sl][:, kc2 * 128:(kc2 + 1) * 128],
                        start=(kc2 == 0), stop=(kc2 == 15)), waits=w,
                        inc="tE" if kc2 == 15 else None)
                ctx.bank_filled(2)
                ctx.emit("scalar", lambda e, hh=hh, qt=qt: e.copy(
                    out=s.attnout[:, hh * T + qt * 128: hh * T + (qt + 1) * 128],
                    in_=p.b[2][0:64, 0:128]),
                    waits=ctx.bank_read_waits(2), inc="aE")
                ctx.bank_read_done(2, "aE")

            for qt in range(18):
                if qt < 16:
                    stageA(qt)
                if 1 <= qt < 17:
                    stageB1(qt - 1)
                if qt >= 2:
                    stageB2(qt - 2)
        ao_v = ctx.cnt["vE"]
        ao_a = ctx.cnt["aE"]
        ctx.meta["at_qkv_free"] = ctx.cnt["tE"]
        for mo in range(8):
            for nt in range(4):
                t0 = nt * 512
                b = (mo * 4 + nt) % 2
                for hh in range(4):
                    w = ()
                    if hh == 0:
                        w = ctx.bank_acquire(b, [("vE", ao_v), ("aE", ao_a),
                                                 ("dW", dWp)])
                    ctx.emit("tensor", lambda e, hh=hh, mo=mo, t0=t0, b=b: e.matmul(
                        p.b[b][:],
                        s.wproj_sb[0:64, hh * 1024 + mo * 128:
                                   hh * 1024 + mo * 128 + 128],
                        s.attnout[:, hh * T + t0: hh * T + t0 + 512],
                        start=(hh == 0), stop=(hh == 3)), waits=w,
                        inc="tE" if hh == 3 else None)
                ctx.bank_filled(b)
                dst = s.x_fm[:, mo * T + t0: mo * T + t0 + 512]
                ctx.emit("vector", lambda e, b=b, dst=dst: e.tensor_add(
                    out=dst, in0=p.b[b][:], in1=dst),
                    waits=ctx.bank_read_waits(b), inc="vE")
                if hg == 0:
                    ctx.emit("vector", lambda e, mo=mo, dst=dst: e.tensor_scalar(
                        out=dst, in0=dst, scalar1=s.bp_sb[:, mo:mo + 1],
                        scalar2=None, op0=ALU.add), inc="vE")
                ctx.bank_read_done(b, "vE")
        ctx.meta["at_wproj_free"] = ctx.cnt["tE"]
    ctx.meta["xfm_vE"] = ctx.cnt["vE"]


def emit_lff(ctx, t, s, p):
    ln_v = ctx.meta["ln_vE"]
    ent_t, ent_v, ent_a = ctx.cnt["tE"], ctx.cnt["vE"], ctx.cnt.get("aE", 0)
    bar = [("tE", ent_t), ("vE", ent_v), ("aE", ent_a)]
    for dst, srcap in ((s.ld_sb, t.lora_down), (s.lu_sb, t.lora_up),
                       (s.upxt_sb, t.upxt), (s.dnaux_sb, t.down_aux),
                       (s.bl_sb, t.bl_cm)):
        ctx.emit("sync", lambda e, a=dst, b=srcap: e.dma_start(
            out=a if isinstance(a, bass.AP) else a[:], in_=b),
            waits=bar, inc="dS", inc_by=16)
    ctx.emit("gpsimd", lambda e: e.dma_start(
        out=s.l1w_bf.rearrange("p (kc o) -> p kc o", kc=8),
        in_=t.l1wt.rearrange("(kc p) o -> p kc o", p=128)),
        waits=bar, inc="dW", inc_by=16)
    dS0, dW0 = ctx.cnt["dS"], ctx.cnt["dW"]
    ctx.emit("vector", lambda e: e.tensor_scalar(
        out=s.hbl_sb[:], in0=s.bl_sb[:], scalar1=0.5, scalar2=None, op0=ALU.mult),
        waits=[("dS", dS0)] + bar, inc="vE")
    for n2 in range(2):
        ctx.emit("tensor", lambda e, n2=n2: e.matmul(
            p.b[2][0:4, :], s.lu_sb, s.upxt_sb[:, n2 * 512:(n2 + 1) * 512],
            start=True, stop=True),
            waits=ctx.bank_acquire(2, [("dS", dS0)]), inc="tE")
        ctx.bank_filled(2)
        vq = ctx.emit("vector", lambda e, n2=n2: e.tensor_copy(
            out=s.p1_sb[:, n2 * 512:(n2 + 1) * 512], in_=p.b[2][0:4, :]),
            waits=ctx.bank_read_waits(2), inc="vE")
        ctx.bank_read_done(2, "vE")
    for n2 in range(2):
        ctx.emit("tensor", lambda e, n2=n2: e.matmul(
            p.b[3][0:100, :], s.ld_sb, s.p1_sb[:, n2 * 512:(n2 + 1) * 512],
            start=True, stop=True),
            waits=ctx.bank_acquire(3, [("vE", vq)]), inc="tE")
        ctx.bank_filled(3)
        vq2 = ctx.emit("vector", lambda e, n2=n2: e.tensor_copy(
            out=s.p2_sb[:, n2 * 512:(n2 + 1) * 512], in_=p.b[3][0:100, :]),
            waits=ctx.bank_read_waits(3), inc="vE")
        ctx.bank_read_done(3, "vE")
    for kc in range(8):
        for n2 in range(2):
            b = 2 + (kc * 2 + n2) % 2
            ctx.emit("tensor", lambda e, kc=kc, n2=n2, b=b: e.matmul(
                p.b[b][:], s.dnaux_sb[:, kc * 128:(kc + 1) * 128],
                s.p2_sb[:, n2 * 512:(n2 + 1) * 512], start=True, stop=True),
                waits=ctx.bank_acquire(b, [("vE", vq2)]), inc="tE")
            ctx.bank_filled(b)
            ctx.emit("vector", lambda e, kc=kc, n2=n2, b=b: e.tensor_add(
                out=s.weff[:, kc * 1024 + n2 * 512: kc * 1024 + n2 * 512 + 512],
                in0=p.b[b][:],
                in1=s.l1w_bf[:, kc * 1024 + n2 * 512: kc * 1024 + n2 * 512 + 512]),
                waits=ctx.bank_read_waits(b, [("dW", dW0)]), inc="vE")
            ctx.bank_read_done(b, "vE")
    weff_v = ctx.cnt["vE"]
    for mo in range(8):
        for nt in range(4):
            t0 = nt * 512
            b = (mo * 4 + nt) % 2
            for kc in range(8):
                w = ()
                if kc == 0:
                    w = ctx.bank_acquire(b, [("vE", max(ln_v, weff_v))])
                ctx.emit("tensor", lambda e, mo=mo, kc=kc, t0=t0, b=b: e.matmul(
                    p.b[b][:],
                    s.weff[:, kc * 1024 + mo * 128: kc * 1024 + mo * 128 + 128],
                    s.x_ln[:, kc * T + t0: kc * T + t0 + 512],
                    start=(kc == 0), stop=(kc == 7)),
                    waits=w, inc="tE" if kc == 7 else None)
            ctx.bank_filled(b)
            dst = s.x_fm[:, mo * T + t0:mo * T + t0 + 512]
            ctx.emit("vector", lambda e, b=b, dst=dst: e.scalar_tensor_tensor(
                out=dst, in0=p.b[b][:], scalar=0.5, in1=dst,
                op0=ALU.mult, op1=ALU.add),
                waits=ctx.bank_read_waits(b), inc="vE")
            ctx.emit("vector", lambda e, mo=mo, dst=dst: e.tensor_scalar(
                out=dst, in0=dst, scalar1=s.hbl_sb[:, mo:mo + 1], scalar2=None,
                op0=ALU.add), inc="vE")
            ctx.bank_read_done(b, "vE")
    ctx.meta["xfm_vE"] = ctx.cnt["vE"]


def build_program(phases, taps=()):
    phi = 0
    nc = bass.Bass("TRN2", target_bir_lowering=False, debug=False)
    t = declare_tensors(nc, phases)
    s = alloc_sbuf(nc, phases)
    p = alloc_psum(nc)
    ctx = Ctx(nc)
    emit_load_x(ctx, t, s)
    emit_load_consts(ctx, t, s, phases)
    for ph in phases:
        emit_ln(ctx, t, s, p)
        if "ln" in taps:
            emit_dump(ctx, nc, "d_xln", s.x_ln[:], NCH * T, BF16)
        if ph == "moe":
            emit_router(ctx, t, s, p)
            if "comb" in taps:
                emit_dump(ctx, nc, "d_comb", s.comb_em[:], 2 * T, BF16)
            if "scr" in taps:
                emit_dump(ctx, nc, "d_scr", s.ar_f32[:, 12288:20480], 8192, F32)
            if "stop_before_moe" in taps:
                break
            emit_moe(ctx, t, s, p)
        elif ph == "mamba":
            emit_mamba(ctx, t, s, p)
        elif ph == "lff":
            emit_lff(ctx, t, s, p)
        elif ph == "attn":
            emit_attn(ctx, t, s, p)
            if "attn_dbg" in taps:
                emit_dump(ctx, nc, "d_qkv", s.qkv_g[:], 6 * T, BF16)
                emit_dump(ctx, nc, "d_ao", s.attnout, 4 * T, BF16, rows=64)
                emit_dump(ctx, nc, "d_attn", s.attn_sb[1][:], T, BF16)
        phi += 1
        if "stream" in taps:
            emit_dump(ctx, nc, "d_ph%d" % phi, s.x_fm[:], NCH * T, F32)
    emit_store_x(ctx, t, s)
    ctx.run()
    return nc



def _consts(phases):
    c = {}
    c["c_eye_bf"] = np.eye(128, dtype=BF)
    c["c_eye_f32"] = np.eye(128, dtype=np.float32)
    c["c_ones_bf"] = np.ones((128, 128), dtype=BF)
    c["c_ones_f32"] = np.ones((1, 128), dtype=np.float32)
    c["c_onescol_bf"] = np.ones((128, 1), dtype=BF)
    if "mamba" in phases:
        sidx = np.arange(128)[:, None]
        tidx = np.arange(128)[None, :]
        c["c_tri"] = (sidx <= tidx).astype(np.float32)
        c["c_minf"] = np.where(tidx >= sidx, 0.0, -1e30).astype(np.float32)
        invf = (1.0 / 10000.0 ** (np.arange(16, dtype=np.float64) / 16.0))
        c["c_invfreq"] = np.tile(invf, 32)[None, :].astype(np.float32)
    if "attn" in phases:
        v = np.arange(4224)[None, :] - 2048 - np.arange(128)[:, None]
        f = np.where(v > 0, -np.log1p(np.maximum(v, 0).astype(np.float64)), 0.0)
        f = np.where(v < 0, -1e30, f)
        c["c_maskband"] = f.astype(BF)
    return c


def _cm(vec, ncols):
    v = np.zeros(128 * ncols, dtype=np.float32)
    v[:len(vec)] = np.asarray(vec, dtype=np.float32)
    return np.ascontiguousarray(v.reshape(ncols, 128).T)


def _layout_weights(inp, phases):
    w = {}
    f32c = lambda a: np.ascontiguousarray(np.asarray(a, dtype=np.float32))
    if "moe" in phases:
        w["wgt"] = f32c(np.asarray(inp["L0_ffn_w_gate"]).transpose(0, 2, 1))
        w["wdt"] = f32c(np.asarray(inp["L0_ffn_w_down"]).transpose(0, 2, 1))
        w["wgst"] = f32c(np.asarray(inp["L0_ffn_w_gate_shared"]).T)
        w["wdst"] = f32c(np.asarray(inp["L0_ffn_w_down_shared"]).T)
        w["rwt"] = f32c(np.asarray(inp["L0_ffn_router_w"]).T)
        w["eb_cm"] = _cm(inp["L0_ffn_expert_bias"], 2)
    if "mamba" in phases:
        w["w_in"] = f32c(inp["L0_attn_W_in"])
        w["w_out"] = f32c(inp["L0_attn_W_out"])
        w["dt_bias"] = f32c(np.asarray(inp["L0_attn_dt_bias"]).reshape(1, 32))
        w["a_log"] = f32c(np.asarray(inp["L0_attn_A_log"]).reshape(1, 32))
        w["dskip_rep"] = f32c(np.broadcast_to(
            np.asarray(inp["L0_attn_D_skip"])[None, :], (128, 32)))
        w["nw_cm"] = _cm(inp["L0_attn_norm_weight"], 16)
    if "attn" in phases:
        w["w_attn"] = f32c(inp["L1_attn_w_attn"])
        w["w_proj"] = f32c(inp["L1_attn_w_proj"])
        w["ba_cm"] = _cm(inp["L1_attn_b_attn"], 24)
        w["bp_cm"] = _cm(inp["L1_attn_b_proj"], 8)
    if "lff" in phases:
        w["l1wt"] = f32c(np.asarray(inp["L1_ffn_W"]).T)
        w["lora_down"] = f32c(inp["L1_ffn_lora_down"])
        w["lora_up"] = f32c(inp["L1_ffn_lora_up"])
        w["down_aux"] = f32c(inp["L1_ffn_down_aux"])
        w["upxt"] = f32c(np.asarray(inp["L1_ffn_up_aux"]).T)
        w["bl_cm"] = _cm(inp["L1_ffn_b"], 8)
    return w


def run_phases(phases, xs, weights):
    nc = build_program(phases)
    consts = _consts(phases)
    in_maps = []
    for b in range(8):
        im = {"xin": xs[b]}
        im.update(consts)
        im.update(weights)
        in_maps.append(im)
    res = run_bass_kernel_spmd(nc, in_maps, core_ids=list(range(8)))
    return [np.asarray(res.results[b]["xout"]) for b in range(8)]


def kernel(**inputs):
    x = np.asarray(inputs["L0_attn_x"])
    xs = [np.ascontiguousarray(np.asarray(x[b], dtype=np.float32).T) for b in range(8)]
    for phases in PHASE_GROUPS:
        weights = _layout_weights(inputs, phases)
        xs = run_phases(phases, xs, weights)
    out = np.stack([xb.T for xb in xs])
    return np.ascontiguousarray(out).astype(np.float32)
